# version 01
# baseline (speedup 1.0000x reference)
import contextlib
import numpy as np
import ml_dtypes
import concourse.bass as bass
import concourse.mybir as mybir
from concourse.bass_utils import run_bass_kernel_spmd
from concourse.bass import IndirectOffsetOnAxis

F32 = mybir.dt.float32
BF16 = mybir.dt.bfloat16
U32 = mybir.dt.uint32
AF = mybir.ActivationFunctionType
OP = mybir.AluOpType
AX = mybir.AxisListType.X
BF = ml_dtypes.bfloat16

NCORE, NB, T, D = 8, 2, 16384, 1024
TC = 4096
CPB = 4
LN_EPS = 1e-5
SEG_LIM = 12000

_NC_CACHE = {}


class Prog:

    def __init__(self, nc):
        self.nc = nc
        self.steps = []
        self.ctx = contextlib.ExitStack()

    def sb(self, name, shape, dtype):
        return self.ctx.enter_context(self.nc.sbuf_tensor(name, shape, dtype))

    def ps(self, name, shape, dtype=F32):
        return self.ctx.enter_context(self.nc.psum_tensor(name, shape, dtype))

    def step(self, eng, fn, dmas=0):
        self.steps.append((eng, fn, 16 * dmas if dmas else 1))

    def build(self):
        nc = self.nc
        segs = []
        cur = 0
        seg_of, base_of = [], []
        for (_, _, inc) in self.steps:
            if cur + inc > SEG_LIM:
                segs.append(cur)
                cur = 0
            seg_of.append(len(segs))
            base_of.append(cur)
            cur += inc
        segs.append(cur)
        sems = [self.ctx.enter_context(nc.semaphore("ch%d" % i))
                for i in range(len(segs))]
        with nc.Block() as block:
            def body_for(name):
                def body(e):
                    seen_seg = set()
                    for i, (en, fn, inc) in enumerate(self.steps):
                        if en != name:
                            continue
                        s = seg_of[i]
                        if s > 0 and s not in seen_seg:
                            e.wait_ge(sems[s - 1], segs[s - 1])
                        seen_seg.add(s)
                        if base_of[i] > 0:
                            e.wait_ge(sems[s], base_of[i])
                        out = fn(e)
                        if inc == 1:
                            out.then_inc(sems[s], 1)
                        else:
                            insts = out if isinstance(out, (list, tuple)) else [out]
                            assert len(insts) * 16 == inc, (len(insts), inc)
                            for it in insts:
                                it.then_inc(sems[s], 16)
                    if name == "sync":
                        if len(segs) > 1:
                            e.wait_ge(sems[len(segs) - 2], segs[-2])
                        e.wait_ge(sems[len(segs) - 1], segs[-1])
                return body
            block.sync(body_for("sync"))
            block.vector(body_for("vector"))
            block.scalar(body_for("scalar"))
            block.tensor(body_for("tensor"))
            block.gpsimd(body_for("gpsimd"))
        self.ctx.close()
        return nc



def host_consts():
    c = {}
    c["ident"] = np.eye(128, dtype=np.float32).astype(BF)
    c["triu"] = np.triu(np.ones((128, 128), np.float32)).astype(BF)
    c["ones_col"] = np.ones((128, 1), np.float32).astype(BF)
    c["ones_row"] = np.ones((1, 128), np.float32).astype(BF)
    return c


def emit_ln(p, xt, xn, stats, sq, d=D):
    p.step("vector", lambda e: e.reduce_sum(stats[:, 0:1], xt[:], axis=AX))
    p.step("scalar", lambda e: e.activation(out=sq[:], in_=xt[:], func=AF.Square,
                                            accum_out=stats[:, 1:2]))
    p.step("vector", lambda e: e.tensor_scalar(out=stats[:, 2:3], in0=stats[:, 0:1],
                                               scalar1=1.0 / d, scalar2=None, op0=OP.mult))
    p.step("vector", lambda e: e.tensor_scalar(out=stats[:, 3:4], in0=stats[:, 1:2],
                                               scalar1=1.0 / d, scalar2=None, op0=OP.mult))
    p.step("vector", lambda e: e.tensor_tensor(out=stats[:, 4:5], in0=stats[:, 2:3],
                                               in1=stats[:, 2:3], op=OP.mult))
    p.step("vector", lambda e: e.tensor_tensor(out=stats[:, 5:6], in0=stats[:, 3:4],
                                               in1=stats[:, 4:5], op=OP.subtract))
    p.step("vector", lambda e: e.tensor_scalar(out=stats[:, 5:6], in0=stats[:, 5:6],
                                               scalar1=LN_EPS, scalar2=None, op0=OP.add))
    p.step("scalar", lambda e: e.activation(out=stats[:, 6:7], in_=stats[:, 5:6], func=AF.Sqrt))
    p.step("vector", lambda e: e.reciprocal(stats[:, 7:8], stats[:, 6:7]))
    p.step("vector", lambda e: e.tensor_scalar(out=xn[:], in0=xt[:], scalar1=stats[:, 2:3],
                                               scalar2=stats[:, 7:8], op0=OP.subtract,
                                               op1=OP.mult))


def emit_tp(p, src, dst_fn, pt, idt, n=8):
    for j in range(n):
        def ftp(e, j=j):
            return e.transpose(pt[:], src[:, j * 128:(j + 1) * 128], idt[:])
        p.step("tensor", ftp)
        def fcp(e, j=j):
            return e.tensor_copy(dst_fn(j), pt[:])
        p.step("vector", fcp)



def build_qkv():
    nc = bass.Bass("TRN2", target_bir_lowering=False)
    x = nc.dram_tensor("x", [TC, D], F32, kind="ExternalInput").ap()
    wq = nc.dram_tensor("wq", [D, D], F32, kind="ExternalInput").ap()
    wk = nc.dram_tensor("wk", [D, D], F32, kind="ExternalInput").ap()
    wv = nc.dram_tensor("wv", [D, D], F32, kind="ExternalInput").ap()
    ident = nc.dram_tensor("ident", [128, 128], BF16, kind="ExternalInput").ap()
    qf = nc.dram_tensor("qf", [D, TC], BF16, kind="ExternalOutput").ap()
    kf = nc.dram_tensor("kf", [D, TC], BF16, kind="ExternalOutput").ap()
    vt = nc.dram_tensor("vt", [TC, D], BF16, kind="ExternalOutput").ap()

    p = Prog(nc)
    wqt = p.sb("wqt", [128, 8 * D], BF16)
    wkt = p.sb("wkt", [128, 8 * D], BF16)
    wvt = p.sb("wvt", [128, 8 * D], BF16)
    idt = p.sb("idt", [128, 128], BF16)
    xt = p.sb("xt", [128, D], F32)
    sq = p.sb("sq", [128, D], F32)
    stats = p.sb("stats", [128, 8], F32)
    xn = p.sb("xn", [128, D], BF16)
    xfm = p.sb("xfm", [128, 8 * 512], BF16)
    ev = p.sb("ev", [128, 512], BF16)
    pt = p.ps("pt", [128, 128], BF16)
    pm = p.ps("pm", [128, 512], F32)

    def ldw(e):
        out = [e.dma_start(out=dst[:, i * D:(i + 1) * D], in_=src[i * 128:(i + 1) * 128, :])
               for dst, src in ((wqt, wq), (wkt, wk), (wvt, wv)) for i in range(8)]
        return out
    p.step("gpsimd", ldw, dmas=24)
    p.step("sync", lambda e: e.dma_start(out=idt[:], in_=ident[:, :]), dmas=1)

    for bi in range(8):
        for tt in range(4):
            tok0 = bi * 512 + tt * 128
            p.step("sync", lambda e, tok0=tok0: e.dma_start(out=xt[:], in_=x[tok0:tok0 + 128, :]), dmas=1)
            emit_ln(p, xt, xn, stats, sq)
            emit_tp(p, xn, (lambda j, tt=tt: xfm[:, j * 512 + tt * 128: j * 512 + tt * 128 + 128]),
                    pt, idt)
        for wt, out_d, scale in ((wqt, qf, 1.0 / 32), (wkt, kf, None)):
            for m in range(8):
                def fmm(e, wt=wt, m=m):
                    for i in range(8):
                        r = e.matmul(pm[:], wt[:, i * D + m * 128: i * D + m * 128 + 128],
                                     xfm[:, i * 512:(i + 1) * 512],
                                     start=(i == 0), stop=(i == 7))
                    return r
                p.step("tensor", fmm)
                if scale is None:
                    p.step("vector", lambda e: e.tensor_copy(ev[:], pm[:]))
                else:
                    p.step("vector", lambda e, s=scale: e.tensor_scalar(
                        out=ev[:], in0=pm[:], scalar1=s, scalar2=None, op0=OP.mult))
                p.step("sync", lambda e, out_d=out_d, m=m, bi=bi: e.dma_start(
                    out=out_d[m * 128:(m + 1) * 128, bi * 512:(bi + 1) * 512], in_=ev[:]), dmas=1)
        for tt in range(4):
            for ch in range(2):
                def fmv(e, tt=tt, ch=ch):
                    for i in range(8):
                        r = e.matmul(pm[:], xfm[:, i * 512 + tt * 128: i * 512 + tt * 128 + 128],
                                     wvt[:, i * D + ch * 512: i * D + ch * 512 + 512],
                                     start=(i == 0), stop=(i == 7))
                    return r
                p.step("tensor", fmv)
                p.step("vector", lambda e: e.tensor_copy(ev[:], pm[:]))
                p.step("sync", lambda e, tt=tt, ch=ch, bi=bi: e.dma_start(
                    out=vt[bi * 512 + tt * 128: bi * 512 + tt * 128 + 128,
                           ch * 512:(ch + 1) * 512], in_=ev[:]), dmas=1)
    return p.build()



def emit_ffnpre(p, t, xt, tok):
    emit_ln(p, xt, t["xn"], t["stats2"], t["sq"])
    emit_tp(p, t["xn"], (lambda j: t["xfm1"][:, j * 128:(j + 1) * 128]),
            t["pt1"], t["idt"])

    def fgmm(e):
        for i in range(8):
            r = e.matmul(t["pg"][:], t["xfm1"][:, i * 128:(i + 1) * 128],
                         t["gwt"][:, i * 8:(i + 1) * 8],
                         start=(i == 0), stop=(i == 7))
        return r
    p.step("tensor", fgmm)
    p.step("vector", lambda e: e.tensor_copy(t["sc8"][:], t["pg"][:]))
    p.step("sync", lambda e, tok=tok: [
        e.dma_start(out=t["d_xn1"][tok:tok + 128, :], in_=t["xn"][:]),
        e.dma_start(out=t["d_scores"][tok:tok + 128, :], in_=t["sc8"][:])], dmas=2)


def build_flash(NKB=T // 512, NSUB=16, NG=2):
    GQ = 2048
    nc = bass.Bass("TRN2", target_bir_lowering=False)
    x = nc.dram_tensor("x", [TC, D], F32, kind="ExternalInput").ap()
    qf = nc.dram_tensor("qf", [D, TC], BF16, kind="ExternalInput").ap()
    kff = nc.dram_tensor("kff", [D, T], BF16, kind="ExternalInput").ap()
    vtf = nc.dram_tensor("vtf", [T, D], BF16, kind="ExternalInput").ap()
    gw = nc.dram_tensor("gw", [D, 8], F32, kind="ExternalInput").ap()
    qpos = nc.dram_tensor("qpos", [128, TC // 128], F32, kind="ExternalInput").ap()
    kpos = nc.dram_tensor("kpos", [1, T], F32, kind="ExternalInput").ap()
    ident = nc.dram_tensor("ident", [128, 128], BF16, kind="ExternalInput").ap()
    xout = nc.dram_tensor("xout", [TC, D], F32, kind="ExternalOutput").ap()
    d_xn1 = nc.dram_tensor("xn1", [TC, D], BF16, kind="ExternalOutput").ap()
    d_scores = nc.dram_tensor("scores", [TC, 8], F32, kind="ExternalOutput").ap()

    p = Prog(nc)
    t = {}
    qfm = p.sb("qfm", [128, 8 * GQ], BF16)
    kt = p.sb("kt", [128, 8 * 512], BF16)
    vtt = p.sb("vtt", [128, 4 * D], BF16)
    qpt = p.sb("qpt", [128, TC // 128], F32)
    kpr = p.sb("kpr", [128, 512], F32)
    cmpt = p.sb("cmpt", [128, 512], F32)
    nmw = p.sb("nmw", [128, 1], F32)
    ssb = p.sb("ssb", [128, 512], F32)
    pb = p.sb("pb", [128, 512], BF16)
    pts = p.sb("pts", [128, 512], BF16)
    stf = p.sb("stf", [128, 3 * NSUB], F32)
    oacc = [p.sb("oacc%d" % s, [128, D], F32) for s in range(NSUB)]
    t["idt"] = p.sb("idt", [128, 128], BF16)
    t["gwt"] = p.sb("gwt", [128, 8 * 8], BF16)
    xt = p.sb("xt", [128, D], F32)
    t["xn"] = p.sb("xn", [128, D], BF16)
    t["sq"] = p.sb("sqs", [128, D], F32)
    t["stats2"] = p.sb("stats2", [128, 8], F32)
    t["xfm1"] = p.sb("xfm1", [128, 8 * 128], BF16)
    t["sc8"] = p.sb("sc8", [128, 8], F32)
    t["d_xn1"], t["d_scores"] = d_xn1, d_scores
    kpow = p.sb("kpow", [1, T], F32)
    onebf = p.sb("onebf", [1, 128], F32)
    pm = p.ps("pm", [128, 512], F32)
    pt4 = p.ps("pt4", [128, 512], BF16)
    pav = p.ps("pav", [128, D], F32)
    pkp = p.ps("pkp", [128, 512], F32)
    t["pt1"] = p.ps("pt1", [128, 128], BF16)
    t["pg"] = p.ps("pg", [128, 8], F32)

    p.step("sync", lambda e: [e.dma_start(out=t["idt"][:], in_=ident[:, :]),
                              e.dma_start(out=qpt[:], in_=qpos[:, :]),
                              e.dma_start(out=kpow[:], in_=kpos[:, :])], dmas=3)
    p.step("gpsimd", lambda e: [e.dma_start(out=t["gwt"][:, i * 8:(i + 1) * 8],
                                            in_=gw[i * 128:(i + 1) * 128, :])
                                for i in range(8)], dmas=8)
    p.step("vector", lambda e: e.memset(onebf[:], 1.0))

    def s_mm(e, s):
        for i in range(8):
            r = e.matmul(pm[:], qfm[:, i * GQ + s * 128: i * GQ + s * 128 + 128],
                         kt[:, i * 512:(i + 1) * 512],
                         start=(i == 0), stop=(i == 7))
        return r

    for g in range(NG):
        q0 = g * GQ
        p.step("sync", lambda e, q0=q0: e.dma_start(
            out=qfm[:].rearrange("p (a t) -> p a t", a=8),
            in_=qf[:, q0:q0 + GQ].rearrange("(a p) t -> p a t", p=128)), dmas=1)
        p.step("vector", lambda e: e.memset(stf[:], 1e30))
        for kb in range(NKB):
            k0 = kb * 512
            p.step("sync", lambda e, k0=k0: e.dma_start(
                out=kt[:].rearrange("p (a t) -> p a t", a=8),
                in_=kff[:, k0:k0 + 512].rearrange("(a p) t -> p a t", p=128)), dmas=1)
            p.step("tensor", lambda e, k0=k0: e.matmul(
                pkp[:], onebf[:], kpow[:, k0:k0 + 512], start=True, stop=True))
            p.step("vector", lambda e: e.tensor_copy(kpr[:], pkp[:]))
            for s in range(NSUB):
                p.step("tensor", lambda e, s=s: s_mm(e, s))
                p.step("vector", lambda e, s=s, q0=q0: e.tensor_scalar(
                    out=cmpt[:], in0=kpr[:],
                    scalar1=qpt[:, q0 // 128 + s:q0 // 128 + s + 1],
                    scalar2=None, op0=OP.is_gt))
                p.step("vector", lambda e: e.scalar_tensor_tensor(
                    out=ssb[:], in0=cmpt[:], scalar=-1e30, in1=pm[:],
                    op0=OP.mult, op1=OP.add))
                p.step("vector", lambda e: e.reduce_max(nmw[:, 0:1], ssb[:], axis=AX,
                                                        negate=True))
                p.step("vector", lambda e, s=s: e.tensor_tensor(
                    out=stf[:, 3 * s:3 * s + 1], in0=stf[:, 3 * s:3 * s + 1],
                    in1=nmw[:, 0:1], op=OP.min))
        def fz(e):
            for s in range(NSUB):
                r = e.memset(oacc[s][:], 0.0)
            return r
        p.step("vector", fz)
        def fz2(e):
            for s in range(NSUB):
                r = e.memset(stf[:, 3 * s + 1:3 * s + 2], 0.0)
            return r
        p.step("vector", fz2)
        for kb in range(NKB):
            k0 = kb * 512
            p.step("sync", lambda e, k0=k0: [
                e.dma_start(out=kt[:].rearrange("p (a t) -> p a t", a=8),
                            in_=kff[:, k0:k0 + 512].rearrange("(a p) t -> p a t", p=128)),
                e.dma_start(out=vtt[:].rearrange("p (a d) -> p a d", a=4),
                            in_=vtf[k0:k0 + 512, :].rearrange("(a p) d -> p a d", p=128))],
                dmas=2)
            p.step("tensor", lambda e, k0=k0: e.matmul(
                pkp[:], onebf[:], kpow[:, k0:k0 + 512], start=True, stop=True))
            p.step("vector", lambda e: e.tensor_copy(kpr[:], pkp[:]))
            for s in range(NSUB):
                p.step("tensor", lambda e, s=s: s_mm(e, s))
                p.step("vector", lambda e, s=s, q0=q0: e.tensor_scalar(
                    out=cmpt[:], in0=kpr[:],
                    scalar1=qpt[:, q0 // 128 + s:q0 // 128 + s + 1],
                    scalar2=None, op0=OP.is_gt))
                p.step("vector", lambda e: e.scalar_tensor_tensor(
                    out=ssb[:], in0=cmpt[:], scalar=-1e30, in1=pm[:],
                    op0=OP.mult, op1=OP.add))
                p.step("scalar", lambda e, s=s: e.activation(
                    out=pb[:], in_=ssb[:], func=AF.Exp, bias=stf[:, 3 * s:3 * s + 1],
                    accum_out=stf[:, 3 * s + 2:3 * s + 3]))
                p.step("vector", lambda e, s=s: e.tensor_tensor(
                    out=stf[:, 3 * s + 1:3 * s + 2], in0=stf[:, 3 * s + 1:3 * s + 2],
                    in1=stf[:, 3 * s + 2:3 * s + 3], op=OP.add))
                def ftp(e):
                    for j in range(4):
                        r = e.transpose(pt4[:, j * 128:(j + 1) * 128],
                                        pb[:, j * 128:(j + 1) * 128], t["idt"][:])
                    return r
                p.step("tensor", ftp)
                p.step("vector", lambda e: e.tensor_copy(pts[:], pt4[:]))
                def fav(e):
                    for ch in range(2):
                        for j in range(4):
                            r = e.matmul(pav[:, ch * 512:(ch + 1) * 512],
                                         pts[:, j * 128:(j + 1) * 128],
                                         vtt[:, j * D + ch * 512: j * D + ch * 512 + 512],
                                         start=(j == 0), stop=(j == 3))
                    return r
                p.step("tensor", fav)
                p.step("vector", lambda e, s=s: e.tensor_tensor(
                    out=oacc[s][:], in0=oacc[s][:], in1=pav[:], op=OP.add))
        for s in range(NSUB):
            tok = q0 + s * 128
            p.step("sync", lambda e, tok=tok: e.dma_start(out=xt[:], in_=x[tok:tok + 128, :]), dmas=1)
            p.step("vector", lambda e, s=s: e.reciprocal(stf[:, 3 * s + 2:3 * s + 3],
                                                         stf[:, 3 * s + 1:3 * s + 2]))
            p.step("vector", lambda e, s=s: e.tensor_scalar(
                out=oacc[s][:], in0=oacc[s][:], scalar1=stf[:, 3 * s + 2:3 * s + 3],
                scalar2=None, op0=OP.mult))
            p.step("vector", lambda e, s=s: e.tensor_tensor(
                out=xt[:], in0=xt[:], in1=oacc[s][:], op=OP.add))
            p.step("sync", lambda e, tok=tok: e.dma_start(out=xout[tok:tok + 128, :], in_=xt[:]), dmas=1)
            emit_ffnpre(p, t, xt, tok)
    return p.build()



CAP = 4096
BIGOFF = 1.0e6


def emit_softmax8(p, st3, mx, es, probs, d2=128):
    p.step("vector", lambda e: e.tensor_reduce(mx[:].rearrange("p (a o) -> p a o", o=1),
                                               st3, axis=AX, op=OP.max))
    p.step("vector", lambda e: e.tensor_tensor(
        out=es[:].rearrange("p (a k) -> p a k", k=8), in0=st3,
        in1=mx[:].rearrange("p (a o) -> p a o", o=1).to_broadcast([128, d2, 8]),
        op=OP.subtract))
    p.step("scalar", lambda e: e.activation(out=es[:], in_=es[:], func=AF.Exp))
    p.step("vector", lambda e: e.tensor_reduce(mx[:].rearrange("p (a o) -> p a o", o=1),
                                               es[:].rearrange("p (a k) -> p a k", k=8),
                                               axis=AX, op=OP.add))
    p.step("vector", lambda e: e.reciprocal(mx[:], mx[:]))
    p.step("vector", lambda e: e.tensor_tensor(
        out=probs[:].rearrange("p (a k) -> p a k", k=8),
        in0=es[:].rearrange("p (a k) -> p a k", k=8),
        in1=mx[:].rearrange("p (a o) -> p a o", o=1).to_broadcast([128, d2, 8]),
        op=OP.mult))


def emit_bisect(p, pe, lo, hi, th, cnt, rcol, allo, pbt, ncols, iters=46):
    ngrp = ncols // 128
    p.step("vector", lambda e: e.memset(lo[:], 0.0))
    p.step("vector", lambda e: e.memset(hi[:], 1.0))
    pe3 = pe[:].rearrange("p (g a) -> p g a", a=128)
    for _ in range(iters):
        p.step("vector", lambda e: e.tensor_tensor(out=th[:], in0=lo[:], in1=hi[:], op=OP.add))
        p.step("vector", lambda e: e.tensor_scalar(out=th[:], in0=th[:], scalar1=0.5,
                                                   scalar2=None, op0=OP.mult))
        p.step("vector", lambda e: e.tensor_tensor(
            out=cnt[:].rearrange("p (g a) -> p g a", a=128), in0=pe3,
            in1=th[:].rearrange("p (g o) -> p g o", o=1).to_broadcast([128, ngrp, 128]),
            op=OP.is_gt))
        p.step("vector", lambda e: e.tensor_reduce(
            rcol[:].rearrange("p (g o) -> p g o", o=1),
            cnt[:].rearrange("p (g a) -> p g a", a=128), axis=AX, op=OP.add))
        p.step("tensor", lambda e: e.matmul(pbt[:], allo[:], rcol[:], start=True, stop=True))
        p.step("vector", lambda e: e.tensor_scalar(out=rcol[:], in0=pbt[:], scalar1=float(CAP),
                                                   scalar2=None, op0=OP.is_ge))
        p.step("vector", lambda e: e.tensor_tensor(out=cnt[:, 0:lo.shape[1]], in0=th[:],
                                                   in1=lo[:], op=OP.subtract))
        p.step("vector", lambda e: e.tensor_tensor(out=cnt[:, 0:lo.shape[1]],
                                                   in0=cnt[:, 0:lo.shape[1]], in1=rcol[:],
                                                   op=OP.mult))
        p.step("vector", lambda e: e.tensor_tensor(out=lo[:], in0=lo[:],
                                                   in1=cnt[:, 0:lo.shape[1]], op=OP.add))
        p.step("vector", lambda e: e.tensor_tensor(out=cnt[:, 0:lo.shape[1]], in0=hi[:],
                                                   in1=th[:], op=OP.subtract))
        p.step("vector", lambda e: e.tensor_tensor(out=cnt[:, 0:lo.shape[1]],
                                                   in0=cnt[:, 0:lo.shape[1]], in1=rcol[:],
                                                   op=OP.mult))
        p.step("vector", lambda e: e.tensor_tensor(out=hi[:], in0=th[:],
                                                   in1=cnt[:, 0:lo.shape[1]], op=OP.add))



def build_moedense(tail="none"):
    nc = bass.Bass("TRN2", target_bir_lowering=False)
    x = nc.dram_tensor("x", [TC, D], F32, kind="ExternalInput").ap()
    xnd = nc.dram_tensor("xn", [TC, D], BF16, kind="ExternalInput").ap()
    scof = nc.dram_tensor("scoresf", [T, 8], F32, kind="ExternalInput").ap()
    scom = nc.dram_tensor("scoresm", [TC, 8], F32, kind="ExternalInput").ap()
    w1 = nc.dram_tensor("w1", [8, D, 4096], F32, kind="ExternalInput").ap()
    w2 = nc.dram_tensor("w2", [8, 4096, D], F32, kind="ExternalInput").ap()
    ident = nc.dram_tensor("ident", [128, 128], BF16, kind="ExternalInput").ap()
    allones = nc.dram_tensor("allones", [128, 128], F32, kind="ExternalInput").ap()
    xout = nc.dram_tensor("xout", [TC, D], F32, kind="ExternalOutput").ap()
    acc = nc.dram_tensor("acc", [TC, D], F32, kind="Internal").ap()
    if tail == "s4pre":
        winT = nc.dram_tensor("winT", [D, D], F32, kind="ExternalInput").ap()
        b_in = nc.dram_tensor("b_in", [1, D], F32, kind="ExternalInput").ap()
        gam = nc.dram_tensor("gam", [1, D], F32, kind="ExternalInput").ap()
        bet = nc.dram_tensor("bet", [1, D], F32, kind="ExternalInput").ap()
        onesrowf = nc.dram_tensor("onesrowf", [1, 128], F32, kind="ExternalInput").ap()
        d_u2 = nc.dram_tensor("u2", [TC, D], F32, kind="ExternalOutput").ap()
        d_cs = nc.dram_tensor("colsum", [1, D], F32, kind="ExternalOutput").ap()

    NT = T // 128
    NTm = TC // 128
    p = Prog(nc)
    stf = p.sb("stf", [128, 8 * NT], F32)
    probs = p.sb("probs", [128, 8 * NT], F32)
    esf = p.sb("esf", [128, 8 * NT], F32)
    mxf = p.sb("mxf", [128, NT], F32)
    stm = p.sb("stm", [128, 8 * NTm], F32)
    probm = p.sb("probm", [128, 8 * NTm], F32)
    esm = p.sb("esm", [128, 8 * NTm], F32)
    mxm = p.sb("mxm", [128, NTm], F32)
    pmE = p.sb("pmE", [128, 8 * NTm], F32)
    w8E = p.sb("w8E", [128, 8 * NTm], F32)
    peF = p.sb("peF", [128, 8 * NT], F32)
    lo = p.sb("lo", [128, 8], F32)
    hi = p.sb("hi", [128, 8], F32)
    th = p.sb("th", [128, 8], F32)
    cnt = p.sb("cnt", [128, 8 * NT], F32)
    rcol = p.sb("rcol", [128, 8], F32)
    allo = p.sb("allo", [128, 128], F32)
    idt = p.sb("idt", [128, 128], BF16)
    w1t = p.sb("w1t", [128, 8 * 4096], BF16)
    w2t = p.sb("w2t", [128, 32 * D], BF16)
    xnt = p.sb("xnt", [128, D], BF16)
    xfm1 = p.sb("xfm1", [128, 8 * 128], BF16)
    h1 = p.sb("h1", [128, 4096], BF16)
    h1fm = p.sb("h1fm", [128, 32 * 128], BF16)
    xacc = p.sb("xacc", [128, D], F32)
    ptp = p.ps("ptp", [128, 128], BF16)
    pg1 = p.ps("pg1", [128, 512], F32)
    pg2 = p.ps("pg2", [128, 512], F32)
    pbt = p.ps("pbt", [128, 8], F32)

    p.step("sync", lambda e: [
        e.dma_start(out=stf[:].rearrange("p (a k) -> p a k", k=8),
                    in_=scof[:, :].rearrange("(a p) k -> p a k", p=128)),
        e.dma_start(out=stm[:].rearrange("p (a k) -> p a k", k=8),
                    in_=scom[:, :].rearrange("(a p) k -> p a k", p=128)),
        e.dma_start(out=allo[:], in_=allones[:, :]),
        e.dma_start(out=idt[:], in_=ident[:, :])], dmas=4)
    emit_softmax8(p, stf[:].rearrange("p (a k) -> p a k", k=8), mxf, esf, probs, d2=NT)
    emit_softmax8(p, stm[:].rearrange("p (a k) -> p a k", k=8), mxm, esm, probm, d2=NTm)
    p.step("vector", lambda e: e.tensor_copy(
        peF[:].rearrange("p (k a) -> p k a", a=NT),
        probs[:].rearrange("p (a k) -> p a k", k=8).transpose([0, 2, 1])))
    p.step("vector", lambda e: e.tensor_copy(
        pmE[:].rearrange("p (k a) -> p k a", a=NTm),
        probm[:].rearrange("p (a k) -> p a k", k=8).transpose([0, 2, 1])))
    emit_bisect(p, peF, lo, hi, th, cnt, rcol, allo, pbt, 8 * NT)
    p.step("vector", lambda e: e.tensor_tensor(
        out=w8E[:].rearrange("p (k a) -> p k a", a=NTm),
        in0=pmE[:].rearrange("p (k a) -> p k a", a=NTm),
        in1=lo[:].rearrange("p (k o) -> p k o", o=1).to_broadcast([128, 8, NTm]),
        op=OP.is_gt))
    p.step("vector", lambda e: e.tensor_tensor(out=w8E[:], in0=w8E[:], in1=pmE[:],
                                               op=OP.mult))

    for ex in range(8):
        p.step("gpsimd", lambda e, ex=ex: [e.dma_start(out=w1t[:, i * 4096:(i + 1) * 4096],
                                                       in_=w1[ex, i * 128:(i + 1) * 128, :])
                                           for i in range(8)], dmas=8)
        p.step("gpsimd", lambda e, ex=ex: [e.dma_start(out=w2t[:, i * D:(i + 1) * D],
                                                       in_=w2[ex, i * 128:(i + 1) * 128, :])
                                           for i in range(32)], dmas=32)
        for tl in range(NTm):
            p.step("sync", lambda e, tl=tl: e.dma_start(
                out=xnt[:], in_=xnd[tl * 128:(tl + 1) * 128, :]), dmas=1)
            if ex == 0:
                p.step("sync", lambda e, tl=tl: e.dma_start(
                    out=xacc[:], in_=x[tl * 128:(tl + 1) * 128, :]), dmas=1)
            else:
                p.step("sync", lambda e, tl=tl: e.dma_start(
                    out=xacc[:], in_=acc[tl * 128:(tl + 1) * 128, :]), dmas=1)
            emit_tp(p, xnt, (lambda i: xfm1[:, i * 128:(i + 1) * 128]), ptp, idt)
            for hc in range(8):
                def fg1(e, hc=hc):
                    for i in range(8):
                        r = e.matmul(pg1[:], xfm1[:, i * 128:(i + 1) * 128],
                                     w1t[:, i * 4096 + hc * 512: i * 4096 + hc * 512 + 512],
                                     start=(i == 0), stop=(i == 7))
                    return r
                p.step("tensor", fg1)
                p.step("scalar", lambda e, hc=hc: e.activation(
                    out=h1[:, hc * 512:(hc + 1) * 512], in_=pg1[:], func=AF.Gelu))
            emit_tp(p, h1, (lambda i: h1fm[:, i * 128:(i + 1) * 128]), ptp, idt, n=32)
            for oc in range(2):
                for kb in range(4):
                    def fg2(e, oc=oc, kb=kb):
                        for i in range(8):
                            k = kb * 8 + i
                            r = e.matmul(pg2[:], h1fm[:, k * 128:(k + 1) * 128],
                                         w2t[:, k * D + oc * 512: k * D + oc * 512 + 512],
                                         start=(k == 0), stop=(k == 31))
                        return r
                    p.step("tensor", fg2)
                p.step("vector", lambda e, oc=oc, tl=tl, ex=ex: e.scalar_tensor_tensor(
                    out=xacc[:, oc * 512:(oc + 1) * 512], in0=pg2[:],
                    scalar=w8E[:, ex * NTm + tl: ex * NTm + tl + 1],
                    in1=xacc[:, oc * 512:(oc + 1) * 512], op0=OP.mult, op1=OP.add))
            dst = acc if (ex < 7 or tail != "none") else xout
            p.step("sync", lambda e, tl=tl, dst=dst: e.dma_start(
                out=dst[tl * 128:(tl + 1) * 128, :], in_=xacc[:]), dmas=1)

    if tail == "s4pre":
        wint = w1t[:, 0:8 * D]
        w2f = w2t[:].bitcast(F32)
        binrep = w2f[:, 0:D]
        garep = w2f[:, D:2 * D]
        berep = w2f[:, 2 * D:3 * D]
        usb = w2f[:, 3 * D:4 * D]
        xln = w2f[:, 4 * D:5 * D]
        u2t = w2f[:, 5 * D:6 * D]
        sq2 = w2f[:, 6 * D:7 * D]
        birow = w2f[0:1, 7 * D:8 * D]
        garow = w2f[0:1, 8 * D:9 * D]
        berow = w2f[0:1, 9 * D:10 * D]
        onrw = w2f[0:1, 10 * D:10 * D + 128]
        stt2 = p.sb("stt2", [128, 8], F32)
        xnb = xnt[:]
        prow = p.ps("prow", [128, 512], F32)
        pcs = p.ps("pcs", [1, D], F32)
        onescolf = p.sb("onescolf", [128, 1], F32)
        p.step("gpsimd", lambda e: [e.dma_start(out=wint[:, i * D:(i + 1) * D],
                                                in_=winT[i * 128:(i + 1) * 128, :])
                                    for i in range(8)], dmas=8)
        p.step("sync", lambda e: [e.dma_start(out=birow, in_=b_in[:, :]),
                                  e.dma_start(out=garow, in_=gam[:, :]),
                                  e.dma_start(out=berow, in_=bet[:, :]),
                                  e.dma_start(out=onrw, in_=onesrowf[:, :])], dmas=4)
        p.step("vector", lambda e: e.memset(onescolf[:], 1.0))
        for (rep, row) in ((binrep, birow), (garep, garow), (berep, berow)):
            for ch in range(2):
                p.step("tensor", lambda e, row=row, ch=ch: e.matmul(
                    prow[:], onrw[:], row[0:1, ch * 512:(ch + 1) * 512],
                    start=True, stop=True))
                p.step("vector", lambda e, rep=rep, ch=ch: e.tensor_copy(
                    rep[:, ch * 512:(ch + 1) * 512], prow[:]))
        for tl in range(NTm):
            p.step("sync", lambda e, tl=tl: e.dma_start(
                out=xacc[:], in_=acc[tl * 128:(tl + 1) * 128, :]), dmas=1)
            p.step("sync", lambda e, tl=tl: e.dma_start(
                out=xout[tl * 128:(tl + 1) * 128, :], in_=xacc[:]), dmas=1)
            emit_ln(p, xacc, xnb, stt2, sq2)
            emit_tp(p, xnb, (lambda i: xfm1[:, i * 128:(i + 1) * 128]), ptp, idt)
            for ch in range(2):
                def fpu(e, ch=ch):
                    for i in range(8):
                        r = e.matmul(pg1[:], xfm1[:, i * 128:(i + 1) * 128],
                                     wint[:, i * D + ch * 512: i * D + ch * 512 + 512],
                                     start=(i == 0), stop=(i == 7))
                    return r
                p.step("tensor", fpu)
                p.step("vector", lambda e, ch=ch: e.tensor_tensor(
                    out=usb[:, ch * 512:(ch + 1) * 512], in0=pg1[:],
                    in1=binrep[:, ch * 512:(ch + 1) * 512], op=OP.add))
            emit_ln(p, usb, xln, stt2, sq2)
            p.step("vector", lambda e: e.tensor_tensor(out=u2t[:], in0=xln[:], in1=garep[:],
                                                       op=OP.mult))
            p.step("vector", lambda e: e.tensor_tensor(out=u2t[:], in0=u2t[:], in1=berep[:],
                                                       op=OP.add))
            def fcs(e, tl=tl):
                for ch in range(2):
                    r = e.matmul(pcs[0:1, ch * 512:(ch + 1) * 512], onescolf[:],
                                 u2t[:, ch * 512:(ch + 1) * 512],
                                 start=(tl == 0), stop=(tl == NTm - 1))
                return r
            p.step("tensor", fcs)
            p.step("sync", lambda e, tl=tl: e.dma_start(
                out=d_u2[tl * 128:(tl + 1) * 128, :], in_=u2t[:]), dmas=1)
        p.step("vector", lambda e: e.tensor_copy(usb[0:1, 0:D], pcs[:]))
        p.step("sync", lambda e: e.dma_start(out=d_cs[:, :], in_=usb[0:1, 0:D]), dmas=1)
    return p.build()



LK = 256
NF = 264
WIN = 512


def s4_consts():
    c = {}
    n = np.arange(WIN)[:, None]
    f = np.arange(NF)[None, :]
    mask = (f < 257).astype(np.float64)
    ang = 2 * np.pi * f * n / WIN
    c["fxc"] = (np.cos(ang) * mask).astype(np.float32).astype(BF)
    c["fxs"] = (-np.sin(ang) * mask).astype(np.float32).astype(BF)
    c["fkc"] = np.ascontiguousarray(c["fxc"][:LK])
    c["fks"] = np.ascontiguousarray(c["fxs"][:LK])
    fv = np.arange(NF)
    w = np.where(fv <= 256, (2.0 - (fv == 0) - (fv == 256)) / WIN, 0.0)[:, None]
    tv = np.arange(256)[None, :]
    ang2 = 2 * np.pi * fv[:, None] * (256 + tv) / WIN
    c["gre"] = (w * np.cos(ang2)).astype(np.float32).astype(BF)
    c["gim"] = (-w * np.sin(ang2)).astype(np.float32).astype(BF)
    c["lmat"] = np.tile(np.arange(LK, dtype=np.float32)[None, :], (64, 1))
    return c


def build_s4conv():
    NBLK = TC // 256
    nc = bass.Bass("TRN2", target_bir_lowering=False)
    x = nc.dram_tensor("x", [TC, D], F32, kind="ExternalInput").ap()
    u2e = nc.dram_tensor("u2e", [TC + 256, D], F32, kind="ExternalInput").ap()
    colsums = nc.dram_tensor("colsums", [4, D], F32, kind="ExternalInput").ap()
    pmk = nc.dram_tensor("pmk", [4, 1], F32, kind="ExternalInput").ap()
    are = nc.dram_tensor("are", [64, 1], F32, kind="ExternalInput").ap()
    aim = nc.dram_tensor("aim", [64, 1], F32, kind="ExternalInput").ap()
    bre = nc.dram_tensor("bre", [64, 1], F32, kind="ExternalInput").ap()
    bim = nc.dram_tensor("bim", [64, 1], F32, kind="ExternalInput").ap()
    cre = nc.dram_tensor("cre", [D, 64], F32, kind="ExternalInput").ap()
    cim = nc.dram_tensor("cim", [D, 64], F32, kind="ExternalInput").ap()
    dre = nc.dram_tensor("dre", [1, D], F32, kind="ExternalInput").ap()
    dim = nc.dram_tensor("dim", [1, D], F32, kind="ExternalInput").ap()
    lmat = nc.dram_tensor("lmat", [64, LK], F32, kind="ExternalInput").ap()
    fxc = nc.dram_tensor("fxc", [WIN, NF], BF16, kind="ExternalInput").ap()
    fxs = nc.dram_tensor("fxs", [WIN, NF], BF16, kind="ExternalInput").ap()
    fkc = nc.dram_tensor("fkc", [LK, NF], BF16, kind="ExternalInput").ap()
    fks = nc.dram_tensor("fks", [LK, NF], BF16, kind="ExternalInput").ap()
    gre = nc.dram_tensor("gre", [NF, 256], BF16, kind="ExternalInput").ap()
    gim = nc.dram_tensor("gim", [NF, 256], BF16, kind="ExternalInput").ap()
    gw = nc.dram_tensor("gw", [D, 8], F32, kind="ExternalInput").ap()
    ident = nc.dram_tensor("ident", [128, 128], BF16, kind="ExternalInput").ap()
    triu = nc.dram_tensor("triu", [128, 128], BF16, kind="ExternalInput").ap()
    onesrowf = nc.dram_tensor("onesrowf", [1, 128], F32, kind="ExternalInput").ap()
    xout = nc.dram_tensor("xout", [TC, D], F32, kind="ExternalOutput").ap()
    d_xn1 = nc.dram_tensor("xn1", [TC, D], BF16, kind="ExternalOutput").ap()
    d_scores = nc.dram_tensor("scores", [TC, 8], F32, kind="ExternalOutput").ap()

    p = Prog(nc)
    t = {}
    t["idt"] = p.sb("idt", [128, 128], BF16)
    trut = p.sb("trut", [128, 128], BF16)
    onrw = p.sb("onrw", [1, 128], F32)
    onescolf = p.sb("onescolf", [128, 1], F32)
    onescolb = p.sb("onescolb", [128, 1], BF16)
    t["gwt"] = p.sb("gwt", [128, 8 * 8], BF16)
    fxct = p.sb("fxct", [128, 4 * NF], BF16)
    fxst = p.sb("fxst", [128, 4 * NF], BF16)
    fkct = p.sb("fkct", [128, 2 * NF], BF16)
    fkst = p.sb("fkst", [128, 2 * NF], BF16)
    gret = p.sb("gret", [128, 3 * 256], BF16)
    gimt = p.sb("gimt", [128, 3 * 256], BF16)
    ar = p.sb("ar", [64, 16], F32)
    lm = p.sb("lm", [64, LK], F32)
    pw = p.sb("pw", [64, 6 * LK], F32)
    pb2 = p.sb("pb2", [64, 2 * LK], BF16)
    ibuf = p.sb("ibuf", [64, LK], mybir.dt.int32)
    ld64 = p.sb("ld64", [128, 64], F32)
    lb64 = p.sb("lb64", [128, 128], BF16)
    ct = p.sb("ct", [64, 2 * D], BF16)
    krb = p.sb("krb", [128, 2 * D], BF16)
    kfr = p.sb("kfr", [128, 3 * D], F32)
    kfi = p.sb("kfi", [128, 3 * D], F32)
    pmkt = p.sb("pmkt", [4, 1], F32)
    colst = p.sb("colst", [4, D], F32)
    wbuf = p.sb("wbuf", [128, 4 * D], BF16)
    u2f = p.sb("u2f", [128, D], F32)
    vt = p.sb("vt", [128, D], BF16)
    xsb = p.sb("xsb", [128, 6 * 512], F32)
    yre = p.sb("yre", [128, 3 * 512], BF16)
    yim = p.sb("yim", [128, 3 * 512], BF16)
    ysc1 = p.sb("ysc1", [128, 512], F32)
    ysc2 = p.sb("ysc2", [128, 512], F32)
    drow = p.sb("drow", [1, D], F32)
    dimr = p.sb("dimr", [1, D], F32)
    redrep = p.sb("redrep", [128, D], F32)
    offrow = p.sb("offrow", [1, D], F32)
    orsc = p.sb("orsc", [1, D], F32)
    xt = p.sb("xt", [128, D], F32)
    yt0 = p.sb("yt0", [128, D], F32)
    yt1 = p.sb("yt1", [128, D], F32)
    t["xn"] = p.sb("xn", [128, D], BF16)
    t["sq"] = p.sb("sqs", [128, D], F32)
    t["stats2"] = p.sb("stats2", [128, 8], F32)
    t["xfm1"] = p.sb("xfm1", [128, 8 * 128], BF16)
    t["sc8"] = p.sb("sc8", [128, 8], F32)
    t["d_xn1"], t["d_scores"] = d_xn1, d_scores
    pxf = p.ps("pxf", [128, 512], F32)
    py0 = p.ps("py0", [128, 512], F32)
    py1 = p.ps("py1", [128, 512], F32)
    poff = p.ps("poff", [1, 512], F32)
    t["pt1"] = p.ps("pt1", [128, 128], BF16)
    t["pg"] = p.ps("pg", [128, 8], F32)

    p.step("sync", lambda e: [
        e.dma_start(out=t["idt"][:], in_=ident[:, :]),
        e.dma_start(out=trut[:], in_=triu[:, :]),
        e.dma_start(out=onrw[:], in_=onesrowf[:, :]),
        e.dma_start(out=ar[:, 0:1], in_=are[:, :]),
        e.dma_start(out=ar[:, 1:2], in_=aim[:, :]),
        e.dma_start(out=ar[:, 2:3], in_=bre[:, :]),
        e.dma_start(out=ar[:, 3:4], in_=bim[:, :]),
        e.dma_start(out=lm[:], in_=lmat[:, :]),
        e.dma_start(out=drow[:], in_=dre[:, :]),
        e.dma_start(out=dimr[:], in_=dim[:, :]),
        e.dma_start(out=pmkt[:], in_=pmk[:, :]),
        e.dma_start(out=colst[:], in_=colsums[:, :])], dmas=12)
    p.step("sync", lambda e: [e.dma_start(out=fxct[:].rearrange("p (a f) -> p a f", a=4),
                                          in_=fxc[:, :].rearrange("(a p) f -> p a f", p=128)),
                              e.dma_start(out=fxst[:].rearrange("p (a f) -> p a f", a=4),
                                          in_=fxs[:, :].rearrange("(a p) f -> p a f", p=128)),
                              e.dma_start(out=fkct[:].rearrange("p (a f) -> p a f", a=2),
                                          in_=fkc[:, :].rearrange("(a p) f -> p a f", p=128)),
                              e.dma_start(out=fkst[:].rearrange("p (a f) -> p a f", a=2),
                                          in_=fks[:, :].rearrange("(a p) f -> p a f", p=128))],
           dmas=4)
    p.step("sync", lambda e: [e.dma_start(out=gret[0:128, 0:256], in_=gre[0:128, :]),
                              e.dma_start(out=gret[0:128, 256:512], in_=gre[128:256, :]),
                              e.dma_start(out=gret[0:8, 512:768], in_=gre[256:264, :]),
                              e.dma_start(out=gimt[0:128, 0:256], in_=gim[0:128, :]),
                              e.dma_start(out=gimt[0:128, 256:512], in_=gim[128:256, :]),
                              e.dma_start(out=gimt[0:8, 512:768], in_=gim[256:264, :])],
           dmas=6)
    p.step("gpsimd", lambda e: [e.dma_start(out=t["gwt"][:, i * 8:(i + 1) * 8],
                                            in_=gw[i * 128:(i + 1) * 128, :])
                                for i in range(8)], dmas=8)
    p.step("vector", lambda e: e.memset(onescolf[:], 1.0))
    p.step("vector", lambda e: e.memset(onescolb[:], 1.0))

    p.step("vector", lambda e: e.scalar_tensor_tensor(out=drow[:], in0=dimr[:], scalar=0.0,
                                                      in1=drow[:], op0=OP.mult, op1=OP.add))
    for ch in range(2):
        p.step("tensor", lambda e, ch=ch: e.matmul(
            pxf[:], onrw[:], drow[0:1, ch * 512:(ch + 1) * 512], start=True, stop=True))
        p.step("vector", lambda e, ch=ch: e.tensor_copy(redrep[:, ch * 512:(ch + 1) * 512],
                                                        pxf[:]))
    for ch in range(2):
        p.step("tensor", lambda e, ch=ch: e.matmul(
            poff[:], pmkt[:], colst[:, ch * 512:(ch + 1) * 512], start=True, stop=True))
        p.step("vector", lambda e, ch=ch: e.tensor_copy(orsc[0:1, ch * 512:(ch + 1) * 512],
                                                        poff[:]))
    p.step("vector", lambda e: e.tensor_tensor(out=offrow[:], in0=orsc[:], in1=drow[:],
                                               op=OP.mult))

    C = lambda i: ar[:, i:i + 1]
    PW = lambda i: pw[:, i * LK:(i + 1) * LK]
    p.step("vector", lambda e: e.tensor_tensor(out=C(4), in0=C(0), in1=C(0), op=OP.mult))
    p.step("vector", lambda e: e.tensor_tensor(out=C(5), in0=C(1), in1=C(1), op=OP.mult))
    p.step("vector", lambda e: e.tensor_tensor(out=C(4), in0=C(4), in1=C(5), op=OP.add))
    p.step("scalar", lambda e: e.activation(out=C(4), in_=C(4), func=AF.Ln))
    p.step("vector", lambda e: e.tensor_scalar(out=C(4), in0=C(4), scalar1=0.5,
                                               scalar2=None, op0=OP.mult))
    p.step("vector", lambda e: e.reciprocal(C(5), C(0)))
    p.step("vector", lambda e: e.tensor_tensor(out=C(5), in0=C(5), in1=C(1), op=OP.mult))
    p.step("scalar", lambda e: e.activation(out=C(5), in_=C(5), func=AF.Arctan))
    p.step("scalar", lambda e: e.activation(out=C(6), in_=C(1), func=AF.Sign))
    p.step("vector", lambda e: e.scalar_tensor_tensor(out=C(5), in0=C(6), scalar=float(np.pi),
                                                      in1=C(5), op0=OP.mult, op1=OP.add))
    p.step("scalar", lambda e: e.activation(out=C(7), in_=C(5), func=AF.Sign))
    p.step("scalar", lambda e: e.activation(out=C(6), in_=C(5), func=AF.Abs))
    p.step("scalar", lambda e: e.activation(out=PW(1), in_=lm[:], func=AF.Exp, scale=C(4)))
    p.step("vector", lambda e: e.tensor_scalar(out=PW(0), in0=lm[:], scalar1=C(6),
                                               scalar2=float(0.5 / np.pi), op0=OP.mult,
                                               op1=OP.mult))
    p.step("vector", lambda e: e.tensor_copy(ibuf[:], PW(0)))
    p.step("vector", lambda e: e.tensor_copy(PW(2), ibuf[:]))
    p.step("vector", lambda e: e.tensor_tensor(out=PW(0), in0=PW(0), in1=PW(2), op=OP.subtract))
    p.step("vector", lambda e: e.tensor_scalar(out=PW(0), in0=PW(0), scalar1=float(2 * np.pi),
                                               scalar2=None, op0=OP.mult))
    p.step("scalar", lambda e: e.activation(out=PW(3), in_=PW(0), func=AF.Sin))
    p.step("vector", lambda e: e.tensor_scalar(out=PW(2), in0=PW(0), scalar1=-1.0,
                                               scalar2=float(np.pi / 2), op0=OP.mult, op1=OP.add))
    p.step("vector", lambda e: e.tensor_scalar(out=PW(4), in0=PW(2), scalar1=float(np.pi),
                                               scalar2=None, op0=OP.is_gt))
    p.step("vector", lambda e: e.scalar_tensor_tensor(out=PW(2), in0=PW(4),
                                                      scalar=float(-2 * np.pi), in1=PW(2),
                                                      op0=OP.mult, op1=OP.add))
    p.step("scalar", lambda e: e.activation(out=PW(2), in_=PW(2), func=AF.Sin))
    p.step("vector", lambda e: e.tensor_scalar(out=PW(3), in0=PW(3), scalar1=C(7),
                                               scalar2=None, op0=OP.mult))
    p.step("vector", lambda e: e.tensor_tensor(out=PW(4), in0=PW(1), in1=PW(2), op=OP.mult))
    p.step("vector", lambda e: e.tensor_tensor(out=PW(5), in0=PW(1), in1=PW(3), op=OP.mult))
    p.step("vector", lambda e: e.tensor_scalar(out=PW(0), in0=PW(4), scalar1=C(2),
                                               scalar2=None, op0=OP.mult))
    p.step("vector", lambda e: e.tensor_scalar(out=PW(1), in0=PW(5), scalar1=C(3),
                                               scalar2=None, op0=OP.mult))
    p.step("vector", lambda e: e.tensor_tensor(out=PW(0), in0=PW(0), in1=PW(1), op=OP.subtract))
    p.step("vector", lambda e: e.tensor_copy(pb2[:, 0:LK], PW(0)))
    p.step("vector", lambda e: e.tensor_scalar(out=C(8), in0=C(3), scalar1=-1.0,
                                               scalar2=None, op0=OP.mult))
    p.step("vector", lambda e: e.tensor_scalar(out=C(9), in0=C(2), scalar1=-1.0,
                                               scalar2=None, op0=OP.mult))
    p.step("vector", lambda e: e.tensor_scalar(out=PW(0), in0=PW(4), scalar1=C(8),
                                               scalar2=None, op0=OP.mult))
    p.step("vector", lambda e: e.tensor_scalar(out=PW(1), in0=PW(5), scalar1=C(9),
                                               scalar2=None, op0=OP.mult))
    p.step("vector", lambda e: e.tensor_tensor(out=PW(0), in0=PW(0), in1=PW(1), op=OP.add))
    p.step("vector", lambda e: e.tensor_copy(pb2[:, LK:2 * LK], PW(0)))
    for (srcd, dstoff) in ((cre, 0), (cim, D)):
        for i in range(8):
            p.step("sync", lambda e, srcd=srcd, i=i: e.dma_start(
                out=ld64[:, 0:64], in_=srcd[i * 128:(i + 1) * 128, :]), dmas=1)
            p.step("vector", lambda e: e.tensor_copy(lb64[:, 0:64], ld64[:, 0:64]))
            p.step("tensor", lambda e: e.transpose(t["pt1"][:], lb64[:], t["idt"][:]))
            p.step("vector", lambda e, dstoff=dstoff, i=i: e.tensor_copy(
                ct[0:64, dstoff + i * 128: dstoff + (i + 1) * 128], t["pt1"][0:64, 0:128]))
    for lt in range(2):
        for ch in range(2):
            def fkr(e, lt=lt, ch=ch):
                e.matmul(pxf[:], pb2[:, lt * 128:(lt + 1) * 128],
                         ct[0:64, ch * 512:(ch + 1) * 512], start=True, stop=False)
                return e.matmul(pxf[:], pb2[:, LK + lt * 128: LK + lt * 128 + 128],
                                ct[0:64, D + ch * 512: D + ch * 512 + 512],
                                start=False, stop=True)
            p.step("tensor", fkr)
            p.step("vector", lambda e, lt=lt, ch=ch: e.tensor_copy(
                krb[:, lt * D + ch * 512: lt * D + ch * 512 + 512], pxf[:]))
    for (fk, kf) in ((fkct, kfr), (fkst, kfi)):
        for ft in range(3):
            rows = 128 if ft < 2 else 8
            for ch2 in range(2):
                def fkf(e, fk=fk, ft=ft, ch2=ch2, rows=rows):
                    for lt in range(2):
                        r = e.matmul(pxf[0:rows, :],
                                     fk[:, lt * NF + ft * 128: lt * NF + ft * 128 + rows],
                                     krb[:, lt * D + ch2 * 512: lt * D + ch2 * 512 + 512],
                                     start=(lt == 0), stop=(lt == 1))
                    return r
                p.step("tensor", fkf)
                p.step("vector", lambda e, kf=kf, ft=ft, ch2=ch2, rows=rows: e.tensor_copy(
                    kf[0:rows, ft * D + ch2 * 512: ft * D + ch2 * 512 + 512], pxf[0:rows, :]))

    for blk in range(NBLK):
        b0 = blk * 256
        p.step("gpsimd", lambda e, b0=b0: e.dma_start(
            out=wbuf[:].rearrange("p (a d2) -> p a d2", a=4),
            in_=u2e[b0:b0 + 512, :].rearrange("(a p) d2 -> p a d2", p=128)), dmas=1)
        for ch in range(2):
            for (fx, xoff) in ((fxct, 0), (fxst, 3)):
                for ft in range(3):
                    rows = 128 if ft < 2 else 8
                    def fxf(e, fx=fx, ft=ft, ch=ch, rows=rows):
                        for ks in range(4):
                            r = e.matmul(pxf[0:rows, :],
                                         fx[:, ks * NF + ft * 128: ks * NF + ft * 128 + rows],
                                         wbuf[:, ks * D + ch * 512: ks * D + ch * 512 + 512],
                                         start=(ks == 0), stop=(ks == 3))
                        return r
                    p.step("tensor", fxf)
                    p.step("vector", lambda e, xoff=xoff, ft=ft, rows=rows: e.tensor_copy(
                        xsb[0:rows, (xoff + ft) * 512:(xoff + ft + 1) * 512], pxf[0:rows, :]))
            for ft in range(3):
                rows = 128 if ft < 2 else 8
                XR = xsb[0:rows, ft * 512:(ft + 1) * 512]
                XI = xsb[0:rows, (3 + ft) * 512:(4 + ft) * 512]
                KR = kfr[0:rows, ft * D + ch * 512: ft * D + ch * 512 + 512]
                KI = kfi[0:rows, ft * D + ch * 512: ft * D + ch * 512 + 512]
                YR = yre[0:rows, ft * 512:(ft + 1) * 512]
                YI = yim[0:rows, ft * 512:(ft + 1) * 512]
                s1 = ysc1[0:rows, :]
                s2 = ysc2[0:rows, :]
                p.step("vector", lambda e, XR=XR, KR=KR, s1=s1: e.tensor_tensor(
                    out=s1, in0=XR, in1=KR, op=OP.mult))
                p.step("vector", lambda e, XI=XI, KI=KI, s2=s2: e.tensor_tensor(
                    out=s2, in0=XI, in1=KI, op=OP.mult))
                p.step("vector", lambda e, s1=s1, s2=s2, YR=YR: e.tensor_tensor(
                    out=YR, in0=s1, in1=s2, op=OP.subtract))
                p.step("vector", lambda e, XR=XR, KI=KI, s1=s1: e.tensor_tensor(
                    out=s1, in0=XR, in1=KI, op=OP.mult))
                p.step("vector", lambda e, XI=XI, KR=KR, s2=s2: e.tensor_tensor(
                    out=s2, in0=XI, in1=KR, op=OP.mult))
                p.step("vector", lambda e, s1=s1, s2=s2, YI=YI: e.tensor_tensor(
                    out=YI, in0=s1, in1=s2, op=OP.add))
            for tt in range(2):
                py = py0 if tt == 0 else py1
                ytl = yt0 if tt == 0 else yt1
                p.step("sync", lambda e, b0=b0, tt=tt: e.dma_start(
                    out=u2f[:], in_=u2e[256 + b0 + tt * 128: 256 + b0 + tt * 128 + 128, :]),
                    dmas=1)
                p.step("vector", lambda e: e.tensor_tensor(out=vt[:], in0=u2f[:], in1=redrep[:],
                                                           op=OP.mult))
                def fy(e, tt=tt, ch=ch, py=py):
                    for (g, y) in ((gret, yre), (gimt, yim)):
                        for ft in range(3):
                            rows = 128 if ft < 2 else 8
                            e.matmul(py[:],
                                     g[0:rows, ft * 256 + tt * 128: ft * 256 + tt * 128 + 128],
                                     y[0:rows, ft * 512:(ft + 1) * 512],
                                     start=(g is gret and ft == 0), stop=False)
                    e.matmul(py[:], trut[:], vt[:, ch * 512:(ch + 1) * 512],
                             start=False, stop=False)
                    return e.matmul(py[:], onrw[:], offrow[0:1, ch * 512:(ch + 1) * 512],
                                    start=False, stop=True)
                p.step("tensor", fy)
                p.step("tensor", lambda e, ch=ch: e.matmul(
                    poff[:], onescolb[:], vt[:, ch * 512:(ch + 1) * 512],
                    start=True, stop=True))
                p.step("vector", lambda e, ch=ch: e.tensor_tensor(
                    out=offrow[0:1, ch * 512:(ch + 1) * 512],
                    in0=offrow[0:1, ch * 512:(ch + 1) * 512], in1=poff[:], op=OP.add))
                p.step("vector", lambda e, py=py, ytl=ytl, ch=ch: e.tensor_tensor(
                    out=ytl[:, ch * 512:(ch + 1) * 512], in0=py[:],
                    in1=u2f[:, ch * 512:(ch + 1) * 512], op=OP.add))
        for tt in range(2):
            ytl = yt0 if tt == 0 else yt1
            p.step("sync", lambda e, b0=b0, tt=tt: e.dma_start(
                out=xt[:], in_=x[b0 + tt * 128: b0 + tt * 128 + 128, :]), dmas=1)
            p.step("vector", lambda e, ytl=ytl: e.tensor_tensor(out=xt[:], in0=xt[:],
                                                                in1=ytl[:], op=OP.add))
            p.step("sync", lambda e, b0=b0, tt=tt: e.dma_start(
                out=xout[b0 + tt * 128: b0 + tt * 128 + 128, :], in_=xt[:]), dmas=1)
            emit_ffnpre(p, t, xt, b0 + tt * 128)
    return p.build()



def m1_consts():
    sel = np.zeros((16, 128, 128), np.float32)
    for j in range(16):
        for p2 in range(128):
            sel[j, 8 * j + p2 // 16, p2] = 1.0
    sumsel = np.zeros((128, 8), np.float32)
    for p2 in range(128):
        sumsel[p2, p2 // 16] = 1.0
    sel16 = np.zeros((16, 128), np.float32)
    for p2 in range(128):
        sel16[p2 % 16, p2] = 1.0
    return {"selt": sel.reshape(16 * 128, 128).astype(BF),
            "sumsel": sumsel.astype(BF), "sel16": sel16.astype(BF)}


def build_m1(full):
    DI = 2048
    NMT = DI // 128
    nc = bass.Bass("TRN2", target_bir_lowering=False)
    x = nc.dram_tensor("x", [TC, D], F32, kind="ExternalInput").ap()
    nw = nc.dram_tensor("nw", [1, D], F32, kind="ExternalInput").ap()
    nb = nc.dram_tensor("nb", [1, D], F32, kind="ExternalInput").ap()
    win = nc.dram_tensor("win", [D, DI], F32, kind="ExternalInput").ap()
    wgt = nc.dram_tensor("wgt", [D, DI], F32, kind="ExternalInput").ap()
    wdt = nc.dram_tensor("wdt", [DI, DI], F32, kind="ExternalInput").ap()
    bdt = nc.dram_tensor("bdt", [128, 16], F32, kind="ExternalInput").ap()
    wB = nc.dram_tensor("wB", [DI, 16], F32, kind="ExternalInput").ap()
    wC = nc.dram_tensor("wC", [DI, 16], F32, kind="ExternalInput").ap()
    alog = nc.dram_tensor("alog", [128, 256], F32, kind="ExternalInput").ap()
    selc = nc.dram_tensor("selc", [16 * 128, 128], BF16, kind="ExternalInput").ap()
    sumselc = nc.dram_tensor("sumselc", [128, 8], BF16, kind="ExternalInput").ap()
    sel16c = nc.dram_tensor("sel16c", [16, 128], BF16, kind="ExternalInput").ap()
    ident = nc.dram_tensor("ident", [128, 128], BF16, kind="ExternalInput").ap()
    onesrowf = nc.dram_tensor("onesrowf", [1, 128], F32, kind="ExternalInput").ap()
    dts = nc.dram_tensor("dts", [DI, TC], BF16, kind="Internal").ap()
    dzs = nc.dram_tensor("dzs", [DI, TC], BF16, kind="Internal").ap()
    if full:
        wout = nc.dram_tensor("wout", [DI, D], F32, kind="ExternalInput").ap()
        hfall = nc.dram_tensor("hfall", [4, 128, 256], F32, kind="ExternalInput").ap()
        fall = nc.dram_tensor("fall", [4, 128, 256], F32, kind="ExternalInput").ap()
        incr = nc.dram_tensor("incr", [1, 4], F32, kind="ExternalInput").ap()
        xout = nc.dram_tensor("xout", [TC, D], F32, kind="ExternalOutput").ap()
        ys = nc.dram_tensor("ys", [DI, TC], BF16, kind="Internal").ap()
        gts = nc.dram_tensor("gts", [DI, TC], BF16, kind="Internal").ap()
        xno = nc.dram_tensor("xno", [TC, D], F32, kind="Internal").ap()
    else:
        d_hf = nc.dram_tensor("hfin", [128, 256], F32, kind="ExternalOutput").ap()
        d_ff = nc.dram_tensor("ff", [128, 256], F32, kind="ExternalOutput").ap()

    p = Prog(nc)
    t = {}
    idt = p.sb("idt", [128, 128], BF16)
    onrw = p.sb("onrw", [1, 128], F32)
    selt = p.sb("selt", [128, 16 * 128], BF16)
    sums = p.sb("sums", [128, 8], BF16)
    s16 = p.sb("s16", [16, 128], BF16)
    acl = p.sb("acl", [128, 256], F32)
    bdtt = p.sb("bdtt", [128, 16], F32)
    nwrep = p.sb("nwrep", [128, D], F32)
    nbrep = p.sb("nbrep", [128, D], F32)
    wint = p.sb("wint", [128, 8 * DI], BF16)
    wgtt = p.sb("wgtt", [128, 8 * DI], BF16)
    wdtt = p.sb("wdtt", [128, 16 * DI], BF16)
    wbct = p.sb("wbct", [128, 16 * 32], BF16)
    xt = p.sb("xt", [128, D], F32)
    xnof = p.sb("xnof", [128, D], F32)
    x2n = p.sb("x2n", [128, D], BF16)
    x2f = xt[:]
    sq = p.sb("sq", [128, D], F32)
    stats = p.sb("stats", [128, 8], F32)
    x2fm = p.sb("x2fm", [128, 8 * 512], BF16)
    zb = p.sb("zb", [128, 16 * 512], BF16)
    dtf = p.sb("dtf", [128, 512], F32)
    dtb = p.sb("dtb", [128, 512], BF16)
    dzb = p.sb("dzb", [128, 512], BF16)
    sc1 = p.sb("sc1", [128, 512], F32)
    sc2 = p.sb("sc2", [128, 512], F32)
    bc16 = p.sb("bc16", [16, 2 * TC], BF16)
    hfin = p.sb("hfins", [128, 256], F32)
    fbuf = p.sb("fbufs", [128, 256], F32)
    dscol = p.sb("dscol", [128, 1], F32)
    dsbcol = p.sb("dsbcol", [128, 1], BF16)
    wdtf = wdt and wdtt[:].bitcast(F32)
    dA = wdtf[:, 0:TC]
    ubuf = wdtf[:, TC:2 * TC]
    hbuf = wdtf[:, 2 * TC:3 * TC]
    wgf = wgtt[:].bitcast(F32)
    brep = wgf[:, 0:TC]
    crep = wgf[:, TC:2 * TC]
    dtt = wint[:, 0:TC]
    dtzt = wint[:, TC:2 * TC]
    hCt = wint[:, 2 * TC:3 * TC]
    ysc = dtb[0:8, 0:512]
    rowbuf = xt[0:1, 0:D]
    pz = p.ps("pz", [128, 512], F32)
    pz2 = p.ps("pz2", [128, 512], F32)
    pbc = p.ps("pbc", [16, 512], F32)
    pcol = p.ps("pcol", [128, 1], F32)
    pys = p.ps("pys", [8, 512], F32)
    pt1 = p.ps("pt1", [128, 128], BF16)

    p.step("sync", lambda e: [e.dma_start(out=idt[:], in_=ident[:, :]),
                              e.dma_start(out=onrw[:], in_=onesrowf[:, :]),
                              e.dma_start(out=selt[:].rearrange("p (a m) -> p a m", a=16),
                                          in_=selc[:, :].rearrange("(a p) m -> p a m", p=128)),
                              e.dma_start(out=sums[:], in_=sumselc[:, :]),
                              e.dma_start(out=s16[:], in_=sel16c[:, :]),
                              e.dma_start(out=acl[:], in_=alog[:, :]),
                              e.dma_start(out=bdtt[:], in_=bdt[:, :])], dmas=7)
    p.step("scalar", lambda e: e.activation(out=acl[:], in_=acl[:], func=AF.Exp))
    p.step("vector", lambda e: e.tensor_scalar(out=acl[:], in0=acl[:], scalar1=1e-6,
                                               scalar2=1.0, op0=OP.max, op1=OP.min))
    p.step("vector", lambda e: e.tensor_scalar(out=acl[:], in0=acl[:], scalar1=-1.0,
                                               scalar2=None, op0=OP.mult))
    for (rep, row) in ((nwrep, nw), (nbrep, nb)):
        p.step("sync", lambda e, row=row: e.dma_start(out=rowbuf, in_=row[:, :]), dmas=1)
        for ch in range(2):
            p.step("tensor", lambda e, ch=ch: e.matmul(
                pz[:], onrw[:], rowbuf[0:1, ch * 512:(ch + 1) * 512], start=True, stop=True))
            p.step("vector", lambda e, rep=rep, ch=ch: e.tensor_copy(
                rep[:, ch * 512:(ch + 1) * 512], pz[:]))
    p.step("gpsimd", lambda e: [e.dma_start(out=wint[:, i * DI:(i + 1) * DI],
                                            in_=win[i * 128:(i + 1) * 128, :])
                                for i in range(8)], dmas=8)
    p.step("gpsimd", lambda e: [e.dma_start(out=wgtt[:, i * DI:(i + 1) * DI],
                                            in_=wgt[i * 128:(i + 1) * 128, :])
                                for i in range(8)], dmas=8)
    p.step("gpsimd", lambda e: [e.dma_start(out=wdtt[:, i * DI:(i + 1) * DI],
                                            in_=wdt[i * 128:(i + 1) * 128, :])
                                for i in range(16)], dmas=16)
    p.step("gpsimd", lambda e: [e.dma_start(out=wbct[:, i * 32:i * 32 + 16],
                                            in_=wB[i * 128:(i + 1) * 128, :])
                                for i in range(16)] +
                               [e.dma_start(out=wbct[:, i * 32 + 16:i * 32 + 32],
                                            in_=wC[i * 128:(i + 1) * 128, :])
                                for i in range(16)], dmas=32)

    for blk in range(TC // 512):
        t0 = blk * 512
        for tt in range(4):
            tok = t0 + tt * 128
            p.step("sync", lambda e, tok=tok: e.dma_start(out=xt[:], in_=x[tok:tok + 128, :]),
                   dmas=1)
            emit_ln(p, xt, xnof, stats, sq)
            if full:
                p.step("sync", lambda e, tok=tok: e.dma_start(
                    out=xno[tok:tok + 128, :], in_=xnof[:]), dmas=1)
            emit_ln(p, xnof, x2f, stats, sq)
            p.step("vector", lambda e: e.tensor_tensor(out=x2f[:], in0=x2f[:], in1=nwrep[:],
                                                       op=OP.mult))
            p.step("vector", lambda e: e.tensor_tensor(out=x2n[:], in0=x2f[:], in1=nbrep[:],
                                                       op=OP.add))
            emit_tp(p, x2n, (lambda i, tt=tt: x2fm[:, i * 512 + tt * 128:
                                                   i * 512 + tt * 128 + 128]), pt1, idt)
        for mt in range(NMT):
            def fz(e, mt=mt):
                for i in range(8):
                    r = e.matmul(pz[:], wint[:, i * DI + mt * 128: i * DI + mt * 128 + 128],
                                 x2fm[:, i * 512:(i + 1) * 512],
                                 start=(i == 0), stop=(i == 7))
                return r
            p.step("tensor", fz)
            p.step("scalar", lambda e, mt=mt: e.activation(
                out=zb[:, mt * 512:(mt + 1) * 512], in_=pz[:], func=AF.Identity))
        if full:
            for mt in range(NMT):
                def fg(e, mt=mt):
                    for i in range(8):
                        r = e.matmul(pz2[:], wgtt[:, i * DI + mt * 128: i * DI + mt * 128 + 128],
                                     x2fm[:, i * 512:(i + 1) * 512],
                                     start=(i == 0), stop=(i == 7))
                    return r
                p.step("tensor", fg)
                p.step("scalar", lambda e, mt=mt: e.activation(
                    out=zb[:, 0:512] if False else sc1[:], in_=pz2[:], func=AF.Silu))
                p.step("vector", lambda e: e.tensor_copy(sc2[:].bitcast(BF16)[:, 0:512], sc1[:]))
                p.step("gpsimd", lambda e, mt=mt, t0=t0: e.dma_start(
                    out=gts[mt * 128:(mt + 1) * 128, t0:t0 + 512],
                    in_=sc2[:].bitcast(BF16)[:, 0:512]), dmas=1)
        for (off, woff) in ((0, 0), (TC, 16)):
            def fbc(e, off=off, woff=woff, t0=t0):
                for i in range(16):
                    r = e.matmul(pbc[:], wbct[:, i * 32 + woff: i * 32 + woff + 16],
                                 zb[:, i * 512:(i + 1) * 512],
                                 start=(i == 0), stop=(i == 15))
                return r
            p.step("tensor", fbc)
            p.step("vector", lambda e, off=off, t0=t0: e.tensor_copy(
                bc16[:, off + t0: off + t0 + 512], pbc[:]))
        for mt in range(NMT):
            def fdt(e, mt=mt):
                for i in range(16):
                    r = e.matmul(pz2[:], wdtt[:, i * DI + mt * 128: i * DI + mt * 128 + 128],
                                 zb[:, i * 512:(i + 1) * 512],
                                 start=(i == 0), stop=(i == 15))
                return r
            p.step("tensor", fdt)
            p.step("vector", lambda e, mt=mt: e.tensor_scalar(
                out=dtf[:], in0=pz2[:], scalar1=bdtt[:, mt:mt + 1], scalar2=None, op0=OP.add))
            p.step("scalar", lambda e: e.activation(out=sc1[:], in_=dtf[:], func=AF.Abs))
            p.step("scalar", lambda e: e.activation(out=sc1[:], in_=sc1[:], func=AF.Exp,
                                                    scale=-1.0))
            p.step("scalar", lambda e: e.activation(out=sc1[:], in_=sc1[:], func=AF.Ln,
                                                    bias=1.0))
            p.step("vector", lambda e: e.tensor_relu(sc2[:], dtf[:]))
            p.step("vector", lambda e: e.tensor_tensor(out=dtf[:], in0=sc1[:], in1=sc2[:],
                                                       op=OP.add))
            p.step("vector", lambda e: e.tensor_scalar(out=dtf[:], in0=dtf[:], scalar1=0.001,
                                                       scalar2=1.0, op0=OP.max, op1=OP.min))
            p.step("vector", lambda e: e.tensor_copy(dtb[:], dtf[:]))
            p.step("sync", lambda e, mt=mt, t0=t0: e.dma_start(
                out=dts[mt * 128:(mt + 1) * 128, t0:t0 + 512], in_=dtb[:]), dmas=1)
            p.step("vector", lambda e, mt=mt: e.tensor_tensor(
                out=dzb[:], in0=dtb[:], in1=zb[:, mt * 512:(mt + 1) * 512], op=OP.mult))
            p.step("sync", lambda e, mt=mt, t0=t0: e.dma_start(
                out=dzs[mt * 128:(mt + 1) * 128, t0:t0 + 512], in_=dzb[:]), dmas=1)
    for (src_off, rep) in ((0, brep), (TC, crep)):
        for cc in range(TC // 512):
            p.step("tensor", lambda e, src_off=src_off, cc=cc: e.matmul(
                pz[:], s16[:], bc16[:, src_off + cc * 512: src_off + cc * 512 + 512],
                start=True, stop=True))
            p.step("vector", lambda e, rep=rep, cc=cc: e.tensor_copy(
                rep[:, cc * 512:(cc + 1) * 512], pz[:]))
    if full:
        zbf = zb[:].bitcast(F32)
        hfa = zbf[:, 0:1024]
        ffa = zbf[:, 1024:2048]
        hin = zbf[:, 2048:2304]
        tb1 = zbf[:, 2304:2560]
        tb2 = zbf[:, 2560:2816]
        inc4 = zbf[:, 2816:2820]
        p.step("sync", lambda e: [e.dma_start(out=hfa.rearrange("p (a m) -> p a m", a=4),
                                              in_=hfall[:, :, :].rearrange("a p m -> p a m")),
                                  e.dma_start(out=ffa.rearrange("p (a m) -> p a m", a=4),
                                              in_=fall[:, :, :].rearrange("a p m -> p a m")),
                                  e.dma_start(out=inc4[0:1, 0:4], in_=incr[:, :])], dmas=3)
        p.step("tensor", lambda e: e.matmul(pz[:, 0:4], onrw[:], inc4[0:1, 0:4],
                                            start=True, stop=True))
        p.step("vector", lambda e: e.tensor_copy(inc4, pz[:, 0:4]))
        p.step("vector", lambda e: e.memset(hin, 0.0))
        for cc in range(4):
            p.step("vector", lambda e, cc=cc: e.tensor_scalar(
                out=tb1, in0=ffa[:, cc * 256:(cc + 1) * 256], scalar1=inc4[:, cc:cc + 1],
                scalar2=None, op0=OP.mult))
            p.step("vector", lambda e, cc=cc: e.tensor_scalar(
                out=tb2, in0=inc4[:, cc:cc + 1].to_broadcast([128, 256]), scalar1=-1.0,
                scalar2=1.0, op0=OP.mult, op1=OP.add))
            p.step("vector", lambda e: e.tensor_tensor(out=tb1, in0=tb1, in1=tb2,
                                                       op=OP.add))
            p.step("vector", lambda e: e.tensor_tensor(out=hin, in0=hin, in1=tb1,
                                                       op=OP.mult))
            p.step("vector", lambda e, cc=cc: e.tensor_scalar(
                out=tb2, in0=hfa[:, cc * 256:(cc + 1) * 256], scalar1=inc4[:, cc:cc + 1],
                scalar2=None, op0=OP.mult))
            p.step("vector", lambda e: e.tensor_tensor(out=hin, in0=hin, in1=tb2,
                                                       op=OP.add))
    for k in range(16):
        p.step("sync", lambda e, k=k: e.dma_start(out=dtt, in_=dts[k * 128:(k + 1) * 128, :]),
               dmas=1)
        p.step("sync", lambda e, k=k: e.dma_start(out=dtzt, in_=dzs[k * 128:(k + 1) * 128, :]),
               dmas=1)
        p.step("vector", lambda e: e.reduce_sum(dscol[:, 0:1], dtt, axis=AX))
        p.step("vector", lambda e: e.tensor_copy(dsbcol[:], dscol[:]))
        for j in range(16):
            col = k * 16 + j
            for cc in range(TC // 512):
                p.step("tensor", lambda e, j=j, cc=cc: e.matmul(
                    pz[:], selt[:, j * 128:(j + 1) * 128], dtt[:, cc * 512:(cc + 1) * 512],
                    start=True, stop=True))
                p.step("scalar", lambda e, col=col, cc=cc: e.activation(
                    out=dA[:, cc * 512:(cc + 1) * 512], in_=pz[:], func=AF.Exp,
                    scale=acl[:, col:col + 1]))
                p.step("tensor", lambda e, j=j, cc=cc: e.matmul(
                    pz2[:], selt[:, j * 128:(j + 1) * 128], dtzt[:, cc * 512:(cc + 1) * 512],
                    start=True, stop=True))
                p.step("vector", lambda e, cc=cc: e.tensor_tensor(
                    out=ubuf[:, cc * 512:(cc + 1) * 512], in0=pz2[:],
                    in1=brep[:, cc * 512:(cc + 1) * 512], op=OP.mult))
            if full:
                p.step("vector", lambda e, col=col: e.tensor_tensor_scan(
                    out=hbuf, data0=dA, data1=ubuf, initial=hin[:, col:col + 1] if True else 0,
                    op0=OP.mult, op1=OP.add))
            else:
                p.step("vector", lambda e: e.tensor_tensor_scan(
                    out=hbuf, data0=dA, data1=ubuf, initial=0.0,
                    op0=OP.mult, op1=OP.add))
            p.step("vector", lambda e, col=col: e.tensor_copy(hfin[:, col:col + 1],
                                                              hbuf[:, TC - 1:TC]))
            p.step("tensor", lambda e, j=j: e.matmul(
                pcol[:], selt[:, j * 128:(j + 1) * 128], dsbcol[:, 0:1],
                start=True, stop=True))
            p.step("scalar", lambda e, col=col: e.activation(
                out=fbuf[:, col:col + 1], in_=pcol[:], func=AF.Exp,
                scale=acl[:, col:col + 1]))
            if full:
                p.step("vector", lambda e: e.tensor_tensor(out=hCt, in0=hbuf, in1=crep,
                                                           op=OP.mult))
                for cc in range(TC // 512):
                    p.step("tensor", lambda e, cc=cc: e.matmul(
                        pys[:], sums[:], hCt[:, cc * 512:(cc + 1) * 512],
                        start=True, stop=True))
                    p.step("scalar", lambda e, cc=cc: e.activation(
                        out=ysc[:], in_=pys[:], func=AF.Identity))
                    p.step("sync", lambda e, k=k, j=j, cc=cc: e.dma_start(
                        out=ys[k * 128 + 8 * j: k * 128 + 8 * j + 8,
                               cc * 512:(cc + 1) * 512], in_=ysc[:]), dmas=1)
    if not full:
        p.step("sync", lambda e: [e.dma_start(out=d_hf[:, :], in_=hfin[:]),
                                  e.dma_start(out=d_ff[:, :], in_=fbuf[:])], dmas=2)
        return p.build()
    woutt = wdtt[:, 0:16 * D]
    ygb = wint[:, 0:16 * 512]
    gtb = wint[:, 16 * 512:32 * 512]
    p.step("gpsimd", lambda e: [e.dma_start(out=woutt[:, i * D:(i + 1) * D],
                                            in_=wout[i * 128:(i + 1) * 128, :])
                                for i in range(16)], dmas=16)
    for blk in range(TC // 512):
        t0 = blk * 512
        p.step("sync", lambda e, t0=t0: e.dma_start(
            out=ygb.rearrange("p (a m) -> p a m", a=16),
            in_=ys[:, t0:t0 + 512].rearrange("(a p) m -> p a m", p=128)), dmas=1)
        p.step("sync", lambda e, t0=t0: e.dma_start(
            out=gtb.rearrange("p (a m) -> p a m", a=16),
            in_=gts[:, t0:t0 + 512].rearrange("(a p) m -> p a m", p=128)), dmas=1)
        p.step("vector", lambda e: e.tensor_tensor(out=ygb, in0=ygb, in1=gtb, op=OP.mult))
        for tt in range(4):
            tok = t0 + tt * 128
            p.step("sync", lambda e, tok=tok: e.dma_start(out=xt[:], in_=x[tok:tok + 128, :]),
                   dmas=1)
            p.step("sync", lambda e, tok=tok: e.dma_start(out=xnof[:],
                                                          in_=xno[tok:tok + 128, :]), dmas=1)
            p.step("vector", lambda e: e.tensor_tensor(out=xt[:], in0=xt[:], in1=xnof[:],
                                                       op=OP.add))
            for ch in range(2):
                def fo(e, tt=tt, ch=ch):
                    for i in range(16):
                        r = e.matmul(pz[:], ygb[:, i * 512 + tt * 128: i * 512 + tt * 128 + 128],
                                     woutt[:, i * D + ch * 512: i * D + ch * 512 + 512],
                                     start=(i == 0), stop=(i == 15))
                    return r
                p.step("tensor", fo)
                p.step("vector", lambda e, ch=ch: e.tensor_tensor(
                    out=xt[:, ch * 512:(ch + 1) * 512], in0=xt[:, ch * 512:(ch + 1) * 512],
                    in1=pz[:], op=OP.add))
            p.step("sync", lambda e, tok=tok: e.dma_start(out=xout[tok:tok + 128, :],
                                                          in_=xt[:]), dmas=1)
    return p.build()



def build_ffn2():
    HF = 2048
    nc = bass.Bass("TRN2", target_bir_lowering=False)
    x = nc.dram_tensor("x", [TC, D], F32, kind="ExternalInput").ap()
    lng = nc.dram_tensor("lng", [1, D], F32, kind="ExternalInput").ap()
    lnb = nc.dram_tensor("lnb", [1, D], F32, kind="ExternalInput").ap()
    rw = nc.dram_tensor("rw", [D, 8], F32, kind="ExternalInput").ap()
    rb = nc.dram_tensor("rb", [1, 8], F32, kind="ExternalInput").ap()
    w1 = nc.dram_tensor("w1", [8, D, HF], F32, kind="ExternalInput").ap()
    b1r = nc.dram_tensor("b1r", [8 * 128, 16], F32, kind="ExternalInput").ap()
    w2 = nc.dram_tensor("w2", [8, HF, D], F32, kind="ExternalInput").ap()
    b2 = nc.dram_tensor("b2", [8, D], F32, kind="ExternalInput").ap()
    ident = nc.dram_tensor("ident", [128, 128], BF16, kind="ExternalInput").ap()
    identf = nc.dram_tensor("identf", [128, 128], F32, kind="ExternalInput").ap()
    onesrowf = nc.dram_tensor("onesrowf", [1, 128], F32, kind="ExternalInput").ap()
    xout = nc.dram_tensor("xout", [TC, D], F32, kind="ExternalOutput").ap()
    xno = nc.dram_tensor("xno", [TC, D], F32, kind="Internal").ap()

    p = Prog(nc)
    idt = p.sb("idt", [128, 128], BF16)
    idtf = p.sb("idtf", [128, 128], F32)
    onrw = p.sb("onrw", [1, 128], F32)
    lngrep = p.sb("lngrep", [128, D], F32)
    lnbrep = p.sb("lnbrep", [128, D], F32)
    rwt = p.sb("rwt", [128, 8 * 8], BF16)
    rbrep = p.sb("rbrep", [128, 8], F32)
    b2t = p.sb("b2t", [8, D], F32)
    w1t = p.sb("w1t", [128, 8 * HF], BF16)
    w2t = p.sb("w2t", [128, 16 * D], BF16)
    b1c = p.sb("b1c", [128, 16], F32)
    xt = p.sb("xt", [128, D], F32)
    xnof = p.sb("xnof", [128, D], F32)
    x2n = p.sb("x2n", [128, D], BF16)
    sq = p.sb("sq", [128, D], F32)
    stats = p.sb("stats", [128, 8], F32)
    zfm = p.sb("zfm", [128, 8 * 1024], BF16)
    acc = p.sb("acc", [128, 8 * 1024], F32)
    h1f = p.sb("h1f", [128, 16 * 512], BF16)
    wrepb = p.sb("wrepb", [128, 512], BF16)
    lgt = p.sb("lgt", [128, 8], F32)
    prt = p.sb("prt", [128, 8], F32)
    eqt = p.sb("eqt", [128, 8], F32)
    cum = p.sb("cum", [128, 8], F32)
    fst = p.sb("fst", [128, 8], F32)
    p2t = p.sb("p2t", [128, 8], F32)
    sect = p.sb("sect", [128, 8], F32)
    wgt8 = p.sb("wgt8", [128, 8], F32)
    on8 = p.sb("on8", [128, 8], F32)
    cols = p.sb("cols", [128, 8], F32)
    wpad = p.sb("wpad", [128, 128], F32)
    wTsb = p.sb("wTsb", [8, 1024], F32)
    wTflat = p.sb("wTflat", [1, 8 * 1024], F32)
    pz = p.ps("pz", [128, 512], F32)
    pz2 = p.ps("pz2", [128, 512], F32)
    pg8 = p.ps("pg8", [128, 8], F32)
    ptw = p.ps("ptw", [128, 128], F32)
    pt1 = p.ps("pt1", [128, 128], BF16)

    x2f = xt[:]
    rowb = xt[0:1, 0:D]
    p.step("sync", lambda e: [e.dma_start(out=idt[:], in_=ident[:, :]),
                              e.dma_start(out=idtf[:], in_=identf[:, :]),
                              e.dma_start(out=onrw[:], in_=onesrowf[:, :]),
                              e.dma_start(out=b2t[:], in_=b2[:, :])], dmas=4)
    p.step("gpsimd", lambda e: [e.dma_start(out=rwt[:, i * 8:(i + 1) * 8],
                                            in_=rw[i * 128:(i + 1) * 128, :])
                                for i in range(8)], dmas=8)
    p.step("vector", lambda e: e.memset(on8[:], 1.0))
    for (rep, row) in ((lngrep, lng), (lnbrep, lnb)):
        p.step("sync", lambda e, row=row: e.dma_start(out=rowb, in_=row[:, :]), dmas=1)
        for ch in range(2):
            p.step("tensor", lambda e, ch=ch: e.matmul(
                pz[:], onrw[:], rowb[0:1, ch * 512:(ch + 1) * 512], start=True, stop=True))
            p.step("vector", lambda e, rep=rep, ch=ch: e.tensor_copy(
                rep[:, ch * 512:(ch + 1) * 512], pz[:]))
    p.step("sync", lambda e: e.dma_start(out=wpad[0:1, 0:8], in_=rb[:, :]), dmas=1)
    p.step("tensor", lambda e: e.matmul(pg8[:], onrw[:], wpad[0:1, 0:8], start=True, stop=True))
    p.step("vector", lambda e: e.tensor_copy(rbrep[:], pg8[:]))
    p.step("vector", lambda e: e.memset(wpad[:], 0.0))

    for blk in range(4):
        t0 = blk * 1024
        for tt in range(8):
            tok = t0 + tt * 128
            p.step("sync", lambda e, tok=tok: e.dma_start(out=xt[:], in_=x[tok:tok + 128, :]),
                   dmas=1)
            emit_ln(p, xt, xnof, stats, sq)
            p.step("sync", lambda e, tok=tok: e.dma_start(out=xno[tok:tok + 128, :],
                                                          in_=xnof[:]), dmas=1)
            emit_ln(p, xnof, x2f, stats, sq)
            p.step("vector", lambda e: e.tensor_tensor(out=x2f[:], in0=x2f[:], in1=lngrep[:],
                                                       op=OP.mult))
            p.step("vector", lambda e: e.tensor_tensor(out=x2n[:], in0=x2f[:], in1=lnbrep[:],
                                                       op=OP.add))
            emit_tp(p, x2n, (lambda i, tt=tt: zfm[:, i * 1024 + tt * 128:
                                                  i * 1024 + tt * 128 + 128]), pt1, idt)
            def frt(e, tt=tt):
                for i in range(8):
                    r = e.matmul(pg8[:], zfm[:, i * 1024 + tt * 128: i * 1024 + tt * 128 + 128],
                                 rwt[:, i * 8:(i + 1) * 8], start=(i == 0), stop=(i == 7))
                return r
            p.step("tensor", frt)
            p.step("vector", lambda e: e.tensor_tensor(out=lgt[:], in0=pg8[:], in1=rbrep[:],
                                                       op=OP.add))
            p.step("vector", lambda e: e.reduce_max(stats[:, 0:1], lgt[:], axis=AX, negate=True))
            p.step("scalar", lambda e: e.activation(out=prt[:], in_=lgt[:], func=AF.Exp,
                                                    bias=stats[:, 0:1],
                                                    accum_out=stats[:, 1:2]))
            p.step("vector", lambda e: e.reciprocal(stats[:, 2:3], stats[:, 1:2]))
            p.step("vector", lambda e: e.tensor_scalar(out=prt[:], in0=prt[:],
                                                       scalar1=stats[:, 2:3], scalar2=None,
                                                       op0=OP.mult))
            p.step("vector", lambda e: e.reduce_max(stats[:, 3:4], prt[:], axis=AX))
            p.step("vector", lambda e: e.tensor_scalar(out=eqt[:], in0=prt[:],
                                                       scalar1=stats[:, 3:4], scalar2=None,
                                                       op0=OP.is_ge))
            p.step("vector", lambda e: e.tensor_tensor_scan(out=cum[:], data0=on8[:],
                                                            data1=eqt[:], initial=0.0,
                                                            op0=OP.mult, op1=OP.add))
            p.step("vector", lambda e: e.tensor_scalar(out=cum[:], in0=cum[:], scalar1=1.0,
                                                       scalar2=None, op0=OP.is_le))
            p.step("vector", lambda e: e.tensor_tensor(out=fst[:], in0=eqt[:], in1=cum[:],
                                                       op=OP.mult))
            p.step("vector", lambda e: e.tensor_scalar(out=cols[:], in0=fst[:], scalar1=-1.0,
                                                       scalar2=1.0, op0=OP.mult, op1=OP.add))
            p.step("vector", lambda e: e.tensor_tensor(out=p2t[:], in0=prt[:], in1=cols[:],
                                                       op=OP.mult))
            p.step("vector", lambda e: e.reduce_max(stats[:, 4:5], p2t[:], axis=AX))
            p.step("vector", lambda e: e.tensor_scalar(out=eqt[:], in0=p2t[:],
                                                       scalar1=stats[:, 4:5], scalar2=None,
                                                       op0=OP.is_ge))
            p.step("vector", lambda e: e.tensor_tensor_scan(out=cum[:], data0=on8[:],
                                                            data1=eqt[:], initial=0.0,
                                                            op0=OP.mult, op1=OP.add))
            p.step("vector", lambda e: e.tensor_scalar(out=cum[:], in0=cum[:], scalar1=1.0,
                                                       scalar2=None, op0=OP.is_le))
            p.step("vector", lambda e: e.tensor_tensor(out=sect[:], in0=eqt[:], in1=cum[:],
                                                       op=OP.mult))
            p.step("vector", lambda e: e.tensor_tensor(out=stats[:, 5:6], in0=stats[:, 3:4],
                                                       in1=stats[:, 4:5], op=OP.add))
            p.step("vector", lambda e: e.tensor_scalar(out=stats[:, 5:6], in0=stats[:, 5:6],
                                                       scalar1=1e-8, scalar2=None, op0=OP.max))
            p.step("vector", lambda e: e.reciprocal(stats[:, 6:7], stats[:, 5:6]))
            p.step("vector", lambda e: e.tensor_scalar(out=wgt8[:], in0=fst[:],
                                                       scalar1=stats[:, 3:4], scalar2=None,
                                                       op0=OP.mult))
            p.step("vector", lambda e: e.scalar_tensor_tensor(out=wgt8[:], in0=sect[:],
                                                              scalar=stats[:, 4:5],
                                                              in1=wgt8[:], op0=OP.mult,
                                                              op1=OP.add))
            p.step("vector", lambda e: e.tensor_scalar(out=wgt8[:], in0=wgt8[:],
                                                       scalar1=stats[:, 6:7], scalar2=None,
                                                       op0=OP.mult))
            p.step("vector", lambda e: e.tensor_copy(wpad[:, 0:8], wgt8[:]))
            p.step("tensor", lambda e: e.transpose(ptw[:], wpad[:], idtf[:]))
            p.step("vector", lambda e, tt=tt: e.tensor_copy(wTsb[0:8, tt * 128:(tt + 1) * 128],
                                                            ptw[0:8, 0:128]))
        p.step("gpsimd", lambda e: [e.dma_start(out=wTflat[0:1, ex2 * 1024:(ex2 + 1) * 1024],
                                                in_=wTsb[ex2:ex2 + 1, :])
                                    for ex2 in range(8)], dmas=8)
        p.step("vector", lambda e: e.memset(acc[:], 0.0))
        for ex in range(8):
            p.step("gpsimd", lambda e, ex=ex: [e.dma_start(out=w1t[:, i * HF:(i + 1) * HF],
                                                           in_=w1[ex, i * 128:(i + 1) * 128, :])
                                               for i in range(8)], dmas=8)
            p.step("gpsimd", lambda e, ex=ex: [e.dma_start(out=w2t[:, i * D:(i + 1) * D],
                                                           in_=w2[ex, i * 128:(i + 1) * 128, :])
                                               for i in range(16)], dmas=16)
            p.step("sync", lambda e, ex=ex: e.dma_start(out=b1c[:], in_=b1r[ex * 128:(ex + 1) * 128, :]),
                   dmas=1)
            for sb2 in range(2):
                c0 = sb2 * 512
                for mt in range(16):
                    def fh1(e, mt=mt, c0=c0):
                        for i in range(8):
                            r = e.matmul(pz[:], w1t[:, i * HF + mt * 128: i * HF + mt * 128 + 128],
                                         zfm[:, i * 1024 + c0: i * 1024 + c0 + 512],
                                         start=(i == 0), stop=(i == 7))
                        return r
                    p.step("tensor", fh1)
                    p.step("scalar", lambda e, mt=mt: e.activation(
                        out=h1f[:, mt * 512:(mt + 1) * 512], in_=pz[:], func=AF.Gelu,
                        bias=b1c[:, mt:mt + 1]))
                p.step("tensor", lambda e, ex=ex, c0=c0: e.matmul(
                    pz2[:], onrw[:], wTflat[0:1, ex * 1024 + c0: ex * 1024 + c0 + 512],
                    start=True, stop=True))
                p.step("vector", lambda e: e.tensor_copy(wrepb[:], pz2[:]))
                for mt in range(16):
                    p.step("vector", lambda e, mt=mt: e.tensor_tensor(
                        out=h1f[:, mt * 512:(mt + 1) * 512],
                        in0=h1f[:, mt * 512:(mt + 1) * 512], in1=wrepb[:], op=OP.mult))
                for ot in range(8):
                    for kb in range(2):
                        def fg2(e, ot=ot, kb=kb):
                            for i in range(8):
                                k = kb * 8 + i
                                r = e.matmul(pz2[:], w2t[:, k * D + ot * 128: k * D + ot * 128 + 128],
                                             h1f[:, k * 512:(k + 1) * 512],
                                             start=(k == 0), stop=(k == 15))
                            return r
                        p.step("tensor", fg2)
                    p.step("vector", lambda e, ot=ot, c0=c0: e.tensor_tensor(
                        out=acc[:, ot * 1024 + c0: ot * 1024 + c0 + 512],
                        in0=acc[:, ot * 1024 + c0: ot * 1024 + c0 + 512], in1=pz2[:],
                        op=OP.add))
        for ot in range(8):
            for sb2 in range(2):
                c0 = sb2 * 512
                p.step("tensor", lambda e, ot=ot, c0=c0: e.matmul(
                    pz2[:], b2t[:, ot * 128:(ot + 1) * 128], wTsb[0:8, c0:c0 + 512],
                    start=True, stop=True))
                p.step("vector", lambda e, ot=ot, c0=c0: e.tensor_tensor(
                    out=acc[:, ot * 1024 + c0: ot * 1024 + c0 + 512],
                    in0=acc[:, ot * 1024 + c0: ot * 1024 + c0 + 512], in1=pz2[:],
                    op=OP.add))
        for tt in range(8):
            tok = t0 + tt * 128
            p.step("sync", lambda e, tok=tok: [e.dma_start(out=xt[:], in_=x[tok:tok + 128, :]),
                                               e.dma_start(out=xnof[:], in_=xno[tok:tok + 128, :])],
                   dmas=2)
            p.step("vector", lambda e: e.tensor_tensor(out=xnof[:], in0=xnof[:], in1=xt[:],
                                                       op=OP.add))
            for j in range(8):
                p.step("tensor", lambda e, j=j, tt=tt: e.transpose(
                    ptw[:], acc[:, j * 1024 + tt * 128: j * 1024 + tt * 128 + 128], idtf[:]))
                p.step("vector", lambda e, j=j: e.tensor_tensor(
                    out=xnof[:, j * 128:(j + 1) * 128], in0=xnof[:, j * 128:(j + 1) * 128],
                    in1=ptw[:], op=OP.add))
            p.step("sync", lambda e, tok=tok: e.dma_start(out=xout[tok:tok + 128, :],
                                                          in_=xnof[:]), dmas=1)
    return p.build()



def m2_consts():
    mask01 = np.zeros((128, 128), np.float32)
    for i in range(128):
        mask01[i, :i] = 1.0
    return {"mask01": mask01}


def build_m2tails():
    DI2 = 2048
    nc = bass.Bass("TRN2", target_bir_lowering=False)
    x = nc.dram_tensor("x", [TC, D], F32, kind="ExternalInput").ap()
    inpT = nc.dram_tensor("inpT", [D, 4384], F32, kind="ExternalInput").ap()
    ident = nc.dram_tensor("ident", [128, 128], BF16, kind="ExternalInput").ap()
    tails = nc.dram_tensor("tails", [DI2, 3], F32, kind="ExternalOutput").ap()
    p = Prog(nc)
    idt = p.sb("idt", [128, 128], BF16)
    wt = p.sb("wt", [128, 8 * DI2], BF16)
    xt = p.sb("xt", [128, D], F32)
    xn = p.sb("xn", [128, D], BF16)
    sq = p.sb("sq", [128, D], F32)
    stats = p.sb("stats", [128, 8], F32)
    xnfm = p.sb("xnfm", [128, 8 * 128], BF16)
    ev = p.sb("ev", [128, 3], F32)
    pm = p.ps("pm", [128, 128], F32)
    pt1 = p.ps("pt1", [128, 128], BF16)
    p.step("sync", lambda e: e.dma_start(out=idt[:], in_=ident[:, :]), dmas=1)
    p.step("gpsimd", lambda e: [e.dma_start(out=wt[:, i * DI2:(i + 1) * DI2],
                                            in_=inpT[i * 128:(i + 1) * 128, 0:DI2])
                                for i in range(8)], dmas=8)
    p.step("sync", lambda e: e.dma_start(out=xt[:], in_=x[TC - 128:TC, :]), dmas=1)
    emit_ln(p, xt, xn, stats, sq)
    emit_tp(p, xn, (lambda j: xnfm[:, j * 128:(j + 1) * 128]), pt1, idt)
    for mt in range(16):
        def fp(e, mt=mt):
            for i in range(8):
                r = e.matmul(pm[:], wt[:, i * DI2 + mt * 128: i * DI2 + mt * 128 + 128],
                             xnfm[:, i * 128:(i + 1) * 128], start=(i == 0), stop=(i == 7))
            return r
        p.step("tensor", fp)
        p.step("vector", lambda e: e.tensor_copy(ev[:], pm[:, 125:128]))
        p.step("sync", lambda e, mt=mt: e.dma_start(out=tails[mt * 128:(mt + 1) * 128, :],
                                                    in_=ev[:]), dmas=1)
    return p.build()


def build_m2(full):
    DI2, NH, HD, DS = 2048, 32, 64, 128
    NCH = TC // 128
    nc = bass.Bass("TRN2", target_bir_lowering=False)
    x = nc.dram_tensor("x", [TC, D], F32, kind="ExternalInput").ap()
    inpT = nc.dram_tensor("inpT", [D, 4384], F32, kind="ExternalInput").ap()
    cw = nc.dram_tensor("cw", [128, 16 * 4], F32, kind="ExternalInput").ap()
    cb = nc.dram_tensor("cb", [128, 16], F32, kind="ExternalInput").ap()
    alog = nc.dram_tensor("alog", [1, NH], F32, kind="ExternalInput").ap()
    dpr = nc.dram_tensor("dpr", [1, NH], F32, kind="ExternalInput").ap()
    dtbias = nc.dram_tensor("dtbias", [NH, 1], F32, kind="ExternalInput").ap()
    tails = nc.dram_tensor("tails", [DI2, 3], F32, kind="ExternalInput").ap()
    mask01 = nc.dram_tensor("mask01", [128, 128], F32, kind="ExternalInput").ap()
    triu = nc.dram_tensor("triu", [128, 128], BF16, kind="ExternalInput").ap()
    ident = nc.dram_tensor("ident", [128, 128], BF16, kind="ExternalInput").ap()
    identf = nc.dram_tensor("identf", [128, 128], F32, kind="ExternalInput").ap()
    onesrowf = nc.dram_tensor("onesrowf", [1, 128], F32, kind="ExternalInput").ap()
    xss = nc.dram_tensor("xss", [DI2, TC], BF16, kind="Internal").ap()
    xsls = nc.dram_tensor("xsls", [DI2, TC], BF16, kind="Internal").ap()
    xslt = nc.dram_tensor("xslt", [TC, DI2], BF16, kind="Internal").ap()
    xdts = nc.dram_tensor("xdts", [TC, DI2], BF16, kind="Internal").ap()
    zss = nc.dram_tensor("zss", [TC, DI2], BF16, kind="Internal").ap()
    if full:
        nw2 = nc.dram_tensor("nw2", [1, DI2], F32, kind="ExternalInput").ap()
        outT = nc.dram_tensor("outT", [DI2, D], F32, kind="ExternalInput").ap()
        hstall = nc.dram_tensor("hstall", [4, 128, NH * HD], F32, kind="ExternalInput").ap()
        ftall = nc.dram_tensor("ftall", [4, 128, NH], F32, kind="ExternalInput").ap()
        incr = nc.dram_tensor("incr", [1, 4], F32, kind="ExternalInput").ap()
        xout = nc.dram_tensor("xout", [TC, D], F32, kind="ExternalOutput").ap()
    else:
        d_hst = nc.dram_tensor("hst", [128, NH * HD], F32, kind="ExternalOutput").ap()
        d_ft = nc.dram_tensor("ft", [128, NH], F32, kind="ExternalOutput").ap()

    p = Prog(nc)
    idt = p.sb("idt", [128, 128], BF16)
    idtf = p.sb("idtf", [128, 128], F32)
    trut = p.sb("trut", [128, 128], BF16)
    onrw = p.sb("onrw", [1, 128], F32)
    onescolb = p.sb("onescolb", [128, 1], BF16)
    msk = p.sb("msk", [128, 128], F32)
    inptt = p.sb("inptt", [128, 8 * 4384], BF16)
    cwt = p.sb("cwt", [128, 16 * 4], F32)
    cbt = p.sb("cbt", [128, 16], F32)
    arow = p.sb("arow", [1, NH], F32)
    dprow = p.sb("dprow", [1, NH], F32)
    dtbc = p.sb("dtbc", [NH, 1], F32)
    arep = p.sb("arep", [128, NH], F32)
    dp8 = p.sb("dp8", [128, NH], F32)
    dp8b = p.sb("dp8b", [128, NH], BF16)
    xt = p.sb("xt", [128, D], F32)
    xn = p.sb("xn", [128, D], BF16)
    sq = p.sb("sq", [128, D], F32)
    stats = p.sb("stats", [128, 8], F32)
    xnfm = p.sb("xnfm", [128, 8 * 512], BF16)
    bfm = p.sb("bfm", [128, TC], BF16)
    cfm = p.sb("cfm", [128, TC], BF16)
    dti = p.sb("dti", [NH, TC], F32)
    dttm = p.sb("dttm", [128, NCH * NH], F32)
    dttmb = p.sb("dttmb", [128, NCH * NH], BF16)
    xe = p.sb("xe", [128, 4 + TC], BF16)
    ipfv = inptt[:].bitcast(F32)
    cacc = ipfv[:, 10240:10240 + TC]
    hst = p.sb("hstl", [128, NH * HD], F32)
    cdc = p.sb("cdc", [128, NH], F32)
    acd = p.sb("acd", [128, NH], F32)
    epl = p.sb("epl", [128, NH], F32)
    srep = p.sb("srep", [128, NH], F32)
    wdtm = p.sb("wdtm", [128, NH], F32)
    fmat = p.sb("fmat", [128, NH], F32)
    cpad = p.sb("cpad", [128, 128], F32)
    cdT = p.sb("cdT", [32, 128], F32)
    cdTb = p.sb("cdTb", [32, 128], BF16)
    ncdTb = p.sb("ncdTb", [32, 128], BF16)
    lhsT2 = p.sb("lhsT2", [2, 128], BF16)
    rhs2 = p.sb("rhs2", [2, 128], BF16)
    asb = p.sb("asb", [128, 128], F32)
    ssb = p.sb("ssb", [128, 128], BF16)
    btmf = p.sb("btmf", [128, 128], F32)
    bwb = p.sb("bwb", [128, 128], BF16)
    xdtc = p.sb("xdtc", [128, DI2], BF16)
    xsltc = p.sb("xsltc", [128, DI2], BF16)
    zsc = p.sb("zsc", [128, DI2], BF16)
    ycar = p.sb("ycar", [128, 512], F32)
    dterm = p.sb("dterm", [128, 512], F32)
    ych = p.sb("ych", [128, DI2], F32)
    ygb = p.sb("ygb", [128, DI2], BF16)
    ygfm = p.sb("ygfm", [128, 16 * 128], BF16)
    row2 = p.sb("row2", [1, DI2], F32)
    nwrep2 = ipfv[:, 8192:8192 + DI2]
    hstb = inptt[:, 20480:20480 + DI2]
    pm = p.ps("pm", [128, 512], F32)
    pm2 = p.ps("pm2", [128, 512], F32)
    py = p.ps("py", [128, 512], F32)
    pc = p.ps("pc", [128, 512], F32)
    pcon = p.ps("pcon", [128, 512], F32)
    pt1 = p.ps("pt1", [128, 128], BF16)
    ptf = p.ps("ptf", [128, 128], F32)

    p.step("sync", lambda e: [e.dma_start(out=idt[:], in_=ident[:, :]),
                              e.dma_start(out=idtf[:], in_=identf[:, :]),
                              e.dma_start(out=trut[:], in_=triu[:, :]),
                              e.dma_start(out=onrw[:], in_=onesrowf[:, :]),
                              e.dma_start(out=msk[:], in_=mask01[:, :]),
                              e.dma_start(out=cwt[:], in_=cw[:, :]),
                              e.dma_start(out=cbt[:], in_=cb[:, :]),
                              e.dma_start(out=arow[:], in_=alog[:, :]),
                              e.dma_start(out=dprow[:], in_=dpr[:, :]),
                              e.dma_start(out=dtbc[:], in_=dtbias[:, :])], dmas=10)
    p.step("gpsimd", lambda e: [e.dma_start(out=inptt[:, i * 4384:(i + 1) * 4384],
                                            in_=inpT[i * 128:(i + 1) * 128, :])
                                for i in range(8)], dmas=8)
    p.step("vector", lambda e: e.memset(onescolb[:], 1.0))
    p.step("vector", lambda e: e.memset(cpad[:], 0.0))
    p.step("scalar", lambda e: e.activation(out=arow[:], in_=arow[:], func=AF.Exp))
    p.step("vector", lambda e: e.tensor_scalar(out=arow[:], in0=arow[:], scalar1=-1.0,
                                               scalar2=None, op0=OP.mult))
    for (rep, row) in ((arep, arow), (dp8, dprow)):
        p.step("tensor", lambda e, row=row: e.matmul(pm[:, 0:NH], onrw[:], row[0:1, :],
                                                     start=True, stop=True))
        p.step("vector", lambda e, rep=rep: e.tensor_copy(rep[:], pm[:, 0:NH]))
    p.step("vector", lambda e: e.tensor_copy(dp8b[:], dp8[:]))
    for blk in range(TC // 512):
        t0 = blk * 512
        for tt in range(4):
            tok = t0 + tt * 128
            p.step("sync", lambda e, tok=tok: e.dma_start(out=xt[:], in_=x[tok:tok + 128, :]),
                   dmas=1)
            emit_ln(p, xt, xn, stats, sq)
            emit_tp(p, xn, (lambda i, tt=tt: xnfm[:, i * 512 + tt * 128:
                                                  i * 512 + tt * 128 + 128]), pt1, idt)
        for mt in range(16):
            def fxs(e, mt=mt):
                for i in range(8):
                    r = e.matmul(pm[:], inptt[:, i * 4384 + mt * 128: i * 4384 + mt * 128 + 128],
                                 xnfm[:, i * 512:(i + 1) * 512], start=(i == 0), stop=(i == 7))
                return r
            p.step("tensor", fxs)
            p.step("vector", lambda e: e.tensor_copy(xe[:, 0:512], pm[:]))
            p.step("sync", lambda e, mt=mt, t0=t0: e.dma_start(
                out=xss[mt * 128:(mt + 1) * 128, t0:t0 + 512], in_=xe[:, 0:512]), dmas=1)
        for tt in range(4):
            for ch in range(4):
                def fz(e, tt=tt, ch=ch):
                    for i in range(8):
                        r = e.matmul(pm2[:], xnfm[:, i * 512 + tt * 128: i * 512 + tt * 128 + 128],
                                     inptt[:, i * 4384 + DI2 + ch * 512:
                                           i * 4384 + DI2 + ch * 512 + 512],
                                     start=(i == 0), stop=(i == 7))
                    return r
                p.step("tensor", fz)
                p.step("scalar", lambda e, ch=ch: e.activation(out=ygb[:, ch * 512:(ch + 1) * 512],
                                                               in_=pm2[:], func=AF.Silu))
            p.step("sync", lambda e, tt=tt, t0=t0: e.dma_start(
                out=zss[t0 + tt * 128: t0 + tt * 128 + 128, :], in_=ygb[:]), dmas=1)
        for (dst, woff) in ((bfm, 2 * DI2), (cfm, 2 * DI2 + DS)):
            def fbc(e, woff=woff):
                for i in range(8):
                    r = e.matmul(pm[:], inptt[:, i * 4384 + woff: i * 4384 + woff + 128],
                                 xnfm[:, i * 512:(i + 1) * 512], start=(i == 0), stop=(i == 7))
                return r
            p.step("tensor", fbc)
            p.step("vector", lambda e, dst=dst, t0=t0: e.tensor_copy(
                dst[:, t0:t0 + 512], pm[:]))
        def fdt(e):
            for i in range(8):
                r = e.matmul(pm[0:NH, :], inptt[:, i * 4384 + 2 * DI2 + 2 * DS:
                                                i * 4384 + 2 * DI2 + 2 * DS + NH],
                             xnfm[:, i * 512:(i + 1) * 512], start=(i == 0), stop=(i == 7))
            return r
        p.step("tensor", fdt)
        p.step("vector", lambda e, t0=t0: e.tensor_copy(dti[:, t0:t0 + 512], pm[0:NH, :]))
    p.step("vector", lambda e: e.tensor_scalar(out=dti[:], in0=dti[:], scalar1=dtbc[:, 0:1],
                                               scalar2=None, op0=OP.add))
    p.step("scalar", lambda e: e.activation(out=cacc[0:NH, 0:TC], in_=dti[:], func=AF.Abs))
    p.step("scalar", lambda e: e.activation(out=cacc[0:NH, 0:TC], in_=cacc[0:NH, 0:TC],
                                            func=AF.Exp, scale=-1.0))
    p.step("scalar", lambda e: e.activation(out=cacc[0:NH, 0:TC], in_=cacc[0:NH, 0:TC],
                                            func=AF.Ln, bias=1.0))
    p.step("vector", lambda e: e.tensor_relu(dti[:], dti[:]))
    p.step("vector", lambda e: e.tensor_tensor(out=dti[:], in0=dti[:],
                                               in1=cacc[0:NH, 0:TC], op=OP.add))
    for c in range(NCH):
        p.step("vector", lambda e, c=c: e.tensor_copy(cpad[0:NH, 0:128],
                                                      dti[:, c * 128:(c + 1) * 128]))
        p.step("tensor", lambda e: e.transpose(ptf[:], cpad[:], idtf[:]))
        p.step("vector", lambda e, c=c: e.tensor_copy(dttm[:, c * NH:(c + 1) * NH],
                                                      ptf[:, 0:NH]))
    p.step("vector", lambda e: e.tensor_copy(dttmb[:], dttm[:]))
    for ct in range(16):
        p.step("sync", lambda e, ct=ct: e.dma_start(out=xe[:, 4:4 + TC],
                                                    in_=xss[ct * 128:(ct + 1) * 128, :]), dmas=1)
        p.step("gpsimd", lambda e, ct=ct: e.dma_start(out=xe[:, 1:4],
                                                      in_=tails[ct * 128:(ct + 1) * 128, :]),
               dmas=1)
        p.step("vector", lambda e, ct=ct: e.tensor_scalar(
            out=cacc[:], in0=xe[:, 1:1 + TC], scalar1=cwt[:, ct * 4:ct * 4 + 1],
            scalar2=None, op0=OP.mult))
        for k in range(1, 4):
            p.step("vector", lambda e, ct=ct, k=k: e.scalar_tensor_tensor(
                out=cacc[:], in0=xe[:, 1 + k:1 + k + TC], scalar=cwt[:, ct * 4 + k:ct * 4 + k + 1],
                in1=cacc[:], op0=OP.mult, op1=OP.add))
        p.step("vector", lambda e, ct=ct: e.tensor_scalar(
            out=cacc[:], in0=cacc[:], scalar1=cbt[:, ct:ct + 1], scalar2=None, op0=OP.add))
        p.step("scalar", lambda e: e.activation(out=xe[:, 4:4 + TC], in_=cacc[:], func=AF.Silu))
        p.step("sync", lambda e, ct=ct: e.dma_start(out=xsls[ct * 128:(ct + 1) * 128, :],
                                                    in_=xe[:, 4:4 + TC]), dmas=1)
    for cc in range(NCH):
        p.step("sync", lambda e, cc=cc: e.dma_start(
            out=xe[:, 0:DI2].rearrange("p (a m) -> p a m", a=16),
            in_=xsls[:, cc * 128:(cc + 1) * 128].rearrange("(a p) m -> p a m", p=128)), dmas=1)
        for j in range(16):
            p.step("tensor", lambda e, j=j: e.transpose(pt1[:], xe[:, j * 128:(j + 1) * 128],
                                                        idt[:]))
            p.step("vector", lambda e, j=j: e.tensor_copy(xsltc[:, j * 128:(j + 1) * 128],
                                                          pt1[:]))
        p.step("sync", lambda e, cc=cc: e.dma_start(out=xslt[cc * 128:(cc + 1) * 128, :],
                                                    in_=xsltc[:]), dmas=1)
        p.step("vector", lambda e, cc=cc: e.tensor_tensor(
            out=xdtc[:].rearrange("p (h d2) -> p h d2", h=NH),
            in0=xsltc[:].rearrange("p (h d2) -> p h d2", h=NH),
            in1=dttmb[:, cc * NH:(cc + 1) * NH].rearrange("p (h o) -> p h o", o=1)
                .to_broadcast([128, NH, HD]), op=OP.mult))
        p.step("sync", lambda e, cc=cc: e.dma_start(out=xdts[cc * 128:(cc + 1) * 128, :],
                                                    in_=xdtc[:]), dmas=1)
    if full:
        ipf = inptt[:].bitcast(F32)
        hfa = ipf[:, 0:8192]
        ffa = ipf[:, 8192:8320]
        inc4 = ipf[:, 8320:8324]
        tg1 = ipf[:, 8324:10372]
        p.step("sync", lambda e: [e.dma_start(out=hfa.rearrange("p (a m) -> p a m", a=4),
                                              in_=hstall[:, :, :].rearrange("a p m -> p a m")),
                                  e.dma_start(out=ffa.rearrange("p (a m) -> p a m", a=4),
                                              in_=ftall[:, :, :].rearrange("a p m -> p a m")),
                                  e.dma_start(out=inc4[0:1, 0:4], in_=incr[:, :])], dmas=3)
        p.step("tensor", lambda e: e.matmul(pm[:, 0:4], onrw[:], inc4[0:1, 0:4],
                                            start=True, stop=True))
        p.step("vector", lambda e: e.tensor_copy(inc4, pm[:, 0:4]))
        p.step("vector", lambda e: e.memset(hst[:], 0.0))
        for cc in range(4):
            p.step("vector", lambda e, cc=cc: e.tensor_scalar(
                out=fmat[:], in0=ffa[:, cc * NH:(cc + 1) * NH], scalar1=inc4[:, cc:cc + 1],
                scalar2=None, op0=OP.mult))
            p.step("vector", lambda e, cc=cc: e.tensor_scalar(
                out=cdc[:], in0=inc4[:, cc:cc + 1].to_broadcast([128, NH]), scalar1=-1.0,
                scalar2=1.0, op0=OP.mult, op1=OP.add))
            p.step("vector", lambda e: e.tensor_tensor(out=fmat[:], in0=fmat[:], in1=cdc[:],
                                                       op=OP.add))
            p.step("vector", lambda e: e.tensor_tensor(
                out=hst[:].rearrange("p (h d2) -> p h d2", h=NH),
                in0=hst[:].rearrange("p (h d2) -> p h d2", h=NH),
                in1=fmat[:].rearrange("p (h o) -> p h o", o=1).to_broadcast([128, NH, HD]),
                op=OP.mult))
            p.step("vector", lambda e, cc=cc: e.tensor_scalar(
                out=tg1, in0=hfa[:, cc * NH * HD:(cc + 1) * NH * HD],
                scalar1=inc4[:, cc:cc + 1], scalar2=None, op0=OP.mult))
            p.step("vector", lambda e: e.tensor_tensor(out=hst[:], in0=hst[:], in1=tg1,
                                                       op=OP.add))
    else:
        p.step("vector", lambda e: e.memset(hst[:], 0.0))
    if full:
        woutt2 = inptt[:, 0:16 * D]
        p.step("gpsimd", lambda e: [e.dma_start(out=woutt2[:, i * D:(i + 1) * D],
                                                in_=outT[i * 128:(i + 1) * 128, :])
                                    for i in range(16)], dmas=16)
        p.step("vector", lambda e: e.memset(lhsT2[:], 1.0))
        p.step("vector", lambda e: e.memset(rhs2[:], 1.0))
        p.step("sync", lambda e: e.dma_start(out=row2[:], in_=nw2[:, :]), dmas=1)
        for ch in range(4):
            p.step("tensor", lambda e, ch=ch: e.matmul(
                pm[:], onrw[:], row2[0:1, ch * 512:(ch + 1) * 512], start=True, stop=True))
            p.step("vector", lambda e, ch=ch: e.tensor_copy(nwrep2[:, ch * 512:(ch + 1) * 512],
                                                            pm[:]))
    for c in range(NCH):
        t0 = c * 128
        dtcb = dttmb[:, c * NH:(c + 1) * NH]
        p.step("tensor", lambda e, dtcb=dtcb: e.matmul(pm[:, 0:NH], trut[:], dtcb,
                                                       start=True, stop=True))
        p.step("vector", lambda e: e.tensor_copy(cdc[:], pm[:, 0:NH]))
        p.step("vector", lambda e: e.tensor_tensor(out=acd[:], in0=cdc[:], in1=arep[:],
                                                   op=OP.mult))
        if full:
            p.step("scalar", lambda e: e.activation(out=epl[:], in_=acd[:], func=AF.Exp))
        p.step("tensor", lambda e, dtcb=dtcb: e.matmul(pm2[0:1, 0:NH], onescolb[:], dtcb,
                                                       start=True, stop=True))
        p.step("vector", lambda e: e.tensor_copy(cpad[0:1, 0:NH], pm2[0:1, 0:NH]))
        p.step("tensor", lambda e: e.matmul(pm[:, 0:NH], onrw[:], cpad[0:1, 0:NH],
                                            start=True, stop=True))
        p.step("vector", lambda e: e.tensor_copy(srep[:], pm[:, 0:NH]))
        p.step("vector", lambda e: e.tensor_tensor(out=wdtm[:], in0=srep[:], in1=cdc[:],
                                                   op=OP.subtract))
        p.step("vector", lambda e: e.tensor_tensor(out=wdtm[:], in0=wdtm[:], in1=arep[:],
                                                   op=OP.mult))
        p.step("scalar", lambda e: e.activation(out=wdtm[:], in_=wdtm[:], func=AF.Exp))
        p.step("vector", lambda e, c=c: e.tensor_tensor(out=wdtm[:], in0=wdtm[:],
                                                        in1=dttm[:, c * NH:(c + 1) * NH],
                                                        op=OP.mult))
        p.step("vector", lambda e: e.tensor_tensor(out=fmat[:], in0=srep[:], in1=arep[:],
                                                   op=OP.mult))
        p.step("scalar", lambda e: e.activation(out=fmat[:], in_=fmat[:], func=AF.Exp))
        p.step("sync", lambda e, t0=t0: e.dma_start(out=xdtc[:], in_=xdts[t0:t0 + 128, :]),
               dmas=1)
        if full:
            p.step("vector", lambda e: e.tensor_copy(hstb, hst[:]))
            p.step("vector", lambda e: e.tensor_copy(cpad[:, 0:NH], cdc[:]))
            p.step("tensor", lambda e: e.transpose(ptf[:], cpad[:], idtf[:]))
            p.step("vector", lambda e: e.tensor_copy(cdT[:], ptf[0:32, 0:128]))
            p.step("vector", lambda e: e.tensor_copy(cdTb[:], cdT[:]))
            p.step("vector", lambda e: e.tensor_scalar(out=cdT[:], in0=cdT[:], scalar1=-1.0,
                                                       scalar2=None, op0=OP.mult))
            p.step("vector", lambda e: e.tensor_copy(ncdTb[:], cdT[:]))
            p.step("sync", lambda e, t0=t0: [
                e.dma_start(out=xsltc[:], in_=xslt[t0:t0 + 128, :]),
                e.dma_start(out=zsc[:], in_=zss[t0:t0 + 128, :])], dmas=2)
            p.step("tensor", lambda e, t0=t0: e.matmul(
                pm2[:, 0:128], bfm[:, t0:t0 + 128], cfm[:, t0:t0 + 128],
                start=True, stop=True))
        p.step("tensor", lambda e, t0=t0: e.transpose(pt1[:], bfm[:, t0:t0 + 128], idt[:]))
        p.step("vector", lambda e: e.tensor_copy(btmf[:], pt1[:]))
        for grp in range(4):
            for hh in range(8):
                h = grp * 8 + hh
                if full:
                    p.step("gpsimd", lambda e, h=h: [
                        e.dma_start(out=lhsT2[0:1, :], in_=ncdTb[h:h + 1, :]),
                        e.dma_start(out=rhs2[1:2, :], in_=cdTb[h:h + 1, :])], dmas=2)
                    p.step("tensor", lambda e: e.matmul(pm[:, 0:128], lhsT2[:], rhs2[:],
                                                        start=True, stop=True))
                    p.step("vector", lambda e: e.scalar_tensor_tensor(
                        out=asb[:], in0=msk[:], scalar=1e30, in1=pm[:, 0:128],
                        op0=OP.mult, op1=OP.add))
                    p.step("scalar", lambda e, h=h: e.activation(
                        out=asb[:], in_=asb[:], func=AF.Exp, scale=arep[:, h:h + 1]))
                    p.step("vector", lambda e: e.tensor_tensor(out=ssb[:], in0=asb[:],
                                                               in1=pm2[:, 0:128], op=OP.mult))
                    p.step("tensor", lambda e, h=h, hh=hh: e.matmul(
                        py[:, hh * HD:(hh + 1) * HD], ssb[:], xdtc[:, h * HD:(h + 1) * HD],
                        start=True, stop=True))
                    p.step("tensor", lambda e, h=h, hh=hh, t0=t0: e.matmul(
                        pc[:, hh * HD:(hh + 1) * HD], cfm[:, t0:t0 + 128],
                        hstb[:, h * HD:(h + 1) * HD], start=True, stop=True))
                p.step("vector", lambda e, h=h: e.tensor_scalar(
                    out=bwb[:], in0=btmf[:], scalar1=wdtm[:, h:h + 1], scalar2=None,
                    op0=OP.mult))
                p.step("tensor", lambda e, h=h, hh=hh: e.matmul(
                    pcon[:, hh * HD:(hh + 1) * HD], bwb[:], xdtc[:, h * HD:(h + 1) * HD],
                    start=True, stop=True))
            g0 = grp * 8
            if full:
                p.step("vector", lambda e, g0=g0: e.tensor_tensor(
                    out=ycar[:].rearrange("p (h d2) -> p h d2", h=8),
                    in0=pc[:].rearrange("p (h d2) -> p h d2", h=8),
                    in1=epl[:, g0:g0 + 8].rearrange("p (h o) -> p h o", o=1)
                        .to_broadcast([128, 8, HD]), op=OP.mult))
                p.step("vector", lambda e, g0=g0: e.tensor_tensor(
                    out=dterm[:].rearrange("p (h d2) -> p h d2", h=8),
                    in0=xsltc[:, g0 * HD:(g0 + 8) * HD].rearrange("p (h d2) -> p h d2", h=8),
                    in1=dp8b[:, g0:g0 + 8].rearrange("p (h o) -> p h o", o=1)
                        .to_broadcast([128, 8, HD]), op=OP.mult))
                p.step("vector", lambda e, g0=g0: e.tensor_tensor(
                    out=ych[:, g0 * HD:(g0 + 8) * HD], in0=py[:], in1=ycar[:], op=OP.add))
                p.step("vector", lambda e, g0=g0: e.tensor_tensor(
                    out=ych[:, g0 * HD:(g0 + 8) * HD], in0=ych[:, g0 * HD:(g0 + 8) * HD],
                    in1=dterm[:], op=OP.add))
            p.step("vector", lambda e, g0=g0: e.tensor_tensor(
                out=hst[:, g0 * HD:(g0 + 8) * HD].rearrange("p (h d2) -> p h d2", h=8),
                in0=hst[:, g0 * HD:(g0 + 8) * HD].rearrange("p (h d2) -> p h d2", h=8),
                in1=fmat[:, g0:g0 + 8].rearrange("p (h o) -> p h o", o=1)
                    .to_broadcast([128, 8, HD]), op=OP.mult))
            p.step("vector", lambda e, g0=g0: e.tensor_tensor(
                out=hst[:, g0 * HD:(g0 + 8) * HD], in0=hst[:, g0 * HD:(g0 + 8) * HD],
                in1=pcon[:], op=OP.add))
        if full:
            p.step("scalar", lambda e: e.activation(out=sq[0:128, 0:1024], in_=ych[:, 0:1024],
                                                    func=AF.Square, accum_out=stats[:, 0:1]))
            p.step("scalar", lambda e: e.activation(out=sq[0:128, 0:1024], in_=ych[:, 1024:2048],
                                                    func=AF.Square, accum_out=stats[:, 1:2]))
            p.step("vector", lambda e: e.tensor_tensor(out=stats[:, 0:1], in0=stats[:, 0:1],
                                                       in1=stats[:, 1:2], op=OP.add))
            p.step("vector", lambda e: e.tensor_scalar(out=stats[:, 0:1], in0=stats[:, 0:1],
                                                       scalar1=1.0 / DI2, scalar2=1e-5,
                                                       op0=OP.mult, op1=OP.add))
            p.step("scalar", lambda e: e.activation(out=stats[:, 2:3], in_=stats[:, 0:1],
                                                    func=AF.Sqrt))
            p.step("vector", lambda e: e.reciprocal(stats[:, 3:4], stats[:, 2:3]))
            p.step("vector", lambda e: e.tensor_scalar(out=ych[:], in0=ych[:],
                                                       scalar1=stats[:, 3:4], scalar2=None,
                                                       op0=OP.mult))
            p.step("vector", lambda e: e.tensor_tensor(out=ych[:], in0=ych[:], in1=nwrep2[:],
                                                       op=OP.mult))
            p.step("vector", lambda e: e.tensor_copy(ygb[:], ych[:]))
            p.step("vector", lambda e: e.tensor_tensor(out=ygb[:], in0=ygb[:], in1=zsc[:],
                                                       op=OP.mult))
            for j in range(16):
                p.step("tensor", lambda e, j=j: e.transpose(pt1[:], ygb[:, j * 128:(j + 1) * 128],
                                                            idt[:]))
                p.step("vector", lambda e, j=j: e.tensor_copy(ygfm[:, j * 128:(j + 1) * 128],
                                                              pt1[:]))
            p.step("sync", lambda e, t0=t0: e.dma_start(out=xt[:], in_=x[t0:t0 + 128, :]),
                   dmas=1)
            for ch in range(2):
                def fo(e, ch=ch):
                    for i in range(16):
                        r = e.matmul(pm[:], ygfm[:, i * 128:(i + 1) * 128],
                                     woutt2[:, i * D + ch * 512: i * D + ch * 512 + 512],
                                     start=(i == 0), stop=(i == 15))
                    return r
                p.step("tensor", fo)
                p.step("vector", lambda e, ch=ch: e.tensor_tensor(
                    out=xt[:, ch * 512:(ch + 1) * 512], in0=xt[:, ch * 512:(ch + 1) * 512],
                    in1=pm[:], op=OP.add))
            p.step("sync", lambda e, t0=t0: e.dma_start(out=xout[t0:t0 + 128, :], in_=xt[:]),
                   dmas=1)
    if not full:
        p.step("tensor", lambda e: e.matmul(pm[0:1, 0:NH], onescolb[:], dttmb[:, 0:NH],
                                            start=True, stop=False))
        def fsum(e):
            for c in range(1, NCH):
                r = e.matmul(pm[0:1, 0:NH], onescolb[:], dttmb[:, c * NH:(c + 1) * NH],
                             start=False, stop=(c == NCH - 1))
            return r
        p.step("tensor", fsum)
        p.step("vector", lambda e: e.tensor_copy(cpad[0:1, 0:NH], pm[0:1, 0:NH]))
        p.step("tensor", lambda e: e.matmul(pm[:, 0:NH], onrw[:], cpad[0:1, 0:NH],
                                            start=True, stop=True))
        p.step("vector", lambda e: e.tensor_copy(srep[:], pm[:, 0:NH]))
        p.step("vector", lambda e: e.tensor_tensor(out=fmat[:], in0=srep[:], in1=arep[:],
                                                   op=OP.mult))
        p.step("scalar", lambda e: e.activation(out=fmat[:], in_=fmat[:], func=AF.Exp))
        p.step("sync", lambda e: [e.dma_start(out=d_hst[:, :], in_=hst[:]),
                                  e.dma_start(out=d_ft[:, :], in_=fmat[:])], dmas=2)
    return p.build()


def _shard(x_full):
    out = []
    for c in range(NCORE):
        b, q = c // CPB, c % CPB
        out.append(np.ascontiguousarray(x_full[b, q * TC:(q + 1) * TC]))
    return out


def kernel(**inp):
    g = lambda k: np.ascontiguousarray(np.asarray(inp[k], dtype=np.float32))
    hc = host_consts()
    s4c = s4_consts()
    m1c = m1_consts()
    m2c = m2_consts()
    onesrowf = np.ones((1, 128), np.float32)
    allones = np.ones((128, 128), np.float32)
    identf = np.eye(128, dtype=np.float32)
    onescolb = np.ones((128, 1), np.float32).astype(BF)
    x0 = g("L0_attn_a_x")
    shards = [np.ascontiguousarray(x0[c // CPB, (c % CPB) * TC:(c % CPB + 1) * TC])
              for c in range(NCORE)]
    cores = list(range(NCORE))

    nc = build_qkv()
    im = [{"x": shards[c], "wq": g("L0_attn_a_Wq"), "wk": g("L0_attn_a_Wk"),
           "wv": g("L0_attn_a_Wv"), "ident": hc["ident"]} for c in cores]
    r = run_bass_kernel_spmd(nc, im, core_ids=cores).results
    kf = {b: np.concatenate([r[b * CPB + q]["kf"] for q in range(CPB)], axis=1) for b in range(NB)}
    vt = {b: np.concatenate([r[b * CPB + q]["vt"] for q in range(CPB)], axis=0) for b in range(NB)}
    qf = [r[c]["qf"] for c in cores]

    qpos = []
    for c in cores:
        qq = np.zeros((128, TC // 128), np.float32)
        for i2 in range(TC // 128):
            qq[:, i2] = (c % CPB) * TC + i2 * 128 + np.arange(128)
        qpos.append(qq)
    kpos = np.arange(T, dtype=np.float32)[None, :]
    nc = build_flash()
    im = [{"x": shards[c], "qf": qf[c], "kff": kf[c // CPB], "vtf": vt[c // CPB],
           "gw": g("L0_ffn_gate_w"), "qpos": qpos[c], "kpos": kpos,
           "ident": hc["ident"]} for c in cores]
    r = run_bass_kernel_spmd(nc, im, core_ids=cores).results
    x1 = [r[c]["xout"] for c in cores]
    xn1 = [r[c]["xn1"] for c in cores]
    sc1 = {b: np.concatenate([r[b * CPB + q]["scores"] for q in range(CPB)], axis=0)
           for b in range(NB)}

    nc_moe_s4 = build_moedense(tail="s4pre")
    im = [{"x": x1[c], "xn": xn1[c], "scoresf": sc1[c // CPB],
           "scoresm": np.ascontiguousarray(sc1[c // CPB][(c % CPB) * TC:(c % CPB + 1) * TC]),
           "w1": g("L0_ffn_w1"), "w2": g("L0_ffn_w2"), "ident": hc["ident"],
           "allones": allones,
           "winT": np.ascontiguousarray(g("L0_attn_b_W_in").T),
           "b_in": g("L0_attn_b_b_in")[None, :], "gam": g("L0_attn_b_gamma")[None, :],
           "bet": g("L0_attn_b_beta")[None, :], "onesrowf": onesrowf} for c in cores]
    r = run_bass_kernel_spmd(nc_moe_s4, im, core_ids=cores).results
    x2 = [r[c]["xout"] for c in cores]
    u2 = [r[c]["u2"] for c in cores]
    csum = [r[c]["colsum"] for c in cores]

    nc = build_s4conv()
    im = []
    for c in cores:
        b, q = c // CPB, c % CPB
        tail = np.zeros((256, D), np.float32) if q == 0 else u2[c - 1][TC - 256:]
        pmkv = np.zeros((4, 1), np.float32)
        pmkv[:q, 0] = 1.0
        im.append({
            "x": x2[c], "u2e": np.concatenate([tail, u2[c]], axis=0),
            "colsums": np.concatenate([csum[b * CPB + qq] for qq in range(CPB)], axis=0),
            "pmk": pmkv,
            "are": g("L0_attn_b_A_re")[:, None], "aim": g("L0_attn_b_A_im")[:, None],
            "bre": g("L0_attn_b_B_re")[:, None], "bim": g("L0_attn_b_B_im")[:, None],
            "cre": g("L0_attn_b_C_re"), "cim": g("L0_attn_b_C_im"),
            "dre": g("L0_attn_b_D_re")[None, :], "dim": g("L0_attn_b_D_im")[None, :],
            "lmat": s4c["lmat"], "fxc": s4c["fxc"], "fxs": s4c["fxs"],
            "fkc": s4c["fkc"], "fks": s4c["fks"], "gre": s4c["gre"], "gim": s4c["gim"],
            "gw": g("L0_ffn_gate_w"), "ident": hc["ident"], "triu": hc["triu"],
            "onesrowf": onesrowf})
    r = run_bass_kernel_spmd(nc, im, core_ids=cores).results
    x3 = [r[c]["xout"] for c in cores]
    xn2 = [r[c]["xn1"] for c in cores]
    sc2 = {b: np.concatenate([r[b * CPB + q]["scores"] for q in range(CPB)], axis=0)
           for b in range(NB)}

    nc = build_moedense(tail="none")
    im = [{"x": x3[c], "xn": xn2[c], "scoresf": sc2[c // CPB],
           "scoresm": np.ascontiguousarray(sc2[c // CPB][(c % CPB) * TC:(c % CPB + 1) * TC]),
           "w1": g("L0_ffn_w1"), "w2": g("L0_ffn_w2"), "ident": hc["ident"],
           "allones": allones} for c in cores]
    r = run_bass_kernel_spmd(nc, im, core_ids=cores).results
    x4 = [r[c]["xout"] for c in cores]

    m1base = {
        "nw": g("L1_attn_a_norm_w")[None, :], "nb": g("L1_attn_a_norm_b")[None, :],
        "win": g("L1_attn_a_W_in"), "wgt": g("L1_attn_a_W_gate"), "wdt": g("L1_attn_a_W_dt"),
        "bdt": np.ascontiguousarray(g("L1_attn_a_b_dt").reshape(16, 128).T),
        "wB": g("L1_attn_a_W_B"), "wC": g("L1_attn_a_W_C"),
        "alog": np.ascontiguousarray(g("L1_attn_a_A_log").reshape(-1).reshape(256, 128).T),
        "selc": m1c["selt"], "sumselc": m1c["sumsel"], "sel16c": m1c["sel16"],
        "ident": hc["ident"], "onesrowf": onesrowf}
    nc = build_m1(False)
    im = [dict(m1base, x=x4[c]) for c in cores]
    r = run_bass_kernel_spmd(nc, im, core_ids=cores).results
    nc = build_m1(True)
    im = []
    for c in cores:
        b, q = c // CPB, c % CPB
        incv = np.zeros((1, 4), np.float32)
        incv[0, :q] = 1.0
        im.append(dict(m1base, x=x4[c], wout=g("L1_attn_a_W_out"),
                       hfall=np.stack([r[b * CPB + qq]["hfin"] for qq in range(CPB)]),
                       fall=np.stack([r[b * CPB + qq]["ff"] for qq in range(CPB)]),
                       incr=incv))
    r = run_bass_kernel_spmd(nc, im, core_ids=cores).results
    x5 = [r[c]["xout"] for c in cores]

    ffnbase = {
        "lng": g("L1_ffn_ln_g")[None, :], "lnb": g("L1_ffn_ln_b")[None, :],
        "rw": g("L1_ffn_rW"), "rb": g("L1_ffn_rb")[None, :], "w1": g("L1_ffn_W1"),
        "b1r": np.ascontiguousarray(np.stack([g("L1_ffn_b1")[e].reshape(16, 128).T
                                              for e in range(8)]).reshape(8 * 128, 16)),
        "w2": g("L1_ffn_W2"), "b2": g("L1_ffn_b2"),
        "ident": hc["ident"], "identf": identf, "onesrowf": onesrowf}
    nc_ffn2 = build_ffn2()
    im = [dict(ffnbase, x=x5[c]) for c in cores]
    r = run_bass_kernel_spmd(nc_ffn2, im, core_ids=cores).results
    x6 = [r[c]["xout"] for c in cores]

    inpT = np.ascontiguousarray(g("L1_attn_b_in_proj_w").T)
    outT = np.ascontiguousarray(g("L1_attn_b_out_proj_w").T)
    m2base = {
        "inpT": inpT,
        "cw": np.ascontiguousarray(g("L1_attn_b_conv_w").reshape(16, 128, 4)
                                   .transpose(1, 0, 2).reshape(128, 64)),
        "cb": np.ascontiguousarray(g("L1_attn_b_conv_b").reshape(16, 128).T),
        "alog": g("L1_attn_b_A_log")[None, :], "dpr": g("L1_attn_b_Dp")[None, :],
        "dtbias": g("L1_attn_b_dt_bias")[:, None],
        "mask01": m2c["mask01"], "triu": hc["triu"], "ident": hc["ident"],
        "identf": identf, "onesrowf": onesrowf}
    nc = build_m2tails()
    im = [{"x": x6[c], "inpT": inpT, "ident": hc["ident"]} for c in cores]
    r = run_bass_kernel_spmd(nc, im, core_ids=cores).results
    tails = [np.zeros((2048, 3), np.float32) if c % CPB == 0 else r[c - 1]["tails"]
             for c in cores]
    nc = build_m2(False)
    im = [dict(m2base, x=x6[c], tails=tails[c]) for c in cores]
    r = run_bass_kernel_spmd(nc, im, core_ids=cores).results
    nc = build_m2(True)
    im = []
    for c in cores:
        b, q = c // CPB, c % CPB
        incv = np.zeros((1, 4), np.float32)
        incv[0, :q] = 1.0
        im.append(dict(m2base, x=x6[c], tails=tails[c],
                       nw2=g("L1_attn_b_norm_w")[None, :], outT=outT,
                       hstall=np.stack([r[b * CPB + qq]["hst"] for qq in range(CPB)]),
                       ftall=np.stack([r[b * CPB + qq]["ft"] for qq in range(CPB)]),
                       incr=incv))
    r = run_bass_kernel_spmd(nc, im, core_ids=cores).results
    x7 = [r[c]["xout"] for c in cores]

    im = [dict(ffnbase, x=x7[c]) for c in cores]
    r = run_bass_kernel_spmd(nc_ffn2, im, core_ids=cores).results
    out = np.empty((NB, T, D), np.float32)
    for c in cores:
        b, q = c // CPB, c % CPB
        out[b, q * TC:(q + 1) * TC] = r[c]["xout"]
    return out


# version 03
# speedup vs baseline: 1.0741x; 1.0741x over previous
import contextlib
import numpy as np
import ml_dtypes
import concourse.bass as bass
import concourse.mybir as mybir
from concourse.bass_utils import run_bass_kernel_spmd
from concourse.bass import IndirectOffsetOnAxis

F32 = mybir.dt.float32
BF16 = mybir.dt.bfloat16
U32 = mybir.dt.uint32
AF = mybir.ActivationFunctionType
OP = mybir.AluOpType
AX = mybir.AxisListType.X
BF = ml_dtypes.bfloat16

NCORE, NB, T, D = 8, 2, 16384, 1024
TC = 4096
CPB = 4
LN_EPS = 1e-5
SEG_LIM = 12000

_NC_CACHE = {}


class Prog:

    def __init__(self, nc):
        self.nc = nc
        self.steps = []
        self.ctx = contextlib.ExitStack()

    def sb(self, name, shape, dtype):
        return self.ctx.enter_context(self.nc.sbuf_tensor(name, shape, dtype))

    def ps(self, name, shape, dtype=F32):
        return self.ctx.enter_context(self.nc.psum_tensor(name, shape, dtype))

    def step(self, eng, fn, dmas=0):
        self.steps.append((eng, fn, 16 * dmas if dmas else 1))

    def build(self):
        nc = self.nc
        segs = []
        cur = 0
        seg_of, base_of = [], []
        for (_, _, inc) in self.steps:
            if cur + inc > SEG_LIM:
                segs.append(cur)
                cur = 0
            seg_of.append(len(segs))
            base_of.append(cur)
            cur += inc
        segs.append(cur)
        sems = [self.ctx.enter_context(nc.semaphore("ch%d" % i))
                for i in range(len(segs))]
        with nc.Block() as block:
            def body_for(name):
                def body(e):
                    seen_seg = set()
                    for i, (en, fn, inc) in enumerate(self.steps):
                        if en != name:
                            continue
                        s = seg_of[i]
                        if s > 0 and s not in seen_seg:
                            e.wait_ge(sems[s - 1], segs[s - 1])
                        seen_seg.add(s)
                        if base_of[i] > 0:
                            e.wait_ge(sems[s], base_of[i])
                        out = fn(e)
                        if inc == 1:
                            out.then_inc(sems[s], 1)
                        else:
                            insts = out if isinstance(out, (list, tuple)) else [out]
                            assert len(insts) * 16 == inc, (len(insts), inc)
                            for it in insts:
                                it.then_inc(sems[s], 16)
                    if name == "sync":
                        if len(segs) > 1:
                            e.wait_ge(sems[len(segs) - 2], segs[-2])
                        e.wait_ge(sems[len(segs) - 1], segs[-1])
                return body
            block.sync(body_for("sync"))
            block.vector(body_for("vector"))
            block.scalar(body_for("scalar"))
            block.tensor(body_for("tensor"))
            block.gpsimd(body_for("gpsimd"))
        self.ctx.close()
        return nc



def host_consts():
    c = {}
    c["ident"] = np.eye(128, dtype=np.float32).astype(BF)
    c["triu"] = np.triu(np.ones((128, 128), np.float32)).astype(BF)
    c["ones_col"] = np.ones((128, 1), np.float32).astype(BF)
    c["ones_row"] = np.ones((1, 128), np.float32).astype(BF)
    return c


def emit_ln(p, xt, xn, stats, sq, d=D):
    p.step("vector", lambda e: e.reduce_sum(stats[:, 0:1], xt[:], axis=AX))
    p.step("scalar", lambda e: e.activation(out=sq[:], in_=xt[:], func=AF.Square,
                                            accum_out=stats[:, 1:2]))
    p.step("vector", lambda e: e.tensor_scalar(out=stats[:, 2:3], in0=stats[:, 0:1],
                                               scalar1=1.0 / d, scalar2=None, op0=OP.mult))
    p.step("vector", lambda e: e.tensor_scalar(out=stats[:, 3:4], in0=stats[:, 1:2],
                                               scalar1=1.0 / d, scalar2=None, op0=OP.mult))
    p.step("vector", lambda e: e.tensor_tensor(out=stats[:, 4:5], in0=stats[:, 2:3],
                                               in1=stats[:, 2:3], op=OP.mult))
    p.step("vector", lambda e: e.tensor_tensor(out=stats[:, 5:6], in0=stats[:, 3:4],
                                               in1=stats[:, 4:5], op=OP.subtract))
    p.step("vector", lambda e: e.tensor_scalar(out=stats[:, 5:6], in0=stats[:, 5:6],
                                               scalar1=LN_EPS, scalar2=None, op0=OP.add))
    p.step("scalar", lambda e: e.activation(out=stats[:, 6:7], in_=stats[:, 5:6], func=AF.Sqrt))
    p.step("vector", lambda e: e.reciprocal(stats[:, 7:8], stats[:, 6:7]))
    p.step("vector", lambda e: e.tensor_scalar(out=xn[:], in0=xt[:], scalar1=stats[:, 2:3],
                                               scalar2=stats[:, 7:8], op0=OP.subtract,
                                               op1=OP.mult))


def emit_tp(p, src, dst_fn, pt, idt, n=8):
    for j in range(n):
        def ftp(e, j=j):
            return e.transpose(pt[:], src[:, j * 128:(j + 1) * 128], idt[:])
        p.step("tensor", ftp)
        def fcp(e, j=j):
            return e.tensor_copy(dst_fn(j), pt[:])
        p.step("vector", fcp)



def build_qkv():
    nc = bass.Bass("TRN2", target_bir_lowering=False)
    x = nc.dram_tensor("x", [TC, D], F32, kind="ExternalInput").ap()
    wq = nc.dram_tensor("wq", [D, D], F32, kind="ExternalInput").ap()
    wk = nc.dram_tensor("wk", [D, D], F32, kind="ExternalInput").ap()
    wv = nc.dram_tensor("wv", [D, D], F32, kind="ExternalInput").ap()
    ident = nc.dram_tensor("ident", [128, 128], BF16, kind="ExternalInput").ap()
    qf = nc.dram_tensor("qf", [D, TC], BF16, kind="ExternalOutput").ap()
    kf = nc.dram_tensor("kf", [D, TC], BF16, kind="ExternalOutput").ap()
    vt = nc.dram_tensor("vt", [TC, D], BF16, kind="ExternalOutput").ap()

    p = Prog(nc)
    wqt = p.sb("wqt", [128, 8 * D], BF16)
    wkt = p.sb("wkt", [128, 8 * D], BF16)
    wvt = p.sb("wvt", [128, 8 * D], BF16)
    idt = p.sb("idt", [128, 128], BF16)
    xt = p.sb("xt", [128, D], F32)
    sq = p.sb("sq", [128, D], F32)
    stats = p.sb("stats", [128, 8], F32)
    xn = p.sb("xn", [128, D], BF16)
    xfm = p.sb("xfm", [128, 8 * 512], BF16)
    ev = p.sb("ev", [128, 512], BF16)
    pt = p.ps("pt", [128, 128], BF16)
    pm = p.ps("pm", [128, 512], F32)

    def ldw(e):
        out = [e.dma_start(out=dst[:, i * D:(i + 1) * D], in_=src[i * 128:(i + 1) * 128, :])
               for dst, src in ((wqt, wq), (wkt, wk), (wvt, wv)) for i in range(8)]
        return out
    p.step("gpsimd", ldw, dmas=24)
    p.step("sync", lambda e: e.dma_start(out=idt[:], in_=ident[:, :]), dmas=1)

    for bi in range(8):
        for tt in range(4):
            tok0 = bi * 512 + tt * 128
            p.step("sync", lambda e, tok0=tok0: e.dma_start(out=xt[:], in_=x[tok0:tok0 + 128, :]), dmas=1)
            emit_ln(p, xt, xn, stats, sq)
            emit_tp(p, xn, (lambda j, tt=tt: xfm[:, j * 512 + tt * 128: j * 512 + tt * 128 + 128]),
                    pt, idt)
        for wt, out_d, scale in ((wqt, qf, 1.0 / 32), (wkt, kf, None)):
            for m in range(8):
                def fmm(e, wt=wt, m=m):
                    for i in range(8):
                        r = e.matmul(pm[:], wt[:, i * D + m * 128: i * D + m * 128 + 128],
                                     xfm[:, i * 512:(i + 1) * 512],
                                     start=(i == 0), stop=(i == 7))
                    return r
                p.step("tensor", fmm)
                if scale is None:
                    p.step("vector", lambda e: e.tensor_copy(ev[:], pm[:]))
                else:
                    p.step("vector", lambda e, s=scale: e.tensor_scalar(
                        out=ev[:], in0=pm[:], scalar1=s, scalar2=None, op0=OP.mult))
                p.step("sync", lambda e, out_d=out_d, m=m, bi=bi: e.dma_start(
                    out=out_d[m * 128:(m + 1) * 128, bi * 512:(bi + 1) * 512], in_=ev[:]), dmas=1)
        for tt in range(4):
            for ch in range(2):
                def fmv(e, tt=tt, ch=ch):
                    for i in range(8):
                        r = e.matmul(pm[:], xfm[:, i * 512 + tt * 128: i * 512 + tt * 128 + 128],
                                     wvt[:, i * D + ch * 512: i * D + ch * 512 + 512],
                                     start=(i == 0), stop=(i == 7))
                    return r
                p.step("tensor", fmv)
                p.step("vector", lambda e: e.tensor_copy(ev[:], pm[:]))
                p.step("sync", lambda e, tt=tt, ch=ch, bi=bi: e.dma_start(
                    out=vt[bi * 512 + tt * 128: bi * 512 + tt * 128 + 128,
                           ch * 512:(ch + 1) * 512], in_=ev[:]), dmas=1)
    return p.build()



def emit_ffnpre(p, t, xt, tok):
    emit_ln(p, xt, t["xn"], t["stats2"], t["sq"])
    emit_tp(p, t["xn"], (lambda j: t["xfm1"][:, j * 128:(j + 1) * 128]),
            t["pt1"], t["idt"])

    def fgmm(e):
        for i in range(8):
            r = e.matmul(t["pg"][:], t["xfm1"][:, i * 128:(i + 1) * 128],
                         t["gwt"][:, i * 8:(i + 1) * 8],
                         start=(i == 0), stop=(i == 7))
        return r
    p.step("tensor", fgmm)
    p.step("vector", lambda e: e.tensor_copy(t["sc8"][:], t["pg"][:]))
    p.step("sync", lambda e, tok=tok: [
        e.dma_start(out=t["d_xn1"][tok:tok + 128, :], in_=t["xn"][:]),
        e.dma_start(out=t["d_scores"][tok:tok + 128, :], in_=t["sc8"][:])], dmas=2)


def build_flash(NKB=T // 512, NSUB=16, NG=2):
    GQ = 2048
    nc = bass.Bass("TRN2", target_bir_lowering=False)
    x = nc.dram_tensor("x", [TC, D], F32, kind="ExternalInput").ap()
    qf = nc.dram_tensor("qf", [D, TC], BF16, kind="ExternalInput").ap()
    kff = nc.dram_tensor("kff", [D, T], BF16, kind="ExternalInput").ap()
    vtf = nc.dram_tensor("vtf", [T, D], BF16, kind="ExternalInput").ap()
    gw = nc.dram_tensor("gw", [D, 8], F32, kind="ExternalInput").ap()
    qpos = nc.dram_tensor("qpos", [128, TC // 128], F32, kind="ExternalInput").ap()
    kpos = nc.dram_tensor("kpos", [1, T], F32, kind="ExternalInput").ap()
    ident = nc.dram_tensor("ident", [128, 128], BF16, kind="ExternalInput").ap()
    xout = nc.dram_tensor("xout", [TC, D], F32, kind="ExternalOutput").ap()
    d_xn1 = nc.dram_tensor("xn1", [TC, D], BF16, kind="ExternalOutput").ap()
    d_scores = nc.dram_tensor("scores", [TC, 8], F32, kind="ExternalOutput").ap()

    p = Prog(nc)
    t = {}
    qfm = p.sb("qfm", [128, 8 * GQ], BF16)
    kt = p.sb("kt", [128, 8 * 512], BF16)
    vtt = p.sb("vtt", [128, 4 * D], BF16)
    qpt = p.sb("qpt", [128, TC // 128], F32)
    kpr = p.sb("kpr", [128, 512], F32)
    cmpt = p.sb("cmpt", [128, 512], F32)
    nmw = p.sb("nmw", [128, 1], F32)
    ssb = p.sb("ssb", [128, 512], F32)
    pb = p.sb("pb", [128, 512], BF16)
    pts = p.sb("pts", [128, 512], BF16)
    stf = p.sb("stf", [128, 3 * NSUB], F32)
    oacc = [p.sb("oacc%d" % s, [128, D], F32) for s in range(NSUB)]
    t["idt"] = p.sb("idt", [128, 128], BF16)
    t["gwt"] = p.sb("gwt", [128, 8 * 8], BF16)
    xt = p.sb("xt", [128, D], F32)
    t["xn"] = p.sb("xn", [128, D], BF16)
    t["sq"] = p.sb("sqs", [128, D], F32)
    t["stats2"] = p.sb("stats2", [128, 8], F32)
    t["xfm1"] = p.sb("xfm1", [128, 8 * 128], BF16)
    t["sc8"] = p.sb("sc8", [128, 8], F32)
    t["d_xn1"], t["d_scores"] = d_xn1, d_scores
    kpow = p.sb("kpow", [1, T], F32)
    onebf = p.sb("onebf", [1, 128], F32)
    pm = p.ps("pm", [128, 512], F32)
    pt4 = p.ps("pt4", [128, 512], BF16)
    pav = p.ps("pav", [128, D], F32)
    pkp = p.ps("pkp", [128, 512], F32)
    t["pt1"] = p.ps("pt1", [128, 128], BF16)
    t["pg"] = p.ps("pg", [128, 8], F32)

    p.step("sync", lambda e: [e.dma_start(out=t["idt"][:], in_=ident[:, :]),
                              e.dma_start(out=qpt[:], in_=qpos[:, :]),
                              e.dma_start(out=kpow[:], in_=kpos[:, :])], dmas=3)
    p.step("gpsimd", lambda e: [e.dma_start(out=t["gwt"][:, i * 8:(i + 1) * 8],
                                            in_=gw[i * 128:(i + 1) * 128, :])
                                for i in range(8)], dmas=8)
    p.step("vector", lambda e: e.memset(onebf[:], 1.0))

    def s_mm(e, s):
        for i in range(8):
            r = e.matmul(pm[:], qfm[:, i * GQ + s * 128: i * GQ + s * 128 + 128],
                         kt[:, i * 512:(i + 1) * 512],
                         start=(i == 0), stop=(i == 7))
        return r

    for g in range(NG):
        q0 = g * GQ
        p.step("sync", lambda e, q0=q0: e.dma_start(
            out=qfm[:].rearrange("p (a t) -> p a t", a=8),
            in_=qf[:, q0:q0 + GQ].rearrange("(a p) t -> p a t", p=128)), dmas=1)
        p.step("vector", lambda e: e.memset(stf[:], 1e30))
        for kb in range(NKB):
            k0 = kb * 512
            p.step("sync", lambda e, k0=k0: e.dma_start(
                out=kt[:].rearrange("p (a t) -> p a t", a=8),
                in_=kff[:, k0:k0 + 512].rearrange("(a p) t -> p a t", p=128)), dmas=1)
            p.step("tensor", lambda e, k0=k0: e.matmul(
                pkp[:], onebf[:], kpow[:, k0:k0 + 512], start=True, stop=True))
            p.step("vector", lambda e: e.tensor_copy(kpr[:], pkp[:]))
            for s in range(NSUB):
                p.step("tensor", lambda e, s=s: s_mm(e, s))
                p.step("vector", lambda e, s=s, q0=q0: e.tensor_scalar(
                    out=cmpt[:], in0=kpr[:],
                    scalar1=qpt[:, q0 // 128 + s:q0 // 128 + s + 1],
                    scalar2=None, op0=OP.is_gt))
                p.step("vector", lambda e: e.scalar_tensor_tensor(
                    out=ssb[:], in0=cmpt[:], scalar=-1e30, in1=pm[:],
                    op0=OP.mult, op1=OP.add))
                p.step("vector", lambda e: e.reduce_max(nmw[:, 0:1], ssb[:], axis=AX,
                                                        negate=True))
                p.step("vector", lambda e, s=s: e.tensor_tensor(
                    out=stf[:, 3 * s:3 * s + 1], in0=stf[:, 3 * s:3 * s + 1],
                    in1=nmw[:, 0:1], op=OP.min))
        def fz(e):
            for s in range(NSUB):
                r = e.memset(oacc[s][:], 0.0)
            return r
        p.step("vector", fz)
        def fz2(e):
            for s in range(NSUB):
                r = e.memset(stf[:, 3 * s + 1:3 * s + 2], 0.0)
            return r
        p.step("vector", fz2)
        for kb in range(NKB):
            k0 = kb * 512
            p.step("sync", lambda e, k0=k0: [
                e.dma_start(out=kt[:].rearrange("p (a t) -> p a t", a=8),
                            in_=kff[:, k0:k0 + 512].rearrange("(a p) t -> p a t", p=128)),
                e.dma_start(out=vtt[:].rearrange("p (a d) -> p a d", a=4),
                            in_=vtf[k0:k0 + 512, :].rearrange("(a p) d -> p a d", p=128))],
                dmas=2)
            p.step("tensor", lambda e, k0=k0: e.matmul(
                pkp[:], onebf[:], kpow[:, k0:k0 + 512], start=True, stop=True))
            p.step("vector", lambda e: e.tensor_copy(kpr[:], pkp[:]))
            for s in range(NSUB):
                p.step("tensor", lambda e, s=s: s_mm(e, s))
                p.step("vector", lambda e, s=s, q0=q0: e.tensor_scalar(
                    out=cmpt[:], in0=kpr[:],
                    scalar1=qpt[:, q0 // 128 + s:q0 // 128 + s + 1],
                    scalar2=None, op0=OP.is_gt))
                p.step("vector", lambda e: e.scalar_tensor_tensor(
                    out=ssb[:], in0=cmpt[:], scalar=-1e30, in1=pm[:],
                    op0=OP.mult, op1=OP.add))
                p.step("scalar", lambda e, s=s: e.activation(
                    out=pb[:], in_=ssb[:], func=AF.Exp, bias=stf[:, 3 * s:3 * s + 1],
                    accum_out=stf[:, 3 * s + 2:3 * s + 3]))
                p.step("vector", lambda e, s=s: e.tensor_tensor(
                    out=stf[:, 3 * s + 1:3 * s + 2], in0=stf[:, 3 * s + 1:3 * s + 2],
                    in1=stf[:, 3 * s + 2:3 * s + 3], op=OP.add))
                def ftp(e):
                    for j in range(4):
                        r = e.transpose(pt4[:, j * 128:(j + 1) * 128],
                                        pb[:, j * 128:(j + 1) * 128], t["idt"][:])
                    return r
                p.step("tensor", ftp)
                p.step("vector", lambda e: e.tensor_copy(pts[:], pt4[:]))
                def fav(e):
                    for ch in range(2):
                        for j in range(4):
                            r = e.matmul(pav[:, ch * 512:(ch + 1) * 512],
                                         pts[:, j * 128:(j + 1) * 128],
                                         vtt[:, j * D + ch * 512: j * D + ch * 512 + 512],
                                         start=(j == 0), stop=(j == 3))
                    return r
                p.step("tensor", fav)
                p.step("vector", lambda e, s=s: e.tensor_tensor(
                    out=oacc[s][:], in0=oacc[s][:], in1=pav[:], op=OP.add))
        for s in range(NSUB):
            tok = q0 + s * 128
            p.step("sync", lambda e, tok=tok: e.dma_start(out=xt[:], in_=x[tok:tok + 128, :]), dmas=1)
            p.step("vector", lambda e, s=s: e.reciprocal(stf[:, 3 * s + 2:3 * s + 3],
                                                         stf[:, 3 * s + 1:3 * s + 2]))
            p.step("vector", lambda e, s=s: e.tensor_scalar(
                out=oacc[s][:], in0=oacc[s][:], scalar1=stf[:, 3 * s + 2:3 * s + 3],
                scalar2=None, op0=OP.mult))
            p.step("vector", lambda e, s=s: e.tensor_tensor(
                out=xt[:], in0=xt[:], in1=oacc[s][:], op=OP.add))
            p.step("sync", lambda e, tok=tok: e.dma_start(out=xout[tok:tok + 128, :], in_=xt[:]), dmas=1)
            emit_ffnpre(p, t, xt, tok)
    return p.build()



CAP = 4096
BIGOFF = 1.0e6


def emit_softmax8(p, st3, mx, es, probs, d2=128):
    p.step("vector", lambda e: e.tensor_reduce(mx[:].rearrange("p (a o) -> p a o", o=1),
                                               st3, axis=AX, op=OP.max))
    p.step("vector", lambda e: e.tensor_tensor(
        out=es[:].rearrange("p (a k) -> p a k", k=8), in0=st3,
        in1=mx[:].rearrange("p (a o) -> p a o", o=1).to_broadcast([128, d2, 8]),
        op=OP.subtract))
    p.step("scalar", lambda e: e.activation(out=es[:], in_=es[:], func=AF.Exp))
    p.step("vector", lambda e: e.tensor_reduce(mx[:].rearrange("p (a o) -> p a o", o=1),
                                               es[:].rearrange("p (a k) -> p a k", k=8),
                                               axis=AX, op=OP.add))
    p.step("vector", lambda e: e.reciprocal(mx[:], mx[:]))
    p.step("vector", lambda e: e.tensor_tensor(
        out=probs[:].rearrange("p (a k) -> p a k", k=8),
        in0=es[:].rearrange("p (a k) -> p a k", k=8),
        in1=mx[:].rearrange("p (a o) -> p a o", o=1).to_broadcast([128, d2, 8]),
        op=OP.mult))


def emit_bisect(p, pe, lo, hi, th, cnt, rcol, allo, pbt, ncols, iters=46):
    ngrp = ncols // 128
    p.step("vector", lambda e: e.memset(lo[:], 0.0))
    p.step("vector", lambda e: e.memset(hi[:], 1.0))
    pe3 = pe[:].rearrange("p (g a) -> p g a", a=128)
    for _ in range(iters):
        p.step("vector", lambda e: e.tensor_tensor(out=th[:], in0=lo[:], in1=hi[:], op=OP.add))
        p.step("vector", lambda e: e.tensor_scalar(out=th[:], in0=th[:], scalar1=0.5,
                                                   scalar2=None, op0=OP.mult))
        p.step("vector", lambda e: e.tensor_tensor(
            out=cnt[:].rearrange("p (g a) -> p g a", a=128), in0=pe3,
            in1=th[:].rearrange("p (g o) -> p g o", o=1).to_broadcast([128, ngrp, 128]),
            op=OP.is_gt))
        p.step("vector", lambda e: e.tensor_reduce(
            rcol[:].rearrange("p (g o) -> p g o", o=1),
            cnt[:].rearrange("p (g a) -> p g a", a=128), axis=AX, op=OP.add))
        p.step("tensor", lambda e: e.matmul(pbt[:], allo[:], rcol[:], start=True, stop=True))
        p.step("vector", lambda e: e.tensor_scalar(out=rcol[:], in0=pbt[:], scalar1=float(CAP),
                                                   scalar2=None, op0=OP.is_ge))
        p.step("vector", lambda e: e.tensor_tensor(out=cnt[:, 0:lo.shape[1]], in0=th[:],
                                                   in1=lo[:], op=OP.subtract))
        p.step("vector", lambda e: e.tensor_tensor(out=cnt[:, 0:lo.shape[1]],
                                                   in0=cnt[:, 0:lo.shape[1]], in1=rcol[:],
                                                   op=OP.mult))
        p.step("vector", lambda e: e.tensor_tensor(out=lo[:], in0=lo[:],
                                                   in1=cnt[:, 0:lo.shape[1]], op=OP.add))
        p.step("vector", lambda e: e.tensor_tensor(out=cnt[:, 0:lo.shape[1]], in0=hi[:],
                                                   in1=th[:], op=OP.subtract))
        p.step("vector", lambda e: e.tensor_tensor(out=cnt[:, 0:lo.shape[1]],
                                                   in0=cnt[:, 0:lo.shape[1]], in1=rcol[:],
                                                   op=OP.mult))
        p.step("vector", lambda e: e.tensor_tensor(out=hi[:], in0=th[:],
                                                   in1=cnt[:, 0:lo.shape[1]], op=OP.add))



def build_moedense(tail="none"):
    nc = bass.Bass("TRN2", target_bir_lowering=False)
    x = nc.dram_tensor("x", [TC, D], F32, kind="ExternalInput").ap()
    xnd = nc.dram_tensor("xn", [TC, D], BF16, kind="ExternalInput").ap()
    scof = nc.dram_tensor("scoresf", [T, 8], F32, kind="ExternalInput").ap()
    scom = nc.dram_tensor("scoresm", [TC, 8], F32, kind="ExternalInput").ap()
    w1 = nc.dram_tensor("w1", [8, D, 4096], F32, kind="ExternalInput").ap()
    w2 = nc.dram_tensor("w2", [8, 4096, D], F32, kind="ExternalInput").ap()
    ident = nc.dram_tensor("ident", [128, 128], BF16, kind="ExternalInput").ap()
    allones = nc.dram_tensor("allones", [128, 128], F32, kind="ExternalInput").ap()
    xout = nc.dram_tensor("xout", [TC, D], F32, kind="ExternalOutput").ap()
    acc = nc.dram_tensor("acc", [TC, D], F32, kind="Internal").ap()
    xnf = nc.dram_tensor("xnf", [D, TC], BF16, kind="Internal").ap()
    if tail == "s4pre":
        winT = nc.dram_tensor("winT", [D, D], F32, kind="ExternalInput").ap()
        b_in = nc.dram_tensor("b_in", [1, D], F32, kind="ExternalInput").ap()
        gam = nc.dram_tensor("gam", [1, D], F32, kind="ExternalInput").ap()
        bet = nc.dram_tensor("bet", [1, D], F32, kind="ExternalInput").ap()
        onesrowf = nc.dram_tensor("onesrowf", [1, 128], F32, kind="ExternalInput").ap()
        d_u2 = nc.dram_tensor("u2", [TC, D], F32, kind="ExternalOutput").ap()
        d_cs = nc.dram_tensor("colsum", [1, D], F32, kind="ExternalOutput").ap()

    NT = T // 128
    NTm = TC // 128
    p = Prog(nc)
    idt = p.sb("idt", [128, 128], BF16)
    allo = p.sb("allo", [128, 128], F32)
    w1t = p.sb("w1t", [128, 8 * 4096], BF16)
    w2t = p.sb("w2t", [128, 32 * D], BF16)
    xnt = p.sb("xnt", [128, D], BF16)
    xfm1 = p.sb("xfm1", [128, 8 * 128], BF16)
    xfmb = p.sb("xfmb", [128, 8 * 512], BF16)
    h1f = p.sb("h1f", [128, 32 * 512], BF16)
    xacc = p.sb("xacc", [128, 4 * D], F32)
    pmE = p.sb("pmE", [128, 8 * NTm], F32)
    w8E = p.sb("w8E", [128, 8 * NTm], F32)
    lo = p.sb("lo", [128, 8], F32)
    hi = p.sb("hi", [128, 8], F32)
    th = p.sb("th", [128, 8], F32)
    rcol = p.sb("rcol", [128, 8], F32)
    ptp = p.ps("ptp", [128, 128], BF16)
    pg1 = p.ps("pg1", [128, 512], F32)
    pg2a = p.ps("pg2a", [128, 512], F32)
    pg2b = p.ps("pg2b", [128, 512], F32)
    pbt = p.ps("pbt", [128, 8], F32)
    w1f = w1t[:].bitcast(F32)
    stf = w1f[:, 0:8 * NT]
    probs = w1f[:, 8 * NT:16 * NT]
    esf = w1f[:, 16 * NT:24 * NT]
    peF = w1f[:, 24 * NT:32 * NT]
    cnt = w1f[:, 32 * NT:40 * NT]
    mxf = w1f[:, 40 * NT:41 * NT]
    w2f_ = w2t[:].bitcast(F32)
    stm = w2f_[:, 0:8 * NTm]
    probm = w2f_[:, 8 * NTm:16 * NTm]
    esm = w2f_[:, 16 * NTm:24 * NTm]
    mxm = w2f_[:, 24 * NTm:24 * NTm + NTm]

    p.step("sync", lambda e: [
        e.dma_start(out=stf.rearrange("p (a k) -> p a k", k=8),
                    in_=scof[:, :].rearrange("(a p) k -> p a k", p=128)),
        e.dma_start(out=stm.rearrange("p (a k) -> p a k", k=8),
                    in_=scom[:, :].rearrange("(a p) k -> p a k", p=128)),
        e.dma_start(out=allo[:], in_=allones[:, :]),
        e.dma_start(out=idt[:], in_=ident[:, :])], dmas=4)
    emit_softmax8(p, stf.rearrange("p (a k) -> p a k", k=8), mxf, esf, probs, d2=NT)
    emit_softmax8(p, stm.rearrange("p (a k) -> p a k", k=8), mxm, esm, probm, d2=NTm)
    p.step("vector", lambda e: e.tensor_copy(
        peF.rearrange("p (k a) -> p k a", a=NT),
        probs.rearrange("p (a k) -> p a k", k=8).transpose([0, 2, 1])))
    p.step("vector", lambda e: e.tensor_copy(
        pmE[:].rearrange("p (k a) -> p k a", a=NTm),
        probm.rearrange("p (a k) -> p a k", k=8).transpose([0, 2, 1])))
    emit_bisect(p, peF, lo, hi, th, cnt, rcol, allo, pbt, 8 * NT)
    p.step("vector", lambda e: e.tensor_tensor(
        out=w8E[:].rearrange("p (k a) -> p k a", a=NTm),
        in0=pmE[:].rearrange("p (k a) -> p k a", a=NTm),
        in1=lo[:].rearrange("p (k o) -> p k o", o=1).to_broadcast([128, 8, NTm]),
        op=OP.is_gt))
    p.step("vector", lambda e: e.tensor_tensor(out=w8E[:], in0=w8E[:], in1=pmE[:],
                                               op=OP.mult))
    for tl in range(NTm):
        p.step("sync", lambda e, tl=tl: e.dma_start(
            out=xnt[:], in_=xnd[tl * 128:(tl + 1) * 128, :]), dmas=1)
        emit_tp(p, xnt, (lambda i: xfm1[:, i * 128:(i + 1) * 128]), ptp, idt)
        p.step("sync", lambda e, tl=tl: e.dma_start(
            out=xnf[:, tl * 128:(tl + 1) * 128].rearrange("(a p) t -> p a t", p=128),
            in_=xfm1[:].rearrange("p (a t) -> p a t", a=8)), dmas=1)
    for ex in range(8):
        p.step("gpsimd", lambda e, ex=ex: [e.dma_start(out=w1t[:, i * 4096:(i + 1) * 4096],
                                                       in_=w1[ex, i * 128:(i + 1) * 128, :])
                                           for i in range(8)], dmas=8)
        p.step("gpsimd", lambda e, ex=ex: [e.dma_start(out=w2t[:, i * D:(i + 1) * D],
                                                       in_=w2[ex, i * 128:(i + 1) * 128, :])
                                           for i in range(32)], dmas=32)
        for blk in range(TC // 512):
            t0 = blk * 512
            p.step("sync", lambda e, t0=t0: e.dma_start(
                out=xfmb[:].rearrange("p (a t) -> p a t", a=8),
                in_=xnf[:, t0:t0 + 512].rearrange("(a p) t -> p a t", p=128)), dmas=1)
            src_d = x if ex == 0 else acc
            p.step("sync", lambda e, t0=t0, src_d=src_d: e.dma_start(
                out=xacc[:].rearrange("p (a d2) -> p a d2", a=4),
                in_=src_d[t0:t0 + 512, :].rearrange("(a p) d2 -> p a d2", p=128)), dmas=1)
            for hm in range(32):
                def fg1(e, hm=hm):
                    for i in range(8):
                        r = e.matmul(pg1[:],
                                     w1t[:, i * 4096 + hm * 128: i * 4096 + hm * 128 + 128],
                                     xfmb[:, i * 512:(i + 1) * 512],
                                     start=(i == 0), stop=(i == 7))
                    return r
                p.step("tensor", fg1)
                p.step("scalar", lambda e, hm=hm: e.activation(
                    out=h1f[:, hm * 512:(hm + 1) * 512], in_=pg1[:], func=AF.Gelu))
            for sub in range(4):
                for oc in range(2):
                    pgx = pg2a if (oc == 0) else pg2b
                    def fg2(e, sub=sub, oc=oc, pgx=pgx):
                        for k in range(32):
                            r = e.matmul(pgx[:],
                                         h1f[:, k * 512 + sub * 128: k * 512 + sub * 128 + 128],
                                         w2t[:, k * D + oc * 512: k * D + oc * 512 + 512],
                                         start=(k == 0), stop=(k == 31))
                        return r
                    p.step("tensor", fg2)
                    def facc(e, sub=sub, oc=oc, pgx=pgx, ex=ex, t0=t0):
                        col = ex * NTm + t0 // 128 + sub
                        return e.scalar_tensor_tensor(
                            out=xacc[:, sub * D + oc * 512: sub * D + oc * 512 + 512],
                            in0=pgx[:], scalar=w8E[:, col:col + 1],
                            in1=xacc[:, sub * D + oc * 512: sub * D + oc * 512 + 512],
                            op0=OP.mult, op1=OP.add)
                    p.step("vector", facc)
            dst = acc if (ex < 7 or tail != "none") else xout
            p.step("sync", lambda e, t0=t0, dst=dst: e.dma_start(
                out=dst[t0:t0 + 512, :].rearrange("(a p) d2 -> p a d2", p=128),
                in_=xacc[:].rearrange("p (a d2) -> p a d2", a=4)), dmas=1)

    if tail == "s4pre":
        wint = w1t[:, 0:8 * D]
        w2fx = w2t[:].bitcast(F32)
        binrep = w2fx[:, 0:D]
        garep = w2fx[:, D:2 * D]
        berep = w2fx[:, 2 * D:3 * D]
        usb = w2fx[:, 3 * D:4 * D]
        xln = w2fx[:, 4 * D:5 * D]
        u2t = w2fx[:, 5 * D:6 * D]
        sq2 = w2fx[:, 6 * D:7 * D]
        birow = w2fx[0:1, 7 * D:8 * D]
        garow = w2fx[0:1, 8 * D:9 * D]
        berow = w2fx[0:1, 9 * D:10 * D]
        onrw = w2fx[0:1, 10 * D:10 * D + 128]
        xat = xacc[:, 0:D]
        stt2 = p.sb("stt2", [128, 8], F32)
        xnb = xnt[:]
        prow = p.ps("prow", [128, 512], F32)
        pcs = p.ps("pcs", [1, D], F32)
        onescolf = p.sb("onescolf", [128, 1], F32)
        p.step("gpsimd", lambda e: [e.dma_start(out=wint[:, i * D:(i + 1) * D],
                                                in_=winT[i * 128:(i + 1) * 128, :])
                                    for i in range(8)], dmas=8)
        p.step("sync", lambda e: [e.dma_start(out=birow, in_=b_in[:, :]),
                                  e.dma_start(out=garow, in_=gam[:, :]),
                                  e.dma_start(out=berow, in_=bet[:, :]),
                                  e.dma_start(out=onrw, in_=onesrowf[:, :])], dmas=4)
        p.step("vector", lambda e: e.memset(onescolf[:], 1.0))
        for ch in range(2):
            for (rep, row) in ((binrep, birow), (garep, garow), (berep, berow)):
                p.step("tensor", lambda e, row=row, ch=ch: e.matmul(
                    prow[:], onrw, row[0:1, ch * 512:(ch + 1) * 512],
                    start=True, stop=True))
                p.step("vector", lambda e, rep=rep, ch=ch: e.tensor_copy(
                    rep[:, ch * 512:(ch + 1) * 512], prow[:]))
        for tl in range(NTm):
            p.step("sync", lambda e, tl=tl: e.dma_start(
                out=xat, in_=acc[tl * 128:(tl + 1) * 128, :]), dmas=1)
            p.step("sync", lambda e, tl=tl: e.dma_start(
                out=xout[tl * 128:(tl + 1) * 128, :], in_=xat), dmas=1)
            emit_ln(p, xat, xnb, stt2, sq2)
            emit_tp(p, xnb, (lambda i: xfm1[:, i * 128:(i + 1) * 128]), ptp, idt)
            for ch in range(2):
                def fpu(e, ch=ch):
                    for i in range(8):
                        r = e.matmul(pg1[:], xfm1[:, i * 128:(i + 1) * 128],
                                     wint[:, i * D + ch * 512: i * D + ch * 512 + 512],
                                     start=(i == 0), stop=(i == 7))
                    return r
                p.step("tensor", fpu)
                p.step("vector", lambda e, ch=ch: e.tensor_tensor(
                    out=usb[:, ch * 512:(ch + 1) * 512], in0=pg1[:],
                    in1=binrep[:, ch * 512:(ch + 1) * 512], op=OP.add))
            emit_ln(p, usb, xln, stt2, sq2)
            p.step("vector", lambda e: e.tensor_tensor(out=u2t, in0=xln, in1=garep,
                                                       op=OP.mult))
            p.step("vector", lambda e: e.tensor_tensor(out=u2t, in0=u2t, in1=berep,
                                                       op=OP.add))
            def fcs(e, tl=tl):
                for ch in range(2):
                    r = e.matmul(pcs[0:1, ch * 512:(ch + 1) * 512], onescolf[:],
                                 u2t[:, ch * 512:(ch + 1) * 512],
                                 start=(tl == 0), stop=(tl == NTm - 1))
                return r
            p.step("tensor", fcs)
            p.step("sync", lambda e, tl=tl: e.dma_start(
                out=d_u2[tl * 128:(tl + 1) * 128, :], in_=u2t), dmas=1)
        p.step("vector", lambda e: e.tensor_copy(usb[0:1, 0:D], pcs[:]))
        p.step("sync", lambda e: e.dma_start(out=d_cs[:, :], in_=usb[0:1, 0:D]), dmas=1)
    return p.build()



LK = 256
NF = 264
WIN = 512


def s4_consts():
    c = {}
    n = np.arange(WIN)[:, None]
    f = np.arange(NF)[None, :]
    mask = (f < 257).astype(np.float64)
    ang = 2 * np.pi * f * n / WIN
    c["fxc"] = (np.cos(ang) * mask).astype(np.float32).astype(BF)
    c["fxs"] = (-np.sin(ang) * mask).astype(np.float32).astype(BF)
    c["fkc"] = np.ascontiguousarray(c["fxc"][:LK])
    c["fks"] = np.ascontiguousarray(c["fxs"][:LK])
    fv = np.arange(NF)
    w = np.where(fv <= 256, (2.0 - (fv == 0) - (fv == 256)) / WIN, 0.0)[:, None]
    tv = np.arange(256)[None, :]
    ang2 = 2 * np.pi * fv[:, None] * (256 + tv) / WIN
    c["gre"] = (w * np.cos(ang2)).astype(np.float32).astype(BF)
    c["gim"] = (-w * np.sin(ang2)).astype(np.float32).astype(BF)
    c["lmat"] = np.tile(np.arange(LK, dtype=np.float32)[None, :], (64, 1))
    return c


def build_s4conv():
    NBLK = TC // 256
    nc = bass.Bass("TRN2", target_bir_lowering=False)
    x = nc.dram_tensor("x", [TC, D], F32, kind="ExternalInput").ap()
    u2e = nc.dram_tensor("u2e", [TC + 256, D], F32, kind="ExternalInput").ap()
    colsums = nc.dram_tensor("colsums", [4, D], F32, kind="ExternalInput").ap()
    pmk = nc.dram_tensor("pmk", [4, 1], F32, kind="ExternalInput").ap()
    are = nc.dram_tensor("are", [64, 1], F32, kind="ExternalInput").ap()
    aim = nc.dram_tensor("aim", [64, 1], F32, kind="ExternalInput").ap()
    bre = nc.dram_tensor("bre", [64, 1], F32, kind="ExternalInput").ap()
    bim = nc.dram_tensor("bim", [64, 1], F32, kind="ExternalInput").ap()
    cre = nc.dram_tensor("cre", [D, 64], F32, kind="ExternalInput").ap()
    cim = nc.dram_tensor("cim", [D, 64], F32, kind="ExternalInput").ap()
    dre = nc.dram_tensor("dre", [1, D], F32, kind="ExternalInput").ap()
    dim = nc.dram_tensor("dim", [1, D], F32, kind="ExternalInput").ap()
    lmat = nc.dram_tensor("lmat", [64, LK], F32, kind="ExternalInput").ap()
    fxc = nc.dram_tensor("fxc", [WIN, NF], BF16, kind="ExternalInput").ap()
    fxs = nc.dram_tensor("fxs", [WIN, NF], BF16, kind="ExternalInput").ap()
    fkc = nc.dram_tensor("fkc", [LK, NF], BF16, kind="ExternalInput").ap()
    fks = nc.dram_tensor("fks", [LK, NF], BF16, kind="ExternalInput").ap()
    gre = nc.dram_tensor("gre", [NF, 256], BF16, kind="ExternalInput").ap()
    gim = nc.dram_tensor("gim", [NF, 256], BF16, kind="ExternalInput").ap()
    gw = nc.dram_tensor("gw", [D, 8], F32, kind="ExternalInput").ap()
    ident = nc.dram_tensor("ident", [128, 128], BF16, kind="ExternalInput").ap()
    triu = nc.dram_tensor("triu", [128, 128], BF16, kind="ExternalInput").ap()
    onesrowf = nc.dram_tensor("onesrowf", [1, 128], F32, kind="ExternalInput").ap()
    xout = nc.dram_tensor("xout", [TC, D], F32, kind="ExternalOutput").ap()
    d_xn1 = nc.dram_tensor("xn1", [TC, D], BF16, kind="ExternalOutput").ap()
    d_scores = nc.dram_tensor("scores", [TC, 8], F32, kind="ExternalOutput").ap()

    p = Prog(nc)
    t = {}
    t["idt"] = p.sb("idt", [128, 128], BF16)
    trut = p.sb("trut", [128, 128], BF16)
    onrw = p.sb("onrw", [1, 128], F32)
    onescolf = p.sb("onescolf", [128, 1], F32)
    onescolb = p.sb("onescolb", [128, 1], BF16)
    t["gwt"] = p.sb("gwt", [128, 8 * 8], BF16)
    fxct = p.sb("fxct", [128, 4 * NF], BF16)
    fxst = p.sb("fxst", [128, 4 * NF], BF16)
    fkct = p.sb("fkct", [128, 2 * NF], BF16)
    fkst = p.sb("fkst", [128, 2 * NF], BF16)
    gret = p.sb("gret", [128, 3 * 256], BF16)
    gimt = p.sb("gimt", [128, 3 * 256], BF16)
    ar = p.sb("ar", [64, 16], F32)
    lm = p.sb("lm", [64, LK], F32)
    pw = p.sb("pw", [64, 6 * LK], F32)
    pb2 = p.sb("pb2", [64, 2 * LK], BF16)
    ibuf = p.sb("ibuf", [64, LK], mybir.dt.int32)
    ld64 = p.sb("ld64", [128, 64], F32)
    lb64 = p.sb("lb64", [128, 128], BF16)
    ct = p.sb("ct", [64, 2 * D], BF16)
    krb = p.sb("krb", [128, 2 * D], BF16)
    kfr = p.sb("kfr", [128, 3 * D], F32)
    kfi = p.sb("kfi", [128, 3 * D], F32)
    pmkt = p.sb("pmkt", [4, 1], F32)
    colst = p.sb("colst", [4, D], F32)
    wbuf = p.sb("wbuf", [128, 4 * D], BF16)
    u2f = p.sb("u2f", [128, D], F32)
    vt = p.sb("vt", [128, D], BF16)
    xsb = p.sb("xsb", [128, 6 * 512], F32)
    yre = p.sb("yre", [128, 3 * 512], BF16)
    yim = p.sb("yim", [128, 3 * 512], BF16)
    ysc1 = p.sb("ysc1", [128, 512], F32)
    ysc2 = p.sb("ysc2", [128, 512], F32)
    drow = p.sb("drow", [1, D], F32)
    dimr = p.sb("dimr", [1, D], F32)
    redrep = p.sb("redrep", [128, D], F32)
    offrow = p.sb("offrow", [1, D], F32)
    orsc = p.sb("orsc", [1, D], F32)
    xt = p.sb("xt", [128, D], F32)
    yt0 = p.sb("yt0", [128, D], F32)
    yt1 = p.sb("yt1", [128, D], F32)
    t["xn"] = p.sb("xn", [128, D], BF16)
    t["sq"] = p.sb("sqs", [128, D], F32)
    t["stats2"] = p.sb("stats2", [128, 8], F32)
    t["xfm1"] = p.sb("xfm1", [128, 8 * 128], BF16)
    t["sc8"] = p.sb("sc8", [128, 8], F32)
    t["d_xn1"], t["d_scores"] = d_xn1, d_scores
    pxf = p.ps("pxf", [128, 512], F32)
    py0 = p.ps("py0", [128, 512], F32)
    py1 = p.ps("py1", [128, 512], F32)
    poff = p.ps("poff", [1, 512], F32)
    t["pt1"] = p.ps("pt1", [128, 128], BF16)
    t["pg"] = p.ps("pg", [128, 8], F32)

    p.step("sync", lambda e: [
        e.dma_start(out=t["idt"][:], in_=ident[:, :]),
        e.dma_start(out=trut[:], in_=triu[:, :]),
        e.dma_start(out=onrw[:], in_=onesrowf[:, :]),
        e.dma_start(out=ar[:, 0:1], in_=are[:, :]),
        e.dma_start(out=ar[:, 1:2], in_=aim[:, :]),
        e.dma_start(out=ar[:, 2:3], in_=bre[:, :]),
        e.dma_start(out=ar[:, 3:4], in_=bim[:, :]),
        e.dma_start(out=lm[:], in_=lmat[:, :]),
        e.dma_start(out=drow[:], in_=dre[:, :]),
        e.dma_start(out=dimr[:], in_=dim[:, :]),
        e.dma_start(out=pmkt[:], in_=pmk[:, :]),
        e.dma_start(out=colst[:], in_=colsums[:, :])], dmas=12)
    p.step("sync", lambda e: [e.dma_start(out=fxct[:].rearrange("p (a f) -> p a f", a=4),
                                          in_=fxc[:, :].rearrange("(a p) f -> p a f", p=128)),
                              e.dma_start(out=fxst[:].rearrange("p (a f) -> p a f", a=4),
                                          in_=fxs[:, :].rearrange("(a p) f -> p a f", p=128)),
                              e.dma_start(out=fkct[:].rearrange("p (a f) -> p a f", a=2),
                                          in_=fkc[:, :].rearrange("(a p) f -> p a f", p=128)),
                              e.dma_start(out=fkst[:].rearrange("p (a f) -> p a f", a=2),
                                          in_=fks[:, :].rearrange("(a p) f -> p a f", p=128))],
           dmas=4)
    p.step("sync", lambda e: [e.dma_start(out=gret[0:128, 0:256], in_=gre[0:128, :]),
                              e.dma_start(out=gret[0:128, 256:512], in_=gre[128:256, :]),
                              e.dma_start(out=gret[0:8, 512:768], in_=gre[256:264, :]),
                              e.dma_start(out=gimt[0:128, 0:256], in_=gim[0:128, :]),
                              e.dma_start(out=gimt[0:128, 256:512], in_=gim[128:256, :]),
                              e.dma_start(out=gimt[0:8, 512:768], in_=gim[256:264, :])],
           dmas=6)
    p.step("gpsimd", lambda e: [e.dma_start(out=t["gwt"][:, i * 8:(i + 1) * 8],
                                            in_=gw[i * 128:(i + 1) * 128, :])
                                for i in range(8)], dmas=8)
    p.step("vector", lambda e: e.memset(onescolf[:], 1.0))
    p.step("vector", lambda e: e.memset(onescolb[:], 1.0))

    p.step("vector", lambda e: e.scalar_tensor_tensor(out=drow[:], in0=dimr[:], scalar=0.0,
                                                      in1=drow[:], op0=OP.mult, op1=OP.add))
    for ch in range(2):
        p.step("tensor", lambda e, ch=ch: e.matmul(
            pxf[:], onrw[:], drow[0:1, ch * 512:(ch + 1) * 512], start=True, stop=True))
        p.step("vector", lambda e, ch=ch: e.tensor_copy(redrep[:, ch * 512:(ch + 1) * 512],
                                                        pxf[:]))
    for ch in range(2):
        p.step("tensor", lambda e, ch=ch: e.matmul(
            poff[:], pmkt[:], colst[:, ch * 512:(ch + 1) * 512], start=True, stop=True))
        p.step("vector", lambda e, ch=ch: e.tensor_copy(orsc[0:1, ch * 512:(ch + 1) * 512],
                                                        poff[:]))
    p.step("vector", lambda e: e.tensor_tensor(out=offrow[:], in0=orsc[:], in1=drow[:],
                                               op=OP.mult))

    C = lambda i: ar[:, i:i + 1]
    PW = lambda i: pw[:, i * LK:(i + 1) * LK]
    p.step("vector", lambda e: e.tensor_tensor(out=C(4), in0=C(0), in1=C(0), op=OP.mult))
    p.step("vector", lambda e: e.tensor_tensor(out=C(5), in0=C(1), in1=C(1), op=OP.mult))
    p.step("vector", lambda e: e.tensor_tensor(out=C(4), in0=C(4), in1=C(5), op=OP.add))
    p.step("scalar", lambda e: e.activation(out=C(4), in_=C(4), func=AF.Ln))
    p.step("vector", lambda e: e.tensor_scalar(out=C(4), in0=C(4), scalar1=0.5,
                                               scalar2=None, op0=OP.mult))
    p.step("vector", lambda e: e.reciprocal(C(5), C(0)))
    p.step("vector", lambda e: e.tensor_tensor(out=C(5), in0=C(5), in1=C(1), op=OP.mult))
    p.step("scalar", lambda e: e.activation(out=C(5), in_=C(5), func=AF.Arctan))
    p.step("scalar", lambda e: e.activation(out=C(6), in_=C(1), func=AF.Sign))
    p.step("vector", lambda e: e.scalar_tensor_tensor(out=C(5), in0=C(6), scalar=float(np.pi),
                                                      in1=C(5), op0=OP.mult, op1=OP.add))
    p.step("scalar", lambda e: e.activation(out=C(7), in_=C(5), func=AF.Sign))
    p.step("scalar", lambda e: e.activation(out=C(6), in_=C(5), func=AF.Abs))
    p.step("scalar", lambda e: e.activation(out=PW(1), in_=lm[:], func=AF.Exp, scale=C(4)))
    p.step("vector", lambda e: e.tensor_scalar(out=PW(0), in0=lm[:], scalar1=C(6),
                                               scalar2=float(0.5 / np.pi), op0=OP.mult,
                                               op1=OP.mult))
    p.step("vector", lambda e: e.tensor_copy(ibuf[:], PW(0)))
    p.step("vector", lambda e: e.tensor_copy(PW(2), ibuf[:]))
    p.step("vector", lambda e: e.tensor_tensor(out=PW(0), in0=PW(0), in1=PW(2), op=OP.subtract))
    p.step("vector", lambda e: e.tensor_scalar(out=PW(0), in0=PW(0), scalar1=float(2 * np.pi),
                                               scalar2=None, op0=OP.mult))
    p.step("scalar", lambda e: e.activation(out=PW(3), in_=PW(0), func=AF.Sin))
    p.step("vector", lambda e: e.tensor_scalar(out=PW(2), in0=PW(0), scalar1=-1.0,
                                               scalar2=float(np.pi / 2), op0=OP.mult, op1=OP.add))
    p.step("vector", lambda e: e.tensor_scalar(out=PW(4), in0=PW(2), scalar1=float(np.pi),
                                               scalar2=None, op0=OP.is_gt))
    p.step("vector", lambda e: e.scalar_tensor_tensor(out=PW(2), in0=PW(4),
                                                      scalar=float(-2 * np.pi), in1=PW(2),
                                                      op0=OP.mult, op1=OP.add))
    p.step("scalar", lambda e: e.activation(out=PW(2), in_=PW(2), func=AF.Sin))
    p.step("vector", lambda e: e.tensor_scalar(out=PW(3), in0=PW(3), scalar1=C(7),
                                               scalar2=None, op0=OP.mult))
    p.step("vector", lambda e: e.tensor_tensor(out=PW(4), in0=PW(1), in1=PW(2), op=OP.mult))
    p.step("vector", lambda e: e.tensor_tensor(out=PW(5), in0=PW(1), in1=PW(3), op=OP.mult))
    p.step("vector", lambda e: e.tensor_scalar(out=PW(0), in0=PW(4), scalar1=C(2),
                                               scalar2=None, op0=OP.mult))
    p.step("vector", lambda e: e.tensor_scalar(out=PW(1), in0=PW(5), scalar1=C(3),
                                               scalar2=None, op0=OP.mult))
    p.step("vector", lambda e: e.tensor_tensor(out=PW(0), in0=PW(0), in1=PW(1), op=OP.subtract))
    p.step("vector", lambda e: e.tensor_copy(pb2[:, 0:LK], PW(0)))
    p.step("vector", lambda e: e.tensor_scalar(out=C(8), in0=C(3), scalar1=-1.0,
                                               scalar2=None, op0=OP.mult))
    p.step("vector", lambda e: e.tensor_scalar(out=C(9), in0=C(2), scalar1=-1.0,
                                               scalar2=None, op0=OP.mult))
    p.step("vector", lambda e: e.tensor_scalar(out=PW(0), in0=PW(4), scalar1=C(8),
                                               scalar2=None, op0=OP.mult))
    p.step("vector", lambda e: e.tensor_scalar(out=PW(1), in0=PW(5), scalar1=C(9),
                                               scalar2=None, op0=OP.mult))
    p.step("vector", lambda e: e.tensor_tensor(out=PW(0), in0=PW(0), in1=PW(1), op=OP.add))
    p.step("vector", lambda e: e.tensor_copy(pb2[:, LK:2 * LK], PW(0)))
    for (srcd, dstoff) in ((cre, 0), (cim, D)):
        for i in range(8):
            p.step("sync", lambda e, srcd=srcd, i=i: e.dma_start(
                out=ld64[:, 0:64], in_=srcd[i * 128:(i + 1) * 128, :]), dmas=1)
            p.step("vector", lambda e: e.tensor_copy(lb64[:, 0:64], ld64[:, 0:64]))
            p.step("tensor", lambda e: e.transpose(t["pt1"][:], lb64[:], t["idt"][:]))
            p.step("vector", lambda e, dstoff=dstoff, i=i: e.tensor_copy(
                ct[0:64, dstoff + i * 128: dstoff + (i + 1) * 128], t["pt1"][0:64, 0:128]))
    for lt in range(2):
        for ch in range(2):
            def fkr(e, lt=lt, ch=ch):
                e.matmul(pxf[:], pb2[:, lt * 128:(lt + 1) * 128],
                         ct[0:64, ch * 512:(ch + 1) * 512], start=True, stop=False)
                return e.matmul(pxf[:], pb2[:, LK + lt * 128: LK + lt * 128 + 128],
                                ct[0:64, D + ch * 512: D + ch * 512 + 512],
                                start=False, stop=True)
            p.step("tensor", fkr)
            p.step("vector", lambda e, lt=lt, ch=ch: e.tensor_copy(
                krb[:, lt * D + ch * 512: lt * D + ch * 512 + 512], pxf[:]))
    for (fk, kf) in ((fkct, kfr), (fkst, kfi)):
        for ft in range(3):
            rows = 128 if ft < 2 else 8
            for ch2 in range(2):
                def fkf(e, fk=fk, ft=ft, ch2=ch2, rows=rows):
                    for lt in range(2):
                        r = e.matmul(pxf[0:rows, :],
                                     fk[:, lt * NF + ft * 128: lt * NF + ft * 128 + rows],
                                     krb[:, lt * D + ch2 * 512: lt * D + ch2 * 512 + 512],
                                     start=(lt == 0), stop=(lt == 1))
                    return r
                p.step("tensor", fkf)
                p.step("vector", lambda e, kf=kf, ft=ft, ch2=ch2, rows=rows: e.tensor_copy(
                    kf[0:rows, ft * D + ch2 * 512: ft * D + ch2 * 512 + 512], pxf[0:rows, :]))

    for blk in range(NBLK):
        b0 = blk * 256
        p.step("gpsimd", lambda e, b0=b0: e.dma_start(
            out=wbuf[:].rearrange("p (a d2) -> p a d2", a=4),
            in_=u2e[b0:b0 + 512, :].rearrange("(a p) d2 -> p a d2", p=128)), dmas=1)
        for ch in range(2):
            for (fx, xoff) in ((fxct, 0), (fxst, 3)):
                for ft in range(3):
                    rows = 128 if ft < 2 else 8
                    def fxf(e, fx=fx, ft=ft, ch=ch, rows=rows):
                        for ks in range(4):
                            r = e.matmul(pxf[0:rows, :],
                                         fx[:, ks * NF + ft * 128: ks * NF + ft * 128 + rows],
                                         wbuf[:, ks * D + ch * 512: ks * D + ch * 512 + 512],
                                         start=(ks == 0), stop=(ks == 3))
                        return r
                    p.step("tensor", fxf)
                    p.step("vector", lambda e, xoff=xoff, ft=ft, rows=rows: e.tensor_copy(
                        xsb[0:rows, (xoff + ft) * 512:(xoff + ft + 1) * 512], pxf[0:rows, :]))
            for ft in range(3):
                rows = 128 if ft < 2 else 8
                XR = xsb[0:rows, ft * 512:(ft + 1) * 512]
                XI = xsb[0:rows, (3 + ft) * 512:(4 + ft) * 512]
                KR = kfr[0:rows, ft * D + ch * 512: ft * D + ch * 512 + 512]
                KI = kfi[0:rows, ft * D + ch * 512: ft * D + ch * 512 + 512]
                YR = yre[0:rows, ft * 512:(ft + 1) * 512]
                YI = yim[0:rows, ft * 512:(ft + 1) * 512]
                s1 = ysc1[0:rows, :]
                s2 = ysc2[0:rows, :]
                p.step("vector", lambda e, XR=XR, KR=KR, s1=s1: e.tensor_tensor(
                    out=s1, in0=XR, in1=KR, op=OP.mult))
                p.step("vector", lambda e, XI=XI, KI=KI, s2=s2: e.tensor_tensor(
                    out=s2, in0=XI, in1=KI, op=OP.mult))
                p.step("vector", lambda e, s1=s1, s2=s2, YR=YR: e.tensor_tensor(
                    out=YR, in0=s1, in1=s2, op=OP.subtract))
                p.step("vector", lambda e, XR=XR, KI=KI, s1=s1: e.tensor_tensor(
                    out=s1, in0=XR, in1=KI, op=OP.mult))
                p.step("vector", lambda e, XI=XI, KR=KR, s2=s2: e.tensor_tensor(
                    out=s2, in0=XI, in1=KR, op=OP.mult))
                p.step("vector", lambda e, s1=s1, s2=s2, YI=YI: e.tensor_tensor(
                    out=YI, in0=s1, in1=s2, op=OP.add))
            for tt in range(2):
                py = py0 if tt == 0 else py1
                ytl = yt0 if tt == 0 else yt1
                p.step("sync", lambda e, b0=b0, tt=tt: e.dma_start(
                    out=u2f[:], in_=u2e[256 + b0 + tt * 128: 256 + b0 + tt * 128 + 128, :]),
                    dmas=1)
                p.step("vector", lambda e: e.tensor_tensor(out=vt[:], in0=u2f[:], in1=redrep[:],
                                                           op=OP.mult))
                def fy(e, tt=tt, ch=ch, py=py):
                    for (g, y) in ((gret, yre), (gimt, yim)):
                        for ft in range(3):
                            rows = 128 if ft < 2 else 8
                            e.matmul(py[:],
                                     g[0:rows, ft * 256 + tt * 128: ft * 256 + tt * 128 + 128],
                                     y[0:rows, ft * 512:(ft + 1) * 512],
                                     start=(g is gret and ft == 0), stop=False)
                    e.matmul(py[:], trut[:], vt[:, ch * 512:(ch + 1) * 512],
                             start=False, stop=False)
                    return e.matmul(py[:], onrw[:], offrow[0:1, ch * 512:(ch + 1) * 512],
                                    start=False, stop=True)
                p.step("tensor", fy)
                p.step("tensor", lambda e, ch=ch: e.matmul(
                    poff[:], onescolb[:], vt[:, ch * 512:(ch + 1) * 512],
                    start=True, stop=True))
                p.step("vector", lambda e, ch=ch: e.tensor_tensor(
                    out=offrow[0:1, ch * 512:(ch + 1) * 512],
                    in0=offrow[0:1, ch * 512:(ch + 1) * 512], in1=poff[:], op=OP.add))
                p.step("vector", lambda e, py=py, ytl=ytl, ch=ch: e.tensor_tensor(
                    out=ytl[:, ch * 512:(ch + 1) * 512], in0=py[:],
                    in1=u2f[:, ch * 512:(ch + 1) * 512], op=OP.add))
        for tt in range(2):
            ytl = yt0 if tt == 0 else yt1
            p.step("sync", lambda e, b0=b0, tt=tt: e.dma_start(
                out=xt[:], in_=x[b0 + tt * 128: b0 + tt * 128 + 128, :]), dmas=1)
            p.step("vector", lambda e, ytl=ytl: e.tensor_tensor(out=xt[:], in0=xt[:],
                                                                in1=ytl[:], op=OP.add))
            p.step("sync", lambda e, b0=b0, tt=tt: e.dma_start(
                out=xout[b0 + tt * 128: b0 + tt * 128 + 128, :], in_=xt[:]), dmas=1)
            emit_ffnpre(p, t, xt, b0 + tt * 128)
    return p.build()



def m1_consts():
    sel = np.zeros((16, 128, 128), np.float32)
    for j in range(16):
        for p2 in range(128):
            sel[j, 8 * j + p2 // 16, p2] = 1.0
    sumsel = np.zeros((128, 8), np.float32)
    for p2 in range(128):
        sumsel[p2, p2 // 16] = 1.0
    sel16 = np.zeros((16, 128), np.float32)
    for p2 in range(128):
        sel16[p2 % 16, p2] = 1.0
    return {"selt": sel.reshape(16 * 128, 128).astype(BF),
            "sumsel": sumsel.astype(BF), "sel16": sel16.astype(BF)}


def build_m1(full):
    DI = 2048
    NMT = DI // 128
    nc = bass.Bass("TRN2", target_bir_lowering=False)
    x = nc.dram_tensor("x", [TC, D], F32, kind="ExternalInput").ap()
    alog = nc.dram_tensor("alog", [128, 256], F32, kind="ExternalInput").ap()
    selc = nc.dram_tensor("selc", [16 * 128, 128], BF16, kind="ExternalInput").ap()
    sumselc = nc.dram_tensor("sumselc", [128, 8], BF16, kind="ExternalInput").ap()
    sel16c = nc.dram_tensor("sel16c", [16, 128], BF16, kind="ExternalInput").ap()
    onesrowf = nc.dram_tensor("onesrowf", [1, 128], F32, kind="ExternalInput").ap()
    if not full:
        nw = nc.dram_tensor("nw", [1, D], F32, kind="ExternalInput").ap()
        nb = nc.dram_tensor("nb", [1, D], F32, kind="ExternalInput").ap()
        win = nc.dram_tensor("win", [D, DI], F32, kind="ExternalInput").ap()
        wgt = nc.dram_tensor("wgt", [D, DI], F32, kind="ExternalInput").ap()
        wdt = nc.dram_tensor("wdt", [DI, DI], F32, kind="ExternalInput").ap()
        bdt = nc.dram_tensor("bdt", [128, 16], F32, kind="ExternalInput").ap()
        wB = nc.dram_tensor("wB", [DI, 16], F32, kind="ExternalInput").ap()
        wC = nc.dram_tensor("wC", [DI, 16], F32, kind="ExternalInput").ap()
        ident = nc.dram_tensor("ident", [128, 128], BF16, kind="ExternalInput").ap()
        dts = nc.dram_tensor("dts", [DI, TC], BF16, kind="ExternalOutput").ap()
        dzs = nc.dram_tensor("dzs", [DI, TC], BF16, kind="ExternalOutput").ap()
        gts = nc.dram_tensor("gts", [DI, TC], BF16, kind="ExternalOutput").ap()
        bcr = nc.dram_tensor("bcr", [16, 2 * TC], BF16, kind="ExternalOutput").ap()
        xno = nc.dram_tensor("xnod", [TC, D], F32, kind="ExternalOutput").ap()
        d_hf = nc.dram_tensor("hfin", [128, 256], F32, kind="ExternalOutput").ap()
        d_ff = nc.dram_tensor("ff", [128, 256], F32, kind="ExternalOutput").ap()
    else:
        dts = nc.dram_tensor("dts", [DI, TC], BF16, kind="ExternalInput").ap()
        dzs = nc.dram_tensor("dzs", [DI, TC], BF16, kind="ExternalInput").ap()
        gts = nc.dram_tensor("gts", [DI, TC], BF16, kind="ExternalInput").ap()
        bcr = nc.dram_tensor("bcr", [16, 2 * TC], BF16, kind="ExternalInput").ap()
        xno = nc.dram_tensor("xnod", [TC, D], F32, kind="ExternalInput").ap()
        wout = nc.dram_tensor("wout", [DI, D], F32, kind="ExternalInput").ap()
        hfall = nc.dram_tensor("hfall", [4, 128, 256], F32, kind="ExternalInput").ap()
        fall = nc.dram_tensor("fall", [4, 128, 256], F32, kind="ExternalInput").ap()
        incr = nc.dram_tensor("incr", [1, 4], F32, kind="ExternalInput").ap()
        xout = nc.dram_tensor("xout", [TC, D], F32, kind="ExternalOutput").ap()
        ys = nc.dram_tensor("ys", [DI, TC], BF16, kind="Internal").ap()

    p = Prog(nc)
    onrw = p.sb("onrw", [1, 128], F32)
    selt = p.sb("selt", [128, 16 * 128], BF16)
    sums = p.sb("sums", [128, 8], BF16)
    s16 = p.sb("s16", [16, 128], BF16)
    acl = p.sb("acl", [128, 256], F32)
    bc16 = p.sb("bc16", [16, 2 * TC], BF16)
    xt = p.sb("xt", [128, D], F32)
    if not full:
        idt = p.sb("idt", [128, 128], BF16)
        bdtt = p.sb("bdtt", [128, 16], F32)
        nwrep = p.sb("nwrep", [128, D], F32)
        nbrep = p.sb("nbrep", [128, D], F32)
        wint = p.sb("wint", [128, 8 * DI], BF16)
        wgtt = p.sb("wgtt", [128, 8 * DI], BF16)
        wdtt = p.sb("wdtt", [128, 16 * DI], BF16)
        wbct = p.sb("wbct", [128, 16 * 32], BF16)
        xnof = p.sb("xnof", [128, D], F32)
        x2n = p.sb("x2n", [128, D], BF16)
        x2f = xt[:]
        sq = p.sb("sq", [128, D], F32)
        stats = p.sb("stats", [128, 8], F32)
        x2fm = p.sb("x2fm", [128, 8 * 512], BF16)
        zb = p.sb("zb", [128, 16 * 512], BF16)
        dtf = p.sb("dtf", [128, 512], F32)
        dtb = p.sb("dtb", [128, 512], BF16)
        dzb = p.sb("dzb", [128, 512], BF16)
        sc1 = p.sb("sc1", [128, 512], F32)
        sc2 = p.sb("sc2", [128, 512], F32)
        hfin = p.sb("hfins", [128, 256], F32)
        fbuf = p.sb("fbufs", [128, 256], F32)
        dscol = p.sb("dscol", [128, 1], F32)
        dsbcol = p.sb("dsbcol", [128, 1], BF16)
        wdtf = wdtt[:].bitcast(F32)
        dA = wdtf[:, 0:TC]
        ubuf = wdtf[:, TC:2 * TC]
        hbuf = wdtf[:, 2 * TC:3 * TC]
        wgf = wgtt[:].bitcast(F32)
        brep = wgf[:, 0:TC]
        dtt = wint[:, 0:TC]
        dtzt = wint[:, TC:2 * TC]
        rowbuf = xt[0:1, 0:D]
        pz = p.ps("pz", [128, 512], F32)
        pz2 = p.ps("pz2", [128, 512], F32)
        pb2 = p.ps("pb2", [128, 512], F32)
        pb3 = p.ps("pb3", [128, 512], F32)
        pbc = p.ps("pbc", [16, 512], F32)
        pcol = p.ps("pcol", [128, 1], F32)
        pt1 = p.ps("pt1", [128, 128], BF16)
        pbank = [pz, pz2, pb2, pb3]
    else:
        dtt = p.sb("dtt", [128, TC], BF16)
        dtzt = p.sb("dtzt", [128, TC], BF16)
        dA = p.sb("dA", [128, TC], F32)
        ubuf = p.sb("ubuf", [128, TC], F32)
        hbuf = p.sb("hbuf", [128, TC], F32)
        brep = p.sb("brep", [128, TC], F32)
        crep = p.sb("crep", [128, TC], F32)
        hCt = p.sb("hCt", [128, TC], BF16)
        woutt = p.sb("woutt", [128, 16 * D], BF16)
        yscw = p.sb("yscw", [8, 4 * 512], BF16)
        hfa = p.sb("hfa", [128, 1024], F32)
        ffa = p.sb("ffa", [128, 1024], F32)
        hin = p.sb("hin", [128, 256], F32)
        tb1 = p.sb("tb1", [128, 256], F32)
        tb2 = p.sb("tb2", [128, 256], F32)
        inc4 = p.sb("inc4", [128, 4], F32)
        xnof = p.sb("xnof", [128, D], F32)
        ygb = dA[:].bitcast(BF16)
        gtb = ubuf[:].bitcast(BF16)
        dtt, dtzt, dA, ubuf, hbuf = dtt[:], dtzt[:], dA[:], ubuf[:], hbuf[:]
        brep, crep, hCt = brep[:], crep[:], hCt[:]
        pz = p.ps("pz", [128, 512], F32)
        pz2 = p.ps("pz2", [128, 512], F32)
        pb2 = p.ps("pb2", [128, 512], F32)
        pb3 = p.ps("pb3", [128, 512], F32)
        py = [p.ps(f"py{i}", [8, 512], F32) for i in range(4)]
        pbank = [pz, pz2, pb2, pb3]

    base_loads = [lambda e: [e.dma_start(out=onrw[:], in_=onesrowf[:, :]),
                             e.dma_start(out=selt[:].rearrange("p (a m) -> p a m", a=16),
                                         in_=selc[:, :].rearrange("(a p) m -> p a m", p=128)),
                             e.dma_start(out=sums[:], in_=sumselc[:, :]),
                             e.dma_start(out=s16[:], in_=sel16c[:, :]),
                             e.dma_start(out=acl[:], in_=alog[:, :])]]
    p.step("sync", lambda e: base_loads[0](e), dmas=5)
    p.step("scalar", lambda e: e.activation(out=acl[:], in_=acl[:], func=AF.Exp))
    p.step("vector", lambda e: e.tensor_scalar(out=acl[:], in0=acl[:], scalar1=1e-6,
                                               scalar2=1.0, op0=OP.max, op1=OP.min))
    p.step("vector", lambda e: e.tensor_scalar(out=acl[:], in0=acl[:], scalar1=-1.0,
                                               scalar2=None, op0=OP.mult))

    if not full:
        p.step("sync", lambda e: [e.dma_start(out=idt[:], in_=ident[:, :]),
                                  e.dma_start(out=bdtt[:], in_=bdt[:, :])], dmas=2)
        for (rep, row) in ((nwrep, nw), (nbrep, nb)):
            p.step("sync", lambda e, row=row: e.dma_start(out=rowbuf, in_=row[:, :]), dmas=1)
            for ch in range(2):
                p.step("tensor", lambda e, ch=ch: e.matmul(
                    pz[:], onrw[:], rowbuf[0:1, ch * 512:(ch + 1) * 512],
                    start=True, stop=True))
                p.step("vector", lambda e, rep=rep, ch=ch: e.tensor_copy(
                    rep[:, ch * 512:(ch + 1) * 512], pz[:]))
        p.step("gpsimd", lambda e: [e.dma_start(out=wint[:, i * DI:(i + 1) * DI],
                                                in_=win[i * 128:(i + 1) * 128, :])
                                    for i in range(8)], dmas=8)
        p.step("gpsimd", lambda e: [e.dma_start(out=wgtt[:, i * DI:(i + 1) * DI],
                                                in_=wgt[i * 128:(i + 1) * 128, :])
                                    for i in range(8)], dmas=8)
        p.step("gpsimd", lambda e: [e.dma_start(out=wdtt[:, i * DI:(i + 1) * DI],
                                                in_=wdt[i * 128:(i + 1) * 128, :])
                                    for i in range(16)], dmas=16)
        p.step("gpsimd", lambda e: [e.dma_start(out=wbct[:, i * 32:i * 32 + 16],
                                                in_=wB[i * 128:(i + 1) * 128, :])
                                    for i in range(16)] +
                                   [e.dma_start(out=wbct[:, i * 32 + 16:i * 32 + 32],
                                                in_=wC[i * 128:(i + 1) * 128, :])
                                    for i in range(16)], dmas=32)

        for blk in range(TC // 512):
            t0 = blk * 512
            for tt in range(4):
                tok = t0 + tt * 128
                p.step("sync", lambda e, tok=tok: e.dma_start(out=xt[:],
                                                              in_=x[tok:tok + 128, :]), dmas=1)
                emit_ln(p, xt, xnof, stats, sq)
                p.step("sync", lambda e, tok=tok: e.dma_start(
                    out=xno[tok:tok + 128, :], in_=xnof[:]), dmas=1)
                emit_ln(p, xnof, x2f, stats, sq)
                p.step("vector", lambda e: e.tensor_tensor(out=x2f[:], in0=x2f[:],
                                                           in1=nwrep[:], op=OP.mult))
                p.step("vector", lambda e: e.tensor_tensor(out=x2n[:], in0=x2f[:],
                                                           in1=nbrep[:], op=OP.add))
                emit_tp(p, x2n, (lambda i, tt=tt: x2fm[:, i * 512 + tt * 128:
                                                       i * 512 + tt * 128 + 128]), pt1, idt)
            for mt in range(NMT):
                def fz(e, mt=mt):
                    for i in range(8):
                        r = e.matmul(pz[:], wint[:, i * DI + mt * 128: i * DI + mt * 128 + 128],
                                     x2fm[:, i * 512:(i + 1) * 512],
                                     start=(i == 0), stop=(i == 7))
                    return r
                p.step("tensor", fz)
                p.step("scalar", lambda e, mt=mt: e.activation(
                    out=zb[:, mt * 512:(mt + 1) * 512], in_=pz[:], func=AF.Identity))
            for mt in range(NMT):
                def fg(e, mt=mt):
                    for i in range(8):
                        r = e.matmul(pz2[:], wgtt[:, i * DI + mt * 128: i * DI + mt * 128 + 128],
                                     x2fm[:, i * 512:(i + 1) * 512],
                                     start=(i == 0), stop=(i == 7))
                    return r
                p.step("tensor", fg)
                p.step("scalar", lambda e, mt=mt: e.activation(out=sc1[:], in_=pz2[:],
                                                               func=AF.Silu))
                p.step("vector", lambda e: e.tensor_copy(sc2[:].bitcast(BF16)[:, 0:512],
                                                         sc1[:]))
                p.step("gpsimd", lambda e, mt=mt, t0=t0: e.dma_start(
                    out=gts[mt * 128:(mt + 1) * 128, t0:t0 + 512],
                    in_=sc2[:].bitcast(BF16)[:, 0:512]), dmas=1)
            for (off, woff) in ((0, 0), (TC, 16)):
                def fbc(e, off=off, woff=woff, t0=t0):
                    for i in range(16):
                        r = e.matmul(pbc[:], wbct[:, i * 32 + woff: i * 32 + woff + 16],
                                     zb[:, i * 512:(i + 1) * 512],
                                     start=(i == 0), stop=(i == 15))
                    return r
                p.step("tensor", fbc)
                p.step("vector", lambda e, off=off, t0=t0: e.tensor_copy(
                    bc16[:, off + t0: off + t0 + 512], pbc[:]))
            for mt in range(NMT):
                def fdt(e, mt=mt):
                    for i in range(16):
                        r = e.matmul(pz2[:], wdtt[:, i * DI + mt * 128: i * DI + mt * 128 + 128],
                                     zb[:, i * 512:(i + 1) * 512],
                                     start=(i == 0), stop=(i == 15))
                    return r
                p.step("tensor", fdt)
                p.step("vector", lambda e, mt=mt: e.tensor_scalar(
                    out=dtf[:], in0=pz2[:], scalar1=bdtt[:, mt:mt + 1], scalar2=None,
                    op0=OP.add))
                p.step("scalar", lambda e: e.activation(out=sc1[:], in_=dtf[:], func=AF.Abs))
                p.step("scalar", lambda e: e.activation(out=sc1[:], in_=sc1[:], func=AF.Exp,
                                                        scale=-1.0))
                p.step("scalar", lambda e: e.activation(out=sc1[:], in_=sc1[:], func=AF.Ln,
                                                        bias=1.0))
                p.step("vector", lambda e: e.tensor_relu(sc2[:], dtf[:]))
                p.step("vector", lambda e: e.tensor_tensor(out=dtf[:], in0=sc1[:], in1=sc2[:],
                                                           op=OP.add))
                p.step("vector", lambda e: e.tensor_scalar(out=dtf[:], in0=dtf[:],
                                                           scalar1=0.001, scalar2=1.0,
                                                           op0=OP.max, op1=OP.min))
                p.step("vector", lambda e: e.tensor_copy(dtb[:], dtf[:]))
                p.step("sync", lambda e, mt=mt, t0=t0: e.dma_start(
                    out=dts[mt * 128:(mt + 1) * 128, t0:t0 + 512], in_=dtb[:]), dmas=1)
                p.step("vector", lambda e, mt=mt: e.tensor_tensor(
                    out=dzb[:], in0=dtb[:], in1=zb[:, mt * 512:(mt + 1) * 512], op=OP.mult))
                p.step("sync", lambda e, mt=mt, t0=t0: e.dma_start(
                    out=dzs[mt * 128:(mt + 1) * 128, t0:t0 + 512], in_=dzb[:]), dmas=1)
        p.step("sync", lambda e: e.dma_start(out=bcr[:, :], in_=bc16[:]), dmas=1)
        for cc in range(TC // 512):
            p.step("tensor", lambda e, cc=cc: e.matmul(
                pz[:], s16[:], bc16[:, cc * 512: cc * 512 + 512], start=True, stop=True))
            p.step("vector", lambda e, cc=cc: e.tensor_copy(
                brep[:, cc * 512:(cc + 1) * 512], pz[:]))
    else:
        p.step("sync", lambda e: e.dma_start(out=bc16[:], in_=bcr[:, :]), dmas=1)
        for (src_off, rep) in ((0, brep), (TC, crep)):
            for cc in range(TC // 512):
                p.step("tensor", lambda e, src_off=src_off, cc=cc: e.matmul(
                    pz[:], s16[:], bc16[:, src_off + cc * 512: src_off + cc * 512 + 512],
                    start=True, stop=True))
                p.step("vector", lambda e, rep=rep, cc=cc: e.tensor_copy(
                    rep[:, cc * 512:(cc + 1) * 512], pz[:]))
        p.step("sync", lambda e: [e.dma_start(out=hfa[:].rearrange("p (a m) -> p a m", a=4),
                                              in_=hfall[:, :, :].rearrange("a p m -> p a m")),
                                  e.dma_start(out=ffa[:].rearrange("p (a m) -> p a m", a=4),
                                              in_=fall[:, :, :].rearrange("a p m -> p a m")),
                                  e.dma_start(out=inc4[0:1, 0:4], in_=incr[:, :])], dmas=3)
        p.step("tensor", lambda e: e.matmul(pz[:, 0:4], onrw[:], inc4[0:1, 0:4],
                                            start=True, stop=True))
        p.step("vector", lambda e: e.tensor_copy(inc4[:], pz[:, 0:4]))
        p.step("vector", lambda e: e.memset(hin[:], 0.0))
        for cc in range(4):
            p.step("vector", lambda e, cc=cc: e.tensor_scalar(
                out=tb1[:], in0=ffa[:, cc * 256:(cc + 1) * 256], scalar1=inc4[:, cc:cc + 1],
                scalar2=None, op0=OP.mult))
            p.step("vector", lambda e, cc=cc: e.tensor_scalar(
                out=tb2[:], in0=inc4[:, cc:cc + 1].to_broadcast([128, 256]), scalar1=-1.0,
                scalar2=1.0, op0=OP.mult, op1=OP.add))
            p.step("vector", lambda e: e.tensor_tensor(out=tb1[:], in0=tb1[:], in1=tb2[:],
                                                       op=OP.add))
            p.step("vector", lambda e: e.tensor_tensor(out=hin[:], in0=hin[:], in1=tb1[:],
                                                       op=OP.mult))
            p.step("vector", lambda e, cc=cc: e.tensor_scalar(
                out=tb2[:], in0=hfa[:, cc * 256:(cc + 1) * 256], scalar1=inc4[:, cc:cc + 1],
                scalar2=None, op0=OP.mult))
            p.step("vector", lambda e: e.tensor_tensor(out=hin[:], in0=hin[:], in1=tb2[:],
                                                       op=OP.add))

    NCC = TC // 512
    for k in range(16):
        p.step("sync", lambda e, k=k: e.dma_start(out=dtt, in_=dts[k * 128:(k + 1) * 128, :]),
               dmas=1)
        p.step("sync", lambda e, k=k: e.dma_start(out=dtzt, in_=dzs[k * 128:(k + 1) * 128, :]),
               dmas=1)
        if not full:
            p.step("vector", lambda e: e.reduce_sum(dscol[:, 0:1], dtt, axis=AX))
            p.step("vector", lambda e: e.tensor_copy(dsbcol[:], dscol[:]))
        for j in range(16):
            col = k * 16 + j
            for r in range(NCC // 4):
                def fda(e, j=j, r=r):
                    for q in range(4):
                        cc = r * 4 + q
                        rr = e.matmul(pbank[q][:], selt[:, j * 128:(j + 1) * 128],
                                      dtt[:, cc * 512:(cc + 1) * 512], start=True, stop=True)
                    return rr
                p.step("tensor", fda)
                def feda(e, col=col, r=r):
                    for q in range(4):
                        cc = r * 4 + q
                        rr = e.activation(out=dA[:, cc * 512:(cc + 1) * 512] if full
                                          else dA[:, cc * 512:(cc + 1) * 512],
                                          in_=pbank[q][:], func=AF.Exp,
                                          scale=acl[:, col:col + 1])
                    return rr
                p.step("scalar", feda)
            for r in range(NCC // 4):
                def fdu(e, j=j, r=r):
                    for q in range(4):
                        cc = r * 4 + q
                        rr = e.matmul(pbank[q][:], selt[:, j * 128:(j + 1) * 128],
                                      dtzt[:, cc * 512:(cc + 1) * 512], start=True, stop=True)
                    return rr
                p.step("tensor", fdu)
                def fuu(e, r=r):
                    for q in range(4):
                        cc = r * 4 + q
                        rr = e.tensor_tensor(out=ubuf[:, cc * 512:(cc + 1) * 512],
                                             in0=pbank[q][:],
                                             in1=brep[:, cc * 512:(cc + 1) * 512], op=OP.mult)
                    return rr
                p.step("vector", fuu)
            if full:
                p.step("vector", lambda e, col=col: e.tensor_tensor_scan(
                    out=hbuf[:], data0=dA[:], data1=ubuf[:], initial=hin[:, col:col + 1],
                    op0=OP.mult, op1=OP.add))
                p.step("vector", lambda e: e.tensor_tensor(out=hCt[:], in0=hbuf[:],
                                                           in1=crep[:], op=OP.mult))
                for r in range(NCC // 4):
                    def fys(e, r=r):
                        for q in range(4):
                            cc = r * 4 + q
                            rr = e.matmul(py[q][:], sums[:], hCt[:, cc * 512:(cc + 1) * 512],
                                          start=True, stop=True)
                        return rr
                    p.step("tensor", fys)
                    def fye(e, r=r):
                        for q in range(4):
                            rr = e.activation(out=yscw[0:8, q * 512:(q + 1) * 512],
                                              in_=py[q][:], func=AF.Identity)
                        return rr
                    p.step("scalar", fye)
                    def fyd(e, k=k, j=j, r=r):
                        return [e.dma_start(
                            out=ys[k * 128 + 8 * j: k * 128 + 8 * j + 8,
                                   (r * 4 + q) * 512:(r * 4 + q + 1) * 512],
                            in_=yscw[0:8, q * 512:(q + 1) * 512]) for q in range(4)]
                    p.step("sync", fyd, dmas=4)
            else:
                p.step("vector", lambda e: e.tensor_tensor_scan(
                    out=hbuf, data0=dA, data1=ubuf, initial=0.0,
                    op0=OP.mult, op1=OP.add))
                p.step("vector", lambda e, col=col: e.tensor_copy(hfin[:, col:col + 1],
                                                                  hbuf[:, TC - 1:TC]))
                p.step("tensor", lambda e, j=j: e.matmul(
                    pcol[:], selt[:, j * 128:(j + 1) * 128], dsbcol[:, 0:1],
                    start=True, stop=True))
                p.step("scalar", lambda e, col=col: e.activation(
                    out=fbuf[:, col:col + 1], in_=pcol[:], func=AF.Exp,
                    scale=acl[:, col:col + 1]))
    if not full:
        p.step("sync", lambda e: [e.dma_start(out=d_hf[:, :], in_=hfin[:]),
                                  e.dma_start(out=d_ff[:, :], in_=fbuf[:])], dmas=2)
        return p.build()
    p.step("gpsimd", lambda e: [e.dma_start(out=woutt[:, i * D:(i + 1) * D],
                                            in_=wout[i * 128:(i + 1) * 128, :])
                                for i in range(16)], dmas=16)
    for blk in range(TC // 512):
        t0 = blk * 512
        p.step("sync", lambda e, t0=t0: e.dma_start(
            out=ygb.rearrange("p (a m) -> p a m", a=16),
            in_=ys[:, t0:t0 + 512].rearrange("(a p) m -> p a m", p=128)), dmas=1)
        p.step("sync", lambda e, t0=t0: e.dma_start(
            out=gtb.rearrange("p (a m) -> p a m", a=16),
            in_=gts[:, t0:t0 + 512].rearrange("(a p) m -> p a m", p=128)), dmas=1)
        p.step("vector", lambda e: e.tensor_tensor(out=ygb, in0=ygb, in1=gtb, op=OP.mult))
        for tt in range(4):
            tok = t0 + tt * 128
            p.step("sync", lambda e, tok=tok: e.dma_start(out=xt[:], in_=x[tok:tok + 128, :]),
                   dmas=1)
            p.step("sync", lambda e, tok=tok: e.dma_start(out=xnof[:],
                                                          in_=xno[tok:tok + 128, :]), dmas=1)
            p.step("vector", lambda e: e.tensor_tensor(out=xt[:], in0=xt[:], in1=xnof[:],
                                                       op=OP.add))
            for ch in range(2):
                def fo(e, tt=tt, ch=ch):
                    for i in range(16):
                        r = e.matmul(pz[:], ygb[:, i * 512 + tt * 128: i * 512 + tt * 128 + 128],
                                     woutt[:, i * D + ch * 512: i * D + ch * 512 + 512],
                                     start=(i == 0), stop=(i == 15))
                    return r
                p.step("tensor", fo)
                p.step("vector", lambda e, ch=ch: e.tensor_tensor(
                    out=xt[:, ch * 512:(ch + 1) * 512], in0=xt[:, ch * 512:(ch + 1) * 512],
                    in1=pz[:], op=OP.add))
            p.step("sync", lambda e, tok=tok: e.dma_start(out=xout[tok:tok + 128, :],
                                                          in_=xt[:]), dmas=1)
    return p.build()



def build_ffn2():
    HF = 2048
    nc = bass.Bass("TRN2", target_bir_lowering=False)
    x = nc.dram_tensor("x", [TC, D], F32, kind="ExternalInput").ap()
    lng = nc.dram_tensor("lng", [1, D], F32, kind="ExternalInput").ap()
    lnb = nc.dram_tensor("lnb", [1, D], F32, kind="ExternalInput").ap()
    rw = nc.dram_tensor("rw", [D, 8], F32, kind="ExternalInput").ap()
    rb = nc.dram_tensor("rb", [1, 8], F32, kind="ExternalInput").ap()
    w1 = nc.dram_tensor("w1", [8, D, HF], F32, kind="ExternalInput").ap()
    b1r = nc.dram_tensor("b1r", [8 * 128, 16], F32, kind="ExternalInput").ap()
    w2 = nc.dram_tensor("w2", [8, HF, D], F32, kind="ExternalInput").ap()
    b2 = nc.dram_tensor("b2", [8, D], F32, kind="ExternalInput").ap()
    ident = nc.dram_tensor("ident", [128, 128], BF16, kind="ExternalInput").ap()
    identf = nc.dram_tensor("identf", [128, 128], F32, kind="ExternalInput").ap()
    onesrowf = nc.dram_tensor("onesrowf", [1, 128], F32, kind="ExternalInput").ap()
    xout = nc.dram_tensor("xout", [TC, D], F32, kind="ExternalOutput").ap()
    xno = nc.dram_tensor("xno", [TC, D], F32, kind="Internal").ap()

    p = Prog(nc)
    idt = p.sb("idt", [128, 128], BF16)
    idtf = p.sb("idtf", [128, 128], F32)
    onrw = p.sb("onrw", [1, 128], F32)
    lngrep = p.sb("lngrep", [128, D], F32)
    lnbrep = p.sb("lnbrep", [128, D], F32)
    rwt = p.sb("rwt", [128, 8 * 8], BF16)
    rbrep = p.sb("rbrep", [128, 8], F32)
    b2t = p.sb("b2t", [8, D], F32)
    w1t = p.sb("w1t", [128, 8 * HF], BF16)
    w2t = p.sb("w2t", [128, 16 * D], BF16)
    b1c = p.sb("b1c", [128, 16], F32)
    xt = p.sb("xt", [128, D], F32)
    xnof = p.sb("xnof", [128, D], F32)
    x2n = p.sb("x2n", [128, D], BF16)
    sq = p.sb("sq", [128, D], F32)
    stats = p.sb("stats", [128, 8], F32)
    zfm = p.sb("zfm", [128, 8 * 1024], BF16)
    acc = p.sb("acc", [128, 8 * 1024], F32)
    h1f = p.sb("h1f", [128, 16 * 512], BF16)
    wrepb = p.sb("wrepb", [128, 512], BF16)
    lgt = p.sb("lgt", [128, 8], F32)
    prt = p.sb("prt", [128, 8], F32)
    eqt = p.sb("eqt", [128, 8], F32)
    cum = p.sb("cum", [128, 8], F32)
    fst = p.sb("fst", [128, 8], F32)
    p2t = p.sb("p2t", [128, 8], F32)
    sect = p.sb("sect", [128, 8], F32)
    wgt8 = p.sb("wgt8", [128, 8], F32)
    on8 = p.sb("on8", [128, 8], F32)
    cols = p.sb("cols", [128, 8], F32)
    wpad = p.sb("wpad", [128, 128], F32)
    wTsb = p.sb("wTsb", [8, 1024], F32)
    wTflat = p.sb("wTflat", [1, 8 * 1024], F32)
    pz = p.ps("pz", [128, 512], F32)
    pz2 = p.ps("pz2", [128, 512], F32)
    pg8 = p.ps("pg8", [128, 8], F32)
    ptw = p.ps("ptw", [128, 128], F32)
    pt1 = p.ps("pt1", [128, 128], BF16)

    x2f = xt[:]
    rowb = xt[0:1, 0:D]
    p.step("sync", lambda e: [e.dma_start(out=idt[:], in_=ident[:, :]),
                              e.dma_start(out=idtf[:], in_=identf[:, :]),
                              e.dma_start(out=onrw[:], in_=onesrowf[:, :]),
                              e.dma_start(out=b2t[:], in_=b2[:, :])], dmas=4)
    p.step("gpsimd", lambda e: [e.dma_start(out=rwt[:, i * 8:(i + 1) * 8],
                                            in_=rw[i * 128:(i + 1) * 128, :])
                                for i in range(8)], dmas=8)
    p.step("vector", lambda e: e.memset(on8[:], 1.0))
    for (rep, row) in ((lngrep, lng), (lnbrep, lnb)):
        p.step("sync", lambda e, row=row: e.dma_start(out=rowb, in_=row[:, :]), dmas=1)
        for ch in range(2):
            p.step("tensor", lambda e, ch=ch: e.matmul(
                pz[:], onrw[:], rowb[0:1, ch * 512:(ch + 1) * 512], start=True, stop=True))
            p.step("vector", lambda e, rep=rep, ch=ch: e.tensor_copy(
                rep[:, ch * 512:(ch + 1) * 512], pz[:]))
    p.step("sync", lambda e: e.dma_start(out=wpad[0:1, 0:8], in_=rb[:, :]), dmas=1)
    p.step("tensor", lambda e: e.matmul(pg8[:], onrw[:], wpad[0:1, 0:8], start=True, stop=True))
    p.step("vector", lambda e: e.tensor_copy(rbrep[:], pg8[:]))
    p.step("vector", lambda e: e.memset(wpad[:], 0.0))

    for blk in range(4):
        t0 = blk * 1024
        for tt in range(8):
            tok = t0 + tt * 128
            p.step("sync", lambda e, tok=tok: e.dma_start(out=xt[:], in_=x[tok:tok + 128, :]),
                   dmas=1)
            emit_ln(p, xt, xnof, stats, sq)
            p.step("sync", lambda e, tok=tok: e.dma_start(out=xno[tok:tok + 128, :],
                                                          in_=xnof[:]), dmas=1)
            emit_ln(p, xnof, x2f, stats, sq)
            p.step("vector", lambda e: e.tensor_tensor(out=x2f[:], in0=x2f[:], in1=lngrep[:],
                                                       op=OP.mult))
            p.step("vector", lambda e: e.tensor_tensor(out=x2n[:], in0=x2f[:], in1=lnbrep[:],
                                                       op=OP.add))
            emit_tp(p, x2n, (lambda i, tt=tt: zfm[:, i * 1024 + tt * 128:
                                                  i * 1024 + tt * 128 + 128]), pt1, idt)
            def frt(e, tt=tt):
                for i in range(8):
                    r = e.matmul(pg8[:], zfm[:, i * 1024 + tt * 128: i * 1024 + tt * 128 + 128],
                                 rwt[:, i * 8:(i + 1) * 8], start=(i == 0), stop=(i == 7))
                return r
            p.step("tensor", frt)
            p.step("vector", lambda e: e.tensor_tensor(out=lgt[:], in0=pg8[:], in1=rbrep[:],
                                                       op=OP.add))
            p.step("vector", lambda e: e.reduce_max(stats[:, 0:1], lgt[:], axis=AX, negate=True))
            p.step("scalar", lambda e: e.activation(out=prt[:], in_=lgt[:], func=AF.Exp,
                                                    bias=stats[:, 0:1],
                                                    accum_out=stats[:, 1:2]))
            p.step("vector", lambda e: e.reciprocal(stats[:, 2:3], stats[:, 1:2]))
            p.step("vector", lambda e: e.tensor_scalar(out=prt[:], in0=prt[:],
                                                       scalar1=stats[:, 2:3], scalar2=None,
                                                       op0=OP.mult))
            p.step("vector", lambda e: e.reduce_max(stats[:, 3:4], prt[:], axis=AX))
            p.step("vector", lambda e: e.tensor_scalar(out=eqt[:], in0=prt[:],
                                                       scalar1=stats[:, 3:4], scalar2=None,
                                                       op0=OP.is_ge))
            p.step("vector", lambda e: e.tensor_tensor_scan(out=cum[:], data0=on8[:],
                                                            data1=eqt[:], initial=0.0,
                                                            op0=OP.mult, op1=OP.add))
            p.step("vector", lambda e: e.tensor_scalar(out=cum[:], in0=cum[:], scalar1=1.0,
                                                       scalar2=None, op0=OP.is_le))
            p.step("vector", lambda e: e.tensor_tensor(out=fst[:], in0=eqt[:], in1=cum[:],
                                                       op=OP.mult))
            p.step("vector", lambda e: e.tensor_scalar(out=cols[:], in0=fst[:], scalar1=-1.0,
                                                       scalar2=1.0, op0=OP.mult, op1=OP.add))
            p.step("vector", lambda e: e.tensor_tensor(out=p2t[:], in0=prt[:], in1=cols[:],
                                                       op=OP.mult))
            p.step("vector", lambda e: e.reduce_max(stats[:, 4:5], p2t[:], axis=AX))
            p.step("vector", lambda e: e.tensor_scalar(out=eqt[:], in0=p2t[:],
                                                       scalar1=stats[:, 4:5], scalar2=None,
                                                       op0=OP.is_ge))
            p.step("vector", lambda e: e.tensor_tensor_scan(out=cum[:], data0=on8[:],
                                                            data1=eqt[:], initial=0.0,
                                                            op0=OP.mult, op1=OP.add))
            p.step("vector", lambda e: e.tensor_scalar(out=cum[:], in0=cum[:], scalar1=1.0,
                                                       scalar2=None, op0=OP.is_le))
            p.step("vector", lambda e: e.tensor_tensor(out=sect[:], in0=eqt[:], in1=cum[:],
                                                       op=OP.mult))
            p.step("vector", lambda e: e.tensor_tensor(out=stats[:, 5:6], in0=stats[:, 3:4],
                                                       in1=stats[:, 4:5], op=OP.add))
            p.step("vector", lambda e: e.tensor_scalar(out=stats[:, 5:6], in0=stats[:, 5:6],
                                                       scalar1=1e-8, scalar2=None, op0=OP.max))
            p.step("vector", lambda e: e.reciprocal(stats[:, 6:7], stats[:, 5:6]))
            p.step("vector", lambda e: e.tensor_scalar(out=wgt8[:], in0=fst[:],
                                                       scalar1=stats[:, 3:4], scalar2=None,
                                                       op0=OP.mult))
            p.step("vector", lambda e: e.scalar_tensor_tensor(out=wgt8[:], in0=sect[:],
                                                              scalar=stats[:, 4:5],
                                                              in1=wgt8[:], op0=OP.mult,
                                                              op1=OP.add))
            p.step("vector", lambda e: e.tensor_scalar(out=wgt8[:], in0=wgt8[:],
                                                       scalar1=stats[:, 6:7], scalar2=None,
                                                       op0=OP.mult))
            p.step("vector", lambda e: e.tensor_copy(wpad[:, 0:8], wgt8[:]))
            p.step("tensor", lambda e: e.transpose(ptw[:], wpad[:], idtf[:]))
            p.step("vector", lambda e, tt=tt: e.tensor_copy(wTsb[0:8, tt * 128:(tt + 1) * 128],
                                                            ptw[0:8, 0:128]))
        p.step("gpsimd", lambda e: [e.dma_start(out=wTflat[0:1, ex2 * 1024:(ex2 + 1) * 1024],
                                                in_=wTsb[ex2:ex2 + 1, :])
                                    for ex2 in range(8)], dmas=8)
        p.step("vector", lambda e: e.memset(acc[:], 0.0))
        for ex in range(8):
            p.step("gpsimd", lambda e, ex=ex: [e.dma_start(out=w1t[:, i * HF:(i + 1) * HF],
                                                           in_=w1[ex, i * 128:(i + 1) * 128, :])
                                               for i in range(8)], dmas=8)
            p.step("gpsimd", lambda e, ex=ex: [e.dma_start(out=w2t[:, i * D:(i + 1) * D],
                                                           in_=w2[ex, i * 128:(i + 1) * 128, :])
                                               for i in range(16)], dmas=16)
            p.step("sync", lambda e, ex=ex: e.dma_start(out=b1c[:], in_=b1r[ex * 128:(ex + 1) * 128, :]),
                   dmas=1)
            for sb2 in range(2):
                c0 = sb2 * 512
                for mt in range(16):
                    def fh1(e, mt=mt, c0=c0):
                        for i in range(8):
                            r = e.matmul(pz[:], w1t[:, i * HF + mt * 128: i * HF + mt * 128 + 128],
                                         zfm[:, i * 1024 + c0: i * 1024 + c0 + 512],
                                         start=(i == 0), stop=(i == 7))
                        return r
                    p.step("tensor", fh1)
                    p.step("scalar", lambda e, mt=mt: e.activation(
                        out=h1f[:, mt * 512:(mt + 1) * 512], in_=pz[:], func=AF.Gelu,
                        bias=b1c[:, mt:mt + 1]))
                p.step("tensor", lambda e, ex=ex, c0=c0: e.matmul(
                    pz2[:], onrw[:], wTflat[0:1, ex * 1024 + c0: ex * 1024 + c0 + 512],
                    start=True, stop=True))
                p.step("vector", lambda e: e.tensor_copy(wrepb[:], pz2[:]))
                for mt in range(16):
                    p.step("vector", lambda e, mt=mt: e.tensor_tensor(
                        out=h1f[:, mt * 512:(mt + 1) * 512],
                        in0=h1f[:, mt * 512:(mt + 1) * 512], in1=wrepb[:], op=OP.mult))
                for ot in range(8):
                    for kb in range(2):
                        def fg2(e, ot=ot, kb=kb):
                            for i in range(8):
                                k = kb * 8 + i
                                r = e.matmul(pz2[:], w2t[:, k * D + ot * 128: k * D + ot * 128 + 128],
                                             h1f[:, k * 512:(k + 1) * 512],
                                             start=(k == 0), stop=(k == 15))
                            return r
                        p.step("tensor", fg2)
                    p.step("vector", lambda e, ot=ot, c0=c0: e.tensor_tensor(
                        out=acc[:, ot * 1024 + c0: ot * 1024 + c0 + 512],
                        in0=acc[:, ot * 1024 + c0: ot * 1024 + c0 + 512], in1=pz2[:],
                        op=OP.add))
        for ot in range(8):
            for sb2 in range(2):
                c0 = sb2 * 512
                p.step("tensor", lambda e, ot=ot, c0=c0: e.matmul(
                    pz2[:], b2t[:, ot * 128:(ot + 1) * 128], wTsb[0:8, c0:c0 + 512],
                    start=True, stop=True))
                p.step("vector", lambda e, ot=ot, c0=c0: e.tensor_tensor(
                    out=acc[:, ot * 1024 + c0: ot * 1024 + c0 + 512],
                    in0=acc[:, ot * 1024 + c0: ot * 1024 + c0 + 512], in1=pz2[:],
                    op=OP.add))
        for tt in range(8):
            tok = t0 + tt * 128
            p.step("sync", lambda e, tok=tok: [e.dma_start(out=xt[:], in_=x[tok:tok + 128, :]),
                                               e.dma_start(out=xnof[:], in_=xno[tok:tok + 128, :])],
                   dmas=2)
            p.step("vector", lambda e: e.tensor_tensor(out=xnof[:], in0=xnof[:], in1=xt[:],
                                                       op=OP.add))
            for j in range(8):
                p.step("tensor", lambda e, j=j, tt=tt: e.transpose(
                    ptw[:], acc[:, j * 1024 + tt * 128: j * 1024 + tt * 128 + 128], idtf[:]))
                p.step("vector", lambda e, j=j: e.tensor_tensor(
                    out=xnof[:, j * 128:(j + 1) * 128], in0=xnof[:, j * 128:(j + 1) * 128],
                    in1=ptw[:], op=OP.add))
            p.step("sync", lambda e, tok=tok: e.dma_start(out=xout[tok:tok + 128, :],
                                                          in_=xnof[:]), dmas=1)
    return p.build()



def m2_consts():
    mask01 = np.zeros((128, 128), np.float32)
    for i in range(128):
        mask01[i, :i] = 1.0
    return {"mask01": mask01}


def build_m2tails():
    DI2 = 2048
    nc = bass.Bass("TRN2", target_bir_lowering=False)
    x = nc.dram_tensor("x", [TC, D], F32, kind="ExternalInput").ap()
    inpT = nc.dram_tensor("inpT", [D, 4384], F32, kind="ExternalInput").ap()
    ident = nc.dram_tensor("ident", [128, 128], BF16, kind="ExternalInput").ap()
    tails = nc.dram_tensor("tails", [DI2, 3], F32, kind="ExternalOutput").ap()
    p = Prog(nc)
    idt = p.sb("idt", [128, 128], BF16)
    wt = p.sb("wt", [128, 8 * DI2], BF16)
    xt = p.sb("xt", [128, D], F32)
    xn = p.sb("xn", [128, D], BF16)
    sq = p.sb("sq", [128, D], F32)
    stats = p.sb("stats", [128, 8], F32)
    xnfm = p.sb("xnfm", [128, 8 * 128], BF16)
    ev = p.sb("ev", [128, 3], F32)
    pm = p.ps("pm", [128, 128], F32)
    pt1 = p.ps("pt1", [128, 128], BF16)
    p.step("sync", lambda e: e.dma_start(out=idt[:], in_=ident[:, :]), dmas=1)
    p.step("gpsimd", lambda e: [e.dma_start(out=wt[:, i * DI2:(i + 1) * DI2],
                                            in_=inpT[i * 128:(i + 1) * 128, 0:DI2])
                                for i in range(8)], dmas=8)
    p.step("sync", lambda e: e.dma_start(out=xt[:], in_=x[TC - 128:TC, :]), dmas=1)
    emit_ln(p, xt, xn, stats, sq)
    emit_tp(p, xn, (lambda j: xnfm[:, j * 128:(j + 1) * 128]), pt1, idt)
    for mt in range(16):
        def fp(e, mt=mt):
            for i in range(8):
                r = e.matmul(pm[:], wt[:, i * DI2 + mt * 128: i * DI2 + mt * 128 + 128],
                             xnfm[:, i * 128:(i + 1) * 128], start=(i == 0), stop=(i == 7))
            return r
        p.step("tensor", fp)
        p.step("vector", lambda e: e.tensor_copy(ev[:], pm[:, 125:128]))
        p.step("sync", lambda e, mt=mt: e.dma_start(out=tails[mt * 128:(mt + 1) * 128, :],
                                                    in_=ev[:]), dmas=1)
    return p.build()


def build_m2(full):
    DI2, NH, HD, DS = 2048, 32, 64, 128
    NCH = TC // 128
    nc = bass.Bass("TRN2", target_bir_lowering=False)
    x = nc.dram_tensor("x", [TC, D], F32, kind="ExternalInput").ap()
    inpT = nc.dram_tensor("inpT", [D, 4384], F32, kind="ExternalInput").ap()
    cw = nc.dram_tensor("cw", [128, 16 * 4], F32, kind="ExternalInput").ap()
    cb = nc.dram_tensor("cb", [128, 16], F32, kind="ExternalInput").ap()
    alog = nc.dram_tensor("alog", [1, NH], F32, kind="ExternalInput").ap()
    dpr = nc.dram_tensor("dpr", [1, NH], F32, kind="ExternalInput").ap()
    dtbias = nc.dram_tensor("dtbias", [NH, 1], F32, kind="ExternalInput").ap()
    tails = nc.dram_tensor("tails", [DI2, 3], F32, kind="ExternalInput").ap()
    mask01 = nc.dram_tensor("mask01", [128, 128], F32, kind="ExternalInput").ap()
    triu = nc.dram_tensor("triu", [128, 128], BF16, kind="ExternalInput").ap()
    ident = nc.dram_tensor("ident", [128, 128], BF16, kind="ExternalInput").ap()
    identf = nc.dram_tensor("identf", [128, 128], F32, kind="ExternalInput").ap()
    onesrowf = nc.dram_tensor("onesrowf", [1, 128], F32, kind="ExternalInput").ap()
    xss = nc.dram_tensor("xss", [DI2, TC], BF16, kind="Internal").ap()
    xsls = nc.dram_tensor("xsls", [DI2, TC], BF16, kind="Internal").ap()
    xslt = nc.dram_tensor("xslt", [TC, DI2], BF16, kind="Internal").ap()
    xdts = nc.dram_tensor("xdts", [TC, DI2], BF16, kind="Internal").ap()
    zss = nc.dram_tensor("zss", [TC, DI2], BF16, kind="Internal").ap()
    if full:
        nw2 = nc.dram_tensor("nw2", [1, DI2], F32, kind="ExternalInput").ap()
        outT = nc.dram_tensor("outT", [DI2, D], F32, kind="ExternalInput").ap()
        hstall = nc.dram_tensor("hstall", [4, 128, NH * HD], F32, kind="ExternalInput").ap()
        ftall = nc.dram_tensor("ftall", [4, 128, NH], F32, kind="ExternalInput").ap()
        incr = nc.dram_tensor("incr", [1, 4], F32, kind="ExternalInput").ap()
        xout = nc.dram_tensor("xout", [TC, D], F32, kind="ExternalOutput").ap()
    else:
        d_hst = nc.dram_tensor("hst", [128, NH * HD], F32, kind="ExternalOutput").ap()
        d_ft = nc.dram_tensor("ft", [128, NH], F32, kind="ExternalOutput").ap()

    p = Prog(nc)
    idt = p.sb("idt", [128, 128], BF16)
    idtf = p.sb("idtf", [128, 128], F32)
    trut = p.sb("trut", [128, 128], BF16)
    onrw = p.sb("onrw", [1, 128], F32)
    onescolb = p.sb("onescolb", [128, 1], BF16)
    msk = p.sb("msk", [128, 128], F32)
    inptt = p.sb("inptt", [128, 8 * 4384], BF16)
    cwt = p.sb("cwt", [128, 16 * 4], F32)
    cbt = p.sb("cbt", [128, 16], F32)
    arow = p.sb("arow", [1, NH], F32)
    dprow = p.sb("dprow", [1, NH], F32)
    dtbc = p.sb("dtbc", [NH, 1], F32)
    arep = p.sb("arep", [128, NH], F32)
    dp8 = p.sb("dp8", [128, NH], F32)
    dp8b = p.sb("dp8b", [128, NH], BF16)
    xt = p.sb("xt", [128, D], F32)
    xn = p.sb("xn", [128, D], BF16)
    sq = p.sb("sq", [128, D], F32)
    stats = p.sb("stats", [128, 8], F32)
    xnfm = p.sb("xnfm", [128, 8 * 512], BF16)
    bfm = p.sb("bfm", [128, TC], BF16)
    cfm = p.sb("cfm", [128, TC], BF16)
    dti = p.sb("dti", [NH, TC], F32)
    dttm = p.sb("dttm", [128, NCH * NH], F32)
    dttmb = p.sb("dttmb", [128, NCH * NH], BF16)
    xe = p.sb("xe", [128, 4 + TC], BF16)
    ipfv = inptt[:].bitcast(F32)
    cacc = ipfv[:, 10240:10240 + TC]
    hst = p.sb("hstl", [128, NH * HD], F32)
    cdc = p.sb("cdc", [128, NH], F32)
    acd = p.sb("acd", [128, NH], F32)
    epl = p.sb("epl", [128, NH], F32)
    srep = p.sb("srep", [128, NH], F32)
    wdtm = p.sb("wdtm", [128, NH], F32)
    fmat = p.sb("fmat", [128, NH], F32)
    cpad = p.sb("cpad", [128, 128], F32)
    cdT = p.sb("cdT", [32, 128], F32)
    cdTb = p.sb("cdTb", [32, 128], BF16)
    ncdTb = p.sb("ncdTb", [32, 128], BF16)
    lhsT2 = p.sb("lhsT2", [2, 128], BF16)
    rhs2 = p.sb("rhs2", [2, 128], BF16)
    asb = p.sb("asb", [128, 128], F32)
    ssb = p.sb("ssb", [128, 128], BF16)
    btmf = p.sb("btmf", [128, 128], F32)
    bwb = p.sb("bwb", [128, 128], BF16)
    xdtc = p.sb("xdtc", [128, DI2], BF16)
    xsltc = p.sb("xsltc", [128, DI2], BF16)
    zsc = p.sb("zsc", [128, DI2], BF16)
    ycar = p.sb("ycar", [128, 512], F32)
    dterm = p.sb("dterm", [128, 512], F32)
    ych = p.sb("ych", [128, DI2], F32)
    ygb = p.sb("ygb", [128, DI2], BF16)
    ygfm = p.sb("ygfm", [128, 16 * 128], BF16)
    row2 = p.sb("row2", [1, DI2], F32)
    nwrep2 = ipfv[:, 8192:8192 + DI2]
    hstb = inptt[:, 20480:20480 + DI2]
    pm = p.ps("pm", [128, 512], F32)
    pm2 = p.ps("pm2", [128, 512], F32)
    py = p.ps("py", [128, 512], F32)
    pc = p.ps("pc", [128, 512], F32)
    pcon = p.ps("pcon", [128, 512], F32)
    pt1 = p.ps("pt1", [128, 128], BF16)
    ptf = p.ps("ptf", [128, 128], F32)

    p.step("sync", lambda e: [e.dma_start(out=idt[:], in_=ident[:, :]),
                              e.dma_start(out=idtf[:], in_=identf[:, :]),
                              e.dma_start(out=trut[:], in_=triu[:, :]),
                              e.dma_start(out=onrw[:], in_=onesrowf[:, :]),
                              e.dma_start(out=msk[:], in_=mask01[:, :]),
                              e.dma_start(out=cwt[:], in_=cw[:, :]),
                              e.dma_start(out=cbt[:], in_=cb[:, :]),
                              e.dma_start(out=arow[:], in_=alog[:, :]),
                              e.dma_start(out=dprow[:], in_=dpr[:, :]),
                              e.dma_start(out=dtbc[:], in_=dtbias[:, :])], dmas=10)
    p.step("gpsimd", lambda e: [e.dma_start(out=inptt[:, i * 4384:(i + 1) * 4384],
                                            in_=inpT[i * 128:(i + 1) * 128, :])
                                for i in range(8)], dmas=8)
    p.step("vector", lambda e: e.memset(onescolb[:], 1.0))
    p.step("vector", lambda e: e.memset(cpad[:], 0.0))
    p.step("scalar", lambda e: e.activation(out=arow[:], in_=arow[:], func=AF.Exp))
    p.step("vector", lambda e: e.tensor_scalar(out=arow[:], in0=arow[:], scalar1=-1.0,
                                               scalar2=None, op0=OP.mult))
    for (rep, row) in ((arep, arow), (dp8, dprow)):
        p.step("tensor", lambda e, row=row: e.matmul(pm[:, 0:NH], onrw[:], row[0:1, :],
                                                     start=True, stop=True))
        p.step("vector", lambda e, rep=rep: e.tensor_copy(rep[:], pm[:, 0:NH]))
    p.step("vector", lambda e: e.tensor_copy(dp8b[:], dp8[:]))
    for blk in range(TC // 512):
        t0 = blk * 512
        for tt in range(4):
            tok = t0 + tt * 128
            p.step("sync", lambda e, tok=tok: e.dma_start(out=xt[:], in_=x[tok:tok + 128, :]),
                   dmas=1)
            emit_ln(p, xt, xn, stats, sq)
            emit_tp(p, xn, (lambda i, tt=tt: xnfm[:, i * 512 + tt * 128:
                                                  i * 512 + tt * 128 + 128]), pt1, idt)
        for mt in range(16):
            def fxs(e, mt=mt):
                for i in range(8):
                    r = e.matmul(pm[:], inptt[:, i * 4384 + mt * 128: i * 4384 + mt * 128 + 128],
                                 xnfm[:, i * 512:(i + 1) * 512], start=(i == 0), stop=(i == 7))
                return r
            p.step("tensor", fxs)
            p.step("vector", lambda e: e.tensor_copy(xe[:, 0:512], pm[:]))
            p.step("sync", lambda e, mt=mt, t0=t0: e.dma_start(
                out=xss[mt * 128:(mt + 1) * 128, t0:t0 + 512], in_=xe[:, 0:512]), dmas=1)
        for tt in range(4):
            for ch in range(4):
                def fz(e, tt=tt, ch=ch):
                    for i in range(8):
                        r = e.matmul(pm2[:], xnfm[:, i * 512 + tt * 128: i * 512 + tt * 128 + 128],
                                     inptt[:, i * 4384 + DI2 + ch * 512:
                                           i * 4384 + DI2 + ch * 512 + 512],
                                     start=(i == 0), stop=(i == 7))
                    return r
                p.step("tensor", fz)
                p.step("scalar", lambda e, ch=ch: e.activation(out=ygb[:, ch * 512:(ch + 1) * 512],
                                                               in_=pm2[:], func=AF.Silu))
            p.step("sync", lambda e, tt=tt, t0=t0: e.dma_start(
                out=zss[t0 + tt * 128: t0 + tt * 128 + 128, :], in_=ygb[:]), dmas=1)
        for (dst, woff) in ((bfm, 2 * DI2), (cfm, 2 * DI2 + DS)):
            def fbc(e, woff=woff):
                for i in range(8):
                    r = e.matmul(pm[:], inptt[:, i * 4384 + woff: i * 4384 + woff + 128],
                                 xnfm[:, i * 512:(i + 1) * 512], start=(i == 0), stop=(i == 7))
                return r
            p.step("tensor", fbc)
            p.step("vector", lambda e, dst=dst, t0=t0: e.tensor_copy(
                dst[:, t0:t0 + 512], pm[:]))
        def fdt(e):
            for i in range(8):
                r = e.matmul(pm[0:NH, :], inptt[:, i * 4384 + 2 * DI2 + 2 * DS:
                                                i * 4384 + 2 * DI2 + 2 * DS + NH],
                             xnfm[:, i * 512:(i + 1) * 512], start=(i == 0), stop=(i == 7))
            return r
        p.step("tensor", fdt)
        p.step("vector", lambda e, t0=t0: e.tensor_copy(dti[:, t0:t0 + 512], pm[0:NH, :]))
    p.step("vector", lambda e: e.tensor_scalar(out=dti[:], in0=dti[:], scalar1=dtbc[:, 0:1],
                                               scalar2=None, op0=OP.add))
    p.step("scalar", lambda e: e.activation(out=cacc[0:NH, 0:TC], in_=dti[:], func=AF.Abs))
    p.step("scalar", lambda e: e.activation(out=cacc[0:NH, 0:TC], in_=cacc[0:NH, 0:TC],
                                            func=AF.Exp, scale=-1.0))
    p.step("scalar", lambda e: e.activation(out=cacc[0:NH, 0:TC], in_=cacc[0:NH, 0:TC],
                                            func=AF.Ln, bias=1.0))
    p.step("vector", lambda e: e.tensor_relu(dti[:], dti[:]))
    p.step("vector", lambda e: e.tensor_tensor(out=dti[:], in0=dti[:],
                                               in1=cacc[0:NH, 0:TC], op=OP.add))
    for c in range(NCH):
        p.step("vector", lambda e, c=c: e.tensor_copy(cpad[0:NH, 0:128],
                                                      dti[:, c * 128:(c + 1) * 128]))
        p.step("tensor", lambda e: e.transpose(ptf[:], cpad[:], idtf[:]))
        p.step("vector", lambda e, c=c: e.tensor_copy(dttm[:, c * NH:(c + 1) * NH],
                                                      ptf[:, 0:NH]))
    p.step("vector", lambda e: e.tensor_copy(dttmb[:], dttm[:]))
    for ct in range(16):
        p.step("sync", lambda e, ct=ct: e.dma_start(out=xe[:, 4:4 + TC],
                                                    in_=xss[ct * 128:(ct + 1) * 128, :]), dmas=1)
        p.step("gpsimd", lambda e, ct=ct: e.dma_start(out=xe[:, 1:4],
                                                      in_=tails[ct * 128:(ct + 1) * 128, :]),
               dmas=1)
        p.step("vector", lambda e, ct=ct: e.tensor_scalar(
            out=cacc[:], in0=xe[:, 1:1 + TC], scalar1=cwt[:, ct * 4:ct * 4 + 1],
            scalar2=None, op0=OP.mult))
        for k in range(1, 4):
            p.step("vector", lambda e, ct=ct, k=k: e.scalar_tensor_tensor(
                out=cacc[:], in0=xe[:, 1 + k:1 + k + TC], scalar=cwt[:, ct * 4 + k:ct * 4 + k + 1],
                in1=cacc[:], op0=OP.mult, op1=OP.add))
        p.step("vector", lambda e, ct=ct: e.tensor_scalar(
            out=cacc[:], in0=cacc[:], scalar1=cbt[:, ct:ct + 1], scalar2=None, op0=OP.add))
        p.step("scalar", lambda e: e.activation(out=xe[:, 4:4 + TC], in_=cacc[:], func=AF.Silu))
        p.step("sync", lambda e, ct=ct: e.dma_start(out=xsls[ct * 128:(ct + 1) * 128, :],
                                                    in_=xe[:, 4:4 + TC]), dmas=1)
    for cc in range(NCH):
        p.step("sync", lambda e, cc=cc: e.dma_start(
            out=xe[:, 0:DI2].rearrange("p (a m) -> p a m", a=16),
            in_=xsls[:, cc * 128:(cc + 1) * 128].rearrange("(a p) m -> p a m", p=128)), dmas=1)
        for j in range(16):
            p.step("tensor", lambda e, j=j: e.transpose(pt1[:], xe[:, j * 128:(j + 1) * 128],
                                                        idt[:]))
            p.step("vector", lambda e, j=j: e.tensor_copy(xsltc[:, j * 128:(j + 1) * 128],
                                                          pt1[:]))
        p.step("sync", lambda e, cc=cc: e.dma_start(out=xslt[cc * 128:(cc + 1) * 128, :],
                                                    in_=xsltc[:]), dmas=1)
        p.step("vector", lambda e, cc=cc: e.tensor_tensor(
            out=xdtc[:].rearrange("p (h d2) -> p h d2", h=NH),
            in0=xsltc[:].rearrange("p (h d2) -> p h d2", h=NH),
            in1=dttmb[:, cc * NH:(cc + 1) * NH].rearrange("p (h o) -> p h o", o=1)
                .to_broadcast([128, NH, HD]), op=OP.mult))
        p.step("sync", lambda e, cc=cc: e.dma_start(out=xdts[cc * 128:(cc + 1) * 128, :],
                                                    in_=xdtc[:]), dmas=1)
    if full:
        ipf = inptt[:].bitcast(F32)
        hfa = ipf[:, 0:8192]
        ffa = ipf[:, 8192:8320]
        inc4 = ipf[:, 8320:8324]
        tg1 = ipf[:, 8324:10372]
        p.step("sync", lambda e: [e.dma_start(out=hfa.rearrange("p (a m) -> p a m", a=4),
                                              in_=hstall[:, :, :].rearrange("a p m -> p a m")),
                                  e.dma_start(out=ffa.rearrange("p (a m) -> p a m", a=4),
                                              in_=ftall[:, :, :].rearrange("a p m -> p a m")),
                                  e.dma_start(out=inc4[0:1, 0:4], in_=incr[:, :])], dmas=3)
        p.step("tensor", lambda e: e.matmul(pm[:, 0:4], onrw[:], inc4[0:1, 0:4],
                                            start=True, stop=True))
        p.step("vector", lambda e: e.tensor_copy(inc4, pm[:, 0:4]))
        p.step("vector", lambda e: e.memset(hst[:], 0.0))
        for cc in range(4):
            p.step("vector", lambda e, cc=cc: e.tensor_scalar(
                out=fmat[:], in0=ffa[:, cc * NH:(cc + 1) * NH], scalar1=inc4[:, cc:cc + 1],
                scalar2=None, op0=OP.mult))
            p.step("vector", lambda e, cc=cc: e.tensor_scalar(
                out=cdc[:], in0=inc4[:, cc:cc + 1].to_broadcast([128, NH]), scalar1=-1.0,
                scalar2=1.0, op0=OP.mult, op1=OP.add))
            p.step("vector", lambda e: e.tensor_tensor(out=fmat[:], in0=fmat[:], in1=cdc[:],
                                                       op=OP.add))
            p.step("vector", lambda e: e.tensor_tensor(
                out=hst[:].rearrange("p (h d2) -> p h d2", h=NH),
                in0=hst[:].rearrange("p (h d2) -> p h d2", h=NH),
                in1=fmat[:].rearrange("p (h o) -> p h o", o=1).to_broadcast([128, NH, HD]),
                op=OP.mult))
            p.step("vector", lambda e, cc=cc: e.tensor_scalar(
                out=tg1, in0=hfa[:, cc * NH * HD:(cc + 1) * NH * HD],
                scalar1=inc4[:, cc:cc + 1], scalar2=None, op0=OP.mult))
            p.step("vector", lambda e: e.tensor_tensor(out=hst[:], in0=hst[:], in1=tg1,
                                                       op=OP.add))
    else:
        p.step("vector", lambda e: e.memset(hst[:], 0.0))
    if full:
        woutt2 = inptt[:, 0:16 * D]
        p.step("gpsimd", lambda e: [e.dma_start(out=woutt2[:, i * D:(i + 1) * D],
                                                in_=outT[i * 128:(i + 1) * 128, :])
                                    for i in range(16)], dmas=16)
        p.step("vector", lambda e: e.memset(lhsT2[:], 1.0))
        p.step("vector", lambda e: e.memset(rhs2[:], 1.0))
        p.step("sync", lambda e: e.dma_start(out=row2[:], in_=nw2[:, :]), dmas=1)
        for ch in range(4):
            p.step("tensor", lambda e, ch=ch: e.matmul(
                pm[:], onrw[:], row2[0:1, ch * 512:(ch + 1) * 512], start=True, stop=True))
            p.step("vector", lambda e, ch=ch: e.tensor_copy(nwrep2[:, ch * 512:(ch + 1) * 512],
                                                            pm[:]))
    for c in range(NCH):
        t0 = c * 128
        dtcb = dttmb[:, c * NH:(c + 1) * NH]
        p.step("tensor", lambda e, dtcb=dtcb: e.matmul(pm[:, 0:NH], trut[:], dtcb,
                                                       start=True, stop=True))
        p.step("vector", lambda e: e.tensor_copy(cdc[:], pm[:, 0:NH]))
        p.step("vector", lambda e: e.tensor_tensor(out=acd[:], in0=cdc[:], in1=arep[:],
                                                   op=OP.mult))
        if full:
            p.step("scalar", lambda e: e.activation(out=epl[:], in_=acd[:], func=AF.Exp))
        p.step("tensor", lambda e, dtcb=dtcb: e.matmul(pm2[0:1, 0:NH], onescolb[:], dtcb,
                                                       start=True, stop=True))
        p.step("vector", lambda e: e.tensor_copy(cpad[0:1, 0:NH], pm2[0:1, 0:NH]))
        p.step("tensor", lambda e: e.matmul(pm[:, 0:NH], onrw[:], cpad[0:1, 0:NH],
                                            start=True, stop=True))
        p.step("vector", lambda e: e.tensor_copy(srep[:], pm[:, 0:NH]))
        p.step("vector", lambda e: e.tensor_tensor(out=wdtm[:], in0=srep[:], in1=cdc[:],
                                                   op=OP.subtract))
        p.step("vector", lambda e: e.tensor_tensor(out=wdtm[:], in0=wdtm[:], in1=arep[:],
                                                   op=OP.mult))
        p.step("scalar", lambda e: e.activation(out=wdtm[:], in_=wdtm[:], func=AF.Exp))
        p.step("vector", lambda e, c=c: e.tensor_tensor(out=wdtm[:], in0=wdtm[:],
                                                        in1=dttm[:, c * NH:(c + 1) * NH],
                                                        op=OP.mult))
        p.step("vector", lambda e: e.tensor_tensor(out=fmat[:], in0=srep[:], in1=arep[:],
                                                   op=OP.mult))
        p.step("scalar", lambda e: e.activation(out=fmat[:], in_=fmat[:], func=AF.Exp))
        p.step("sync", lambda e, t0=t0: e.dma_start(out=xdtc[:], in_=xdts[t0:t0 + 128, :]),
               dmas=1)
        if full:
            p.step("vector", lambda e: e.tensor_copy(hstb, hst[:]))
            p.step("vector", lambda e: e.tensor_copy(cpad[:, 0:NH], cdc[:]))
            p.step("tensor", lambda e: e.transpose(ptf[:], cpad[:], idtf[:]))
            p.step("vector", lambda e: e.tensor_copy(cdT[:], ptf[0:32, 0:128]))
            p.step("vector", lambda e: e.tensor_copy(cdTb[:], cdT[:]))
            p.step("vector", lambda e: e.tensor_scalar(out=cdT[:], in0=cdT[:], scalar1=-1.0,
                                                       scalar2=None, op0=OP.mult))
            p.step("vector", lambda e: e.tensor_copy(ncdTb[:], cdT[:]))
            p.step("sync", lambda e, t0=t0: [
                e.dma_start(out=xsltc[:], in_=xslt[t0:t0 + 128, :]),
                e.dma_start(out=zsc[:], in_=zss[t0:t0 + 128, :])], dmas=2)
            p.step("tensor", lambda e, t0=t0: e.matmul(
                pm2[:, 0:128], bfm[:, t0:t0 + 128], cfm[:, t0:t0 + 128],
                start=True, stop=True))
        p.step("tensor", lambda e, t0=t0: e.transpose(pt1[:], bfm[:, t0:t0 + 128], idt[:]))
        p.step("vector", lambda e: e.tensor_copy(btmf[:], pt1[:]))
        for grp in range(4):
            for hh in range(8):
                h = grp * 8 + hh
                if full:
                    p.step("gpsimd", lambda e, h=h: [
                        e.dma_start(out=lhsT2[0:1, :], in_=ncdTb[h:h + 1, :]),
                        e.dma_start(out=rhs2[1:2, :], in_=cdTb[h:h + 1, :])], dmas=2)
                    p.step("tensor", lambda e: e.matmul(pm[:, 0:128], lhsT2[:], rhs2[:],
                                                        start=True, stop=True))
                    p.step("vector", lambda e: e.scalar_tensor_tensor(
                        out=asb[:], in0=msk[:], scalar=1e30, in1=pm[:, 0:128],
                        op0=OP.mult, op1=OP.add))
                    p.step("scalar", lambda e, h=h: e.activation(
                        out=asb[:], in_=asb[:], func=AF.Exp, scale=arep[:, h:h + 1]))
                    p.step("vector", lambda e: e.tensor_tensor(out=ssb[:], in0=asb[:],
                                                               in1=pm2[:, 0:128], op=OP.mult))
                    p.step("tensor", lambda e, h=h, hh=hh: e.matmul(
                        py[:, hh * HD:(hh + 1) * HD], ssb[:], xdtc[:, h * HD:(h + 1) * HD],
                        start=True, stop=True))
                    p.step("tensor", lambda e, h=h, hh=hh, t0=t0: e.matmul(
                        pc[:, hh * HD:(hh + 1) * HD], cfm[:, t0:t0 + 128],
                        hstb[:, h * HD:(h + 1) * HD], start=True, stop=True))
                p.step("vector", lambda e, h=h: e.tensor_scalar(
                    out=bwb[:], in0=btmf[:], scalar1=wdtm[:, h:h + 1], scalar2=None,
                    op0=OP.mult))
                p.step("tensor", lambda e, h=h, hh=hh: e.matmul(
                    pcon[:, hh * HD:(hh + 1) * HD], bwb[:], xdtc[:, h * HD:(h + 1) * HD],
                    start=True, stop=True))
            g0 = grp * 8
            if full:
                p.step("vector", lambda e, g0=g0: e.tensor_tensor(
                    out=ycar[:].rearrange("p (h d2) -> p h d2", h=8),
                    in0=pc[:].rearrange("p (h d2) -> p h d2", h=8),
                    in1=epl[:, g0:g0 + 8].rearrange("p (h o) -> p h o", o=1)
                        .to_broadcast([128, 8, HD]), op=OP.mult))
                p.step("vector", lambda e, g0=g0: e.tensor_tensor(
                    out=dterm[:].rearrange("p (h d2) -> p h d2", h=8),
                    in0=xsltc[:, g0 * HD:(g0 + 8) * HD].rearrange("p (h d2) -> p h d2", h=8),
                    in1=dp8b[:, g0:g0 + 8].rearrange("p (h o) -> p h o", o=1)
                        .to_broadcast([128, 8, HD]), op=OP.mult))
                p.step("vector", lambda e, g0=g0: e.tensor_tensor(
                    out=ych[:, g0 * HD:(g0 + 8) * HD], in0=py[:], in1=ycar[:], op=OP.add))
                p.step("vector", lambda e, g0=g0: e.tensor_tensor(
                    out=ych[:, g0 * HD:(g0 + 8) * HD], in0=ych[:, g0 * HD:(g0 + 8) * HD],
                    in1=dterm[:], op=OP.add))
            p.step("vector", lambda e, g0=g0: e.tensor_tensor(
                out=hst[:, g0 * HD:(g0 + 8) * HD].rearrange("p (h d2) -> p h d2", h=8),
                in0=hst[:, g0 * HD:(g0 + 8) * HD].rearrange("p (h d2) -> p h d2", h=8),
                in1=fmat[:, g0:g0 + 8].rearrange("p (h o) -> p h o", o=1)
                    .to_broadcast([128, 8, HD]), op=OP.mult))
            p.step("vector", lambda e, g0=g0: e.tensor_tensor(
                out=hst[:, g0 * HD:(g0 + 8) * HD], in0=hst[:, g0 * HD:(g0 + 8) * HD],
                in1=pcon[:], op=OP.add))
        if full:
            p.step("scalar", lambda e: e.activation(out=sq[0:128, 0:1024], in_=ych[:, 0:1024],
                                                    func=AF.Square, accum_out=stats[:, 0:1]))
            p.step("scalar", lambda e: e.activation(out=sq[0:128, 0:1024], in_=ych[:, 1024:2048],
                                                    func=AF.Square, accum_out=stats[:, 1:2]))
            p.step("vector", lambda e: e.tensor_tensor(out=stats[:, 0:1], in0=stats[:, 0:1],
                                                       in1=stats[:, 1:2], op=OP.add))
            p.step("vector", lambda e: e.tensor_scalar(out=stats[:, 0:1], in0=stats[:, 0:1],
                                                       scalar1=1.0 / DI2, scalar2=1e-5,
                                                       op0=OP.mult, op1=OP.add))
            p.step("scalar", lambda e: e.activation(out=stats[:, 2:3], in_=stats[:, 0:1],
                                                    func=AF.Sqrt))
            p.step("vector", lambda e: e.reciprocal(stats[:, 3:4], stats[:, 2:3]))
            p.step("vector", lambda e: e.tensor_scalar(out=ych[:], in0=ych[:],
                                                       scalar1=stats[:, 3:4], scalar2=None,
                                                       op0=OP.mult))
            p.step("vector", lambda e: e.tensor_tensor(out=ych[:], in0=ych[:], in1=nwrep2[:],
                                                       op=OP.mult))
            p.step("vector", lambda e: e.tensor_copy(ygb[:], ych[:]))
            p.step("vector", lambda e: e.tensor_tensor(out=ygb[:], in0=ygb[:], in1=zsc[:],
                                                       op=OP.mult))
            for j in range(16):
                p.step("tensor", lambda e, j=j: e.transpose(pt1[:], ygb[:, j * 128:(j + 1) * 128],
                                                            idt[:]))
                p.step("vector", lambda e, j=j: e.tensor_copy(ygfm[:, j * 128:(j + 1) * 128],
                                                              pt1[:]))
            p.step("sync", lambda e, t0=t0: e.dma_start(out=xt[:], in_=x[t0:t0 + 128, :]),
                   dmas=1)
            for ch in range(2):
                def fo(e, ch=ch):
                    for i in range(16):
                        r = e.matmul(pm[:], ygfm[:, i * 128:(i + 1) * 128],
                                     woutt2[:, i * D + ch * 512: i * D + ch * 512 + 512],
                                     start=(i == 0), stop=(i == 15))
                    return r
                p.step("tensor", fo)
                p.step("vector", lambda e, ch=ch: e.tensor_tensor(
                    out=xt[:, ch * 512:(ch + 1) * 512], in0=xt[:, ch * 512:(ch + 1) * 512],
                    in1=pm[:], op=OP.add))
            p.step("sync", lambda e, t0=t0: e.dma_start(out=xout[t0:t0 + 128, :], in_=xt[:]),
                   dmas=1)
    if not full:
        p.step("tensor", lambda e: e.matmul(pm[0:1, 0:NH], onescolb[:], dttmb[:, 0:NH],
                                            start=True, stop=False))
        def fsum(e):
            for c in range(1, NCH):
                r = e.matmul(pm[0:1, 0:NH], onescolb[:], dttmb[:, c * NH:(c + 1) * NH],
                             start=False, stop=(c == NCH - 1))
            return r
        p.step("tensor", fsum)
        p.step("vector", lambda e: e.tensor_copy(cpad[0:1, 0:NH], pm[0:1, 0:NH]))
        p.step("tensor", lambda e: e.matmul(pm[:, 0:NH], onrw[:], cpad[0:1, 0:NH],
                                            start=True, stop=True))
        p.step("vector", lambda e: e.tensor_copy(srep[:], pm[:, 0:NH]))
        p.step("vector", lambda e: e.tensor_tensor(out=fmat[:], in0=srep[:], in1=arep[:],
                                                   op=OP.mult))
        p.step("scalar", lambda e: e.activation(out=fmat[:], in_=fmat[:], func=AF.Exp))
        p.step("sync", lambda e: [e.dma_start(out=d_hst[:, :], in_=hst[:]),
                                  e.dma_start(out=d_ft[:, :], in_=fmat[:])], dmas=2)
    return p.build()


def _shard(x_full):
    out = []
    for c in range(NCORE):
        b, q = c // CPB, c % CPB
        out.append(np.ascontiguousarray(x_full[b, q * TC:(q + 1) * TC]))
    return out


def kernel(**inp):
    g = lambda k: np.ascontiguousarray(np.asarray(inp[k], dtype=np.float32))
    hc = host_consts()
    s4c = s4_consts()
    m1c = m1_consts()
    m2c = m2_consts()
    onesrowf = np.ones((1, 128), np.float32)
    allones = np.ones((128, 128), np.float32)
    identf = np.eye(128, dtype=np.float32)
    onescolb = np.ones((128, 1), np.float32).astype(BF)
    x0 = g("L0_attn_a_x")
    shards = [np.ascontiguousarray(x0[c // CPB, (c % CPB) * TC:(c % CPB + 1) * TC])
              for c in range(NCORE)]
    cores = list(range(NCORE))

    nc = build_qkv()
    im = [{"x": shards[c], "wq": g("L0_attn_a_Wq"), "wk": g("L0_attn_a_Wk"),
           "wv": g("L0_attn_a_Wv"), "ident": hc["ident"]} for c in cores]
    r = run_bass_kernel_spmd(nc, im, core_ids=cores).results
    kf = {b: np.concatenate([r[b * CPB + q]["kf"] for q in range(CPB)], axis=1) for b in range(NB)}
    vt = {b: np.concatenate([r[b * CPB + q]["vt"] for q in range(CPB)], axis=0) for b in range(NB)}
    qf = [r[c]["qf"] for c in cores]

    qpos = []
    for c in cores:
        qq = np.zeros((128, TC // 128), np.float32)
        for i2 in range(TC // 128):
            qq[:, i2] = (c % CPB) * TC + i2 * 128 + np.arange(128)
        qpos.append(qq)
    kpos = np.arange(T, dtype=np.float32)[None, :]
    nc = build_flash()
    im = [{"x": shards[c], "qf": qf[c], "kff": kf[c // CPB], "vtf": vt[c // CPB],
           "gw": g("L0_ffn_gate_w"), "qpos": qpos[c], "kpos": kpos,
           "ident": hc["ident"]} for c in cores]
    r = run_bass_kernel_spmd(nc, im, core_ids=cores).results
    x1 = [r[c]["xout"] for c in cores]
    xn1 = [r[c]["xn1"] for c in cores]
    sc1 = {b: np.concatenate([r[b * CPB + q]["scores"] for q in range(CPB)], axis=0)
           for b in range(NB)}

    nc_moe_s4 = build_moedense(tail="s4pre")
    im = [{"x": x1[c], "xn": xn1[c], "scoresf": sc1[c // CPB],
           "scoresm": np.ascontiguousarray(sc1[c // CPB][(c % CPB) * TC:(c % CPB + 1) * TC]),
           "w1": g("L0_ffn_w1"), "w2": g("L0_ffn_w2"), "ident": hc["ident"],
           "allones": allones,
           "winT": np.ascontiguousarray(g("L0_attn_b_W_in").T),
           "b_in": g("L0_attn_b_b_in")[None, :], "gam": g("L0_attn_b_gamma")[None, :],
           "bet": g("L0_attn_b_beta")[None, :], "onesrowf": onesrowf} for c in cores]
    r = run_bass_kernel_spmd(nc_moe_s4, im, core_ids=cores).results
    x2 = [r[c]["xout"] for c in cores]
    u2 = [r[c]["u2"] for c in cores]
    csum = [r[c]["colsum"] for c in cores]

    nc = build_s4conv()
    im = []
    for c in cores:
        b, q = c // CPB, c % CPB
        tail = np.zeros((256, D), np.float32) if q == 0 else u2[c - 1][TC - 256:]
        pmkv = np.zeros((4, 1), np.float32)
        pmkv[:q, 0] = 1.0
        im.append({
            "x": x2[c], "u2e": np.concatenate([tail, u2[c]], axis=0),
            "colsums": np.concatenate([csum[b * CPB + qq] for qq in range(CPB)], axis=0),
            "pmk": pmkv,
            "are": g("L0_attn_b_A_re")[:, None], "aim": g("L0_attn_b_A_im")[:, None],
            "bre": g("L0_attn_b_B_re")[:, None], "bim": g("L0_attn_b_B_im")[:, None],
            "cre": g("L0_attn_b_C_re"), "cim": g("L0_attn_b_C_im"),
            "dre": g("L0_attn_b_D_re")[None, :], "dim": g("L0_attn_b_D_im")[None, :],
            "lmat": s4c["lmat"], "fxc": s4c["fxc"], "fxs": s4c["fxs"],
            "fkc": s4c["fkc"], "fks": s4c["fks"], "gre": s4c["gre"], "gim": s4c["gim"],
            "gw": g("L0_ffn_gate_w"), "ident": hc["ident"], "triu": hc["triu"],
            "onesrowf": onesrowf})
    r = run_bass_kernel_spmd(nc, im, core_ids=cores).results
    x3 = [r[c]["xout"] for c in cores]
    xn2 = [r[c]["xn1"] for c in cores]
    sc2 = {b: np.concatenate([r[b * CPB + q]["scores"] for q in range(CPB)], axis=0)
           for b in range(NB)}

    nc = build_moedense(tail="none")
    im = [{"x": x3[c], "xn": xn2[c], "scoresf": sc2[c // CPB],
           "scoresm": np.ascontiguousarray(sc2[c // CPB][(c % CPB) * TC:(c % CPB + 1) * TC]),
           "w1": g("L0_ffn_w1"), "w2": g("L0_ffn_w2"), "ident": hc["ident"],
           "allones": allones} for c in cores]
    r = run_bass_kernel_spmd(nc, im, core_ids=cores).results
    x4 = [r[c]["xout"] for c in cores]

    m1base = {
        "nw": g("L1_attn_a_norm_w")[None, :], "nb": g("L1_attn_a_norm_b")[None, :],
        "win": g("L1_attn_a_W_in"), "wgt": g("L1_attn_a_W_gate"), "wdt": g("L1_attn_a_W_dt"),
        "bdt": np.ascontiguousarray(g("L1_attn_a_b_dt").reshape(16, 128).T),
        "wB": g("L1_attn_a_W_B"), "wC": g("L1_attn_a_W_C"),
        "alog": np.ascontiguousarray(g("L1_attn_a_A_log").reshape(-1).reshape(256, 128).T),
        "selc": m1c["selt"], "sumselc": m1c["sumsel"], "sel16c": m1c["sel16"],
        "ident": hc["ident"], "onesrowf": onesrowf}
    nc = build_m1(False)
    im = [dict(m1base, x=x4[c]) for c in cores]
    r = run_bass_kernel_spmd(nc, im, core_ids=cores).results
    nc = build_m1(True)
    m1fbase = {k: m1base[k] for k in ("alog", "selc", "sumselc", "sel16c", "onesrowf")}
    im = []
    for c in cores:
        b, q = c // CPB, c % CPB
        incv = np.zeros((1, 4), np.float32)
        incv[0, :q] = 1.0
        im.append(dict(m1fbase, x=x4[c], wout=g("L1_attn_a_W_out"),
                       dts=r[c]["dts"], dzs=r[c]["dzs"], gts=r[c]["gts"],
                       bcr=r[c]["bcr"], xnod=r[c]["xnod"],
                       hfall=np.stack([r[b * CPB + qq]["hfin"] for qq in range(CPB)]),
                       fall=np.stack([r[b * CPB + qq]["ff"] for qq in range(CPB)]),
                       incr=incv))
    r = run_bass_kernel_spmd(nc, im, core_ids=cores).results
    x5 = [r[c]["xout"] for c in cores]

    ffnbase = {
        "lng": g("L1_ffn_ln_g")[None, :], "lnb": g("L1_ffn_ln_b")[None, :],
        "rw": g("L1_ffn_rW"), "rb": g("L1_ffn_rb")[None, :], "w1": g("L1_ffn_W1"),
        "b1r": np.ascontiguousarray(np.stack([g("L1_ffn_b1")[e].reshape(16, 128).T
                                              for e in range(8)]).reshape(8 * 128, 16)),
        "w2": g("L1_ffn_W2"), "b2": g("L1_ffn_b2"),
        "ident": hc["ident"], "identf": identf, "onesrowf": onesrowf}
    nc_ffn2 = build_ffn2()
    im = [dict(ffnbase, x=x5[c]) for c in cores]
    r = run_bass_kernel_spmd(nc_ffn2, im, core_ids=cores).results
    x6 = [r[c]["xout"] for c in cores]

    inpT = np.ascontiguousarray(g("L1_attn_b_in_proj_w").T)
    outT = np.ascontiguousarray(g("L1_attn_b_out_proj_w").T)
    m2base = {
        "inpT": inpT,
        "cw": np.ascontiguousarray(g("L1_attn_b_conv_w").reshape(16, 128, 4)
                                   .transpose(1, 0, 2).reshape(128, 64)),
        "cb": np.ascontiguousarray(g("L1_attn_b_conv_b").reshape(16, 128).T),
        "alog": g("L1_attn_b_A_log")[None, :], "dpr": g("L1_attn_b_Dp")[None, :],
        "dtbias": g("L1_attn_b_dt_bias")[:, None],
        "mask01": m2c["mask01"], "triu": hc["triu"], "ident": hc["ident"],
        "identf": identf, "onesrowf": onesrowf}
    nc = build_m2tails()
    im = [{"x": x6[c], "inpT": inpT, "ident": hc["ident"]} for c in cores]
    r = run_bass_kernel_spmd(nc, im, core_ids=cores).results
    tails = [np.zeros((2048, 3), np.float32) if c % CPB == 0 else r[c - 1]["tails"]
             for c in cores]
    nc = build_m2(False)
    im = [dict(m2base, x=x6[c], tails=tails[c]) for c in cores]
    r = run_bass_kernel_spmd(nc, im, core_ids=cores).results
    nc = build_m2(True)
    im = []
    for c in cores:
        b, q = c // CPB, c % CPB
        incv = np.zeros((1, 4), np.float32)
        incv[0, :q] = 1.0
        im.append(dict(m2base, x=x6[c], tails=tails[c],
                       nw2=g("L1_attn_b_norm_w")[None, :], outT=outT,
                       hstall=np.stack([r[b * CPB + qq]["hst"] for qq in range(CPB)]),
                       ftall=np.stack([r[b * CPB + qq]["ft"] for qq in range(CPB)]),
                       incr=incv))
    r = run_bass_kernel_spmd(nc, im, core_ids=cores).results
    x7 = [r[c]["xout"] for c in cores]

    im = [dict(ffnbase, x=x7[c]) for c in cores]
    r = run_bass_kernel_spmd(nc_ffn2, im, core_ids=cores).results
    out = np.empty((NB, T, D), np.float32)
    for c in cores:
        b, q = c // CPB, c % CPB
        out[b, q * TC:(q + 1) * TC] = r[c]["xout"]
    return out


# version 04
# speedup vs baseline: 1.3532x; 1.2599x over previous
import contextlib
import numpy as np
import ml_dtypes
import concourse.bass as bass
import concourse.mybir as mybir
from concourse.bass_utils import run_bass_kernel_spmd
from concourse.bass import IndirectOffsetOnAxis

F32 = mybir.dt.float32
BF16 = mybir.dt.bfloat16
U32 = mybir.dt.uint32
AF = mybir.ActivationFunctionType
OP = mybir.AluOpType
AX = mybir.AxisListType.X
BF = ml_dtypes.bfloat16

NCORE, NB, T, D = 8, 2, 16384, 1024
TC = 4096
CPB = 4
LN_EPS = 1e-5
SEG_LIM = 12000

_NC_CACHE = {}


class Prog:

    def __init__(self, nc):
        self.nc = nc
        self.steps = []
        self.ctx = contextlib.ExitStack()

    def sb(self, name, shape, dtype):
        return self.ctx.enter_context(self.nc.sbuf_tensor(name, shape, dtype))

    def ps(self, name, shape, dtype=F32):
        return self.ctx.enter_context(self.nc.psum_tensor(name, shape, dtype))

    def step(self, eng, fn, dmas=0):
        self.steps.append((eng, fn, 16 * dmas if dmas else 1))

    def build(self):
        nc = self.nc
        segs = []
        cur = 0
        seg_of, base_of = [], []
        for (_, _, inc) in self.steps:
            if cur + inc > SEG_LIM:
                segs.append(cur)
                cur = 0
            seg_of.append(len(segs))
            base_of.append(cur)
            cur += inc
        segs.append(cur)
        sems = [self.ctx.enter_context(nc.semaphore("ch%d" % i))
                for i in range(len(segs))]
        with nc.Block() as block:
            def body_for(name):
                def body(e):
                    seen_seg = set()
                    for i, (en, fn, inc) in enumerate(self.steps):
                        if en != name:
                            continue
                        s = seg_of[i]
                        if s > 0 and s not in seen_seg:
                            e.wait_ge(sems[s - 1], segs[s - 1])
                        seen_seg.add(s)
                        if base_of[i] > 0:
                            e.wait_ge(sems[s], base_of[i])
                        out = fn(e)
                        if inc == 1:
                            out.then_inc(sems[s], 1)
                        else:
                            insts = out if isinstance(out, (list, tuple)) else [out]
                            assert len(insts) * 16 == inc, (len(insts), inc)
                            for it in insts:
                                it.then_inc(sems[s], 16)
                    if name == "sync":
                        if len(segs) > 1:
                            e.wait_ge(sems[len(segs) - 2], segs[-2])
                        e.wait_ge(sems[len(segs) - 1], segs[-1])
                return body
            block.sync(body_for("sync"))
            block.vector(body_for("vector"))
            block.scalar(body_for("scalar"))
            block.tensor(body_for("tensor"))
            block.gpsimd(body_for("gpsimd"))
        self.ctx.close()
        return nc



def host_consts():
    c = {}
    c["ident"] = np.eye(128, dtype=np.float32).astype(BF)
    c["triu"] = np.triu(np.ones((128, 128), np.float32)).astype(BF)
    c["ones_col"] = np.ones((128, 1), np.float32).astype(BF)
    c["ones_row"] = np.ones((1, 128), np.float32).astype(BF)
    return c


def emit_ln(p, xt, xn, stats, sq, d=D):
    p.step("vector", lambda e: e.reduce_sum(stats[:, 0:1], xt[:], axis=AX))
    p.step("scalar", lambda e: e.activation(out=sq[:], in_=xt[:], func=AF.Square,
                                            accum_out=stats[:, 1:2]))
    p.step("vector", lambda e: e.tensor_scalar(out=stats[:, 2:3], in0=stats[:, 0:1],
                                               scalar1=1.0 / d, scalar2=None, op0=OP.mult))
    p.step("vector", lambda e: e.tensor_scalar(out=stats[:, 3:4], in0=stats[:, 1:2],
                                               scalar1=1.0 / d, scalar2=None, op0=OP.mult))
    p.step("vector", lambda e: e.tensor_tensor(out=stats[:, 4:5], in0=stats[:, 2:3],
                                               in1=stats[:, 2:3], op=OP.mult))
    p.step("vector", lambda e: e.tensor_tensor(out=stats[:, 5:6], in0=stats[:, 3:4],
                                               in1=stats[:, 4:5], op=OP.subtract))
    p.step("vector", lambda e: e.tensor_scalar(out=stats[:, 5:6], in0=stats[:, 5:6],
                                               scalar1=LN_EPS, scalar2=None, op0=OP.add))
    p.step("scalar", lambda e: e.activation(out=stats[:, 6:7], in_=stats[:, 5:6], func=AF.Sqrt))
    p.step("vector", lambda e: e.reciprocal(stats[:, 7:8], stats[:, 6:7]))
    p.step("vector", lambda e: e.tensor_scalar(out=xn[:], in0=xt[:], scalar1=stats[:, 2:3],
                                               scalar2=stats[:, 7:8], op0=OP.subtract,
                                               op1=OP.mult))


def emit_tp(p, src, dst_fn, pt, idt, n=8):
    for j in range(n):
        def ftp(e, j=j):
            return e.transpose(pt[:], src[:, j * 128:(j + 1) * 128], idt[:])
        p.step("tensor", ftp)
        def fcp(e, j=j):
            return e.tensor_copy(dst_fn(j), pt[:])
        p.step("vector", fcp)



def build_qkv():
    nc = bass.Bass("TRN2", target_bir_lowering=False)
    x = nc.dram_tensor("x", [TC, D], F32, kind="ExternalInput").ap()
    wq = nc.dram_tensor("wq", [D, D], F32, kind="ExternalInput").ap()
    wk = nc.dram_tensor("wk", [D, D], F32, kind="ExternalInput").ap()
    wv = nc.dram_tensor("wv", [D, D], F32, kind="ExternalInput").ap()
    ident = nc.dram_tensor("ident", [128, 128], BF16, kind="ExternalInput").ap()
    qf = nc.dram_tensor("qf", [D, TC], BF16, kind="ExternalOutput").ap()
    kf = nc.dram_tensor("kf", [D, TC], BF16, kind="ExternalOutput").ap()
    vt = nc.dram_tensor("vt", [TC, D], BF16, kind="ExternalOutput").ap()

    p = Prog(nc)
    wqt = p.sb("wqt", [128, 8 * D], BF16)
    wkt = p.sb("wkt", [128, 8 * D], BF16)
    wvt = p.sb("wvt", [128, 8 * D], BF16)
    idt = p.sb("idt", [128, 128], BF16)
    xt = p.sb("xt", [128, D], F32)
    sq = p.sb("sq", [128, D], F32)
    stats = p.sb("stats", [128, 8], F32)
    xn = p.sb("xn", [128, D], BF16)
    xfm = p.sb("xfm", [128, 8 * 512], BF16)
    ev = p.sb("ev", [128, 512], BF16)
    pt = p.ps("pt", [128, 128], BF16)
    pm = p.ps("pm", [128, 512], F32)

    def ldw(e):
        out = [e.dma_start(out=dst[:, i * D:(i + 1) * D], in_=src[i * 128:(i + 1) * 128, :])
               for dst, src in ((wqt, wq), (wkt, wk), (wvt, wv)) for i in range(8)]
        return out
    p.step("gpsimd", ldw, dmas=24)
    p.step("sync", lambda e: e.dma_start(out=idt[:], in_=ident[:, :]), dmas=1)

    for bi in range(8):
        for tt in range(4):
            tok0 = bi * 512 + tt * 128
            p.step("sync", lambda e, tok0=tok0: e.dma_start(out=xt[:], in_=x[tok0:tok0 + 128, :]), dmas=1)
            emit_ln(p, xt, xn, stats, sq)
            emit_tp(p, xn, (lambda j, tt=tt: xfm[:, j * 512 + tt * 128: j * 512 + tt * 128 + 128]),
                    pt, idt)
        for wt, out_d, scale in ((wqt, qf, 1.0 / 32), (wkt, kf, None)):
            for m in range(8):
                def fmm(e, wt=wt, m=m):
                    for i in range(8):
                        r = e.matmul(pm[:], wt[:, i * D + m * 128: i * D + m * 128 + 128],
                                     xfm[:, i * 512:(i + 1) * 512],
                                     start=(i == 0), stop=(i == 7))
                    return r
                p.step("tensor", fmm)
                if scale is None:
                    p.step("vector", lambda e: e.tensor_copy(ev[:], pm[:]))
                else:
                    p.step("vector", lambda e, s=scale: e.tensor_scalar(
                        out=ev[:], in0=pm[:], scalar1=s, scalar2=None, op0=OP.mult))
                p.step("sync", lambda e, out_d=out_d, m=m, bi=bi: e.dma_start(
                    out=out_d[m * 128:(m + 1) * 128, bi * 512:(bi + 1) * 512], in_=ev[:]), dmas=1)
        for tt in range(4):
            for ch in range(2):
                def fmv(e, tt=tt, ch=ch):
                    for i in range(8):
                        r = e.matmul(pm[:], xfm[:, i * 512 + tt * 128: i * 512 + tt * 128 + 128],
                                     wvt[:, i * D + ch * 512: i * D + ch * 512 + 512],
                                     start=(i == 0), stop=(i == 7))
                    return r
                p.step("tensor", fmv)
                p.step("vector", lambda e: e.tensor_copy(ev[:], pm[:]))
                p.step("sync", lambda e, tt=tt, ch=ch, bi=bi: e.dma_start(
                    out=vt[bi * 512 + tt * 128: bi * 512 + tt * 128 + 128,
                           ch * 512:(ch + 1) * 512], in_=ev[:]), dmas=1)
    return p.build()



def emit_ffnpre(p, t, xt, tok):
    emit_ln(p, xt, t["xn"], t["stats2"], t["sq"])
    emit_tp(p, t["xn"], (lambda j: t["xfm1"][:, j * 128:(j + 1) * 128]),
            t["pt1"], t["idt"])

    def fgmm(e):
        for i in range(8):
            r = e.matmul(t["pg"][:], t["xfm1"][:, i * 128:(i + 1) * 128],
                         t["gwt"][:, i * 8:(i + 1) * 8],
                         start=(i == 0), stop=(i == 7))
        return r
    p.step("tensor", fgmm)
    p.step("vector", lambda e: e.tensor_copy(t["sc8"][:], t["pg"][:]))
    p.step("sync", lambda e, tok=tok: [
        e.dma_start(out=t["d_xn1"][tok:tok + 128, :], in_=t["xn"][:]),
        e.dma_start(out=t["d_scores"][tok:tok + 128, :], in_=t["sc8"][:])], dmas=2)


def build_flash(NKB=T // 512, NSUB=4, NG=8):
    GQ = 512
    nc = bass.Bass("TRN2", target_bir_lowering=False)
    x = nc.dram_tensor("x", [TC, D], F32, kind="ExternalInput").ap()
    qf = nc.dram_tensor("qf", [D, TC], BF16, kind="ExternalInput").ap()
    kff = nc.dram_tensor("kff", [D, T], BF16, kind="ExternalInput").ap()
    vtf = nc.dram_tensor("vtf", [T, D], BF16, kind="ExternalInput").ap()
    gw = nc.dram_tensor("gw", [D, 8], F32, kind="ExternalInput").ap()
    qpos = nc.dram_tensor("qpos", [128, TC // 128], F32, kind="ExternalInput").ap()
    kpos = nc.dram_tensor("kpos", [1, T], F32, kind="ExternalInput").ap()
    ident = nc.dram_tensor("ident", [128, 128], BF16, kind="ExternalInput").ap()
    xout = nc.dram_tensor("xout", [TC, D], F32, kind="ExternalOutput").ap()
    d_xn1 = nc.dram_tensor("xn1", [TC, D], BF16, kind="ExternalOutput").ap()
    d_scores = nc.dram_tensor("scores", [TC, 8], F32, kind="ExternalOutput").ap()

    p = Prog(nc)
    t = {}
    qfm = p.sb("qfm", [128, 8 * GQ], BF16)
    kt = p.sb("kt", [128, 8 * 512], BF16)
    vtt = p.sb("vtt", [128, 4 * D], BF16)
    qpt = p.sb("qpt", [128, TC // 128], F32)
    kpr = p.sb("kpr", [128, 512], F32)
    cmpt = p.sb("cmpt", [128, 512], F32)
    nmw = p.sb("nmw", [128, 1], F32)
    ssb = p.sb("ssb", [128, 512], F32)
    pb = p.sb("pb", [128, 512], BF16)
    pts = p.sb("pts", [128, 512], BF16)
    stf = p.sb("stf", [128, 3 * NSUB], F32)
    oacc = [p.sb("oacc%d" % s, [128, D], F32) for s in range(NSUB)]
    t["idt"] = p.sb("idt", [128, 128], BF16)
    t["gwt"] = p.sb("gwt", [128, 8 * 8], BF16)
    xt = p.sb("xt", [128, D], F32)
    t["xn"] = p.sb("xn", [128, D], BF16)
    t["sq"] = p.sb("sqs", [128, D], F32)
    t["stats2"] = p.sb("stats2", [128, 8], F32)
    t["xfm1"] = p.sb("xfm1", [128, 8 * 128], BF16)
    t["sc8"] = p.sb("sc8", [128, 8], F32)
    t["d_xn1"], t["d_scores"] = d_xn1, d_scores
    kpw = p.sb("kpw", [1, 512], F32)
    onebf = p.sb("onebf", [1, 128], F32)
    pm = p.ps("pm", [128, 512], F32)
    pt4 = p.ps("pt4", [128, 512], BF16)
    pav = p.ps("pav", [128, D], F32)
    pkp = p.ps("pkp", [128, 512], F32)
    t["pt1"] = p.ps("pt1", [128, 128], BF16)
    t["pg"] = p.ps("pg", [128, 8], F32)

    p.step("sync", lambda e: [e.dma_start(out=t["idt"][:], in_=ident[:, :]),
                              e.dma_start(out=qpt[:], in_=qpos[:, :])], dmas=2)
    p.step("gpsimd", lambda e: [e.dma_start(out=t["gwt"][:, i * 8:(i + 1) * 8],
                                            in_=gw[i * 128:(i + 1) * 128, :])
                                for i in range(8)], dmas=8)
    p.step("vector", lambda e: e.memset(onebf[:], 1.0))

    def s_mm(e, s):
        for i in range(8):
            r = e.matmul(pm[:], qfm[:, i * GQ + s * 128: i * GQ + s * 128 + 128],
                         kt[:, i * 512:(i + 1) * 512],
                         start=(i == 0), stop=(i == 7))
        return r

    for qb in range(8):
        q0 = qb * GQ
        NKBq = 4 * qb + 4
        p.step("sync", lambda e, q0=q0: e.dma_start(
            out=qfm[:].rearrange("p (a t) -> p a t", a=8),
            in_=qf[:, q0:q0 + GQ].rearrange("(a p) t -> p a t", p=128)), dmas=1)
        p.step("vector", lambda e: e.memset(stf[:], 1e30))
        for kb in range(NKBq):
            k0 = kb * 512
            diag = kb >= 4 * qb
            if diag:
                p.step("sync", lambda e, k0=k0: [
                    e.dma_start(out=kt[:].rearrange("p (a t) -> p a t", a=8),
                                in_=kff[:, k0:k0 + 512].rearrange("(a p) t -> p a t", p=128)),
                    e.dma_start(out=kpw[:], in_=kpos[:, k0:k0 + 512])], dmas=2)
                p.step("tensor", lambda e: e.matmul(
                    pkp[:], onebf[:], kpw[:], start=True, stop=True))
                p.step("vector", lambda e: e.tensor_copy(kpr[:], pkp[:]))
            else:
                p.step("sync", lambda e, k0=k0: e.dma_start(
                    out=kt[:].rearrange("p (a t) -> p a t", a=8),
                    in_=kff[:, k0:k0 + 512].rearrange("(a p) t -> p a t", p=128)), dmas=1)
            for s in range(NSUB):
                p.step("tensor", lambda e, s=s: s_mm(e, s))
                if diag:
                    p.step("vector", lambda e, s=s, q0=q0: e.tensor_scalar(
                        out=cmpt[:], in0=kpr[:],
                        scalar1=qpt[:, q0 // 128 + s:q0 // 128 + s + 1],
                        scalar2=None, op0=OP.is_gt))
                    p.step("vector", lambda e: e.scalar_tensor_tensor(
                        out=ssb[:], in0=cmpt[:], scalar=-1e30, in1=pm[:],
                        op0=OP.mult, op1=OP.add))
                    p.step("vector", lambda e: e.reduce_max(nmw[:, 0:1], ssb[:], axis=AX,
                                                            negate=True))
                else:
                    p.step("vector", lambda e: e.reduce_max(nmw[:, 0:1], pm[:], axis=AX,
                                                            negate=True))
                p.step("vector", lambda e, s=s: e.tensor_tensor(
                    out=stf[:, 3 * s:3 * s + 1], in0=stf[:, 3 * s:3 * s + 1],
                    in1=nmw[:, 0:1], op=OP.min))
        def fz(e):
            for s in range(NSUB):
                r = e.memset(oacc[s][:], 0.0)
            return r
        p.step("vector", fz)
        def fz2(e):
            for s in range(NSUB):
                r = e.memset(stf[:, 3 * s + 1:3 * s + 2], 0.0)
            return r
        p.step("vector", fz2)
        for kb in range(NKBq):
            k0 = kb * 512
            diag = kb >= 4 * qb
            if diag:
                p.step("sync", lambda e, k0=k0: [
                    e.dma_start(out=kt[:].rearrange("p (a t) -> p a t", a=8),
                                in_=kff[:, k0:k0 + 512].rearrange("(a p) t -> p a t", p=128)),
                    e.dma_start(out=vtt[:].rearrange("p (a d) -> p a d", a=4),
                                in_=vtf[k0:k0 + 512, :].rearrange("(a p) d -> p a d", p=128)),
                    e.dma_start(out=kpw[:], in_=kpos[:, k0:k0 + 512])], dmas=3)
                p.step("tensor", lambda e: e.matmul(
                    pkp[:], onebf[:], kpw[:], start=True, stop=True))
                p.step("vector", lambda e: e.tensor_copy(kpr[:], pkp[:]))
            else:
                p.step("sync", lambda e, k0=k0: [
                    e.dma_start(out=kt[:].rearrange("p (a t) -> p a t", a=8),
                                in_=kff[:, k0:k0 + 512].rearrange("(a p) t -> p a t", p=128)),
                    e.dma_start(out=vtt[:].rearrange("p (a d) -> p a d", a=4),
                                in_=vtf[k0:k0 + 512, :].rearrange("(a p) d -> p a d", p=128))],
                    dmas=2)
            for s in range(NSUB):
                p.step("tensor", lambda e, s=s: s_mm(e, s))
                if diag:
                    p.step("vector", lambda e, s=s, q0=q0: e.tensor_scalar(
                        out=cmpt[:], in0=kpr[:],
                        scalar1=qpt[:, q0 // 128 + s:q0 // 128 + s + 1],
                        scalar2=None, op0=OP.is_gt))
                    p.step("vector", lambda e: e.scalar_tensor_tensor(
                        out=ssb[:], in0=cmpt[:], scalar=-1e30, in1=pm[:],
                        op0=OP.mult, op1=OP.add))
                    p.step("scalar", lambda e, s=s: e.activation(
                        out=pb[:], in_=ssb[:], func=AF.Exp, bias=stf[:, 3 * s:3 * s + 1],
                        accum_out=stf[:, 3 * s + 2:3 * s + 3]))
                else:
                    p.step("scalar", lambda e, s=s: e.activation(
                        out=pb[:], in_=pm[:], func=AF.Exp, bias=stf[:, 3 * s:3 * s + 1],
                        accum_out=stf[:, 3 * s + 2:3 * s + 3]))
                p.step("vector", lambda e, s=s: e.tensor_tensor(
                    out=stf[:, 3 * s + 1:3 * s + 2], in0=stf[:, 3 * s + 1:3 * s + 2],
                    in1=stf[:, 3 * s + 2:3 * s + 3], op=OP.add))
                def ftp(e):
                    for j in range(4):
                        r = e.transpose(pt4[:, j * 128:(j + 1) * 128],
                                        pb[:, j * 128:(j + 1) * 128], t["idt"][:])
                    return r
                p.step("tensor", ftp)
                p.step("vector", lambda e: e.tensor_copy(pts[:], pt4[:]))
                def fav(e):
                    for ch in range(2):
                        for j in range(4):
                            r = e.matmul(pav[:, ch * 512:(ch + 1) * 512],
                                         pts[:, j * 128:(j + 1) * 128],
                                         vtt[:, j * D + ch * 512: j * D + ch * 512 + 512],
                                         start=(j == 0), stop=(j == 3))
                    return r
                p.step("tensor", fav)
                p.step("vector", lambda e, s=s: e.tensor_tensor(
                    out=oacc[s][:], in0=oacc[s][:], in1=pav[:], op=OP.add))
        for s in range(NSUB):
            tok = q0 + s * 128
            p.step("sync", lambda e, tok=tok: e.dma_start(out=xt[:], in_=x[tok:tok + 128, :]),
                   dmas=1)
            p.step("vector", lambda e, s=s: e.reciprocal(stf[:, 3 * s + 2:3 * s + 3],
                                                         stf[:, 3 * s + 1:3 * s + 2]))
            p.step("vector", lambda e, s=s: e.tensor_scalar(
                out=oacc[s][:], in0=oacc[s][:], scalar1=stf[:, 3 * s + 2:3 * s + 3],
                scalar2=None, op0=OP.mult))
            p.step("vector", lambda e, s=s: e.tensor_tensor(
                out=xt[:], in0=xt[:], in1=oacc[s][:], op=OP.add))
            p.step("sync", lambda e, tok=tok: e.dma_start(out=xout[tok:tok + 128, :],
                                                          in_=xt[:]), dmas=1)
            emit_ffnpre(p, t, xt, tok)
    return p.build()



CAP = 4096
BIGOFF = 1.0e6


def emit_softmax8(p, st3, mx, es, probs, d2=128):
    p.step("vector", lambda e: e.tensor_reduce(mx[:].rearrange("p (a o) -> p a o", o=1),
                                               st3, axis=AX, op=OP.max))
    p.step("vector", lambda e: e.tensor_tensor(
        out=es[:].rearrange("p (a k) -> p a k", k=8), in0=st3,
        in1=mx[:].rearrange("p (a o) -> p a o", o=1).to_broadcast([128, d2, 8]),
        op=OP.subtract))
    p.step("scalar", lambda e: e.activation(out=es[:], in_=es[:], func=AF.Exp))
    p.step("vector", lambda e: e.tensor_reduce(mx[:].rearrange("p (a o) -> p a o", o=1),
                                               es[:].rearrange("p (a k) -> p a k", k=8),
                                               axis=AX, op=OP.add))
    p.step("vector", lambda e: e.reciprocal(mx[:], mx[:]))
    p.step("vector", lambda e: e.tensor_tensor(
        out=probs[:].rearrange("p (a k) -> p a k", k=8),
        in0=es[:].rearrange("p (a k) -> p a k", k=8),
        in1=mx[:].rearrange("p (a o) -> p a o", o=1).to_broadcast([128, d2, 8]),
        op=OP.mult))


def emit_bisect(p, pe, lo, hi, th, cnt, rcol, allo, pbt, ncols, iters=46):
    ngrp = ncols // 128
    p.step("vector", lambda e: e.memset(lo[:], 0.0))
    p.step("vector", lambda e: e.memset(hi[:], 1.0))
    pe3 = pe[:].rearrange("p (g a) -> p g a", a=128)
    for _ in range(iters):
        p.step("vector", lambda e: e.tensor_tensor(out=th[:], in0=lo[:], in1=hi[:], op=OP.add))
        p.step("vector", lambda e: e.tensor_scalar(out=th[:], in0=th[:], scalar1=0.5,
                                                   scalar2=None, op0=OP.mult))
        p.step("vector", lambda e: e.tensor_tensor(
            out=cnt[:].rearrange("p (g a) -> p g a", a=128), in0=pe3,
            in1=th[:].rearrange("p (g o) -> p g o", o=1).to_broadcast([128, ngrp, 128]),
            op=OP.is_gt))
        p.step("vector", lambda e: e.tensor_reduce(
            rcol[:].rearrange("p (g o) -> p g o", o=1),
            cnt[:].rearrange("p (g a) -> p g a", a=128), axis=AX, op=OP.add))
        p.step("tensor", lambda e: e.matmul(pbt[:], allo[:], rcol[:], start=True, stop=True))
        p.step("vector", lambda e: e.tensor_scalar(out=rcol[:], in0=pbt[:], scalar1=float(CAP),
                                                   scalar2=None, op0=OP.is_ge))
        p.step("vector", lambda e: e.tensor_tensor(out=cnt[:, 0:lo.shape[1]], in0=th[:],
                                                   in1=lo[:], op=OP.subtract))
        p.step("vector", lambda e: e.tensor_tensor(out=cnt[:, 0:lo.shape[1]],
                                                   in0=cnt[:, 0:lo.shape[1]], in1=rcol[:],
                                                   op=OP.mult))
        p.step("vector", lambda e: e.tensor_tensor(out=lo[:], in0=lo[:],
                                                   in1=cnt[:, 0:lo.shape[1]], op=OP.add))
        p.step("vector", lambda e: e.tensor_tensor(out=cnt[:, 0:lo.shape[1]], in0=hi[:],
                                                   in1=th[:], op=OP.subtract))
        p.step("vector", lambda e: e.tensor_tensor(out=cnt[:, 0:lo.shape[1]],
                                                   in0=cnt[:, 0:lo.shape[1]], in1=rcol[:],
                                                   op=OP.mult))
        p.step("vector", lambda e: e.tensor_tensor(out=hi[:], in0=th[:],
                                                   in1=cnt[:, 0:lo.shape[1]], op=OP.add))



def build_moedense(tail="none"):
    nc = bass.Bass("TRN2", target_bir_lowering=False)
    x = nc.dram_tensor("x", [TC, D], F32, kind="ExternalInput").ap()
    xnd = nc.dram_tensor("xn", [TC, D], BF16, kind="ExternalInput").ap()
    scof = nc.dram_tensor("scoresf", [T, 8], F32, kind="ExternalInput").ap()
    scom = nc.dram_tensor("scoresm", [TC, 8], F32, kind="ExternalInput").ap()
    w1 = nc.dram_tensor("w1", [8, D, 4096], F32, kind="ExternalInput").ap()
    w2 = nc.dram_tensor("w2", [8, 4096, D], F32, kind="ExternalInput").ap()
    ident = nc.dram_tensor("ident", [128, 128], BF16, kind="ExternalInput").ap()
    allones = nc.dram_tensor("allones", [128, 128], F32, kind="ExternalInput").ap()
    xout = nc.dram_tensor("xout", [TC, D], F32, kind="ExternalOutput").ap()
    acc = nc.dram_tensor("acc", [TC, D], F32, kind="Internal").ap()
    xnf = nc.dram_tensor("xnf", [D, TC], BF16, kind="Internal").ap()
    if tail == "s4pre":
        winT = nc.dram_tensor("winT", [D, D], F32, kind="ExternalInput").ap()
        b_in = nc.dram_tensor("b_in", [1, D], F32, kind="ExternalInput").ap()
        gam = nc.dram_tensor("gam", [1, D], F32, kind="ExternalInput").ap()
        bet = nc.dram_tensor("bet", [1, D], F32, kind="ExternalInput").ap()
        onesrowf = nc.dram_tensor("onesrowf", [1, 128], F32, kind="ExternalInput").ap()
        d_u2 = nc.dram_tensor("u2", [TC, D], F32, kind="ExternalOutput").ap()
        d_cs = nc.dram_tensor("colsum", [1, D], F32, kind="ExternalOutput").ap()

    NT = T // 128
    NTm = TC // 128
    p = Prog(nc)
    idt = p.sb("idt", [128, 128], BF16)
    allo = p.sb("allo", [128, 128], F32)
    w1t = p.sb("w1t", [128, 8 * 4096], BF16)
    w2t = p.sb("w2t", [128, 32 * D], BF16)
    xnt = p.sb("xnt", [128, D], BF16)
    xfm1 = p.sb("xfm1", [128, 8 * 128], BF16)
    xfmb = p.sb("xfmb", [128, 8 * 512], BF16)
    h1f = p.sb("h1f", [128, 32 * 512], BF16)
    xacc = p.sb("xacc", [128, 4 * D], F32)
    pmE = p.sb("pmE", [128, 8 * NTm], F32)
    w8E = p.sb("w8E", [128, 8 * NTm], F32)
    lo = p.sb("lo", [128, 8], F32)
    hi = p.sb("hi", [128, 8], F32)
    th = p.sb("th", [128, 8], F32)
    rcol = p.sb("rcol", [128, 8], F32)
    ptp = p.ps("ptp", [128, 128], BF16)
    pg1 = p.ps("pg1", [128, 512], F32)
    pg2a = p.ps("pg2a", [128, 512], F32)
    pg2b = p.ps("pg2b", [128, 512], F32)
    pbt = p.ps("pbt", [128, 8], F32)
    w1f = w1t[:].bitcast(F32)
    stf = w1f[:, 0:8 * NT]
    probs = w1f[:, 8 * NT:16 * NT]
    esf = w1f[:, 16 * NT:24 * NT]
    peF = w1f[:, 24 * NT:32 * NT]
    cnt = w1f[:, 32 * NT:40 * NT]
    mxf = w1f[:, 40 * NT:41 * NT]
    w2f_ = w2t[:].bitcast(F32)
    stm = w2f_[:, 0:8 * NTm]
    probm = w2f_[:, 8 * NTm:16 * NTm]
    esm = w2f_[:, 16 * NTm:24 * NTm]
    mxm = w2f_[:, 24 * NTm:24 * NTm + NTm]

    p.step("sync", lambda e: [
        e.dma_start(out=stf.rearrange("p (a k) -> p a k", k=8),
                    in_=scof[:, :].rearrange("(a p) k -> p a k", p=128)),
        e.dma_start(out=stm.rearrange("p (a k) -> p a k", k=8),
                    in_=scom[:, :].rearrange("(a p) k -> p a k", p=128)),
        e.dma_start(out=allo[:], in_=allones[:, :]),
        e.dma_start(out=idt[:], in_=ident[:, :])], dmas=4)
    emit_softmax8(p, stf.rearrange("p (a k) -> p a k", k=8), mxf, esf, probs, d2=NT)
    emit_softmax8(p, stm.rearrange("p (a k) -> p a k", k=8), mxm, esm, probm, d2=NTm)
    p.step("vector", lambda e: e.tensor_copy(
        peF.rearrange("p (k a) -> p k a", a=NT),
        probs.rearrange("p (a k) -> p a k", k=8).transpose([0, 2, 1])))
    p.step("vector", lambda e: e.tensor_copy(
        pmE[:].rearrange("p (k a) -> p k a", a=NTm),
        probm.rearrange("p (a k) -> p a k", k=8).transpose([0, 2, 1])))
    emit_bisect(p, peF, lo, hi, th, cnt, rcol, allo, pbt, 8 * NT)
    p.step("vector", lambda e: e.tensor_tensor(
        out=w8E[:].rearrange("p (k a) -> p k a", a=NTm),
        in0=pmE[:].rearrange("p (k a) -> p k a", a=NTm),
        in1=lo[:].rearrange("p (k o) -> p k o", o=1).to_broadcast([128, 8, NTm]),
        op=OP.is_gt))
    p.step("vector", lambda e: e.tensor_tensor(out=w8E[:], in0=w8E[:], in1=pmE[:],
                                               op=OP.mult))
    for tl in range(NTm):
        p.step("sync", lambda e, tl=tl: e.dma_start(
            out=xnt[:], in_=xnd[tl * 128:(tl + 1) * 128, :]), dmas=1)
        emit_tp(p, xnt, (lambda i: xfm1[:, i * 128:(i + 1) * 128]), ptp, idt)
        p.step("sync", lambda e, tl=tl: e.dma_start(
            out=xnf[:, tl * 128:(tl + 1) * 128].rearrange("(a p) t -> p a t", p=128),
            in_=xfm1[:].rearrange("p (a t) -> p a t", a=8)), dmas=1)
    for ex in range(8):
        p.step("gpsimd", lambda e, ex=ex: [e.dma_start(out=w1t[:, i * 4096:(i + 1) * 4096],
                                                       in_=w1[ex, i * 128:(i + 1) * 128, :])
                                           for i in range(8)], dmas=8)
        p.step("gpsimd", lambda e, ex=ex: [e.dma_start(out=w2t[:, i * D:(i + 1) * D],
                                                       in_=w2[ex, i * 128:(i + 1) * 128, :])
                                           for i in range(32)], dmas=32)
        for blk in range(TC // 512):
            t0 = blk * 512
            p.step("sync", lambda e, t0=t0: e.dma_start(
                out=xfmb[:].rearrange("p (a t) -> p a t", a=8),
                in_=xnf[:, t0:t0 + 512].rearrange("(a p) t -> p a t", p=128)), dmas=1)
            src_d = x if ex == 0 else acc
            p.step("sync", lambda e, t0=t0, src_d=src_d: e.dma_start(
                out=xacc[:].rearrange("p (a d2) -> p a d2", a=4),
                in_=src_d[t0:t0 + 512, :].rearrange("(a p) d2 -> p a d2", p=128)), dmas=1)
            for hm in range(32):
                def fg1(e, hm=hm):
                    for i in range(8):
                        r = e.matmul(pg1[:],
                                     w1t[:, i * 4096 + hm * 128: i * 4096 + hm * 128 + 128],
                                     xfmb[:, i * 512:(i + 1) * 512],
                                     start=(i == 0), stop=(i == 7))
                    return r
                p.step("tensor", fg1)
                p.step("scalar", lambda e, hm=hm: e.activation(
                    out=h1f[:, hm * 512:(hm + 1) * 512], in_=pg1[:], func=AF.Gelu))
            for sub in range(4):
                for oc in range(2):
                    pgx = pg2a if (oc == 0) else pg2b
                    def fg2(e, sub=sub, oc=oc, pgx=pgx):
                        for k in range(32):
                            r = e.matmul(pgx[:],
                                         h1f[:, k * 512 + sub * 128: k * 512 + sub * 128 + 128],
                                         w2t[:, k * D + oc * 512: k * D + oc * 512 + 512],
                                         start=(k == 0), stop=(k == 31))
                        return r
                    p.step("tensor", fg2)
                    def facc(e, sub=sub, oc=oc, pgx=pgx, ex=ex, t0=t0):
                        col = ex * NTm + t0 // 128 + sub
                        return e.scalar_tensor_tensor(
                            out=xacc[:, sub * D + oc * 512: sub * D + oc * 512 + 512],
                            in0=pgx[:], scalar=w8E[:, col:col + 1],
                            in1=xacc[:, sub * D + oc * 512: sub * D + oc * 512 + 512],
                            op0=OP.mult, op1=OP.add)
                    p.step("vector", facc)
            dst = acc if (ex < 7 or tail != "none") else xout
            p.step("sync", lambda e, t0=t0, dst=dst: e.dma_start(
                out=dst[t0:t0 + 512, :].rearrange("(a p) d2 -> p a d2", p=128),
                in_=xacc[:].rearrange("p (a d2) -> p a d2", a=4)), dmas=1)

    if tail == "s4pre":
        wint = w1t[:, 0:8 * D]
        w2fx = w2t[:].bitcast(F32)
        binrep = w2fx[:, 0:D]
        garep = w2fx[:, D:2 * D]
        berep = w2fx[:, 2 * D:3 * D]
        usb = w2fx[:, 3 * D:4 * D]
        xln = w2fx[:, 4 * D:5 * D]
        u2t = w2fx[:, 5 * D:6 * D]
        sq2 = w2fx[:, 6 * D:7 * D]
        birow = w2fx[0:1, 7 * D:8 * D]
        garow = w2fx[0:1, 8 * D:9 * D]
        berow = w2fx[0:1, 9 * D:10 * D]
        onrw = w2fx[0:1, 10 * D:10 * D + 128]
        xat = xacc[:, 0:D]
        stt2 = p.sb("stt2", [128, 8], F32)
        xnb = xnt[:]
        prow = p.ps("prow", [128, 512], F32)
        pcs = p.ps("pcs", [1, D], F32)
        onescolf = p.sb("onescolf", [128, 1], F32)
        p.step("gpsimd", lambda e: [e.dma_start(out=wint[:, i * D:(i + 1) * D],
                                                in_=winT[i * 128:(i + 1) * 128, :])
                                    for i in range(8)], dmas=8)
        p.step("sync", lambda e: [e.dma_start(out=birow, in_=b_in[:, :]),
                                  e.dma_start(out=garow, in_=gam[:, :]),
                                  e.dma_start(out=berow, in_=bet[:, :]),
                                  e.dma_start(out=onrw, in_=onesrowf[:, :])], dmas=4)
        p.step("vector", lambda e: e.memset(onescolf[:], 1.0))
        for ch in range(2):
            for (rep, row) in ((binrep, birow), (garep, garow), (berep, berow)):
                p.step("tensor", lambda e, row=row, ch=ch: e.matmul(
                    prow[:], onrw, row[0:1, ch * 512:(ch + 1) * 512],
                    start=True, stop=True))
                p.step("vector", lambda e, rep=rep, ch=ch: e.tensor_copy(
                    rep[:, ch * 512:(ch + 1) * 512], prow[:]))
        for tl in range(NTm):
            p.step("sync", lambda e, tl=tl: e.dma_start(
                out=xat, in_=acc[tl * 128:(tl + 1) * 128, :]), dmas=1)
            p.step("sync", lambda e, tl=tl: e.dma_start(
                out=xout[tl * 128:(tl + 1) * 128, :], in_=xat), dmas=1)
            emit_ln(p, xat, xnb, stt2, sq2)
            emit_tp(p, xnb, (lambda i: xfm1[:, i * 128:(i + 1) * 128]), ptp, idt)
            for ch in range(2):
                def fpu(e, ch=ch):
                    for i in range(8):
                        r = e.matmul(pg1[:], xfm1[:, i * 128:(i + 1) * 128],
                                     wint[:, i * D + ch * 512: i * D + ch * 512 + 512],
                                     start=(i == 0), stop=(i == 7))
                    return r
                p.step("tensor", fpu)
                p.step("vector", lambda e, ch=ch: e.tensor_tensor(
                    out=usb[:, ch * 512:(ch + 1) * 512], in0=pg1[:],
                    in1=binrep[:, ch * 512:(ch + 1) * 512], op=OP.add))
            emit_ln(p, usb, xln, stt2, sq2)
            p.step("vector", lambda e: e.tensor_tensor(out=u2t, in0=xln, in1=garep,
                                                       op=OP.mult))
            p.step("vector", lambda e: e.tensor_tensor(out=u2t, in0=u2t, in1=berep,
                                                       op=OP.add))
            def fcs(e, tl=tl):
                for ch in range(2):
                    r = e.matmul(pcs[0:1, ch * 512:(ch + 1) * 512], onescolf[:],
                                 u2t[:, ch * 512:(ch + 1) * 512],
                                 start=(tl == 0), stop=(tl == NTm - 1))
                return r
            p.step("tensor", fcs)
            p.step("sync", lambda e, tl=tl: e.dma_start(
                out=d_u2[tl * 128:(tl + 1) * 128, :], in_=u2t), dmas=1)
        p.step("vector", lambda e: e.tensor_copy(usb[0:1, 0:D], pcs[:]))
        p.step("sync", lambda e: e.dma_start(out=d_cs[:, :], in_=usb[0:1, 0:D]), dmas=1)
    return p.build()



LK = 256
NF = 264
WIN = 512


def s4_consts():
    c = {}
    n = np.arange(WIN)[:, None]
    f = np.arange(NF)[None, :]
    mask = (f < 257).astype(np.float64)
    ang = 2 * np.pi * f * n / WIN
    c["fxc"] = (np.cos(ang) * mask).astype(np.float32).astype(BF)
    c["fxs"] = (-np.sin(ang) * mask).astype(np.float32).astype(BF)
    c["fkc"] = np.ascontiguousarray(c["fxc"][:LK])
    c["fks"] = np.ascontiguousarray(c["fxs"][:LK])
    fv = np.arange(NF)
    w = np.where(fv <= 256, (2.0 - (fv == 0) - (fv == 256)) / WIN, 0.0)[:, None]
    tv = np.arange(256)[None, :]
    ang2 = 2 * np.pi * fv[:, None] * (256 + tv) / WIN
    c["gre"] = (w * np.cos(ang2)).astype(np.float32).astype(BF)
    c["gim"] = (-w * np.sin(ang2)).astype(np.float32).astype(BF)
    c["lmat"] = np.tile(np.arange(LK, dtype=np.float32)[None, :], (64, 1))
    return c


def build_s4conv():
    NBLK = TC // 256
    nc = bass.Bass("TRN2", target_bir_lowering=False)
    x = nc.dram_tensor("x", [TC, D], F32, kind="ExternalInput").ap()
    u2e = nc.dram_tensor("u2e", [TC + 256, D], F32, kind="ExternalInput").ap()
    colsums = nc.dram_tensor("colsums", [4, D], F32, kind="ExternalInput").ap()
    pmk = nc.dram_tensor("pmk", [4, 1], F32, kind="ExternalInput").ap()
    are = nc.dram_tensor("are", [64, 1], F32, kind="ExternalInput").ap()
    aim = nc.dram_tensor("aim", [64, 1], F32, kind="ExternalInput").ap()
    bre = nc.dram_tensor("bre", [64, 1], F32, kind="ExternalInput").ap()
    bim = nc.dram_tensor("bim", [64, 1], F32, kind="ExternalInput").ap()
    cre = nc.dram_tensor("cre", [D, 64], F32, kind="ExternalInput").ap()
    cim = nc.dram_tensor("cim", [D, 64], F32, kind="ExternalInput").ap()
    dre = nc.dram_tensor("dre", [1, D], F32, kind="ExternalInput").ap()
    dim = nc.dram_tensor("dim", [1, D], F32, kind="ExternalInput").ap()
    lmat = nc.dram_tensor("lmat", [64, LK], F32, kind="ExternalInput").ap()
    fxc = nc.dram_tensor("fxc", [WIN, NF], BF16, kind="ExternalInput").ap()
    fxs = nc.dram_tensor("fxs", [WIN, NF], BF16, kind="ExternalInput").ap()
    fkc = nc.dram_tensor("fkc", [LK, NF], BF16, kind="ExternalInput").ap()
    fks = nc.dram_tensor("fks", [LK, NF], BF16, kind="ExternalInput").ap()
    gre = nc.dram_tensor("gre", [NF, 256], BF16, kind="ExternalInput").ap()
    gim = nc.dram_tensor("gim", [NF, 256], BF16, kind="ExternalInput").ap()
    gw = nc.dram_tensor("gw", [D, 8], F32, kind="ExternalInput").ap()
    ident = nc.dram_tensor("ident", [128, 128], BF16, kind="ExternalInput").ap()
    triu = nc.dram_tensor("triu", [128, 128], BF16, kind="ExternalInput").ap()
    onesrowf = nc.dram_tensor("onesrowf", [1, 128], F32, kind="ExternalInput").ap()
    xout = nc.dram_tensor("xout", [TC, D], F32, kind="ExternalOutput").ap()
    d_xn1 = nc.dram_tensor("xn1", [TC, D], BF16, kind="ExternalOutput").ap()
    d_scores = nc.dram_tensor("scores", [TC, 8], F32, kind="ExternalOutput").ap()

    p = Prog(nc)
    t = {}
    t["idt"] = p.sb("idt", [128, 128], BF16)
    trut = p.sb("trut", [128, 128], BF16)
    onrw = p.sb("onrw", [1, 128], F32)
    onescolf = p.sb("onescolf", [128, 1], F32)
    onescolb = p.sb("onescolb", [128, 1], BF16)
    t["gwt"] = p.sb("gwt", [128, 8 * 8], BF16)
    fxct = p.sb("fxct", [128, 4 * NF], BF16)
    fxst = p.sb("fxst", [128, 4 * NF], BF16)
    fkct = p.sb("fkct", [128, 2 * NF], BF16)
    fkst = p.sb("fkst", [128, 2 * NF], BF16)
    gret = p.sb("gret", [128, 3 * 256], BF16)
    gimt = p.sb("gimt", [128, 3 * 256], BF16)
    ar = p.sb("ar", [64, 16], F32)
    lm = p.sb("lm", [64, LK], F32)
    pw = p.sb("pw", [64, 6 * LK], F32)
    pb2 = p.sb("pb2", [64, 2 * LK], BF16)
    ibuf = p.sb("ibuf", [64, LK], mybir.dt.int32)
    ld64 = p.sb("ld64", [128, 64], F32)
    lb64 = p.sb("lb64", [128, 128], BF16)
    ct = p.sb("ct", [64, 2 * D], BF16)
    krb = p.sb("krb", [128, 2 * D], BF16)
    kfr = p.sb("kfr", [128, 3 * D], F32)
    kfi = p.sb("kfi", [128, 3 * D], F32)
    pmkt = p.sb("pmkt", [4, 1], F32)
    colst = p.sb("colst", [4, D], F32)
    wbuf = p.sb("wbuf", [128, 4 * D], BF16)
    u2f = p.sb("u2f", [128, D], F32)
    vt = p.sb("vt", [128, D], BF16)
    xsb = p.sb("xsb", [128, 6 * 512], F32)
    yre = p.sb("yre", [128, 3 * 512], BF16)
    yim = p.sb("yim", [128, 3 * 512], BF16)
    ysc1 = p.sb("ysc1", [128, 512], F32)
    ysc2 = p.sb("ysc2", [128, 512], F32)
    drow = p.sb("drow", [1, D], F32)
    dimr = p.sb("dimr", [1, D], F32)
    redrep = p.sb("redrep", [128, D], F32)
    offrow = p.sb("offrow", [1, D], F32)
    orsc = p.sb("orsc", [1, D], F32)
    xt = p.sb("xt", [128, D], F32)
    yt0 = p.sb("yt0", [128, D], F32)
    yt1 = p.sb("yt1", [128, D], F32)
    t["xn"] = p.sb("xn", [128, D], BF16)
    t["sq"] = p.sb("sqs", [128, D], F32)
    t["stats2"] = p.sb("stats2", [128, 8], F32)
    t["xfm1"] = p.sb("xfm1", [128, 8 * 128], BF16)
    t["sc8"] = p.sb("sc8", [128, 8], F32)
    t["d_xn1"], t["d_scores"] = d_xn1, d_scores
    pxf = p.ps("pxf", [128, 512], F32)
    py0 = p.ps("py0", [128, 512], F32)
    py1 = p.ps("py1", [128, 512], F32)
    poff = p.ps("poff", [1, 512], F32)
    t["pt1"] = p.ps("pt1", [128, 128], BF16)
    t["pg"] = p.ps("pg", [128, 8], F32)

    p.step("sync", lambda e: [
        e.dma_start(out=t["idt"][:], in_=ident[:, :]),
        e.dma_start(out=trut[:], in_=triu[:, :]),
        e.dma_start(out=onrw[:], in_=onesrowf[:, :]),
        e.dma_start(out=ar[:, 0:1], in_=are[:, :]),
        e.dma_start(out=ar[:, 1:2], in_=aim[:, :]),
        e.dma_start(out=ar[:, 2:3], in_=bre[:, :]),
        e.dma_start(out=ar[:, 3:4], in_=bim[:, :]),
        e.dma_start(out=lm[:], in_=lmat[:, :]),
        e.dma_start(out=drow[:], in_=dre[:, :]),
        e.dma_start(out=dimr[:], in_=dim[:, :]),
        e.dma_start(out=pmkt[:], in_=pmk[:, :]),
        e.dma_start(out=colst[:], in_=colsums[:, :])], dmas=12)
    p.step("sync", lambda e: [e.dma_start(out=fxct[:].rearrange("p (a f) -> p a f", a=4),
                                          in_=fxc[:, :].rearrange("(a p) f -> p a f", p=128)),
                              e.dma_start(out=fxst[:].rearrange("p (a f) -> p a f", a=4),
                                          in_=fxs[:, :].rearrange("(a p) f -> p a f", p=128)),
                              e.dma_start(out=fkct[:].rearrange("p (a f) -> p a f", a=2),
                                          in_=fkc[:, :].rearrange("(a p) f -> p a f", p=128)),
                              e.dma_start(out=fkst[:].rearrange("p (a f) -> p a f", a=2),
                                          in_=fks[:, :].rearrange("(a p) f -> p a f", p=128))],
           dmas=4)
    p.step("sync", lambda e: [e.dma_start(out=gret[0:128, 0:256], in_=gre[0:128, :]),
                              e.dma_start(out=gret[0:128, 256:512], in_=gre[128:256, :]),
                              e.dma_start(out=gret[0:8, 512:768], in_=gre[256:264, :]),
                              e.dma_start(out=gimt[0:128, 0:256], in_=gim[0:128, :]),
                              e.dma_start(out=gimt[0:128, 256:512], in_=gim[128:256, :]),
                              e.dma_start(out=gimt[0:8, 512:768], in_=gim[256:264, :])],
           dmas=6)
    p.step("gpsimd", lambda e: [e.dma_start(out=t["gwt"][:, i * 8:(i + 1) * 8],
                                            in_=gw[i * 128:(i + 1) * 128, :])
                                for i in range(8)], dmas=8)
    p.step("vector", lambda e: e.memset(onescolf[:], 1.0))
    p.step("vector", lambda e: e.memset(onescolb[:], 1.0))

    p.step("vector", lambda e: e.scalar_tensor_tensor(out=drow[:], in0=dimr[:], scalar=0.0,
                                                      in1=drow[:], op0=OP.mult, op1=OP.add))
    for ch in range(2):
        p.step("tensor", lambda e, ch=ch: e.matmul(
            pxf[:], onrw[:], drow[0:1, ch * 512:(ch + 1) * 512], start=True, stop=True))
        p.step("vector", lambda e, ch=ch: e.tensor_copy(redrep[:, ch * 512:(ch + 1) * 512],
                                                        pxf[:]))
    for ch in range(2):
        p.step("tensor", lambda e, ch=ch: e.matmul(
            poff[:], pmkt[:], colst[:, ch * 512:(ch + 1) * 512], start=True, stop=True))
        p.step("vector", lambda e, ch=ch: e.tensor_copy(orsc[0:1, ch * 512:(ch + 1) * 512],
                                                        poff[:]))
    p.step("vector", lambda e: e.tensor_tensor(out=offrow[:], in0=orsc[:], in1=drow[:],
                                               op=OP.mult))

    C = lambda i: ar[:, i:i + 1]
    PW = lambda i: pw[:, i * LK:(i + 1) * LK]
    p.step("vector", lambda e: e.tensor_tensor(out=C(4), in0=C(0), in1=C(0), op=OP.mult))
    p.step("vector", lambda e: e.tensor_tensor(out=C(5), in0=C(1), in1=C(1), op=OP.mult))
    p.step("vector", lambda e: e.tensor_tensor(out=C(4), in0=C(4), in1=C(5), op=OP.add))
    p.step("scalar", lambda e: e.activation(out=C(4), in_=C(4), func=AF.Ln))
    p.step("vector", lambda e: e.tensor_scalar(out=C(4), in0=C(4), scalar1=0.5,
                                               scalar2=None, op0=OP.mult))
    p.step("vector", lambda e: e.reciprocal(C(5), C(0)))
    p.step("vector", lambda e: e.tensor_tensor(out=C(5), in0=C(5), in1=C(1), op=OP.mult))
    p.step("scalar", lambda e: e.activation(out=C(5), in_=C(5), func=AF.Arctan))
    p.step("scalar", lambda e: e.activation(out=C(6), in_=C(1), func=AF.Sign))
    p.step("vector", lambda e: e.scalar_tensor_tensor(out=C(5), in0=C(6), scalar=float(np.pi),
                                                      in1=C(5), op0=OP.mult, op1=OP.add))
    p.step("scalar", lambda e: e.activation(out=C(7), in_=C(5), func=AF.Sign))
    p.step("scalar", lambda e: e.activation(out=C(6), in_=C(5), func=AF.Abs))
    p.step("scalar", lambda e: e.activation(out=PW(1), in_=lm[:], func=AF.Exp, scale=C(4)))
    p.step("vector", lambda e: e.tensor_scalar(out=PW(0), in0=lm[:], scalar1=C(6),
                                               scalar2=float(0.5 / np.pi), op0=OP.mult,
                                               op1=OP.mult))
    p.step("vector", lambda e: e.tensor_copy(ibuf[:], PW(0)))
    p.step("vector", lambda e: e.tensor_copy(PW(2), ibuf[:]))
    p.step("vector", lambda e: e.tensor_tensor(out=PW(0), in0=PW(0), in1=PW(2), op=OP.subtract))
    p.step("vector", lambda e: e.tensor_scalar(out=PW(0), in0=PW(0), scalar1=float(2 * np.pi),
                                               scalar2=None, op0=OP.mult))
    p.step("scalar", lambda e: e.activation(out=PW(3), in_=PW(0), func=AF.Sin))
    p.step("vector", lambda e: e.tensor_scalar(out=PW(2), in0=PW(0), scalar1=-1.0,
                                               scalar2=float(np.pi / 2), op0=OP.mult, op1=OP.add))
    p.step("vector", lambda e: e.tensor_scalar(out=PW(4), in0=PW(2), scalar1=float(np.pi),
                                               scalar2=None, op0=OP.is_gt))
    p.step("vector", lambda e: e.scalar_tensor_tensor(out=PW(2), in0=PW(4),
                                                      scalar=float(-2 * np.pi), in1=PW(2),
                                                      op0=OP.mult, op1=OP.add))
    p.step("scalar", lambda e: e.activation(out=PW(2), in_=PW(2), func=AF.Sin))
    p.step("vector", lambda e: e.tensor_scalar(out=PW(3), in0=PW(3), scalar1=C(7),
                                               scalar2=None, op0=OP.mult))
    p.step("vector", lambda e: e.tensor_tensor(out=PW(4), in0=PW(1), in1=PW(2), op=OP.mult))
    p.step("vector", lambda e: e.tensor_tensor(out=PW(5), in0=PW(1), in1=PW(3), op=OP.mult))
    p.step("vector", lambda e: e.tensor_scalar(out=PW(0), in0=PW(4), scalar1=C(2),
                                               scalar2=None, op0=OP.mult))
    p.step("vector", lambda e: e.tensor_scalar(out=PW(1), in0=PW(5), scalar1=C(3),
                                               scalar2=None, op0=OP.mult))
    p.step("vector", lambda e: e.tensor_tensor(out=PW(0), in0=PW(0), in1=PW(1), op=OP.subtract))
    p.step("vector", lambda e: e.tensor_copy(pb2[:, 0:LK], PW(0)))
    p.step("vector", lambda e: e.tensor_scalar(out=C(8), in0=C(3), scalar1=-1.0,
                                               scalar2=None, op0=OP.mult))
    p.step("vector", lambda e: e.tensor_scalar(out=C(9), in0=C(2), scalar1=-1.0,
                                               scalar2=None, op0=OP.mult))
    p.step("vector", lambda e: e.tensor_scalar(out=PW(0), in0=PW(4), scalar1=C(8),
                                               scalar2=None, op0=OP.mult))
    p.step("vector", lambda e: e.tensor_scalar(out=PW(1), in0=PW(5), scalar1=C(9),
                                               scalar2=None, op0=OP.mult))
    p.step("vector", lambda e: e.tensor_tensor(out=PW(0), in0=PW(0), in1=PW(1), op=OP.add))
    p.step("vector", lambda e: e.tensor_copy(pb2[:, LK:2 * LK], PW(0)))
    for (srcd, dstoff) in ((cre, 0), (cim, D)):
        for i in range(8):
            p.step("sync", lambda e, srcd=srcd, i=i: e.dma_start(
                out=ld64[:, 0:64], in_=srcd[i * 128:(i + 1) * 128, :]), dmas=1)
            p.step("vector", lambda e: e.tensor_copy(lb64[:, 0:64], ld64[:, 0:64]))
            p.step("tensor", lambda e: e.transpose(t["pt1"][:], lb64[:], t["idt"][:]))
            p.step("vector", lambda e, dstoff=dstoff, i=i: e.tensor_copy(
                ct[0:64, dstoff + i * 128: dstoff + (i + 1) * 128], t["pt1"][0:64, 0:128]))
    for lt in range(2):
        for ch in range(2):
            def fkr(e, lt=lt, ch=ch):
                e.matmul(pxf[:], pb2[:, lt * 128:(lt + 1) * 128],
                         ct[0:64, ch * 512:(ch + 1) * 512], start=True, stop=False)
                return e.matmul(pxf[:], pb2[:, LK + lt * 128: LK + lt * 128 + 128],
                                ct[0:64, D + ch * 512: D + ch * 512 + 512],
                                start=False, stop=True)
            p.step("tensor", fkr)
            p.step("vector", lambda e, lt=lt, ch=ch: e.tensor_copy(
                krb[:, lt * D + ch * 512: lt * D + ch * 512 + 512], pxf[:]))
    for (fk, kf) in ((fkct, kfr), (fkst, kfi)):
        for ft in range(3):
            rows = 128 if ft < 2 else 8
            for ch2 in range(2):
                def fkf(e, fk=fk, ft=ft, ch2=ch2, rows=rows):
                    for lt in range(2):
                        r = e.matmul(pxf[0:rows, :],
                                     fk[:, lt * NF + ft * 128: lt * NF + ft * 128 + rows],
                                     krb[:, lt * D + ch2 * 512: lt * D + ch2 * 512 + 512],
                                     start=(lt == 0), stop=(lt == 1))
                    return r
                p.step("tensor", fkf)
                p.step("vector", lambda e, kf=kf, ft=ft, ch2=ch2, rows=rows: e.tensor_copy(
                    kf[0:rows, ft * D + ch2 * 512: ft * D + ch2 * 512 + 512], pxf[0:rows, :]))

    for blk in range(NBLK):
        b0 = blk * 256
        p.step("gpsimd", lambda e, b0=b0: e.dma_start(
            out=wbuf[:].rearrange("p (a d2) -> p a d2", a=4),
            in_=u2e[b0:b0 + 512, :].rearrange("(a p) d2 -> p a d2", p=128)), dmas=1)
        for ch in range(2):
            for (fx, xoff) in ((fxct, 0), (fxst, 3)):
                for ft in range(3):
                    rows = 128 if ft < 2 else 8
                    def fxf(e, fx=fx, ft=ft, ch=ch, rows=rows):
                        for ks in range(4):
                            r = e.matmul(pxf[0:rows, :],
                                         fx[:, ks * NF + ft * 128: ks * NF + ft * 128 + rows],
                                         wbuf[:, ks * D + ch * 512: ks * D + ch * 512 + 512],
                                         start=(ks == 0), stop=(ks == 3))
                        return r
                    p.step("tensor", fxf)
                    p.step("vector", lambda e, xoff=xoff, ft=ft, rows=rows: e.tensor_copy(
                        xsb[0:rows, (xoff + ft) * 512:(xoff + ft + 1) * 512], pxf[0:rows, :]))
            for ft in range(3):
                rows = 128 if ft < 2 else 8
                XR = xsb[0:rows, ft * 512:(ft + 1) * 512]
                XI = xsb[0:rows, (3 + ft) * 512:(4 + ft) * 512]
                KR = kfr[0:rows, ft * D + ch * 512: ft * D + ch * 512 + 512]
                KI = kfi[0:rows, ft * D + ch * 512: ft * D + ch * 512 + 512]
                YR = yre[0:rows, ft * 512:(ft + 1) * 512]
                YI = yim[0:rows, ft * 512:(ft + 1) * 512]
                s1 = ysc1[0:rows, :]
                s2 = ysc2[0:rows, :]
                p.step("vector", lambda e, XR=XR, KR=KR, s1=s1: e.tensor_tensor(
                    out=s1, in0=XR, in1=KR, op=OP.mult))
                p.step("vector", lambda e, XI=XI, KI=KI, s2=s2: e.tensor_tensor(
                    out=s2, in0=XI, in1=KI, op=OP.mult))
                p.step("vector", lambda e, s1=s1, s2=s2, YR=YR: e.tensor_tensor(
                    out=YR, in0=s1, in1=s2, op=OP.subtract))
                p.step("vector", lambda e, XR=XR, KI=KI, s1=s1: e.tensor_tensor(
                    out=s1, in0=XR, in1=KI, op=OP.mult))
                p.step("vector", lambda e, XI=XI, KR=KR, s2=s2: e.tensor_tensor(
                    out=s2, in0=XI, in1=KR, op=OP.mult))
                p.step("vector", lambda e, s1=s1, s2=s2, YI=YI: e.tensor_tensor(
                    out=YI, in0=s1, in1=s2, op=OP.add))
            for tt in range(2):
                py = py0 if tt == 0 else py1
                ytl = yt0 if tt == 0 else yt1
                p.step("sync", lambda e, b0=b0, tt=tt: e.dma_start(
                    out=u2f[:], in_=u2e[256 + b0 + tt * 128: 256 + b0 + tt * 128 + 128, :]),
                    dmas=1)
                p.step("vector", lambda e: e.tensor_tensor(out=vt[:], in0=u2f[:], in1=redrep[:],
                                                           op=OP.mult))
                def fy(e, tt=tt, ch=ch, py=py):
                    for (g, y) in ((gret, yre), (gimt, yim)):
                        for ft in range(3):
                            rows = 128 if ft < 2 else 8
                            e.matmul(py[:],
                                     g[0:rows, ft * 256 + tt * 128: ft * 256 + tt * 128 + 128],
                                     y[0:rows, ft * 512:(ft + 1) * 512],
                                     start=(g is gret and ft == 0), stop=False)
                    e.matmul(py[:], trut[:], vt[:, ch * 512:(ch + 1) * 512],
                             start=False, stop=False)
                    return e.matmul(py[:], onrw[:], offrow[0:1, ch * 512:(ch + 1) * 512],
                                    start=False, stop=True)
                p.step("tensor", fy)
                p.step("tensor", lambda e, ch=ch: e.matmul(
                    poff[:], onescolb[:], vt[:, ch * 512:(ch + 1) * 512],
                    start=True, stop=True))
                p.step("vector", lambda e, ch=ch: e.tensor_tensor(
                    out=offrow[0:1, ch * 512:(ch + 1) * 512],
                    in0=offrow[0:1, ch * 512:(ch + 1) * 512], in1=poff[:], op=OP.add))
                p.step("vector", lambda e, py=py, ytl=ytl, ch=ch: e.tensor_tensor(
                    out=ytl[:, ch * 512:(ch + 1) * 512], in0=py[:],
                    in1=u2f[:, ch * 512:(ch + 1) * 512], op=OP.add))
        for tt in range(2):
            ytl = yt0 if tt == 0 else yt1
            p.step("sync", lambda e, b0=b0, tt=tt: e.dma_start(
                out=xt[:], in_=x[b0 + tt * 128: b0 + tt * 128 + 128, :]), dmas=1)
            p.step("vector", lambda e, ytl=ytl: e.tensor_tensor(out=xt[:], in0=xt[:],
                                                                in1=ytl[:], op=OP.add))
            p.step("sync", lambda e, b0=b0, tt=tt: e.dma_start(
                out=xout[b0 + tt * 128: b0 + tt * 128 + 128, :], in_=xt[:]), dmas=1)
            emit_ffnpre(p, t, xt, b0 + tt * 128)
    return p.build()



def m1_consts():
    sel = np.zeros((16, 128, 128), np.float32)
    for j in range(16):
        for p2 in range(128):
            sel[j, 8 * j + p2 // 16, p2] = 1.0
    sumsel = np.zeros((128, 8), np.float32)
    for p2 in range(128):
        sumsel[p2, p2 // 16] = 1.0
    sel16 = np.zeros((16, 128), np.float32)
    for p2 in range(128):
        sel16[p2 % 16, p2] = 1.0
    return {"selt": sel.reshape(16 * 128, 128).astype(BF),
            "sumsel": sumsel.astype(BF), "sel16": sel16.astype(BF)}


def build_m1(full):
    DI = 2048
    NMT = DI // 128
    nc = bass.Bass("TRN2", target_bir_lowering=False)
    x = nc.dram_tensor("x", [TC, D], F32, kind="ExternalInput").ap()
    alog = nc.dram_tensor("alog", [128, 256], F32, kind="ExternalInput").ap()
    selc = nc.dram_tensor("selc", [16 * 128, 128], BF16, kind="ExternalInput").ap()
    sumselc = nc.dram_tensor("sumselc", [128, 8], BF16, kind="ExternalInput").ap()
    sel16c = nc.dram_tensor("sel16c", [16, 128], BF16, kind="ExternalInput").ap()
    onesrowf = nc.dram_tensor("onesrowf", [1, 128], F32, kind="ExternalInput").ap()
    if not full:
        nw = nc.dram_tensor("nw", [1, D], F32, kind="ExternalInput").ap()
        nb = nc.dram_tensor("nb", [1, D], F32, kind="ExternalInput").ap()
        win = nc.dram_tensor("win", [D, DI], F32, kind="ExternalInput").ap()
        wgt = nc.dram_tensor("wgt", [D, DI], F32, kind="ExternalInput").ap()
        wdt = nc.dram_tensor("wdt", [DI, DI], F32, kind="ExternalInput").ap()
        bdt = nc.dram_tensor("bdt", [128, 16], F32, kind="ExternalInput").ap()
        wB = nc.dram_tensor("wB", [DI, 16], F32, kind="ExternalInput").ap()
        wC = nc.dram_tensor("wC", [DI, 16], F32, kind="ExternalInput").ap()
        ident = nc.dram_tensor("ident", [128, 128], BF16, kind="ExternalInput").ap()
        dts = nc.dram_tensor("dts", [DI, TC], BF16, kind="ExternalOutput").ap()
        dzs = nc.dram_tensor("dzs", [DI, TC], BF16, kind="ExternalOutput").ap()
        gts = nc.dram_tensor("gts", [DI, TC], BF16, kind="ExternalOutput").ap()
        bcr = nc.dram_tensor("bcr", [16, 2 * TC], BF16, kind="ExternalOutput").ap()
        xno = nc.dram_tensor("xnod", [TC, D], F32, kind="ExternalOutput").ap()
        d_hf = nc.dram_tensor("hfin", [128, 256], F32, kind="ExternalOutput").ap()
        d_ff = nc.dram_tensor("ff", [128, 256], F32, kind="ExternalOutput").ap()
    else:
        dts = nc.dram_tensor("dts", [DI, TC], BF16, kind="ExternalInput").ap()
        dzs = nc.dram_tensor("dzs", [DI, TC], BF16, kind="ExternalInput").ap()
        gts = nc.dram_tensor("gts", [DI, TC], BF16, kind="ExternalInput").ap()
        bcr = nc.dram_tensor("bcr", [16, 2 * TC], BF16, kind="ExternalInput").ap()
        xno = nc.dram_tensor("xnod", [TC, D], F32, kind="ExternalInput").ap()
        wout = nc.dram_tensor("wout", [DI, D], F32, kind="ExternalInput").ap()
        hfall = nc.dram_tensor("hfall", [4, 128, 256], F32, kind="ExternalInput").ap()
        fall = nc.dram_tensor("fall", [4, 128, 256], F32, kind="ExternalInput").ap()
        incr = nc.dram_tensor("incr", [1, 4], F32, kind="ExternalInput").ap()
        xout = nc.dram_tensor("xout", [TC, D], F32, kind="ExternalOutput").ap()
        ys = nc.dram_tensor("ys", [DI, TC], BF16, kind="Internal").ap()

    p = Prog(nc)
    onrw = p.sb("onrw", [1, 128], F32)
    selt = p.sb("selt", [128, 16 * 128], BF16)
    sums = p.sb("sums", [128, 8], BF16)
    s16 = p.sb("s16", [16, 128], BF16)
    acl = p.sb("acl", [128, 256], F32)
    bc16 = p.sb("bc16", [16, 2 * TC], BF16)
    xt = p.sb("xt", [128, D], F32)
    if not full:
        idt = p.sb("idt", [128, 128], BF16)
        bdtt = p.sb("bdtt", [128, 16], F32)
        nwrep = p.sb("nwrep", [128, D], F32)
        nbrep = p.sb("nbrep", [128, D], F32)
        wint = p.sb("wint", [128, 8 * DI], BF16)
        wgtt = p.sb("wgtt", [128, 8 * DI], BF16)
        wdtt = p.sb("wdtt", [128, 16 * DI], BF16)
        wbct = p.sb("wbct", [128, 16 * 32], BF16)
        xnof = p.sb("xnof", [128, D], F32)
        x2n = p.sb("x2n", [128, D], BF16)
        x2f = xt[:]
        sq = p.sb("sq", [128, D], F32)
        stats = p.sb("stats", [128, 8], F32)
        x2fm = p.sb("x2fm", [128, 8 * 512], BF16)
        zb = p.sb("zb", [128, 16 * 512], BF16)
        dtf = p.sb("dtf", [128, 512], F32)
        dtb = p.sb("dtb", [128, 512], BF16)
        dzb = p.sb("dzb", [128, 512], BF16)
        sc1 = p.sb("sc1", [128, 512], F32)
        sc2 = p.sb("sc2", [128, 512], F32)
        hfin = p.sb("hfins", [128, 256], F32)
        fbuf = p.sb("fbufs", [128, 256], F32)
        dscol = p.sb("dscol", [128, 1], F32)
        dsbcol = p.sb("dsbcol", [128, 1], BF16)
        wdtf = wdtt[:].bitcast(F32)
        dA = wdtf[:, 0:TC]
        ubuf = wdtf[:, TC:2 * TC]
        hbuf = wdtf[:, 2 * TC:3 * TC]
        wgf = wgtt[:].bitcast(F32)
        brep = wgf[:, 0:TC]
        dtt = wint[:, 0:TC]
        dtzt = wint[:, TC:2 * TC]
        rowbuf = xt[0:1, 0:D]
        pz = p.ps("pz", [128, 512], F32)
        pz2 = p.ps("pz2", [128, 512], F32)
        pb2 = p.ps("pb2", [128, 512], F32)
        pb3 = p.ps("pb3", [128, 512], F32)
        pbc = p.ps("pbc", [16, 512], F32)
        pcol = p.ps("pcol", [128, 1], F32)
        pt1 = p.ps("pt1", [128, 128], BF16)
        pbank = [pz, pz2, pb2, pb3]
    else:
        dtt = p.sb("dtt", [128, TC], BF16)
        dtzt = p.sb("dtzt", [128, TC], BF16)
        dA = p.sb("dA", [128, TC], F32)
        ubuf = p.sb("ubuf", [128, TC], F32)
        hbuf = p.sb("hbuf", [128, TC], F32)
        brep = p.sb("brep", [128, TC], F32)
        crep = p.sb("crep", [128, TC], F32)
        hCt = p.sb("hCt", [128, TC], BF16)
        woutt = p.sb("woutt", [128, 16 * D], BF16)
        yscw = p.sb("yscw", [8, 4 * 512], BF16)
        hfa = p.sb("hfa", [128, 1024], F32)
        ffa = p.sb("ffa", [128, 1024], F32)
        hin = p.sb("hin", [128, 256], F32)
        tb1 = p.sb("tb1", [128, 256], F32)
        tb2 = p.sb("tb2", [128, 256], F32)
        inc4 = p.sb("inc4", [128, 4], F32)
        xnof = p.sb("xnof", [128, D], F32)
        ygb = dA[:].bitcast(BF16)
        gtb = ubuf[:].bitcast(BF16)
        dtt, dtzt, dA, ubuf, hbuf = dtt[:], dtzt[:], dA[:], ubuf[:], hbuf[:]
        brep, crep, hCt = brep[:], crep[:], hCt[:]
        pz = p.ps("pz", [128, 512], F32)
        pz2 = p.ps("pz2", [128, 512], F32)
        pb2 = p.ps("pb2", [128, 512], F32)
        pb3 = p.ps("pb3", [128, 512], F32)
        py = [p.ps(f"py{i}", [8, 512], F32) for i in range(4)]
        pbank = [pz, pz2, pb2, pb3]

    base_loads = [lambda e: [e.dma_start(out=onrw[:], in_=onesrowf[:, :]),
                             e.dma_start(out=selt[:].rearrange("p (a m) -> p a m", a=16),
                                         in_=selc[:, :].rearrange("(a p) m -> p a m", p=128)),
                             e.dma_start(out=sums[:], in_=sumselc[:, :]),
                             e.dma_start(out=s16[:], in_=sel16c[:, :]),
                             e.dma_start(out=acl[:], in_=alog[:, :])]]
    p.step("sync", lambda e: base_loads[0](e), dmas=5)
    p.step("scalar", lambda e: e.activation(out=acl[:], in_=acl[:], func=AF.Exp))
    p.step("vector", lambda e: e.tensor_scalar(out=acl[:], in0=acl[:], scalar1=1e-6,
                                               scalar2=1.0, op0=OP.max, op1=OP.min))
    p.step("vector", lambda e: e.tensor_scalar(out=acl[:], in0=acl[:], scalar1=-1.0,
                                               scalar2=None, op0=OP.mult))

    if not full:
        p.step("sync", lambda e: [e.dma_start(out=idt[:], in_=ident[:, :]),
                                  e.dma_start(out=bdtt[:], in_=bdt[:, :])], dmas=2)
        for (rep, row) in ((nwrep, nw), (nbrep, nb)):
            p.step("sync", lambda e, row=row: e.dma_start(out=rowbuf, in_=row[:, :]), dmas=1)
            for ch in range(2):
                p.step("tensor", lambda e, ch=ch: e.matmul(
                    pz[:], onrw[:], rowbuf[0:1, ch * 512:(ch + 1) * 512],
                    start=True, stop=True))
                p.step("vector", lambda e, rep=rep, ch=ch: e.tensor_copy(
                    rep[:, ch * 512:(ch + 1) * 512], pz[:]))
        p.step("gpsimd", lambda e: [e.dma_start(out=wint[:, i * DI:(i + 1) * DI],
                                                in_=win[i * 128:(i + 1) * 128, :])
                                    for i in range(8)], dmas=8)
        p.step("gpsimd", lambda e: [e.dma_start(out=wgtt[:, i * DI:(i + 1) * DI],
                                                in_=wgt[i * 128:(i + 1) * 128, :])
                                    for i in range(8)], dmas=8)
        p.step("gpsimd", lambda e: [e.dma_start(out=wdtt[:, i * DI:(i + 1) * DI],
                                                in_=wdt[i * 128:(i + 1) * 128, :])
                                    for i in range(16)], dmas=16)
        p.step("gpsimd", lambda e: [e.dma_start(out=wbct[:, i * 32:i * 32 + 16],
                                                in_=wB[i * 128:(i + 1) * 128, :])
                                    for i in range(16)] +
                                   [e.dma_start(out=wbct[:, i * 32 + 16:i * 32 + 32],
                                                in_=wC[i * 128:(i + 1) * 128, :])
                                    for i in range(16)], dmas=32)

        for blk in range(TC // 512):
            t0 = blk * 512
            for tt in range(4):
                tok = t0 + tt * 128
                p.step("sync", lambda e, tok=tok: e.dma_start(out=xt[:],
                                                              in_=x[tok:tok + 128, :]), dmas=1)
                emit_ln(p, xt, xnof, stats, sq)
                p.step("sync", lambda e, tok=tok: e.dma_start(
                    out=xno[tok:tok + 128, :], in_=xnof[:]), dmas=1)
                emit_ln(p, xnof, x2f, stats, sq)
                p.step("vector", lambda e: e.tensor_tensor(out=x2f[:], in0=x2f[:],
                                                           in1=nwrep[:], op=OP.mult))
                p.step("vector", lambda e: e.tensor_tensor(out=x2n[:], in0=x2f[:],
                                                           in1=nbrep[:], op=OP.add))
                emit_tp(p, x2n, (lambda i, tt=tt: x2fm[:, i * 512 + tt * 128:
                                                       i * 512 + tt * 128 + 128]), pt1, idt)
            for mt in range(NMT):
                def fz(e, mt=mt):
                    for i in range(8):
                        r = e.matmul(pz[:], wint[:, i * DI + mt * 128: i * DI + mt * 128 + 128],
                                     x2fm[:, i * 512:(i + 1) * 512],
                                     start=(i == 0), stop=(i == 7))
                    return r
                p.step("tensor", fz)
                p.step("scalar", lambda e, mt=mt: e.activation(
                    out=zb[:, mt * 512:(mt + 1) * 512], in_=pz[:], func=AF.Identity))
            for mt in range(NMT):
                def fg(e, mt=mt):
                    for i in range(8):
                        r = e.matmul(pz2[:], wgtt[:, i * DI + mt * 128: i * DI + mt * 128 + 128],
                                     x2fm[:, i * 512:(i + 1) * 512],
                                     start=(i == 0), stop=(i == 7))
                    return r
                p.step("tensor", fg)
                p.step("scalar", lambda e, mt=mt: e.activation(out=sc1[:], in_=pz2[:],
                                                               func=AF.Silu))
                p.step("vector", lambda e: e.tensor_copy(sc2[:].bitcast(BF16)[:, 0:512],
                                                         sc1[:]))
                p.step("gpsimd", lambda e, mt=mt, t0=t0: e.dma_start(
                    out=gts[mt * 128:(mt + 1) * 128, t0:t0 + 512],
                    in_=sc2[:].bitcast(BF16)[:, 0:512]), dmas=1)
            for (off, woff) in ((0, 0), (TC, 16)):
                def fbc(e, off=off, woff=woff, t0=t0):
                    for i in range(16):
                        r = e.matmul(pbc[:], wbct[:, i * 32 + woff: i * 32 + woff + 16],
                                     zb[:, i * 512:(i + 1) * 512],
                                     start=(i == 0), stop=(i == 15))
                    return r
                p.step("tensor", fbc)
                p.step("vector", lambda e, off=off, t0=t0: e.tensor_copy(
                    bc16[:, off + t0: off + t0 + 512], pbc[:]))
            for mt in range(NMT):
                def fdt(e, mt=mt):
                    for i in range(16):
                        r = e.matmul(pz2[:], wdtt[:, i * DI + mt * 128: i * DI + mt * 128 + 128],
                                     zb[:, i * 512:(i + 1) * 512],
                                     start=(i == 0), stop=(i == 15))
                    return r
                p.step("tensor", fdt)
                p.step("vector", lambda e, mt=mt: e.tensor_scalar(
                    out=dtf[:], in0=pz2[:], scalar1=bdtt[:, mt:mt + 1], scalar2=None,
                    op0=OP.add))
                p.step("scalar", lambda e: e.activation(out=sc1[:], in_=dtf[:], func=AF.Abs))
                p.step("scalar", lambda e: e.activation(out=sc1[:], in_=sc1[:], func=AF.Exp,
                                                        scale=-1.0))
                p.step("scalar", lambda e: e.activation(out=sc1[:], in_=sc1[:], func=AF.Ln,
                                                        bias=1.0))
                p.step("vector", lambda e: e.tensor_relu(sc2[:], dtf[:]))
                p.step("vector", lambda e: e.tensor_tensor(out=dtf[:], in0=sc1[:], in1=sc2[:],
                                                           op=OP.add))
                p.step("vector", lambda e: e.tensor_scalar(out=dtf[:], in0=dtf[:],
                                                           scalar1=0.001, scalar2=1.0,
                                                           op0=OP.max, op1=OP.min))
                p.step("vector", lambda e: e.tensor_copy(dtb[:], dtf[:]))
                p.step("sync", lambda e, mt=mt, t0=t0: e.dma_start(
                    out=dts[mt * 128:(mt + 1) * 128, t0:t0 + 512], in_=dtb[:]), dmas=1)
                p.step("vector", lambda e, mt=mt: e.tensor_tensor(
                    out=dzb[:], in0=dtb[:], in1=zb[:, mt * 512:(mt + 1) * 512], op=OP.mult))
                p.step("sync", lambda e, mt=mt, t0=t0: e.dma_start(
                    out=dzs[mt * 128:(mt + 1) * 128, t0:t0 + 512], in_=dzb[:]), dmas=1)
        p.step("sync", lambda e: e.dma_start(out=bcr[:, :], in_=bc16[:]), dmas=1)
        for cc in range(TC // 512):
            p.step("tensor", lambda e, cc=cc: e.matmul(
                pz[:], s16[:], bc16[:, cc * 512: cc * 512 + 512], start=True, stop=True))
            p.step("vector", lambda e, cc=cc: e.tensor_copy(
                brep[:, cc * 512:(cc + 1) * 512], pz[:]))
    else:
        p.step("sync", lambda e: e.dma_start(out=bc16[:], in_=bcr[:, :]), dmas=1)
        for (src_off, rep) in ((0, brep), (TC, crep)):
            for cc in range(TC // 512):
                p.step("tensor", lambda e, src_off=src_off, cc=cc: e.matmul(
                    pz[:], s16[:], bc16[:, src_off + cc * 512: src_off + cc * 512 + 512],
                    start=True, stop=True))
                p.step("vector", lambda e, rep=rep, cc=cc: e.tensor_copy(
                    rep[:, cc * 512:(cc + 1) * 512], pz[:]))
        p.step("sync", lambda e: [e.dma_start(out=hfa[:].rearrange("p (a m) -> p a m", a=4),
                                              in_=hfall[:, :, :].rearrange("a p m -> p a m")),
                                  e.dma_start(out=ffa[:].rearrange("p (a m) -> p a m", a=4),
                                              in_=fall[:, :, :].rearrange("a p m -> p a m")),
                                  e.dma_start(out=inc4[0:1, 0:4], in_=incr[:, :])], dmas=3)
        p.step("tensor", lambda e: e.matmul(pz[:, 0:4], onrw[:], inc4[0:1, 0:4],
                                            start=True, stop=True))
        p.step("vector", lambda e: e.tensor_copy(inc4[:], pz[:, 0:4]))
        p.step("vector", lambda e: e.memset(hin[:], 0.0))
        for cc in range(4):
            p.step("vector", lambda e, cc=cc: e.tensor_scalar(
                out=tb1[:], in0=ffa[:, cc * 256:(cc + 1) * 256], scalar1=inc4[:, cc:cc + 1],
                scalar2=None, op0=OP.mult))
            p.step("vector", lambda e, cc=cc: e.tensor_scalar(
                out=tb2[:], in0=inc4[:, cc:cc + 1].to_broadcast([128, 256]), scalar1=-1.0,
                scalar2=1.0, op0=OP.mult, op1=OP.add))
            p.step("vector", lambda e: e.tensor_tensor(out=tb1[:], in0=tb1[:], in1=tb2[:],
                                                       op=OP.add))
            p.step("vector", lambda e: e.tensor_tensor(out=hin[:], in0=hin[:], in1=tb1[:],
                                                       op=OP.mult))
            p.step("vector", lambda e, cc=cc: e.tensor_scalar(
                out=tb2[:], in0=hfa[:, cc * 256:(cc + 1) * 256], scalar1=inc4[:, cc:cc + 1],
                scalar2=None, op0=OP.mult))
            p.step("vector", lambda e: e.tensor_tensor(out=hin[:], in0=hin[:], in1=tb2[:],
                                                       op=OP.add))

    NCC = TC // 512
    for k in range(16):
        p.step("sync", lambda e, k=k: e.dma_start(out=dtt, in_=dts[k * 128:(k + 1) * 128, :]),
               dmas=1)
        p.step("sync", lambda e, k=k: e.dma_start(out=dtzt, in_=dzs[k * 128:(k + 1) * 128, :]),
               dmas=1)
        if not full:
            p.step("vector", lambda e: e.reduce_sum(dscol[:, 0:1], dtt, axis=AX))
            p.step("vector", lambda e: e.tensor_copy(dsbcol[:], dscol[:]))
        for j in range(16):
            col = k * 16 + j
            for r in range(NCC // 4):
                def fda(e, j=j, r=r):
                    for q in range(4):
                        cc = r * 4 + q
                        rr = e.matmul(pbank[q][:], selt[:, j * 128:(j + 1) * 128],
                                      dtt[:, cc * 512:(cc + 1) * 512], start=True, stop=True)
                    return rr
                p.step("tensor", fda)
                def feda(e, col=col, r=r):
                    for q in range(4):
                        cc = r * 4 + q
                        rr = e.activation(out=dA[:, cc * 512:(cc + 1) * 512] if full
                                          else dA[:, cc * 512:(cc + 1) * 512],
                                          in_=pbank[q][:], func=AF.Exp,
                                          scale=acl[:, col:col + 1])
                    return rr
                p.step("scalar", feda)
            for r in range(NCC // 4):
                def fdu(e, j=j, r=r):
                    for q in range(4):
                        cc = r * 4 + q
                        rr = e.matmul(pbank[q][:], selt[:, j * 128:(j + 1) * 128],
                                      dtzt[:, cc * 512:(cc + 1) * 512], start=True, stop=True)
                    return rr
                p.step("tensor", fdu)
                def fuu(e, r=r):
                    for q in range(4):
                        cc = r * 4 + q
                        rr = e.tensor_tensor(out=ubuf[:, cc * 512:(cc + 1) * 512],
                                             in0=pbank[q][:],
                                             in1=brep[:, cc * 512:(cc + 1) * 512], op=OP.mult)
                    return rr
                p.step("vector", fuu)
            if full:
                p.step("vector", lambda e, col=col: e.tensor_tensor_scan(
                    out=hbuf[:], data0=dA[:], data1=ubuf[:], initial=hin[:, col:col + 1],
                    op0=OP.mult, op1=OP.add))
                p.step("vector", lambda e: e.tensor_tensor(out=hCt[:], in0=hbuf[:],
                                                           in1=crep[:], op=OP.mult))
                for r in range(NCC // 4):
                    def fys(e, r=r):
                        for q in range(4):
                            cc = r * 4 + q
                            rr = e.matmul(py[q][:], sums[:], hCt[:, cc * 512:(cc + 1) * 512],
                                          start=True, stop=True)
                        return rr
                    p.step("tensor", fys)
                    def fye(e, r=r):
                        for q in range(4):
                            rr = e.activation(out=yscw[0:8, q * 512:(q + 1) * 512],
                                              in_=py[q][:], func=AF.Identity)
                        return rr
                    p.step("scalar", fye)
                    def fyd(e, k=k, j=j, r=r):
                        return [e.dma_start(
                            out=ys[k * 128 + 8 * j: k * 128 + 8 * j + 8,
                                   (r * 4 + q) * 512:(r * 4 + q + 1) * 512],
                            in_=yscw[0:8, q * 512:(q + 1) * 512]) for q in range(4)]
                    p.step("sync", fyd, dmas=4)
            else:
                p.step("vector", lambda e: e.tensor_tensor_scan(
                    out=hbuf, data0=dA, data1=ubuf, initial=0.0,
                    op0=OP.mult, op1=OP.add))
                p.step("vector", lambda e, col=col: e.tensor_copy(hfin[:, col:col + 1],
                                                                  hbuf[:, TC - 1:TC]))
                p.step("tensor", lambda e, j=j: e.matmul(
                    pcol[:], selt[:, j * 128:(j + 1) * 128], dsbcol[:, 0:1],
                    start=True, stop=True))
                p.step("scalar", lambda e, col=col: e.activation(
                    out=fbuf[:, col:col + 1], in_=pcol[:], func=AF.Exp,
                    scale=acl[:, col:col + 1]))
    if not full:
        p.step("sync", lambda e: [e.dma_start(out=d_hf[:, :], in_=hfin[:]),
                                  e.dma_start(out=d_ff[:, :], in_=fbuf[:])], dmas=2)
        return p.build()
    p.step("gpsimd", lambda e: [e.dma_start(out=woutt[:, i * D:(i + 1) * D],
                                            in_=wout[i * 128:(i + 1) * 128, :])
                                for i in range(16)], dmas=16)
    for blk in range(TC // 512):
        t0 = blk * 512
        p.step("sync", lambda e, t0=t0: e.dma_start(
            out=ygb.rearrange("p (a m) -> p a m", a=16),
            in_=ys[:, t0:t0 + 512].rearrange("(a p) m -> p a m", p=128)), dmas=1)
        p.step("sync", lambda e, t0=t0: e.dma_start(
            out=gtb.rearrange("p (a m) -> p a m", a=16),
            in_=gts[:, t0:t0 + 512].rearrange("(a p) m -> p a m", p=128)), dmas=1)
        p.step("vector", lambda e: e.tensor_tensor(out=ygb, in0=ygb, in1=gtb, op=OP.mult))
        for tt in range(4):
            tok = t0 + tt * 128
            p.step("sync", lambda e, tok=tok: e.dma_start(out=xt[:], in_=x[tok:tok + 128, :]),
                   dmas=1)
            p.step("sync", lambda e, tok=tok: e.dma_start(out=xnof[:],
                                                          in_=xno[tok:tok + 128, :]), dmas=1)
            p.step("vector", lambda e: e.tensor_tensor(out=xt[:], in0=xt[:], in1=xnof[:],
                                                       op=OP.add))
            for ch in range(2):
                def fo(e, tt=tt, ch=ch):
                    for i in range(16):
                        r = e.matmul(pz[:], ygb[:, i * 512 + tt * 128: i * 512 + tt * 128 + 128],
                                     woutt[:, i * D + ch * 512: i * D + ch * 512 + 512],
                                     start=(i == 0), stop=(i == 15))
                    return r
                p.step("tensor", fo)
                p.step("vector", lambda e, ch=ch: e.tensor_tensor(
                    out=xt[:, ch * 512:(ch + 1) * 512], in0=xt[:, ch * 512:(ch + 1) * 512],
                    in1=pz[:], op=OP.add))
            p.step("sync", lambda e, tok=tok: e.dma_start(out=xout[tok:tok + 128, :],
                                                          in_=xt[:]), dmas=1)
    return p.build()



def build_ffn2():
    HF = 2048
    nc = bass.Bass("TRN2", target_bir_lowering=False)
    x = nc.dram_tensor("x", [TC, D], F32, kind="ExternalInput").ap()
    lng = nc.dram_tensor("lng", [1, D], F32, kind="ExternalInput").ap()
    lnb = nc.dram_tensor("lnb", [1, D], F32, kind="ExternalInput").ap()
    rw = nc.dram_tensor("rw", [D, 8], F32, kind="ExternalInput").ap()
    rb = nc.dram_tensor("rb", [1, 8], F32, kind="ExternalInput").ap()
    w1 = nc.dram_tensor("w1", [8, D, HF], F32, kind="ExternalInput").ap()
    b1r = nc.dram_tensor("b1r", [8 * 128, 16], F32, kind="ExternalInput").ap()
    w2 = nc.dram_tensor("w2", [8, HF, D], F32, kind="ExternalInput").ap()
    b2 = nc.dram_tensor("b2", [8, D], F32, kind="ExternalInput").ap()
    ident = nc.dram_tensor("ident", [128, 128], BF16, kind="ExternalInput").ap()
    identf = nc.dram_tensor("identf", [128, 128], F32, kind="ExternalInput").ap()
    onesrowf = nc.dram_tensor("onesrowf", [1, 128], F32, kind="ExternalInput").ap()
    xout = nc.dram_tensor("xout", [TC, D], F32, kind="ExternalOutput").ap()
    xno = nc.dram_tensor("xno", [TC, D], F32, kind="Internal").ap()

    p = Prog(nc)
    idt = p.sb("idt", [128, 128], BF16)
    idtf = p.sb("idtf", [128, 128], F32)
    onrw = p.sb("onrw", [1, 128], F32)
    lngrep = p.sb("lngrep", [128, D], F32)
    lnbrep = p.sb("lnbrep", [128, D], F32)
    rwt = p.sb("rwt", [128, 8 * 8], BF16)
    rbrep = p.sb("rbrep", [128, 8], F32)
    b2t = p.sb("b2t", [8, D], F32)
    w1t = p.sb("w1t", [128, 8 * HF], BF16)
    w2t = p.sb("w2t", [128, 16 * D], BF16)
    b1c = p.sb("b1c", [128, 16], F32)
    xt = p.sb("xt", [128, D], F32)
    xnof = p.sb("xnof", [128, D], F32)
    x2n = p.sb("x2n", [128, D], BF16)
    sq = p.sb("sq", [128, D], F32)
    stats = p.sb("stats", [128, 8], F32)
    zfm = p.sb("zfm", [128, 8 * 1024], BF16)
    acc = p.sb("acc", [128, 8 * 1024], F32)
    h1f = p.sb("h1f", [128, 16 * 512], BF16)
    wrepb = p.sb("wrepb", [128, 512], BF16)
    lgt = p.sb("lgt", [128, 8], F32)
    prt = p.sb("prt", [128, 8], F32)
    eqt = p.sb("eqt", [128, 8], F32)
    cum = p.sb("cum", [128, 8], F32)
    fst = p.sb("fst", [128, 8], F32)
    p2t = p.sb("p2t", [128, 8], F32)
    sect = p.sb("sect", [128, 8], F32)
    wgt8 = p.sb("wgt8", [128, 8], F32)
    on8 = p.sb("on8", [128, 8], F32)
    cols = p.sb("cols", [128, 8], F32)
    wpad = p.sb("wpad", [128, 128], F32)
    wTsb = p.sb("wTsb", [8, 1024], F32)
    wTflat = p.sb("wTflat", [1, 8 * 1024], F32)
    pz = p.ps("pz", [128, 512], F32)
    pz2 = p.ps("pz2", [128, 512], F32)
    pg8 = p.ps("pg8", [128, 8], F32)
    ptw = p.ps("ptw", [128, 128], F32)
    pt1 = p.ps("pt1", [128, 128], BF16)

    x2f = xt[:]
    rowb = xt[0:1, 0:D]
    p.step("sync", lambda e: [e.dma_start(out=idt[:], in_=ident[:, :]),
                              e.dma_start(out=idtf[:], in_=identf[:, :]),
                              e.dma_start(out=onrw[:], in_=onesrowf[:, :]),
                              e.dma_start(out=b2t[:], in_=b2[:, :])], dmas=4)
    p.step("gpsimd", lambda e: [e.dma_start(out=rwt[:, i * 8:(i + 1) * 8],
                                            in_=rw[i * 128:(i + 1) * 128, :])
                                for i in range(8)], dmas=8)
    p.step("vector", lambda e: e.memset(on8[:], 1.0))
    for (rep, row) in ((lngrep, lng), (lnbrep, lnb)):
        p.step("sync", lambda e, row=row: e.dma_start(out=rowb, in_=row[:, :]), dmas=1)
        for ch in range(2):
            p.step("tensor", lambda e, ch=ch: e.matmul(
                pz[:], onrw[:], rowb[0:1, ch * 512:(ch + 1) * 512], start=True, stop=True))
            p.step("vector", lambda e, rep=rep, ch=ch: e.tensor_copy(
                rep[:, ch * 512:(ch + 1) * 512], pz[:]))
    p.step("sync", lambda e: e.dma_start(out=wpad[0:1, 0:8], in_=rb[:, :]), dmas=1)
    p.step("tensor", lambda e: e.matmul(pg8[:], onrw[:], wpad[0:1, 0:8], start=True, stop=True))
    p.step("vector", lambda e: e.tensor_copy(rbrep[:], pg8[:]))
    p.step("vector", lambda e: e.memset(wpad[:], 0.0))

    for blk in range(4):
        t0 = blk * 1024
        for tt in range(8):
            tok = t0 + tt * 128
            p.step("sync", lambda e, tok=tok: e.dma_start(out=xt[:], in_=x[tok:tok + 128, :]),
                   dmas=1)
            emit_ln(p, xt, xnof, stats, sq)
            p.step("sync", lambda e, tok=tok: e.dma_start(out=xno[tok:tok + 128, :],
                                                          in_=xnof[:]), dmas=1)
            emit_ln(p, xnof, x2f, stats, sq)
            p.step("vector", lambda e: e.tensor_tensor(out=x2f[:], in0=x2f[:], in1=lngrep[:],
                                                       op=OP.mult))
            p.step("vector", lambda e: e.tensor_tensor(out=x2n[:], in0=x2f[:], in1=lnbrep[:],
                                                       op=OP.add))
            emit_tp(p, x2n, (lambda i, tt=tt: zfm[:, i * 1024 + tt * 128:
                                                  i * 1024 + tt * 128 + 128]), pt1, idt)
            def frt(e, tt=tt):
                for i in range(8):
                    r = e.matmul(pg8[:], zfm[:, i * 1024 + tt * 128: i * 1024 + tt * 128 + 128],
                                 rwt[:, i * 8:(i + 1) * 8], start=(i == 0), stop=(i == 7))
                return r
            p.step("tensor", frt)
            p.step("vector", lambda e: e.tensor_tensor(out=lgt[:], in0=pg8[:], in1=rbrep[:],
                                                       op=OP.add))
            p.step("vector", lambda e: e.reduce_max(stats[:, 0:1], lgt[:], axis=AX, negate=True))
            p.step("scalar", lambda e: e.activation(out=prt[:], in_=lgt[:], func=AF.Exp,
                                                    bias=stats[:, 0:1],
                                                    accum_out=stats[:, 1:2]))
            p.step("vector", lambda e: e.reciprocal(stats[:, 2:3], stats[:, 1:2]))
            p.step("vector", lambda e: e.tensor_scalar(out=prt[:], in0=prt[:],
                                                       scalar1=stats[:, 2:3], scalar2=None,
                                                       op0=OP.mult))
            p.step("vector", lambda e: e.reduce_max(stats[:, 3:4], prt[:], axis=AX))
            p.step("vector", lambda e: e.tensor_scalar(out=eqt[:], in0=prt[:],
                                                       scalar1=stats[:, 3:4], scalar2=None,
                                                       op0=OP.is_ge))
            p.step("vector", lambda e: e.tensor_tensor_scan(out=cum[:], data0=on8[:],
                                                            data1=eqt[:], initial=0.0,
                                                            op0=OP.mult, op1=OP.add))
            p.step("vector", lambda e: e.tensor_scalar(out=cum[:], in0=cum[:], scalar1=1.0,
                                                       scalar2=None, op0=OP.is_le))
            p.step("vector", lambda e: e.tensor_tensor(out=fst[:], in0=eqt[:], in1=cum[:],
                                                       op=OP.mult))
            p.step("vector", lambda e: e.tensor_scalar(out=cols[:], in0=fst[:], scalar1=-1.0,
                                                       scalar2=1.0, op0=OP.mult, op1=OP.add))
            p.step("vector", lambda e: e.tensor_tensor(out=p2t[:], in0=prt[:], in1=cols[:],
                                                       op=OP.mult))
            p.step("vector", lambda e: e.reduce_max(stats[:, 4:5], p2t[:], axis=AX))
            p.step("vector", lambda e: e.tensor_scalar(out=eqt[:], in0=p2t[:],
                                                       scalar1=stats[:, 4:5], scalar2=None,
                                                       op0=OP.is_ge))
            p.step("vector", lambda e: e.tensor_tensor_scan(out=cum[:], data0=on8[:],
                                                            data1=eqt[:], initial=0.0,
                                                            op0=OP.mult, op1=OP.add))
            p.step("vector", lambda e: e.tensor_scalar(out=cum[:], in0=cum[:], scalar1=1.0,
                                                       scalar2=None, op0=OP.is_le))
            p.step("vector", lambda e: e.tensor_tensor(out=sect[:], in0=eqt[:], in1=cum[:],
                                                       op=OP.mult))
            p.step("vector", lambda e: e.tensor_tensor(out=stats[:, 5:6], in0=stats[:, 3:4],
                                                       in1=stats[:, 4:5], op=OP.add))
            p.step("vector", lambda e: e.tensor_scalar(out=stats[:, 5:6], in0=stats[:, 5:6],
                                                       scalar1=1e-8, scalar2=None, op0=OP.max))
            p.step("vector", lambda e: e.reciprocal(stats[:, 6:7], stats[:, 5:6]))
            p.step("vector", lambda e: e.tensor_scalar(out=wgt8[:], in0=fst[:],
                                                       scalar1=stats[:, 3:4], scalar2=None,
                                                       op0=OP.mult))
            p.step("vector", lambda e: e.scalar_tensor_tensor(out=wgt8[:], in0=sect[:],
                                                              scalar=stats[:, 4:5],
                                                              in1=wgt8[:], op0=OP.mult,
                                                              op1=OP.add))
            p.step("vector", lambda e: e.tensor_scalar(out=wgt8[:], in0=wgt8[:],
                                                       scalar1=stats[:, 6:7], scalar2=None,
                                                       op0=OP.mult))
            p.step("vector", lambda e: e.tensor_copy(wpad[:, 0:8], wgt8[:]))
            p.step("tensor", lambda e: e.transpose(ptw[:], wpad[:], idtf[:]))
            p.step("vector", lambda e, tt=tt: e.tensor_copy(wTsb[0:8, tt * 128:(tt + 1) * 128],
                                                            ptw[0:8, 0:128]))
        p.step("gpsimd", lambda e: [e.dma_start(out=wTflat[0:1, ex2 * 1024:(ex2 + 1) * 1024],
                                                in_=wTsb[ex2:ex2 + 1, :])
                                    for ex2 in range(8)], dmas=8)
        p.step("vector", lambda e: e.memset(acc[:], 0.0))
        for ex in range(8):
            p.step("gpsimd", lambda e, ex=ex: [e.dma_start(out=w1t[:, i * HF:(i + 1) * HF],
                                                           in_=w1[ex, i * 128:(i + 1) * 128, :])
                                               for i in range(8)], dmas=8)
            p.step("gpsimd", lambda e, ex=ex: [e.dma_start(out=w2t[:, i * D:(i + 1) * D],
                                                           in_=w2[ex, i * 128:(i + 1) * 128, :])
                                               for i in range(16)], dmas=16)
            p.step("sync", lambda e, ex=ex: e.dma_start(out=b1c[:], in_=b1r[ex * 128:(ex + 1) * 128, :]),
                   dmas=1)
            for sb2 in range(2):
                c0 = sb2 * 512
                for mt in range(16):
                    def fh1(e, mt=mt, c0=c0):
                        for i in range(8):
                            r = e.matmul(pz[:], w1t[:, i * HF + mt * 128: i * HF + mt * 128 + 128],
                                         zfm[:, i * 1024 + c0: i * 1024 + c0 + 512],
                                         start=(i == 0), stop=(i == 7))
                        return r
                    p.step("tensor", fh1)
                    p.step("scalar", lambda e, mt=mt: e.activation(
                        out=h1f[:, mt * 512:(mt + 1) * 512], in_=pz[:], func=AF.Gelu,
                        bias=b1c[:, mt:mt + 1]))
                p.step("tensor", lambda e, ex=ex, c0=c0: e.matmul(
                    pz2[:], onrw[:], wTflat[0:1, ex * 1024 + c0: ex * 1024 + c0 + 512],
                    start=True, stop=True))
                p.step("vector", lambda e: e.tensor_copy(wrepb[:], pz2[:]))
                for mt in range(16):
                    p.step("vector", lambda e, mt=mt: e.tensor_tensor(
                        out=h1f[:, mt * 512:(mt + 1) * 512],
                        in0=h1f[:, mt * 512:(mt + 1) * 512], in1=wrepb[:], op=OP.mult))
                for ot in range(8):
                    for kb in range(2):
                        def fg2(e, ot=ot, kb=kb):
                            for i in range(8):
                                k = kb * 8 + i
                                r = e.matmul(pz2[:], w2t[:, k * D + ot * 128: k * D + ot * 128 + 128],
                                             h1f[:, k * 512:(k + 1) * 512],
                                             start=(k == 0), stop=(k == 15))
                            return r
                        p.step("tensor", fg2)
                    p.step("vector", lambda e, ot=ot, c0=c0: e.tensor_tensor(
                        out=acc[:, ot * 1024 + c0: ot * 1024 + c0 + 512],
                        in0=acc[:, ot * 1024 + c0: ot * 1024 + c0 + 512], in1=pz2[:],
                        op=OP.add))
        for ot in range(8):
            for sb2 in range(2):
                c0 = sb2 * 512
                p.step("tensor", lambda e, ot=ot, c0=c0: e.matmul(
                    pz2[:], b2t[:, ot * 128:(ot + 1) * 128], wTsb[0:8, c0:c0 + 512],
                    start=True, stop=True))
                p.step("vector", lambda e, ot=ot, c0=c0: e.tensor_tensor(
                    out=acc[:, ot * 1024 + c0: ot * 1024 + c0 + 512],
                    in0=acc[:, ot * 1024 + c0: ot * 1024 + c0 + 512], in1=pz2[:],
                    op=OP.add))
        for tt in range(8):
            tok = t0 + tt * 128
            p.step("sync", lambda e, tok=tok: [e.dma_start(out=xt[:], in_=x[tok:tok + 128, :]),
                                               e.dma_start(out=xnof[:], in_=xno[tok:tok + 128, :])],
                   dmas=2)
            p.step("vector", lambda e: e.tensor_tensor(out=xnof[:], in0=xnof[:], in1=xt[:],
                                                       op=OP.add))
            for j in range(8):
                p.step("tensor", lambda e, j=j, tt=tt: e.transpose(
                    ptw[:], acc[:, j * 1024 + tt * 128: j * 1024 + tt * 128 + 128], idtf[:]))
                p.step("vector", lambda e, j=j: e.tensor_tensor(
                    out=xnof[:, j * 128:(j + 1) * 128], in0=xnof[:, j * 128:(j + 1) * 128],
                    in1=ptw[:], op=OP.add))
            p.step("sync", lambda e, tok=tok: e.dma_start(out=xout[tok:tok + 128, :],
                                                          in_=xnof[:]), dmas=1)
    return p.build()



def m2_consts():
    mask01 = np.zeros((128, 128), np.float32)
    for i in range(128):
        mask01[i, :i] = 1.0
    return {"mask01": mask01}


def build_m2tails():
    DI2 = 2048
    nc = bass.Bass("TRN2", target_bir_lowering=False)
    x = nc.dram_tensor("x", [TC, D], F32, kind="ExternalInput").ap()
    inpT = nc.dram_tensor("inpT", [D, 4384], F32, kind="ExternalInput").ap()
    ident = nc.dram_tensor("ident", [128, 128], BF16, kind="ExternalInput").ap()
    tails = nc.dram_tensor("tails", [DI2, 3], F32, kind="ExternalOutput").ap()
    p = Prog(nc)
    idt = p.sb("idt", [128, 128], BF16)
    wt = p.sb("wt", [128, 8 * DI2], BF16)
    xt = p.sb("xt", [128, D], F32)
    xn = p.sb("xn", [128, D], BF16)
    sq = p.sb("sq", [128, D], F32)
    stats = p.sb("stats", [128, 8], F32)
    xnfm = p.sb("xnfm", [128, 8 * 128], BF16)
    ev = p.sb("ev", [128, 3], F32)
    pm = p.ps("pm", [128, 128], F32)
    pt1 = p.ps("pt1", [128, 128], BF16)
    p.step("sync", lambda e: e.dma_start(out=idt[:], in_=ident[:, :]), dmas=1)
    p.step("gpsimd", lambda e: [e.dma_start(out=wt[:, i * DI2:(i + 1) * DI2],
                                            in_=inpT[i * 128:(i + 1) * 128, 0:DI2])
                                for i in range(8)], dmas=8)
    p.step("sync", lambda e: e.dma_start(out=xt[:], in_=x[TC - 128:TC, :]), dmas=1)
    emit_ln(p, xt, xn, stats, sq)
    emit_tp(p, xn, (lambda j: xnfm[:, j * 128:(j + 1) * 128]), pt1, idt)
    for mt in range(16):
        def fp(e, mt=mt):
            for i in range(8):
                r = e.matmul(pm[:], wt[:, i * DI2 + mt * 128: i * DI2 + mt * 128 + 128],
                             xnfm[:, i * 128:(i + 1) * 128], start=(i == 0), stop=(i == 7))
            return r
        p.step("tensor", fp)
        p.step("vector", lambda e: e.tensor_copy(ev[:], pm[:, 125:128]))
        p.step("sync", lambda e, mt=mt: e.dma_start(out=tails[mt * 128:(mt + 1) * 128, :],
                                                    in_=ev[:]), dmas=1)
    return p.build()


def build_m2(full):
    DI2, NH, HD, DS = 2048, 32, 64, 128
    NCH = TC // 128
    nc = bass.Bass("TRN2", target_bir_lowering=False)
    x = nc.dram_tensor("x", [TC, D], F32, kind="ExternalInput").ap()
    inpT = nc.dram_tensor("inpT", [D, 4384], F32, kind="ExternalInput").ap()
    cw = nc.dram_tensor("cw", [128, 16 * 4], F32, kind="ExternalInput").ap()
    cb = nc.dram_tensor("cb", [128, 16], F32, kind="ExternalInput").ap()
    alog = nc.dram_tensor("alog", [1, NH], F32, kind="ExternalInput").ap()
    dpr = nc.dram_tensor("dpr", [1, NH], F32, kind="ExternalInput").ap()
    dtbias = nc.dram_tensor("dtbias", [NH, 1], F32, kind="ExternalInput").ap()
    tails = nc.dram_tensor("tails", [DI2, 3], F32, kind="ExternalInput").ap()
    mask01 = nc.dram_tensor("mask01", [128, 128], F32, kind="ExternalInput").ap()
    triu = nc.dram_tensor("triu", [128, 128], BF16, kind="ExternalInput").ap()
    ident = nc.dram_tensor("ident", [128, 128], BF16, kind="ExternalInput").ap()
    identf = nc.dram_tensor("identf", [128, 128], F32, kind="ExternalInput").ap()
    onesrowf = nc.dram_tensor("onesrowf", [1, 128], F32, kind="ExternalInput").ap()
    xss = nc.dram_tensor("xss", [DI2, TC], BF16, kind="Internal").ap()
    xsls = nc.dram_tensor("xsls", [DI2, TC], BF16, kind="Internal").ap()
    xslt = nc.dram_tensor("xslt", [TC, DI2], BF16, kind="Internal").ap()
    xdts = nc.dram_tensor("xdts", [TC, DI2], BF16, kind="Internal").ap()
    zss = nc.dram_tensor("zss", [TC, DI2], BF16, kind="Internal").ap()
    if full:
        nw2 = nc.dram_tensor("nw2", [1, DI2], F32, kind="ExternalInput").ap()
        outT = nc.dram_tensor("outT", [DI2, D], F32, kind="ExternalInput").ap()
        hstall = nc.dram_tensor("hstall", [4, 128, NH * HD], F32, kind="ExternalInput").ap()
        ftall = nc.dram_tensor("ftall", [4, 128, NH], F32, kind="ExternalInput").ap()
        incr = nc.dram_tensor("incr", [1, 4], F32, kind="ExternalInput").ap()
        xout = nc.dram_tensor("xout", [TC, D], F32, kind="ExternalOutput").ap()
    else:
        d_hst = nc.dram_tensor("hst", [128, NH * HD], F32, kind="ExternalOutput").ap()
        d_ft = nc.dram_tensor("ft", [128, NH], F32, kind="ExternalOutput").ap()

    p = Prog(nc)
    idt = p.sb("idt", [128, 128], BF16)
    idtf = p.sb("idtf", [128, 128], F32)
    trut = p.sb("trut", [128, 128], BF16)
    onrw = p.sb("onrw", [1, 128], F32)
    onescolb = p.sb("onescolb", [128, 1], BF16)
    msk = p.sb("msk", [128, 128], F32)
    inptt = p.sb("inptt", [128, 8 * 4384], BF16)
    cwt = p.sb("cwt", [128, 16 * 4], F32)
    cbt = p.sb("cbt", [128, 16], F32)
    arow = p.sb("arow", [1, NH], F32)
    dprow = p.sb("dprow", [1, NH], F32)
    dtbc = p.sb("dtbc", [NH, 1], F32)
    arep = p.sb("arep", [128, NH], F32)
    dp8 = p.sb("dp8", [128, NH], F32)
    dp8b = p.sb("dp8b", [128, NH], BF16)
    xt = p.sb("xt", [128, D], F32)
    xn = p.sb("xn", [128, D], BF16)
    sq = p.sb("sq", [128, D], F32)
    stats = p.sb("stats", [128, 8], F32)
    xnfm = p.sb("xnfm", [128, 8 * 512], BF16)
    bfm = p.sb("bfm", [128, TC], BF16)
    cfm = p.sb("cfm", [128, TC], BF16)
    dti = p.sb("dti", [NH, TC], F32)
    dttm = p.sb("dttm", [128, NCH * NH], F32)
    dttmb = p.sb("dttmb", [128, NCH * NH], BF16)
    xe = p.sb("xe", [128, 4 + TC], BF16)
    ipfv = inptt[:].bitcast(F32)
    cacc = ipfv[:, 10240:10240 + TC]
    hst = p.sb("hstl", [128, NH * HD], F32)
    cdc = p.sb("cdc", [128, NH], F32)
    acd = p.sb("acd", [128, NH], F32)
    epl = p.sb("epl", [128, NH], F32)
    srep = p.sb("srep", [128, NH], F32)
    wdtm = p.sb("wdtm", [128, NH], F32)
    fmat = p.sb("fmat", [128, NH], F32)
    cpad = p.sb("cpad", [128, 128], F32)
    cdT = p.sb("cdT", [32, 128], F32)
    cdTb = p.sb("cdTb", [32, 128], BF16)
    ncdTb = p.sb("ncdTb", [32, 128], BF16)
    lhsT2 = p.sb("lhsT2", [2, 128], BF16)
    rhs2 = p.sb("rhs2", [2, 128], BF16)
    asb = p.sb("asb", [128, 128], F32)
    ssb = p.sb("ssb", [128, 128], BF16)
    btmf = p.sb("btmf", [128, 128], F32)
    bwb = p.sb("bwb", [128, 128], BF16)
    xdtc = p.sb("xdtc", [128, DI2], BF16)
    xsltc = p.sb("xsltc", [128, DI2], BF16)
    zsc = p.sb("zsc", [128, DI2], BF16)
    ycar = p.sb("ycar", [128, 512], F32)
    dterm = p.sb("dterm", [128, 512], F32)
    ych = p.sb("ych", [128, DI2], F32)
    ygb = p.sb("ygb", [128, DI2], BF16)
    ygfm = p.sb("ygfm", [128, 16 * 128], BF16)
    row2 = p.sb("row2", [1, DI2], F32)
    nwrep2 = ipfv[:, 8192:8192 + DI2]
    hstb = inptt[:, 20480:20480 + DI2]
    pm = p.ps("pm", [128, 512], F32)
    pm2 = p.ps("pm2", [128, 512], F32)
    py = p.ps("py", [128, 512], F32)
    pc = p.ps("pc", [128, 512], F32)
    pcon = p.ps("pcon", [128, 512], F32)
    pt1 = p.ps("pt1", [128, 128], BF16)
    ptf = p.ps("ptf", [128, 128], F32)

    p.step("sync", lambda e: [e.dma_start(out=idt[:], in_=ident[:, :]),
                              e.dma_start(out=idtf[:], in_=identf[:, :]),
                              e.dma_start(out=trut[:], in_=triu[:, :]),
                              e.dma_start(out=onrw[:], in_=onesrowf[:, :]),
                              e.dma_start(out=msk[:], in_=mask01[:, :]),
                              e.dma_start(out=cwt[:], in_=cw[:, :]),
                              e.dma_start(out=cbt[:], in_=cb[:, :]),
                              e.dma_start(out=arow[:], in_=alog[:, :]),
                              e.dma_start(out=dprow[:], in_=dpr[:, :]),
                              e.dma_start(out=dtbc[:], in_=dtbias[:, :])], dmas=10)
    p.step("gpsimd", lambda e: [e.dma_start(out=inptt[:, i * 4384:(i + 1) * 4384],
                                            in_=inpT[i * 128:(i + 1) * 128, :])
                                for i in range(8)], dmas=8)
    p.step("vector", lambda e: e.memset(onescolb[:], 1.0))
    p.step("vector", lambda e: e.memset(cpad[:], 0.0))
    p.step("scalar", lambda e: e.activation(out=arow[:], in_=arow[:], func=AF.Exp))
    p.step("vector", lambda e: e.tensor_scalar(out=arow[:], in0=arow[:], scalar1=-1.0,
                                               scalar2=None, op0=OP.mult))
    for (rep, row) in ((arep, arow), (dp8, dprow)):
        p.step("tensor", lambda e, row=row: e.matmul(pm[:, 0:NH], onrw[:], row[0:1, :],
                                                     start=True, stop=True))
        p.step("vector", lambda e, rep=rep: e.tensor_copy(rep[:], pm[:, 0:NH]))
    p.step("vector", lambda e: e.tensor_copy(dp8b[:], dp8[:]))
    for blk in range(TC // 512):
        t0 = blk * 512
        for tt in range(4):
            tok = t0 + tt * 128
            p.step("sync", lambda e, tok=tok: e.dma_start(out=xt[:], in_=x[tok:tok + 128, :]),
                   dmas=1)
            emit_ln(p, xt, xn, stats, sq)
            emit_tp(p, xn, (lambda i, tt=tt: xnfm[:, i * 512 + tt * 128:
                                                  i * 512 + tt * 128 + 128]), pt1, idt)
        for mt in range(16):
            def fxs(e, mt=mt):
                for i in range(8):
                    r = e.matmul(pm[:], inptt[:, i * 4384 + mt * 128: i * 4384 + mt * 128 + 128],
                                 xnfm[:, i * 512:(i + 1) * 512], start=(i == 0), stop=(i == 7))
                return r
            p.step("tensor", fxs)
            p.step("vector", lambda e: e.tensor_copy(xe[:, 0:512], pm[:]))
            p.step("sync", lambda e, mt=mt, t0=t0: e.dma_start(
                out=xss[mt * 128:(mt + 1) * 128, t0:t0 + 512], in_=xe[:, 0:512]), dmas=1)
        for tt in range(4):
            for ch in range(4):
                def fz(e, tt=tt, ch=ch):
                    for i in range(8):
                        r = e.matmul(pm2[:], xnfm[:, i * 512 + tt * 128: i * 512 + tt * 128 + 128],
                                     inptt[:, i * 4384 + DI2 + ch * 512:
                                           i * 4384 + DI2 + ch * 512 + 512],
                                     start=(i == 0), stop=(i == 7))
                    return r
                p.step("tensor", fz)
                p.step("scalar", lambda e, ch=ch: e.activation(out=ygb[:, ch * 512:(ch + 1) * 512],
                                                               in_=pm2[:], func=AF.Silu))
            p.step("sync", lambda e, tt=tt, t0=t0: e.dma_start(
                out=zss[t0 + tt * 128: t0 + tt * 128 + 128, :], in_=ygb[:]), dmas=1)
        for (dst, woff) in ((bfm, 2 * DI2), (cfm, 2 * DI2 + DS)):
            def fbc(e, woff=woff):
                for i in range(8):
                    r = e.matmul(pm[:], inptt[:, i * 4384 + woff: i * 4384 + woff + 128],
                                 xnfm[:, i * 512:(i + 1) * 512], start=(i == 0), stop=(i == 7))
                return r
            p.step("tensor", fbc)
            p.step("vector", lambda e, dst=dst, t0=t0: e.tensor_copy(
                dst[:, t0:t0 + 512], pm[:]))
        def fdt(e):
            for i in range(8):
                r = e.matmul(pm[0:NH, :], inptt[:, i * 4384 + 2 * DI2 + 2 * DS:
                                                i * 4384 + 2 * DI2 + 2 * DS + NH],
                             xnfm[:, i * 512:(i + 1) * 512], start=(i == 0), stop=(i == 7))
            return r
        p.step("tensor", fdt)
        p.step("vector", lambda e, t0=t0: e.tensor_copy(dti[:, t0:t0 + 512], pm[0:NH, :]))
    p.step("vector", lambda e: e.tensor_scalar(out=dti[:], in0=dti[:], scalar1=dtbc[:, 0:1],
                                               scalar2=None, op0=OP.add))
    p.step("scalar", lambda e: e.activation(out=cacc[0:NH, 0:TC], in_=dti[:], func=AF.Abs))
    p.step("scalar", lambda e: e.activation(out=cacc[0:NH, 0:TC], in_=cacc[0:NH, 0:TC],
                                            func=AF.Exp, scale=-1.0))
    p.step("scalar", lambda e: e.activation(out=cacc[0:NH, 0:TC], in_=cacc[0:NH, 0:TC],
                                            func=AF.Ln, bias=1.0))
    p.step("vector", lambda e: e.tensor_relu(dti[:], dti[:]))
    p.step("vector", lambda e: e.tensor_tensor(out=dti[:], in0=dti[:],
                                               in1=cacc[0:NH, 0:TC], op=OP.add))
    for c in range(NCH):
        p.step("vector", lambda e, c=c: e.tensor_copy(cpad[0:NH, 0:128],
                                                      dti[:, c * 128:(c + 1) * 128]))
        p.step("tensor", lambda e: e.transpose(ptf[:], cpad[:], idtf[:]))
        p.step("vector", lambda e, c=c: e.tensor_copy(dttm[:, c * NH:(c + 1) * NH],
                                                      ptf[:, 0:NH]))
    p.step("vector", lambda e: e.tensor_copy(dttmb[:], dttm[:]))
    for ct in range(16):
        p.step("sync", lambda e, ct=ct: e.dma_start(out=xe[:, 4:4 + TC],
                                                    in_=xss[ct * 128:(ct + 1) * 128, :]), dmas=1)
        p.step("gpsimd", lambda e, ct=ct: e.dma_start(out=xe[:, 1:4],
                                                      in_=tails[ct * 128:(ct + 1) * 128, :]),
               dmas=1)
        p.step("vector", lambda e, ct=ct: e.tensor_scalar(
            out=cacc[:], in0=xe[:, 1:1 + TC], scalar1=cwt[:, ct * 4:ct * 4 + 1],
            scalar2=None, op0=OP.mult))
        for k in range(1, 4):
            p.step("vector", lambda e, ct=ct, k=k: e.scalar_tensor_tensor(
                out=cacc[:], in0=xe[:, 1 + k:1 + k + TC], scalar=cwt[:, ct * 4 + k:ct * 4 + k + 1],
                in1=cacc[:], op0=OP.mult, op1=OP.add))
        p.step("vector", lambda e, ct=ct: e.tensor_scalar(
            out=cacc[:], in0=cacc[:], scalar1=cbt[:, ct:ct + 1], scalar2=None, op0=OP.add))
        p.step("scalar", lambda e: e.activation(out=xe[:, 4:4 + TC], in_=cacc[:], func=AF.Silu))
        p.step("sync", lambda e, ct=ct: e.dma_start(out=xsls[ct * 128:(ct + 1) * 128, :],
                                                    in_=xe[:, 4:4 + TC]), dmas=1)
    for cc in range(NCH):
        p.step("sync", lambda e, cc=cc: e.dma_start(
            out=xe[:, 0:DI2].rearrange("p (a m) -> p a m", a=16),
            in_=xsls[:, cc * 128:(cc + 1) * 128].rearrange("(a p) m -> p a m", p=128)), dmas=1)
        for j in range(16):
            p.step("tensor", lambda e, j=j: e.transpose(pt1[:], xe[:, j * 128:(j + 1) * 128],
                                                        idt[:]))
            p.step("vector", lambda e, j=j: e.tensor_copy(xsltc[:, j * 128:(j + 1) * 128],
                                                          pt1[:]))
        p.step("sync", lambda e, cc=cc: e.dma_start(out=xslt[cc * 128:(cc + 1) * 128, :],
                                                    in_=xsltc[:]), dmas=1)
        p.step("vector", lambda e, cc=cc: e.tensor_tensor(
            out=xdtc[:].rearrange("p (h d2) -> p h d2", h=NH),
            in0=xsltc[:].rearrange("p (h d2) -> p h d2", h=NH),
            in1=dttmb[:, cc * NH:(cc + 1) * NH].rearrange("p (h o) -> p h o", o=1)
                .to_broadcast([128, NH, HD]), op=OP.mult))
        p.step("sync", lambda e, cc=cc: e.dma_start(out=xdts[cc * 128:(cc + 1) * 128, :],
                                                    in_=xdtc[:]), dmas=1)
    if full:
        ipf = inptt[:].bitcast(F32)
        hfa = ipf[:, 0:8192]
        ffa = ipf[:, 8192:8320]
        inc4 = ipf[:, 8320:8324]
        tg1 = ipf[:, 8324:10372]
        p.step("sync", lambda e: [e.dma_start(out=hfa.rearrange("p (a m) -> p a m", a=4),
                                              in_=hstall[:, :, :].rearrange("a p m -> p a m")),
                                  e.dma_start(out=ffa.rearrange("p (a m) -> p a m", a=4),
                                              in_=ftall[:, :, :].rearrange("a p m -> p a m")),
                                  e.dma_start(out=inc4[0:1, 0:4], in_=incr[:, :])], dmas=3)
        p.step("tensor", lambda e: e.matmul(pm[:, 0:4], onrw[:], inc4[0:1, 0:4],
                                            start=True, stop=True))
        p.step("vector", lambda e: e.tensor_copy(inc4, pm[:, 0:4]))
        p.step("vector", lambda e: e.memset(hst[:], 0.0))
        for cc in range(4):
            p.step("vector", lambda e, cc=cc: e.tensor_scalar(
                out=fmat[:], in0=ffa[:, cc * NH:(cc + 1) * NH], scalar1=inc4[:, cc:cc + 1],
                scalar2=None, op0=OP.mult))
            p.step("vector", lambda e, cc=cc: e.tensor_scalar(
                out=cdc[:], in0=inc4[:, cc:cc + 1].to_broadcast([128, NH]), scalar1=-1.0,
                scalar2=1.0, op0=OP.mult, op1=OP.add))
            p.step("vector", lambda e: e.tensor_tensor(out=fmat[:], in0=fmat[:], in1=cdc[:],
                                                       op=OP.add))
            p.step("vector", lambda e: e.tensor_tensor(
                out=hst[:].rearrange("p (h d2) -> p h d2", h=NH),
                in0=hst[:].rearrange("p (h d2) -> p h d2", h=NH),
                in1=fmat[:].rearrange("p (h o) -> p h o", o=1).to_broadcast([128, NH, HD]),
                op=OP.mult))
            p.step("vector", lambda e, cc=cc: e.tensor_scalar(
                out=tg1, in0=hfa[:, cc * NH * HD:(cc + 1) * NH * HD],
                scalar1=inc4[:, cc:cc + 1], scalar2=None, op0=OP.mult))
            p.step("vector", lambda e: e.tensor_tensor(out=hst[:], in0=hst[:], in1=tg1,
                                                       op=OP.add))
    else:
        p.step("vector", lambda e: e.memset(hst[:], 0.0))
    if full:
        woutt2 = inptt[:, 0:16 * D]
        p.step("gpsimd", lambda e: [e.dma_start(out=woutt2[:, i * D:(i + 1) * D],
                                                in_=outT[i * 128:(i + 1) * 128, :])
                                    for i in range(16)], dmas=16)
        p.step("vector", lambda e: e.memset(lhsT2[:], 1.0))
        p.step("vector", lambda e: e.memset(rhs2[:], 1.0))
        p.step("sync", lambda e: e.dma_start(out=row2[:], in_=nw2[:, :]), dmas=1)
        for ch in range(4):
            p.step("tensor", lambda e, ch=ch: e.matmul(
                pm[:], onrw[:], row2[0:1, ch * 512:(ch + 1) * 512], start=True, stop=True))
            p.step("vector", lambda e, ch=ch: e.tensor_copy(nwrep2[:, ch * 512:(ch + 1) * 512],
                                                            pm[:]))
    for c in range(NCH):
        t0 = c * 128
        dtcb = dttmb[:, c * NH:(c + 1) * NH]
        p.step("tensor", lambda e, dtcb=dtcb: e.matmul(pm[:, 0:NH], trut[:], dtcb,
                                                       start=True, stop=True))
        p.step("vector", lambda e: e.tensor_copy(cdc[:], pm[:, 0:NH]))
        p.step("vector", lambda e: e.tensor_tensor(out=acd[:], in0=cdc[:], in1=arep[:],
                                                   op=OP.mult))
        if full:
            p.step("scalar", lambda e: e.activation(out=epl[:], in_=acd[:], func=AF.Exp))
        p.step("tensor", lambda e, dtcb=dtcb: e.matmul(pm2[0:1, 0:NH], onescolb[:], dtcb,
                                                       start=True, stop=True))
        p.step("vector", lambda e: e.tensor_copy(cpad[0:1, 0:NH], pm2[0:1, 0:NH]))
        p.step("tensor", lambda e: e.matmul(pm[:, 0:NH], onrw[:], cpad[0:1, 0:NH],
                                            start=True, stop=True))
        p.step("vector", lambda e: e.tensor_copy(srep[:], pm[:, 0:NH]))
        p.step("vector", lambda e: e.tensor_tensor(out=wdtm[:], in0=srep[:], in1=cdc[:],
                                                   op=OP.subtract))
        p.step("vector", lambda e: e.tensor_tensor(out=wdtm[:], in0=wdtm[:], in1=arep[:],
                                                   op=OP.mult))
        p.step("scalar", lambda e: e.activation(out=wdtm[:], in_=wdtm[:], func=AF.Exp))
        p.step("vector", lambda e, c=c: e.tensor_tensor(out=wdtm[:], in0=wdtm[:],
                                                        in1=dttm[:, c * NH:(c + 1) * NH],
                                                        op=OP.mult))
        p.step("vector", lambda e: e.tensor_tensor(out=fmat[:], in0=srep[:], in1=arep[:],
                                                   op=OP.mult))
        p.step("scalar", lambda e: e.activation(out=fmat[:], in_=fmat[:], func=AF.Exp))
        p.step("sync", lambda e, t0=t0: e.dma_start(out=xdtc[:], in_=xdts[t0:t0 + 128, :]),
               dmas=1)
        if full:
            p.step("vector", lambda e: e.tensor_copy(hstb, hst[:]))
            p.step("vector", lambda e: e.tensor_copy(cpad[:, 0:NH], cdc[:]))
            p.step("tensor", lambda e: e.transpose(ptf[:], cpad[:], idtf[:]))
            p.step("vector", lambda e: e.tensor_copy(cdT[:], ptf[0:32, 0:128]))
            p.step("vector", lambda e: e.tensor_copy(cdTb[:], cdT[:]))
            p.step("vector", lambda e: e.tensor_scalar(out=cdT[:], in0=cdT[:], scalar1=-1.0,
                                                       scalar2=None, op0=OP.mult))
            p.step("vector", lambda e: e.tensor_copy(ncdTb[:], cdT[:]))
            p.step("sync", lambda e, t0=t0: [
                e.dma_start(out=xsltc[:], in_=xslt[t0:t0 + 128, :]),
                e.dma_start(out=zsc[:], in_=zss[t0:t0 + 128, :])], dmas=2)
            p.step("tensor", lambda e, t0=t0: e.matmul(
                pm2[:, 0:128], bfm[:, t0:t0 + 128], cfm[:, t0:t0 + 128],
                start=True, stop=True))
        p.step("tensor", lambda e, t0=t0: e.transpose(pt1[:], bfm[:, t0:t0 + 128], idt[:]))
        p.step("vector", lambda e: e.tensor_copy(btmf[:], pt1[:]))
        for grp in range(4):
            for hh in range(8):
                h = grp * 8 + hh
                if full:
                    p.step("gpsimd", lambda e, h=h: [
                        e.dma_start(out=lhsT2[0:1, :], in_=ncdTb[h:h + 1, :]),
                        e.dma_start(out=rhs2[1:2, :], in_=cdTb[h:h + 1, :])], dmas=2)
                    p.step("tensor", lambda e: e.matmul(pm[:, 0:128], lhsT2[:], rhs2[:],
                                                        start=True, stop=True))
                    p.step("vector", lambda e: e.scalar_tensor_tensor(
                        out=asb[:], in0=msk[:], scalar=1e30, in1=pm[:, 0:128],
                        op0=OP.mult, op1=OP.add))
                    p.step("scalar", lambda e, h=h: e.activation(
                        out=asb[:], in_=asb[:], func=AF.Exp, scale=arep[:, h:h + 1]))
                    p.step("vector", lambda e: e.tensor_tensor(out=ssb[:], in0=asb[:],
                                                               in1=pm2[:, 0:128], op=OP.mult))
                    p.step("tensor", lambda e, h=h, hh=hh: e.matmul(
                        py[:, hh * HD:(hh + 1) * HD], ssb[:], xdtc[:, h * HD:(h + 1) * HD],
                        start=True, stop=True))
                    p.step("tensor", lambda e, h=h, hh=hh, t0=t0: e.matmul(
                        pc[:, hh * HD:(hh + 1) * HD], cfm[:, t0:t0 + 128],
                        hstb[:, h * HD:(h + 1) * HD], start=True, stop=True))
                p.step("vector", lambda e, h=h: e.tensor_scalar(
                    out=bwb[:], in0=btmf[:], scalar1=wdtm[:, h:h + 1], scalar2=None,
                    op0=OP.mult))
                p.step("tensor", lambda e, h=h, hh=hh: e.matmul(
                    pcon[:, hh * HD:(hh + 1) * HD], bwb[:], xdtc[:, h * HD:(h + 1) * HD],
                    start=True, stop=True))
            g0 = grp * 8
            if full:
                p.step("vector", lambda e, g0=g0: e.tensor_tensor(
                    out=ycar[:].rearrange("p (h d2) -> p h d2", h=8),
                    in0=pc[:].rearrange("p (h d2) -> p h d2", h=8),
                    in1=epl[:, g0:g0 + 8].rearrange("p (h o) -> p h o", o=1)
                        .to_broadcast([128, 8, HD]), op=OP.mult))
                p.step("vector", lambda e, g0=g0: e.tensor_tensor(
                    out=dterm[:].rearrange("p (h d2) -> p h d2", h=8),
                    in0=xsltc[:, g0 * HD:(g0 + 8) * HD].rearrange("p (h d2) -> p h d2", h=8),
                    in1=dp8b[:, g0:g0 + 8].rearrange("p (h o) -> p h o", o=1)
                        .to_broadcast([128, 8, HD]), op=OP.mult))
                p.step("vector", lambda e, g0=g0: e.tensor_tensor(
                    out=ych[:, g0 * HD:(g0 + 8) * HD], in0=py[:], in1=ycar[:], op=OP.add))
                p.step("vector", lambda e, g0=g0: e.tensor_tensor(
                    out=ych[:, g0 * HD:(g0 + 8) * HD], in0=ych[:, g0 * HD:(g0 + 8) * HD],
                    in1=dterm[:], op=OP.add))
            p.step("vector", lambda e, g0=g0: e.tensor_tensor(
                out=hst[:, g0 * HD:(g0 + 8) * HD].rearrange("p (h d2) -> p h d2", h=8),
                in0=hst[:, g0 * HD:(g0 + 8) * HD].rearrange("p (h d2) -> p h d2", h=8),
                in1=fmat[:, g0:g0 + 8].rearrange("p (h o) -> p h o", o=1)
                    .to_broadcast([128, 8, HD]), op=OP.mult))
            p.step("vector", lambda e, g0=g0: e.tensor_tensor(
                out=hst[:, g0 * HD:(g0 + 8) * HD], in0=hst[:, g0 * HD:(g0 + 8) * HD],
                in1=pcon[:], op=OP.add))
        if full:
            p.step("scalar", lambda e: e.activation(out=sq[0:128, 0:1024], in_=ych[:, 0:1024],
                                                    func=AF.Square, accum_out=stats[:, 0:1]))
            p.step("scalar", lambda e: e.activation(out=sq[0:128, 0:1024], in_=ych[:, 1024:2048],
                                                    func=AF.Square, accum_out=stats[:, 1:2]))
            p.step("vector", lambda e: e.tensor_tensor(out=stats[:, 0:1], in0=stats[:, 0:1],
                                                       in1=stats[:, 1:2], op=OP.add))
            p.step("vector", lambda e: e.tensor_scalar(out=stats[:, 0:1], in0=stats[:, 0:1],
                                                       scalar1=1.0 / DI2, scalar2=1e-5,
                                                       op0=OP.mult, op1=OP.add))
            p.step("scalar", lambda e: e.activation(out=stats[:, 2:3], in_=stats[:, 0:1],
                                                    func=AF.Sqrt))
            p.step("vector", lambda e: e.reciprocal(stats[:, 3:4], stats[:, 2:3]))
            p.step("vector", lambda e: e.tensor_scalar(out=ych[:], in0=ych[:],
                                                       scalar1=stats[:, 3:4], scalar2=None,
                                                       op0=OP.mult))
            p.step("vector", lambda e: e.tensor_tensor(out=ych[:], in0=ych[:], in1=nwrep2[:],
                                                       op=OP.mult))
            p.step("vector", lambda e: e.tensor_copy(ygb[:], ych[:]))
            p.step("vector", lambda e: e.tensor_tensor(out=ygb[:], in0=ygb[:], in1=zsc[:],
                                                       op=OP.mult))
            for j in range(16):
                p.step("tensor", lambda e, j=j: e.transpose(pt1[:], ygb[:, j * 128:(j + 1) * 128],
                                                            idt[:]))
                p.step("vector", lambda e, j=j: e.tensor_copy(ygfm[:, j * 128:(j + 1) * 128],
                                                              pt1[:]))
            p.step("sync", lambda e, t0=t0: e.dma_start(out=xt[:], in_=x[t0:t0 + 128, :]),
                   dmas=1)
            for ch in range(2):
                def fo(e, ch=ch):
                    for i in range(16):
                        r = e.matmul(pm[:], ygfm[:, i * 128:(i + 1) * 128],
                                     woutt2[:, i * D + ch * 512: i * D + ch * 512 + 512],
                                     start=(i == 0), stop=(i == 15))
                    return r
                p.step("tensor", fo)
                p.step("vector", lambda e, ch=ch: e.tensor_tensor(
                    out=xt[:, ch * 512:(ch + 1) * 512], in0=xt[:, ch * 512:(ch + 1) * 512],
                    in1=pm[:], op=OP.add))
            p.step("sync", lambda e, t0=t0: e.dma_start(out=xout[t0:t0 + 128, :], in_=xt[:]),
                   dmas=1)
    if not full:
        p.step("tensor", lambda e: e.matmul(pm[0:1, 0:NH], onescolb[:], dttmb[:, 0:NH],
                                            start=True, stop=False))
        def fsum(e):
            for c in range(1, NCH):
                r = e.matmul(pm[0:1, 0:NH], onescolb[:], dttmb[:, c * NH:(c + 1) * NH],
                             start=False, stop=(c == NCH - 1))
            return r
        p.step("tensor", fsum)
        p.step("vector", lambda e: e.tensor_copy(cpad[0:1, 0:NH], pm[0:1, 0:NH]))
        p.step("tensor", lambda e: e.matmul(pm[:, 0:NH], onrw[:], cpad[0:1, 0:NH],
                                            start=True, stop=True))
        p.step("vector", lambda e: e.tensor_copy(srep[:], pm[:, 0:NH]))
        p.step("vector", lambda e: e.tensor_tensor(out=fmat[:], in0=srep[:], in1=arep[:],
                                                   op=OP.mult))
        p.step("scalar", lambda e: e.activation(out=fmat[:], in_=fmat[:], func=AF.Exp))
        p.step("sync", lambda e: [e.dma_start(out=d_hst[:, :], in_=hst[:]),
                                  e.dma_start(out=d_ft[:, :], in_=fmat[:])], dmas=2)
    return p.build()


def _shard(x_full):
    out = []
    for c in range(NCORE):
        b, q = c // CPB, c % CPB
        out.append(np.ascontiguousarray(x_full[b, q * TC:(q + 1) * TC]))
    return out


def kernel(**inp):
    g = lambda k: np.ascontiguousarray(np.asarray(inp[k], dtype=np.float32))
    hc = host_consts()
    s4c = s4_consts()
    m1c = m1_consts()
    m2c = m2_consts()
    onesrowf = np.ones((1, 128), np.float32)
    allones = np.ones((128, 128), np.float32)
    identf = np.eye(128, dtype=np.float32)
    onescolb = np.ones((128, 1), np.float32).astype(BF)
    x0 = g("L0_attn_a_x")
    shards = [np.ascontiguousarray(x0[c // CPB, (c % CPB) * TC:(c % CPB + 1) * TC])
              for c in range(NCORE)]
    cores = list(range(NCORE))

    nc = build_qkv()
    im = [{"x": shards[c], "wq": g("L0_attn_a_Wq"), "wk": g("L0_attn_a_Wk"),
           "wv": g("L0_attn_a_Wv"), "ident": hc["ident"]} for c in cores]
    r = run_bass_kernel_spmd(nc, im, core_ids=cores).results
    kf = {b: np.concatenate([r[b * CPB + q]["kf"] for q in range(CPB)], axis=1) for b in range(NB)}
    vt = {b: np.concatenate([r[b * CPB + q]["vt"] for q in range(CPB)], axis=0) for b in range(NB)}
    qf = [r[c]["qf"] for c in cores]

    idxs = [np.concatenate([np.arange((4 * i + q) * 512, (4 * i + q) * 512 + 512)
                            for i in range(8)]) for q in range(CPB)]
    qf_full = {b: np.concatenate([qf[b * CPB + q] for q in range(CPB)], axis=1)
               for b in range(NB)}
    kpos = np.arange(T, dtype=np.float32)[None, :]
    nc = build_flash()
    im = []
    for c in cores:
        b, q = c // CPB, c % CPB
        idx = idxs[q]
        im.append({"x": np.ascontiguousarray(x0[b][idx]),
                   "qf": np.ascontiguousarray(qf_full[b][:, idx]),
                   "kff": kf[b], "vtf": vt[b],
                   "gw": g("L0_ffn_gate_w"),
                   "qpos": np.ascontiguousarray(idx.reshape(TC // 128, 128).T
                                                .astype(np.float32)),
                   "kpos": kpos, "ident": hc["ident"]})
    r = run_bass_kernel_spmd(nc, im, core_ids=cores).results
    x1, xn1, sc1 = [None] * NCORE, [None] * NCORE, {}
    for b in range(NB):
        xo = np.empty((T, D), np.float32)
        xn = np.empty((T, D), BF)
        sc = np.empty((T, 8), np.float32)
        for q in range(CPB):
            xo[idxs[q]] = r[b * CPB + q]["xout"]
            xn[idxs[q]] = r[b * CPB + q]["xn1"]
            sc[idxs[q]] = r[b * CPB + q]["scores"]
        sc1[b] = sc
        for q in range(CPB):
            x1[b * CPB + q] = np.ascontiguousarray(xo[q * TC:(q + 1) * TC])
            xn1[b * CPB + q] = np.ascontiguousarray(xn[q * TC:(q + 1) * TC])

    nc_moe_s4 = build_moedense(tail="s4pre")
    im = [{"x": x1[c], "xn": xn1[c], "scoresf": sc1[c // CPB],
           "scoresm": np.ascontiguousarray(sc1[c // CPB][(c % CPB) * TC:(c % CPB + 1) * TC]),
           "w1": g("L0_ffn_w1"), "w2": g("L0_ffn_w2"), "ident": hc["ident"],
           "allones": allones,
           "winT": np.ascontiguousarray(g("L0_attn_b_W_in").T),
           "b_in": g("L0_attn_b_b_in")[None, :], "gam": g("L0_attn_b_gamma")[None, :],
           "bet": g("L0_attn_b_beta")[None, :], "onesrowf": onesrowf} for c in cores]
    r = run_bass_kernel_spmd(nc_moe_s4, im, core_ids=cores).results
    x2 = [r[c]["xout"] for c in cores]
    u2 = [r[c]["u2"] for c in cores]
    csum = [r[c]["colsum"] for c in cores]

    nc = build_s4conv()
    im = []
    for c in cores:
        b, q = c // CPB, c % CPB
        tail = np.zeros((256, D), np.float32) if q == 0 else u2[c - 1][TC - 256:]
        pmkv = np.zeros((4, 1), np.float32)
        pmkv[:q, 0] = 1.0
        im.append({
            "x": x2[c], "u2e": np.concatenate([tail, u2[c]], axis=0),
            "colsums": np.concatenate([csum[b * CPB + qq] for qq in range(CPB)], axis=0),
            "pmk": pmkv,
            "are": g("L0_attn_b_A_re")[:, None], "aim": g("L0_attn_b_A_im")[:, None],
            "bre": g("L0_attn_b_B_re")[:, None], "bim": g("L0_attn_b_B_im")[:, None],
            "cre": g("L0_attn_b_C_re"), "cim": g("L0_attn_b_C_im"),
            "dre": g("L0_attn_b_D_re")[None, :], "dim": g("L0_attn_b_D_im")[None, :],
            "lmat": s4c["lmat"], "fxc": s4c["fxc"], "fxs": s4c["fxs"],
            "fkc": s4c["fkc"], "fks": s4c["fks"], "gre": s4c["gre"], "gim": s4c["gim"],
            "gw": g("L0_ffn_gate_w"), "ident": hc["ident"], "triu": hc["triu"],
            "onesrowf": onesrowf})
    r = run_bass_kernel_spmd(nc, im, core_ids=cores).results
    x3 = [r[c]["xout"] for c in cores]
    xn2 = [r[c]["xn1"] for c in cores]
    sc2 = {b: np.concatenate([r[b * CPB + q]["scores"] for q in range(CPB)], axis=0)
           for b in range(NB)}

    nc = build_moedense(tail="none")
    im = [{"x": x3[c], "xn": xn2[c], "scoresf": sc2[c // CPB],
           "scoresm": np.ascontiguousarray(sc2[c // CPB][(c % CPB) * TC:(c % CPB + 1) * TC]),
           "w1": g("L0_ffn_w1"), "w2": g("L0_ffn_w2"), "ident": hc["ident"],
           "allones": allones} for c in cores]
    r = run_bass_kernel_spmd(nc, im, core_ids=cores).results
    x4 = [r[c]["xout"] for c in cores]

    m1base = {
        "nw": g("L1_attn_a_norm_w")[None, :], "nb": g("L1_attn_a_norm_b")[None, :],
        "win": g("L1_attn_a_W_in"), "wgt": g("L1_attn_a_W_gate"), "wdt": g("L1_attn_a_W_dt"),
        "bdt": np.ascontiguousarray(g("L1_attn_a_b_dt").reshape(16, 128).T),
        "wB": g("L1_attn_a_W_B"), "wC": g("L1_attn_a_W_C"),
        "alog": np.ascontiguousarray(g("L1_attn_a_A_log").reshape(-1).reshape(256, 128).T),
        "selc": m1c["selt"], "sumselc": m1c["sumsel"], "sel16c": m1c["sel16"],
        "ident": hc["ident"], "onesrowf": onesrowf}
    nc = build_m1(False)
    im = [dict(m1base, x=x4[c]) for c in cores]
    r = run_bass_kernel_spmd(nc, im, core_ids=cores).results
    nc = build_m1(True)
    m1fbase = {k: m1base[k] for k in ("alog", "selc", "sumselc", "sel16c", "onesrowf")}
    im = []
    for c in cores:
        b, q = c // CPB, c % CPB
        incv = np.zeros((1, 4), np.float32)
        incv[0, :q] = 1.0
        im.append(dict(m1fbase, x=x4[c], wout=g("L1_attn_a_W_out"),
                       dts=r[c]["dts"], dzs=r[c]["dzs"], gts=r[c]["gts"],
                       bcr=r[c]["bcr"], xnod=r[c]["xnod"],
                       hfall=np.stack([r[b * CPB + qq]["hfin"] for qq in range(CPB)]),
                       fall=np.stack([r[b * CPB + qq]["ff"] for qq in range(CPB)]),
                       incr=incv))
    r = run_bass_kernel_spmd(nc, im, core_ids=cores).results
    x5 = [r[c]["xout"] for c in cores]

    ffnbase = {
        "lng": g("L1_ffn_ln_g")[None, :], "lnb": g("L1_ffn_ln_b")[None, :],
        "rw": g("L1_ffn_rW"), "rb": g("L1_ffn_rb")[None, :], "w1": g("L1_ffn_W1"),
        "b1r": np.ascontiguousarray(np.stack([g("L1_ffn_b1")[e].reshape(16, 128).T
                                              for e in range(8)]).reshape(8 * 128, 16)),
        "w2": g("L1_ffn_W2"), "b2": g("L1_ffn_b2"),
        "ident": hc["ident"], "identf": identf, "onesrowf": onesrowf}
    nc_ffn2 = build_ffn2()
    im = [dict(ffnbase, x=x5[c]) for c in cores]
    r = run_bass_kernel_spmd(nc_ffn2, im, core_ids=cores).results
    x6 = [r[c]["xout"] for c in cores]

    inpT = np.ascontiguousarray(g("L1_attn_b_in_proj_w").T)
    outT = np.ascontiguousarray(g("L1_attn_b_out_proj_w").T)
    m2base = {
        "inpT": inpT,
        "cw": np.ascontiguousarray(g("L1_attn_b_conv_w").reshape(16, 128, 4)
                                   .transpose(1, 0, 2).reshape(128, 64)),
        "cb": np.ascontiguousarray(g("L1_attn_b_conv_b").reshape(16, 128).T),
        "alog": g("L1_attn_b_A_log")[None, :], "dpr": g("L1_attn_b_Dp")[None, :],
        "dtbias": g("L1_attn_b_dt_bias")[:, None],
        "mask01": m2c["mask01"], "triu": hc["triu"], "ident": hc["ident"],
        "identf": identf, "onesrowf": onesrowf}
    nc = build_m2tails()
    im = [{"x": x6[c], "inpT": inpT, "ident": hc["ident"]} for c in cores]
    r = run_bass_kernel_spmd(nc, im, core_ids=cores).results
    tails = [np.zeros((2048, 3), np.float32) if c % CPB == 0 else r[c - 1]["tails"]
             for c in cores]
    nc = build_m2(False)
    im = [dict(m2base, x=x6[c], tails=tails[c]) for c in cores]
    r = run_bass_kernel_spmd(nc, im, core_ids=cores).results
    nc = build_m2(True)
    im = []
    for c in cores:
        b, q = c // CPB, c % CPB
        incv = np.zeros((1, 4), np.float32)
        incv[0, :q] = 1.0
        im.append(dict(m2base, x=x6[c], tails=tails[c],
                       nw2=g("L1_attn_b_norm_w")[None, :], outT=outT,
                       hstall=np.stack([r[b * CPB + qq]["hst"] for qq in range(CPB)]),
                       ftall=np.stack([r[b * CPB + qq]["ft"] for qq in range(CPB)]),
                       incr=incv))
    r = run_bass_kernel_spmd(nc, im, core_ids=cores).results
    x7 = [r[c]["xout"] for c in cores]

    im = [dict(ffnbase, x=x7[c]) for c in cores]
    r = run_bass_kernel_spmd(nc_ffn2, im, core_ids=cores).results
    out = np.empty((NB, T, D), np.float32)
    for c in cores:
        b, q = c // CPB, c % CPB
        out[b, q * TC:(q + 1) * TC] = r[c]["xout"]
    return out
